# version 03
# baseline (speedup 1.0000x reference)
import numpy as np
import ml_dtypes
from contextlib import ExitStack

import concourse.bass as bass
import concourse.mybir as mybir
from concourse.bass_utils import run_bass_kernel_spmd

F32 = mybir.dt.float32
BF16 = mybir.dt.bfloat16
I32 = mybir.dt.int32
AF = mybir.ActivationFunctionType
ALU = mybir.AluOpType
AX = mybir.AxisListType.X

B, S, DM, H, DH, INNER = 4, 4096, 1024, 16, 64, 4096
NT = 2048
CAPS = [8, 16, 24, 32]
JS = {0: [0, 3, 4, 7], 1: [1, 2, 5, 6]}
NKB = S // 128
EPS = 1e-5
EXPB = -10.0
RC = 256.0
DEBUG_TAPS = False
PHASE_LIMIT = 99


class Prog:
    ENGS = ("sync", "gpsimd", "vector", "scalar", "tensor")
    LANES = {"sL": 4, "sO": 2, "sGD": 2}

    def __init__(self, nc, es):
        self.nc = nc
        self.q = {e: [] for e in self.ENGS}
        self.sem = {}
        for name in ("sG", "sV", "sS", "sT"):
            self.sem[name] = [es.enter_context(nc.semaphore(name)), 0]
        for g, n in self.LANES.items():
            for i in range(n):
                nm = "%s%d" % (g, i)
                self.sem[nm] = [es.enter_context(nc.semaphore(nm)), 0]
        self.rr = {g: 0 for g in self.LANES}

    def _w(self, eng, dep):
        for d in dep:
            if not d:
                continue
            sn, val = d
            if val <= 0:
                continue
            h = self.sem[sn][0]
            self.q[eng].append(lambda e, h=h, v=val: e.wait_ge(h, v))

    AUTODRAIN = ("vector", "gpsimd", "scalar")

    def emit(self, eng, fn, dep=(), inc=None, n=1):
        self._w(eng, dep)
        if eng in self.AUTODRAIN:
            self.q[eng].append(lambda e: e.drain())
        if inc is None:
            self.q[eng].append(fn)
            return None
        self.sem[inc][1] += n
        tgt = self.sem[inc][1]
        h = self.sem[inc][0]

        def run(e, fn=fn, h=h, n=n):
            fn(e).then_inc(h, n)
        self.q[eng].append(run)
        return tgt

    def drain(self, eng):
        self.q[eng].append(lambda e: e.drain())

    def val(self, semname):
        return self.sem[semname][1]

    def fence(self, group):
        return [("%s%d" % (group, i), self.sem["%s%d" % (group, i)][1])
                for i in range(self.LANES[group])
                if self.sem["%s%d" % (group, i)][1] > 0]

    def _dma(self, eng, fn, dep, group):
        i = self.rr[group]
        self.rr[group] = (i + 1) % self.LANES[group]
        lane = "%s%d" % (group, i)
        prev = self.sem[lane][1]
        dep = list(dep) + [(lane, prev)]
        tgt = self.emit(eng, fn, dep=dep, inc=lane, n=16)
        return (lane, tgt)

    def dma(self, eng, dst, src, dep=(), sem=None, **kw):
        group = sem or ("sL" if eng == "sync" else "sGD")
        return self._dma(eng, lambda e: e.dma_start(dst, src, **kw),
                         dep, group)

    def dmat(self, eng, dst, src, dep=(), sem="sL"):
        return self._dma(eng, lambda e: e.dma_start_transpose(dst, src),
                         dep, group=sem)


def build_nc():
    nc = bass.Bass("TRN2", target_bir_lowering=False)
    es = ExitStack()

    xkv_h = nc.dram_tensor("xkv", [S, DM], F32, kind="ExternalInput")
    xq_h = nc.dram_tensor("xq", [NT, DM], F32, kind="ExternalInput")
    wqkv_h = nc.dram_tensor("wqkv", [DM, 3 * DM], F32, kind="ExternalInput")
    wo_h = nc.dram_tensor("wo", [DM, DM], F32, kind="ExternalInput")
    wg_h = nc.dram_tensor("wg", [INNER, DM], F32, kind="ExternalInput")
    wu_h = nc.dram_tensor("wu", [INNER, DM], F32, kind="ExternalInput")
    wd_h = nc.dram_tensor("wd", [DM, INNER], F32, kind="ExternalInput")
    mask_h = nc.dram_tensor("masks", [128, 4 * 8 * 512], BF16,
                            kind="ExternalInput")
    pck_h = nc.dram_tensor("poscolk", [128, NKB], F32, kind="ExternalInput")
    pcq_h = nc.dram_tensor("poscolq", [128, 16], F32, kind="ExternalInput")
    frac_h = nc.dram_tensor("fracb", [128, 32], F32, kind="ExternalInput")
    idn_h = nc.dram_tensor("ident", [128, 128], BF16, kind="ExternalInput")
    out_h = nc.dram_tensor("out", [NT, DM], F32, kind="ExternalOutput")

    kd = "ExternalOutput" if DEBUG_TAPS else "Internal"
    wgq_h = nc.dram_tensor("wgq", [INNER, DM], BF16, kind=kd)
    wuq_h = nc.dram_tensor("wuq", [INNER, DM], BF16, kind=kd)
    wdq_h = nc.dram_tensor("wdq", [DM, INNER], BF16, kind=kd)
    kt2_h = nc.dram_tensor("kt2", [128, NKB * 512], BF16, kind=kd)
    v2_h = nc.dram_tensor("v2", [128, NKB * 520], BF16, kind=kd)
    ctxA_h = nc.dram_tensor("ctxA", [64, 4 * 8 * 512], BF16, kind=kd)
    x1_h = nc.dram_tensor("x1", [NT, DM], F32, kind=kd)
    xqq_h = nc.dram_tensor("xqq", [NT, DM], BF16, kind=kd)
    hq_h = nc.dram_tensor("hqs", [NT, INNER], BF16, kind=kd)

    lp = es.enter_context(nc.psum_tensor("lp", [128, 2048], F32))
    cp = es.enter_context(nc.psum_tensor("cp", [128, 1024], F32))
    db = es.enter_context(nc.psum_tensor("db", [128, 512], F32))
    tp = es.enter_context(nc.psum_tensor("tp", [128, 1024], BF16))

    sb = lambda name, shape, dt: es.enter_context(
        nc.sbuf_tensor(name, shape, dt))
    idn = sb("idn", [128, 128], BF16)
    onesf = sb("onesf", [128, 128], F32)
    cset = sb("cset", [128, 12], F32)
    pck = sb("pck", [128, NKB], F32)
    pcq = sb("pcq", [128, 16], F32)
    frac = sb("frac", [128, 32], F32)
    stats = sb("stats", [128, 16], F32)
    acc3 = sb("acc3", [128, 4], F32)
    thr3 = sb("thr3", [128, 4], F32)
    nthr3 = sb("nthr3", [128, 4], F32)
    wmean3 = sb("wmean3", [128, 4], F32)
    sxinv = sb("sxinv", [128, 16], F32)
    sq = sb("sq", [128, 1024], F32)
    ra = sb("ra", [128, 512], F32)
    rb = sb("rb", [128, 512], F32)
    lns = sb("lns", [128, 1024], F32)
    qt_ = sb("qt", [128, 1024], F32)
    qi_ = sb("qi", [128, 1024], I32)
    qd_ = sb("qd", [128, 1024], F32)
    qb_ = sb("qb", [128, 1024], BF16)

    P = Prog(nc, es)

    def matmul(out, lhsT, rhs, start, stop, dep=(), inc=None):
        return P.emit("tensor",
                      lambda e: e.matmul(out, lhsT, rhs, start=start,
                                         stop=stop),
                      dep=dep, inc=inc)

    def ln_chain(src_ap, dst_ap, dep_v, dep_s, psum=False):
        if psum:
            cv = P.emit("vector",
                        lambda e, o=dst_ap, i=src_ap: e.tensor_copy(o, i),
                        dep=dep_v, inc="sV")
            dep_v = [("sV", cv)]
            dep_s = [("sV", cv)]
            src_ap = dst_ap
        P.emit("vector", lambda e: e.reduce_sum(stats[:, 0:1], src_ap,
                                                axis=AX), dep=dep_v)
        s1 = P.emit("scalar",
                    lambda e: e.activation(sq[:], src_ap, AF.Square,
                                           accum_out=stats[:, 1:2]),
                    dep=dep_s, inc="sS")
        P.drain("vector")
        P.emit("vector", lambda e: e.tensor_scalar(
            stats[:, 2:3], stats[:, 0:1], 1.0 / DM, None, ALU.mult))
        P.emit("vector", lambda e: e.tensor_scalar(
            stats[:, 3:4], stats[:, 1:2], 1.0 / DM, EPS, ALU.mult, ALU.add),
            dep=[("sS", s1)])
        P.drain("vector")
        P.emit("vector", lambda e: e.tensor_mul(
            stats[:, 4:5], stats[:, 2:3], stats[:, 2:3]))
        P.drain("vector")
        P.emit("vector", lambda e: e.tensor_sub(
            stats[:, 5:6], stats[:, 3:4], stats[:, 4:5]))
        P.drain("vector")
        v2t = P.emit("vector", lambda e: e.reciprocal(stats[:, 6:7],
                                                      stats[:, 5:6]),
                     inc="sV")
        s2 = P.emit("scalar", lambda e: e.activation(
            stats[:, 7:8], stats[:, 6:7], AF.Sqrt), dep=[("sV", v2t)],
            inc="sS")
        vt = P.emit("vector", lambda e: e.tensor_scalar(
            dst_ap, src_ap, stats[:, 2:3], stats[:, 7:8],
            ALU.subtract, ALU.mult), dep=[("sS", s2)], inc="sV")
        return vt

    def transpose8(src_sb, dep_t, dep_v, out_ap, in_ap):
        tt = None
        for j in range(8):
            tt = P.emit("tensor", lambda e, j=j: e.transpose(
                tp[:, j * 128:(j + 1) * 128],
                src_sb[:, j * 128:(j + 1) * 128], idn[:]),
                dep=(dep_t if j == 0 else ()),
                inc="sT" if j == 7 else None)
        vt = P.emit("vector", lambda e: e.tensor_copy(out_ap, in_ap),
                    dep=[("sT", tt)] + list(dep_v), inc="sV")
        return vt

    aq_state = {"g": 0}

    def act_quant_chain(src_ap, n, s_ap, dep_v):
        t1 = P.emit("vector", lambda e: e.tensor_scalar(
            qt_[:, 0:n], src_ap, s_ap, RC, ALU.mult, ALU.add),
            dep=list(dep_v) + [("sG", aq_state["g"])], inc="sV")
        g1 = P.emit("gpsimd", lambda e: e.tensor_copy(qi_[:, 0:n],
                                                      qt_[:, 0:n]),
                    dep=[("sV", t1)], inc="sG")
        P.drain("gpsimd")
        g2 = P.emit("gpsimd", lambda e: e.tensor_sub(
            qd_[:, 0:n], qt_[:, 0:n], qi_[:, 0:n]), inc="sG")
        aq_state["g"] = g2
        P.emit("vector", lambda e: e.scalar_tensor_tensor(
            qt_[:, 0:n], qd_[:, 0:n], 0.5, qi_[:, 0:n], ALU.is_ge, ALU.add),
            dep=[("sG", g2)])
        P.drain("vector")
        v3 = P.emit("vector", lambda e: e.tensor_scalar(
            qb_[:, 0:n], qt_[:, 0:n], RC, None, ALU.subtract), inc="sV")
        return v3

    def rope_block(src_psum_off, cos_sb, sin_sb, toff, tw, dst_sb, dep_first):
        in1 = bass.AP(lp, src_psum_off, [[2048, 128], [64, 16], [1, 32]])
        in2 = bass.AP(lp, src_psum_off + 32, [[2048, 128], [64, 16], [1, 32]])
        cs = bass.AP(cos_sb, toff * 32, [[tw * 32, 128], [0, 16], [1, 32]])
        sn = bass.AP(sin_sb, toff * 32, [[tw * 32, 128], [0, 16], [1, 32]])
        o1 = bass.AP(dst_sb, 0, [[1024, 128], [64, 16], [1, 32]])
        o2 = bass.AP(dst_sb, 32, [[1024, 128], [64, 16], [1, 32]])
        P.emit("vector", lambda e: e.tensor_tensor(ra[:], in1, cs, ALU.mult),
               dep=dep_first)
        P.emit("vector", lambda e: e.tensor_tensor(rb[:], in2, sn, ALU.mult))
        P.drain("vector")
        P.emit("vector", lambda e: e.tensor_sub(o1, ra[:], rb[:]))
        P.emit("vector", lambda e: e.tensor_tensor(ra[:], in2, cs, ALU.mult))
        P.emit("vector", lambda e: e.tensor_tensor(rb[:], in1, sn, ALU.mult))
        P.drain("vector")
        return P.emit("vector", lambda e: e.tensor_add(o2, ra[:], rb[:]),
                      inc="sV")

    esA = ExitStack()
    sbA = lambda name, shape, dt: esA.enter_context(
        nc.sbuf_tensor(name, shape, dt))
    kTc = sbA("kTc", [128, NKB * 512], BF16)
    vc = sbA("vc", [128, NKB * 520], BF16)
    xin = sbA("xin", [128, 2 * 1024], F32)
    qT = [sbA("qT%d" % s_, [128, 8 * 512], BF16) for s_ in range(4)]
    esA2 = ExitStack()
    sbA2 = lambda name, shape, dt: esA2.enter_context(
        nc.sbuf_tensor(name, shape, dt))
    wqq = sbA2("wqq", [128, 8 * 1024], BF16)
    xn = sbA2("xn", [128, 2 * 1024], BF16)
    xnT = sbA2("xnT", [128, 2 * 1024], BF16)
    ktm = sbA2("ktm", [128, 1024], BF16)
    cosq = sbA2("cosq", [128, 16 * 32], F32)
    sinq = sbA2("sinq", [128, 16 * 32], F32)
    esA1 = ExitStack()
    sbA1 = lambda name, shape, dt: esA1.enter_context(
        nc.sbuf_tensor(name, shape, dt))
    wkv = sbA1("wkv", [128, 8 * 2048], BF16)
    kstg = sbA1("kstg", [128, 512], BF16)
    vstg = sbA1("vstg", [128, 520], BF16)
    wst = sbA1("wst", [128, 1024], F32)
    cosk = sbA1("cosk", [128, NKB * 32], BF16)
    sink = sbA1("sink", [128, NKB * 32], BF16)

    lid = P.dma("sync", idn[:], idn_h.ap())
    lpk = P.dma("sync", pck[:], pck_h.ap())
    lpq = P.dma("sync", pcq[:], pcq_h.ap())
    lfr = P.dma("sync", frac[:], frac_h.ap())

    P.emit("gpsimd", lambda e: e.memset(onesf[:], 1.0))
    P.emit("gpsimd", lambda e: e.memset(cset[:, 0:1], -1.0))
    P.emit("gpsimd", lambda e: e.memset(cset[:, 5:6], EXPB))
    P.emit("gpsimd", lambda e: e.memset(cset[:, 6:7], 0.125))
    P.emit("gpsimd", lambda e: e.memset(sq[:, 0:1], 10000.0))
    P.emit("gpsimd", lambda e: e.memset(sq[:, 1:2], 1.0))
    P.emit("gpsimd", lambda e: e.memset(acc3[:], 0.0))
    P.drain("gpsimd")
    g0 = P.emit("gpsimd", lambda e: e.engine_nop(), inc="sG")

    s_ln = P.emit("scalar", lambda e: e.activation(
        cset[:, 1:2], sq[:, 0:1], AF.Ln), dep=[("sG", g0)], inc="sS")
    s_at = P.emit("scalar", lambda e: e.activation(
        cset[:, 7:8], sq[:, 1:2], AF.Arctan), inc="sS")
    P.emit("vector", lambda e: e.tensor_scalar(
        cset[:, 2:3], cset[:, 7:8], 2.0, None, ALU.mult), dep=[("sS", s_at)])
    P.emit("vector", lambda e: e.tensor_scalar(
        cset[:, 3:4], cset[:, 7:8], 8.0, None, ALU.mult))
    P.emit("vector", lambda e: e.tensor_scalar(
        cset[:, 8:9], cset[:, 7:8], 4.0, None, ALU.mult))
    P.emit("vector", lambda e: e.tensor_scalar(
        cset[:, 9:10], cset[:, 7:8], -4.0, None, ALU.mult))
    P.emit("vector", lambda e: e.tensor_scalar(
        cset[:, 10:11], cset[:, 7:8], -8.0, None, ALU.mult))
    v_2p = P.emit("vector", lambda e: e.reciprocal(cset[:, 4:5],
                                                   cset[:, 3:4]), inc="sV")
    v_fl = P.emit("vector", lambda e: e.tensor_scalar(
        sq[:, 32:64], frac[:], cset[:, 1:2], None, ALU.mult),
        dep=[lfr, ("sS", s_ln), ("sV", v_2p)], inc="sV")
    s_it = P.emit("scalar", lambda e: e.activation(
        frac[:], sq[:, 32:64], AF.Exp, scale=cset[:, 0:1]),
        dep=[("sV", v_fl)], inc="sS")

    def build_tables(n_tiles, pc_sb, cos_sb, sin_sb, dep0):
        deps = list(dep0)
        st = 0
        for t in range(n_tiles):
            P.emit("vector", lambda e, t=t: e.tensor_scalar(
                ra[:, 0:32], frac[:], pc_sb[:, t:t + 1], None, ALU.mult),
                dep=deps + [("sS", st)])
            deps = []
            P.drain("vector")
            P.emit("vector", lambda e: e.tensor_scalar(
                ra[:, 32:64], ra[:, 0:32], 1.0, cset[:, 2:3],
                ALU.mult, ALU.add))
            P.drain("vector")
            vm = P.emit("vector", lambda e: e.tensor_scalar(
                rb[:, 0:64], ra[:, 0:64], cset[:, 4:5], None, ALU.mult),
                inc="sV")
            gk = P.emit("gpsimd", lambda e: e.tensor_copy(
                qi_[:, 0:64], rb[:, 0:64]), dep=[("sV", vm)], inc="sG")
            P.emit("vector", lambda e: e.scalar_tensor_tensor(
                rb[:, 0:64], qi_[:, 0:64], cset[:, 10:11], ra[:, 0:64],
                ALU.mult, ALU.add), dep=[("sG", gk)])
            P.emit("vector", lambda e: e.tensor_scalar(
                rb[:, 64:128], rb[:, 0:64], cset[:, 8:9], None, ALU.is_gt))
            P.emit("vector", lambda e: e.scalar_tensor_tensor(
                rb[:, 0:64], rb[:, 64:128], cset[:, 10:11], rb[:, 0:64],
                ALU.mult, ALU.add))
            P.emit("vector", lambda e: e.tensor_scalar(
                rb[:, 64:128], rb[:, 0:64], cset[:, 9:10], None, ALU.is_lt))
            vr = P.emit("vector", lambda e: e.scalar_tensor_tensor(
                rb[:, 0:64], rb[:, 64:128], cset[:, 3:4], rb[:, 0:64],
                ALU.mult, ALU.add), inc="sV")
            P.emit("scalar", lambda e, t=t: e.activation(
                sin_sb[:, t * 32:(t + 1) * 32], rb[:, 0:32], AF.Sin),
                dep=[("sV", vr)])
            st = P.emit("scalar", lambda e, t=t: e.activation(
                cos_sb[:, t * 32:(t + 1) * 32], rb[:, 32:64], AF.Sin),
                inc="sS")
        return st

    s_tabk = build_tables(NKB, pck, cosk, sink,
                          [("sS", s_it), lpk, lpq, lid])
    s_tabq = build_tables(16, pcq, cosq, sinq, [('sS', s_tabk)])


    for c in range(8):
        P.dma("gpsimd", wkv[:, c * 2048:(c + 1) * 2048],
              wqkv_h.ap()[c * 128:(c + 1) * 128, 1024:3072])
        P.dma("gpsimd", wqq[:, c * 1024:(c + 1) * 1024],
              wqkv_h.ap()[c * 128:(c + 1) * 128, 0:1024])
    gw_done = P.fence("sGD")

    P.emit("gpsimd", lambda e: e.memset(
        bass.AP(vc, 64, [[NKB * 520, 128], [520, NKB], [65, 8]]), 1.0))
    P.emit("gpsimd", lambda e: e.memset(
        bass.AP(vstg, 64, [[520, 128], [65, 8]]), 1.0))
    P.drain("gpsimd")
    g_ones = P.emit("gpsimd", lambda e: e.engine_nop(), inc="sG")

    last_abs = 0
    last_ga = g_ones
    for wh, wi in ((wg_h, 0), (wu_h, 1), (wd_h, 2)):
        wre = wh.ap().rearrange("(a p) f -> p a f", p=128)
        n_a = wre.shape[1]
        n_f = wre.shape[2] // 1024
        chunks = [(ai, fj) for ai in range(n_a) for fj in range(n_f)]
        for ch, (ai, fj) in enumerate(chunks):
            ld = P.dma("sync", wst[:],
                       wre[:, ai, fj * 1024:(fj + 1) * 1024],
                       dep=[("sS", last_abs)])
            last_abs = P.emit("scalar", lambda e: e.activation(
                sq[:], wst[:], AF.Abs,
                accum_out=stats[:, 8:9]), dep=[ld, ("sG", last_ga)],
                inc="sS")
            last_ga = P.emit("gpsimd", lambda e, wi=wi: e.tensor_tensor(
                acc3[:, wi:wi + 1], acc3[:, wi:wi + 1], stats[:, 8:9],
                ALU.add), dep=[("sS", last_abs), ("sG", last_ga)], inc="sG")
            P.drain("gpsimd")
    mt = matmul(db[0:1, 0:4], onesf[:, 0:1], acc3[:], start=True, stop=True,
                dep=[("sG", last_ga)], inc="sT")
    P.emit("vector", lambda e: e.tensor_scalar(
        stats[0:1, 8:12], db[0:1, 0:4], 1.0 / (INNER * DM), 1e-5,
        ALU.mult, ALU.max), dep=[("sT", mt)])
    P.drain("vector")
    v_thr = P.emit("vector", lambda e: e.tensor_scalar(
        stats[0:1, 12:16], stats[0:1, 8:12], 0.5, None, ALU.mult), inc="sV")
    m_b1 = matmul(db[:, 0:4], onesf[0:1, 0:128], stats[0:1, 12:16],
                  start=True, stop=True, dep=[("sV", v_thr)], inc="sT")
    vb1 = P.emit("vector", lambda e: e.tensor_copy(thr3[:], db[:, 0:4]),
                 dep=[("sT", m_b1)], inc="sV")
    m_b2 = matmul(db[:, 0:4], onesf[0:1, 0:128], stats[0:1, 8:12],
                  start=True, stop=True, dep=[("sV", vb1)], inc="sT")
    vb2 = P.emit("vector", lambda e: e.tensor_copy(wmean3[:], db[:, 0:4]),
                 dep=[("sT", m_b2)], inc="sV")
    P.drain("vector")
    v_thrb = P.emit("vector", lambda e: e.tensor_scalar(
        nthr3[:], thr3[:], -1.0, None, ALU.mult), inc="sV")

    last_store = None
    last_vq = 0
    for wh, wq_out, wi in ((wg_h, wgq_h, 0), (wu_h, wuq_h, 1),
                           (wd_h, wdq_h, 2)):
        wre = wh.ap().rearrange("(a p) f -> p a f", p=128)
        wqre = wq_out.ap().rearrange("(a p) f -> p a f", p=128)
        n_a = wre.shape[1]
        n_f = wre.shape[2] // 1024
        chunks = [(ai, fj) for ai in range(n_a) for fj in range(n_f)]
        for ch, (ai, fj) in enumerate(chunks):
            ld = P.dma("sync", wst[:],
                       wre[:, ai, fj * 1024:(fj + 1) * 1024],
                       dep=[("sV", last_vq), ("sS", last_abs)])
            g1 = P.emit("gpsimd", lambda e, wi=wi: e.tensor_scalar(
                qd_[:, 0:1024], wst[:],
                nthr3[:, wi:wi + 1], None, ALU.is_le),
                dep=[ld, ("sV", v_thrb), last_store],
                inc="sG")
            last_vq = P.emit("vector", lambda e, wi=wi:
                             e.scalar_tensor_tensor(
                                 qb_[:, 0:1024],
                                 wst[:],
                                 thr3[:, wi:wi + 1], qd_[:, 0:1024],
                                 ALU.is_ge, ALU.subtract),
                             dep=[("sG", g1)], inc="sV")
            last_store = P.dma("sync",
                               wqre[:, ai, fj * 1024:(fj + 1) * 1024],
                               qb_[:, 0:1024], dep=[("sV", last_vq)],
                               sem="sO")
    wq_ready = last_store
    if PHASE_LIMIT <= 2:
        P._w("sync", P.fence("sO") + P.fence("sL") + P.fence("sGD"))
        return _finish(nc, P, es)

    tp_free_v = 0
    lp_free = [("sV", 0)]
    ln_prev = 0
    st_k_hist = {}
    st_v_hist = {}
    for kb in range(NKB):
        pa = kb % 2
        ld = P.dma("sync", xin[:, pa * 1024:(pa + 1) * 1024],
                   xkv_h.ap()[kb * 128:(kb + 1) * 128, :],
                   dep=[("sV", ln_prev)])
        vt = ln_chain(xin[:, pa * 1024:(pa + 1) * 1024],
                      xn[:, pa * 1024:(pa + 1) * 1024],
                      dep_v=[ld], dep_s=[ld])
        ln_prev = vt
        vtr = transpose8(xn[:, pa * 1024:(pa + 1) * 1024],
                         dep_t=[("sV", vt), ("sV", tp_free_v)] + gw_done,
                         dep_v=[],
                         out_ap=xnT[:, pa * 1024:(pa + 1) * 1024],
                         in_ap=tp[:])
        tp_free_v = vtr
        t_kv = None
        for ni in range(4):
            for c in range(8):
                dd = ([("sV", vtr)] + lp_free) if (ni == 0 and c == 0) else ()
                t_kv = matmul(
                    lp[:, ni * 512:(ni + 1) * 512],
                    xnT[:, pa * 1024 + c * 128:pa * 1024 + (c + 1) * 128],
                    wkv[:, c * 2048 + ni * 512:c * 2048 + (ni + 1) * 512],
                    start=(c == 0), stop=(c == 7), dep=dd,
                    inc="sT" if (ni == 3 and c == 7) else None)
        v_rope = rope_block(0, cosk, sink, kb, NKB, ktm,
                            [("sT", t_kv), ("sS", s_tabk)])
        v_lo_in = bass.AP(lp, 1024, [[2048, 128], [64, 8], [1, 64]])
        v_hi_in = bass.AP(lp, 1024 + 512, [[2048, 128], [64, 8], [1, 64]])
        v_lo_out = bass.AP(vc, kb * 520, [[NKB * 520, 128], [65, 8], [1, 64]])
        v_hi_out = bass.AP(vstg, 0, [[520, 128], [65, 8], [1, 64]])
        dep_vstg = [st_v_hist.get(kb - 1)]
        P.emit("vector", lambda e, i=v_lo_in, o=v_lo_out: e.tensor_copy(o, i),
               dep=[("sG", g_ones)])
        v_ev = P.emit("vector",
                      lambda e, i=v_hi_in, o=v_hi_out: e.tensor_copy(o, i),
                      dep=dep_vstg, inc="sV")
        lp_free = [("sV", v_ev)]
        vtr2 = transpose8(ktm, dep_t=[("sV", v_rope), ("sV", tp_free_v)],
                          dep_v=[],
                          out_ap=bass.AP(kTc, kb * 512,
                                         [[NKB * 512, 128], [1, 512]]),
                          in_ap=tp[:, 0:512])
        v_k2 = P.emit("vector", lambda e: e.tensor_copy(
            kstg[:], tp[:, 512:1024]),
            dep=[st_k_hist.get(kb - 1)], inc="sV")
        tp_free_v = v_k2
        st_k_hist[kb] = P.dma("sync", kt2_h.ap()[:, kb * 512:(kb + 1) * 512],
                              kstg[:],
                              dep=[("sV", v_k2)], sem="sO")
        st_v_hist[kb] = P.dma("sync", v2_h.ap()[:, kb * 520:(kb + 1) * 520],
                              vstg[:],
                              dep=[("sV", v_ev)], sem="sO")
    p1_done_o = P.fence("sO")
    p1_v = P.val("sV")
    esA1.close()
    if PHASE_LIMIT <= 3:
        P._w("sync", P.fence("sO") + P.fence("sL") + P.fence("sGD"))
        return _finish(nc, P, es)


    lp_free = [("sV", 0)]
    ln_prev2 = p1_v
    for s_ in range(4):
        for tq in range(4):
            ti = s_ * 4 + tq
            pa = ti % 2
            ld = P.dma("sync", xin[:, pa * 1024:(pa + 1) * 1024],
                       xq_h.ap()[ti * 128:(ti + 1) * 128, :],
                       dep=[("sV", ln_prev2)])
            vt = ln_chain(xin[:, pa * 1024:(pa + 1) * 1024],
                          xn[:, pa * 1024:(pa + 1) * 1024],
                          dep_v=[ld], dep_s=[ld])
            ln_prev2 = vt
            vtr = transpose8(xn[:, pa * 1024:(pa + 1) * 1024],
                             dep_t=[("sV", vt), ("sV", tp_free_v)], dep_v=[],
                             out_ap=xnT[:, pa * 1024:(pa + 1) * 1024],
                             in_ap=tp[:])
            tp_free_v = vtr
            t_q = None
            for ni in range(2):
                for c in range(8):
                    dd = ([("sV", vtr)] + lp_free) if (ni == 0 and c == 0) \
                        else ()
                    t_q = matmul(
                        lp[:, ni * 512:(ni + 1) * 512],
                        xnT[:, pa * 1024 + c * 128:
                            pa * 1024 + (c + 1) * 128],
                        wqq[:, c * 1024 + ni * 512:
                            c * 1024 + (ni + 1) * 512],
                        start=(c == 0), stop=(c == 7), dep=dd,
                        inc="sT" if (ni == 1 and c == 7) else None)
            v_rope = rope_block(0, cosq, sinq, ti, 16, ktm,
                                [("sT", t_q), ("sS", s_tabq)])
            lp_free = [("sV", v_rope)]
            vq = transpose8(ktm, dep_t=[("sV", v_rope), ("sV", tp_free_v)],
                            dep_v=[],
                            out_ap=bass.AP(qT[s_], tq * 128,
                                           [[8 * 512, 128], [512, 8],
                                            [1, 128]]),
                            in_ap=bass.AP(tp, 0,
                                          [[1024, 128], [128, 8], [1, 128]]))
            tp_free_v = vq
    qT_done = P.val("sV")
    esA2.close()
    if PHASE_LIMIT <= 4:
        P._w("sync", P.fence("sO") + P.fence("sL") + P.fence("sGD"))
        P.emit("sync", lambda e: e.wait_ge(P.sem["sV"][0], P.val("sV")))
        return _finish(nc, P, es)
    esAtt = ExitStack()
    sbT = lambda name, shape, dt: esAtt.enter_context(
        nc.sbuf_tensor(name, shape, dt))
    maskb = sbT("maskb", [128, 8 * 512], BF16)
    ctxT = sbT("ctxT", [64, 16 * 512], BF16)
    den = sbT("den", [128, 1024], F32)
    rcp = sbT("rcp", [128, 1024], F32)
    pt = sbT("pt", [128, 2 * 1024], BF16)
    x1t = sbT("x1t", [128, 1024], F32)
    woS = sbT("woS", [64, 16 * 1024], BF16)

    gwo = 0
    arena_guard = ([("sT", P.val("sT")), ("sV", P.val("sV")),
                    ("sS", P.val("sS"))]
                   + P.fence("sO") + P.fence("sL"))
    for hh in range(16):
        P.dma("gpsimd", woS[0:64, hh * 1024:(hh + 1) * 1024],
              wo_h.ap()[hh * 64:(hh + 1) * 64, :],
              dep=arena_guard if hh == 0 else ())
    gwo = P.fence("sGD")

    att = {
        "last_exp": {0: 0, 1: 0},
        "last_pv": {0: 0, 1: 0},
        "cp_free": [("sV", 0), ("sS", 0)],
        "mask_free": 0,
        "evac_extra": [],
    }

    def attn_slot(stage, s_, kt_sb, v_sb, dep_mask):
        cap = CAPS[s_]
        mld = P.dma("sync", maskb[:],
                    mask_h.ap()[:, s_ * 4096:(s_ + 1) * 4096],
                    dep=dep_mask + [("sV", att["mask_free"])])
        last_mask_v = 0
        for hpl in range(4):
            hp = stage * 4 + hpl
            for kb in range(cap):
                pa = kb % 2
                depq = [mld, ("sV", qT_done)]
                if att["last_exp"][pa]:
                    depq.append(("sS", att["last_exp"][pa]))
                depq += lp_free
                matmul(lp[:, pa * 1024:pa * 1024 + 512],
                       kt_sb[0:64, kb * 512 + hpl * 128:
                             kb * 512 + (hpl) * 128 + 128],
                       qT[s_][0:64, hp * 512:(hp + 1) * 512],
                       start=True, stop=True, dep=depq)
                tqk = matmul(lp[:, pa * 1024 + 512:pa * 1024 + 1024],
                             kt_sb[64:128, kb * 512 + hpl * 128:
                                   kb * 512 + hpl * 128 + 128],
                             qT[s_][64:128, hp * 512:(hp + 1) * 512],
                             start=True, stop=True, inc="sT")
                deps = [("sT", tqk)]
                if att["last_pv"][pa]:
                    deps.append(("sT", att["last_pv"][pa]))
                sexp = P.emit("scalar", lambda e, pa=pa: e.activation(
                    pt[:, pa * 1024:(pa + 1) * 1024],
                    lp[:, pa * 1024:(pa + 1) * 1024],
                    AF.Exp, bias=cset[:, 5:6], scale=cset[:, 6:7]),
                    dep=deps, inc="sS")
                att["last_exp"][pa] = sexp
                dep_pv = [("sS", sexp)]
                if kb >= cap - 8:
                    moff = (kb - (cap - 8)) * 512
                    pt_pair = bass.AP(pt, pa * 1024,
                                      [[2048, 128], [512, 2], [1, 512]])
                    m_b = bass.AP(maskb, moff,
                                  [[8 * 512, 128], [0, 2], [1, 512]])
                    vm = P.emit("vector", lambda e, o=pt_pair, m=m_b:
                                e.tensor_tensor(o, o, m, ALU.mult),
                                dep=[("sS", sexp)], inc="sV")
                    last_mask_v = vm
                    dep_pv = [("sV", vm)]
                if kb == 0:
                    dep_pv = dep_pv + att["cp_free"]
                matmul(cp[0:65, 0:512],
                       v_sb[:, kb * 520 + (2 * hpl) * 65:
                            kb * 520 + (2 * hpl) * 65 + 65],
                       pt[:, pa * 1024:pa * 1024 + 512],
                       start=(kb == 0), stop=(kb == cap - 1), dep=dep_pv)
                tpv = matmul(cp[0:65, 512:1024],
                             v_sb[:, kb * 520 + (2 * hpl + 1) * 65:
                                  kb * 520 + (2 * hpl + 1) * 65 + 65],
                             pt[:, pa * 1024 + 512:(pa + 1) * 1024],
                             start=(kb == 0), stop=(kb == cap - 1), inc="sT")
                att["last_pv"][pa] = tpv
            tdone = att["last_pv"][(cap - 1) % 2]
            P.emit("vector", lambda e: e.reciprocal(
                den[64:65, 0:512], cp[64:65, 0:512]), dep=[("sT", tdone)])
            P.drain("vector")
            vden = P.emit("vector", lambda e: e.reciprocal(
                den[64:65, 512:1024], cp[64:65, 512:1024]), inc="sV")
            mb1 = matmul(db[:], onesf[64:65, 0:128], den[64:65, 0:512],
                         start=True, stop=True, dep=[("sV", vden)], inc="sT")
            vc1 = P.emit("vector", lambda e: e.tensor_copy(
                rcp[:, 0:512], db[:]), dep=[("sT", mb1)], inc="sV")
            mb2 = matmul(db[:], onesf[64:65, 0:128], den[64:65, 512:1024],
                         start=True, stop=True, dep=[("sV", vc1)], inc="sT")
            vc2 = P.emit("vector", lambda e: e.tensor_copy(
                rcp[:, 512:1024], db[:]), dep=[("sT", mb2)], inc="sV")
            hA, hB = 2 * hp, 2 * hp + 1
            extra = att["evac_extra"]
            att["evac_extra"] = []
            P.emit("vector", lambda e, hA=hA: e.tensor_tensor(
                ctxT[0:64, hA * 512:(hA + 1) * 512], cp[0:64, 0:512],
                rcp[0:64, 0:512], ALU.mult), dep=extra)
            vev = P.emit("vector", lambda e, hB=hB: e.tensor_tensor(
                ctxT[0:64, hB * 512:(hB + 1) * 512], cp[0:64, 512:1024],
                rcp[0:64, 512:1024], ALU.mult), dep=[("sV", vc2)], inc="sV")
            att["cp_free"] = [("sV", vev)]
        att["mask_free"] = last_mask_v if last_mask_v else att["mask_free"]

    ctxA_store = 0
    dep0 = [("sV", qT_done)]
    for s_ in range(4):
        attn_slot(0, s_, kTc, vc, dep0)
        dep0 = []
        ctxA_store = P.dma("sync", ctxA_h.ap()[:, s_ * 4096:(s_ + 1) * 4096],
                           ctxT[0:64, 0:4096], dep=[("sV", P.val("sV"))],
                           sem="sO")
        att["evac_extra"] = [ctxA_store]

    if PHASE_LIMIT <= 5:
        P._w("sync", P.fence("sO") + P.fence("sL") + P.fence("sGD"))
        P.emit("sync", lambda e: e.wait_ge(P.sem["sV"][0], P.val("sV")))
        P.emit("sync", lambda e: e.wait_ge(P.sem["sS"][0], P.val("sS")))
        P.emit("sync", lambda e: e.wait_ge(P.sem["sT"][0], P.val("sT")))
        return _finish(nc, P, es)
    rk = P.dma("sync", kTc[:], kt2_h.ap()[:],
               dep=[ctxA_store] + p1_done_o +
                    [("sS", att["last_exp"][0]), ("sS", att["last_exp"][1]),
                     ("sT", att["last_pv"][0]), ("sT", att["last_pv"][1])])
    rv = P.dma("sync", vc[:], v2_h.ap()[:])
    if PHASE_LIMIT == 50:
        P._w("sync", P.fence("sO") + P.fence("sL") + P.fence("sGD"))
        return _finish(nc, P, es)

    st_x1_prev = None
    st_xq_prev = None
    out_dep_first = list(gwo)
    t_op_prev = 0
    v_x1 = 0
    for s_ in range(4):
        if 53 <= PHASE_LIMIT <= 56 and s_ >= PHASE_LIMIT - 52:
            break
        attn_slot(1, s_, kTc, vc, [rv, rk])
        if PHASE_LIMIT == 51 or 53 <= PHASE_LIMIT <= 56:
            att["cp_free"] = [("sV", P.val("sV")), ("sS", P.val("sS"))]
            continue
        rc_ld = P.dma("sync", ctxT[0:64, 0:4096],
                      ctxA_h.ap()[:, s_ * 4096:(s_ + 1) * 4096],
                      dep=[("sV", P.val("sV")), ("sT", t_op_prev)])
        if PHASE_LIMIT == 520:
            att["cp_free"] = [("sV", P.val("sV")), ("sS", P.val("sS"))]
            continue
        t_op = 0
        vt2 = 0
        for tq in range(4):
            ti = s_ * 4 + tq
            ldx = P.dma("sync", xin[:, 0:1024],
                        xq_h.ap()[ti * 128:(ti + 1) * 128, :],
                        dep=[("sV", ln_prev2 if ti == 0 else v_x1)])
            for ni in range(2):
                for hh in range(16):
                    dd = ([rc_ld, ("sV", vt2)] + out_dep_first
                          + att["cp_free"]) \
                        if (ni == 0 and hh == 0) else ()
                    t_op = matmul(
                        cp[:, ni * 512:(ni + 1) * 512],
                        ctxT[0:64, hh * 512 + tq * 128:
                             hh * 512 + tq * 128 + 128],
                        woS[0:64, hh * 1024 + ni * 512:
                            hh * 1024 + (ni + 1) * 512],
                        start=(hh == 0), stop=(hh == 15), dep=dd,
                        inc="sT" if (ni == 1 and hh == 15) else None)
            out_dep_first = []
            if PHASE_LIMIT == 521:
                vt2 = 0
                continue
            vt2 = ln_chain(cp[:, 0:1024], lns[:],
                           dep_v=[("sT", t_op)], dep_s=[("sT", t_op)],
                           psum=True)
            if PHASE_LIMIT == 522:
                continue
            P.drain("vector")
            v_x1 = P.emit("vector", lambda e: e.tensor_add(
                x1t[:], xin[:, 0:1024], lns[:]),
                dep=[("sV", vt2), ldx, st_x1_prev],
                inc="sV")
            st_x1_prev = P.dma("sync",
                               x1_h.ap()[ti * 128:(ti + 1) * 128, :],
                               x1t[:], dep=[("sV", v_x1)], sem="sO")
            if PHASE_LIMIT == 52:
                continue
            vt3 = ln_chain(x1t[:], lns[:], dep_v=[("sV", v_x1)],
                           dep_s=[("sV", v_x1)])
            P.emit("vector", lambda e: e.reduce_max(
                stats[:, 9:10], lns[:], axis=AX, apply_absolute_value=True),
                dep=[("sV", vt3)])
            P.drain("vector")
            P.emit("vector", lambda e: e.tensor_scalar(
                stats[:, 10:11], stats[:, 9:10], 1e-5, None, ALU.max))
            P.drain("vector")
            P.emit("vector", lambda e, ti=ti: e.tensor_scalar(
                sxinv[:, ti:ti + 1], stats[:, 10:11], 1.0 / 127.0, None,
                ALU.mult))
            P.emit("vector", lambda e: e.reciprocal(
                stats[:, 11:12], stats[:, 10:11]))
            P.drain("vector")
            vsc = P.emit("vector", lambda e: e.tensor_scalar(
                stats[:, 12:13], stats[:, 11:12], 127.0, None, ALU.mult),
                inc="sV")
            vq = act_quant_chain(lns[:], 1024, stats[:, 12:13],
                                 dep_v=[("sV", vsc), st_xq_prev])
            st_xq_prev = P.dma("sync",
                               xqq_h.ap()[ti * 128:(ti + 1) * 128, :],
                               qb_[:, 0:1024], dep=[("sV", vq)], sem="sO")
        att["cp_free"] = [("sV", P.val("sV")), ("sS", P.val("sS"))]
        att["evac_extra"] = [("sT", t_op)]
        t_op_prev = t_op
    p2_done = P.fence("sO")
    if PHASE_LIMIT <= 6 or PHASE_LIMIT in (50, 51, 52, 53, 54, 55, 56, 520, 521, 522):
        P._w("sync", P.fence("sO") + P.fence("sL") + P.fence("sGD"))
        P.emit("sync", lambda e: e.wait_ge(P.sem["sV"][0], P.val("sV")))
        P.emit("sync", lambda e: e.wait_ge(P.sem["sS"][0], P.val("sS")))
        P.emit("sync", lambda e: e.wait_ge(P.sem["sT"][0], P.val("sT")))
        return _finish(nc, P, es)

    esAtt.close()
    esA.close()
    esB = ExitStack()
    sbB = lambda name, shape, dt: esB.enter_context(
        nc.sbuf_tensor(name, shape, dt))
    wgT = sbB("wgT", [128, 8 * 4096], BF16)
    wuT = sbB("wuT", [128, 8 * 4096], BF16)
    xqT = sbB("xqT", [128, 8 * 512], BF16)
    hbuf = sbB("hbuf", [128, 4096], F32)
    hqb = sbB("hqb", [128, 4096], BF16)
    sgu = sbB("sgu", [128, 4], F32)

    wl_toks = []
    for c in range(8):
        wl_toks.append(P.dmat("sync", wgT[:, c * 4096:(c + 1) * 4096],
                              wgq_h.ap()[:, c * 128:(c + 1) * 128],
                              dep=p2_done if c == 0 else []))
        wl_toks.append(P.dmat("sync", wuT[:, c * 4096:(c + 1) * 4096],
                              wuq_h.ap()[:, c * 128:(c + 1) * 128]))
    wl_done = [max((t for ln2, t in wl_toks if ln2 == ln), default=0)
               for ln in {ln for ln, _ in wl_toks}]
    wl_done = [(ln, max(t for l2, t in wl_toks if l2 == ln))
               for ln in {ln for ln, _ in wl_toks}]

    hq_store = {}
    lpB_free = [("sS", 0)]
    silu_guard = {0: 0, 1: 0}
    for g in range(4):
        xl_toks = []
        for c in range(8):
            xl_toks.append(P.dmat("sync", xqT[:, c * 512:(c + 1) * 512],
                                  xqq_h.ap()[g * 512:(g + 1) * 512,
                                             c * 128:(c + 1) * 128],
                                  dep=wl_done
                                  if (g == 0 and c == 0) else []))
        xl_done = [(ln, max(t for l2, t in xl_toks if l2 == ln))
                   for ln in {ln for ln, _ in xl_toks}]
        for tl in range(4):
            ti = g * 4 + tl
            P.emit("vector", lambda e, ti=ti: e.tensor_tensor(
                sgu[:, 0:1], sxinv[:, ti:ti + 1], wmean3[:, 0:1], ALU.mult),
                dep=xl_done + [("sS", P.val("sS"))])
            P.drain("vector")
            vsg = P.emit("vector", lambda e, ti=ti: e.tensor_tensor(
                sgu[:, 1:2], sxinv[:, ti:ti + 1], wmean3[:, 1:2], ALU.mult),
                inc="sV")
            h_written = {}
            for nci in range(8):
                pa = nci % 2
                dd0 = [("sV", vsg)] + lpB_free
                if nci >= 2:
                    dd0 = [("sV", h_written[nci - 2])]
                for c in range(8):
                    matmul(lp[:, pa * 512:(pa + 1) * 512],
                           xqT[:, c * 512 + tl * 128:c * 512 + tl * 128 + 128],
                           wgT[:, c * 4096 + nci * 512:
                               c * 4096 + (nci + 1) * 512],
                           start=(c == 0), stop=(c == 7),
                           dep=dd0 if c == 0 else ())
                tgu = None
                for c in range(8):
                    tgu = matmul(
                        lp[:, 1024 + pa * 512:1024 + (pa + 1) * 512],
                        xqT[:, c * 512 + tl * 128:c * 512 + tl * 128 + 128],
                        wuT[:, c * 4096 + nci * 512:
                            c * 4096 + (nci + 1) * 512],
                        start=(c == 0), stop=(c == 7),
                        inc="sT" if c == 7 else None)
                ssdep = [("sT", tgu)]
                if silu_guard[pa]:
                    ssdep.append(("sV", silu_guard[pa]))
                ss = P.emit("scalar", lambda e, pa=pa: e.activation(
                    sq[:, pa * 512:(pa + 1) * 512],
                    lp[:, pa * 512:(pa + 1) * 512], AF.Silu,
                    scale=sgu[:, 0:1]), dep=ssdep, inc="sS")
                vh = P.emit("vector", lambda e, nci=nci, pa=pa:
                            e.scalar_tensor_tensor(
                                hbuf[:, nci * 512:(nci + 1) * 512],
                                lp[:, 1024 + pa * 512:1024 + (pa + 1) * 512],
                                sgu[:, 1:2], sq[:, pa * 512:(pa + 1) * 512],
                                ALU.mult, ALU.mult),
                            dep=[("sS", ss)], inc="sV")
                h_written[nci] = vh
                silu_guard[pa] = vh
            P.emit("vector", lambda e: e.reduce_max(
                stats[:, 9:10], hbuf[:], axis=AX, apply_absolute_value=True),
                dep=[("sV", h_written[7])])
            P.drain("vector")
            P.emit("vector", lambda e: e.tensor_scalar(
                stats[:, 10:11], stats[:, 9:10], 1e-5, None, ALU.max))
            P.drain("vector")
            P.emit("vector", lambda e: e.reciprocal(
                stats[:, 11:12], stats[:, 10:11]))
            P.drain("vector")
            vsc = P.emit("vector", lambda e: e.tensor_scalar(
                stats[:, 12:13], stats[:, 11:12], 127.0, None, ALU.mult),
                inc="sV")
            lastq = vsc
            for qh in range(4):
                vq = act_quant_chain(hbuf[:, qh * 1024:(qh + 1) * 1024],
                                     1024, stats[:, 12:13],
                                     dep_v=[("sV", lastq),
                                            hq_store.get(ti - 1)])
                lastq = P.emit("vector", lambda e, qh=qh: e.tensor_copy(
                    hqb[:, qh * 1024:(qh + 1) * 1024], qb_[:, 0:1024]),
                    dep=[("sV", vq)], inc="sV")
            lpB_free = [("sV", lastq)]
            hq_store[ti] = P.dma("sync", hq_h.ap()[ti * 128:(ti + 1) * 128, :],
                                 hqb[:], dep=[("sV", lastq)], sem="sO")
    b1_done = P.fence("sO")
    b1_v = P.val("sV")
    b1_s = P.val("sS")
    if PHASE_LIMIT <= 7:
        P._w("sync", P.fence("sO") + P.fence("sL") + P.fence("sGD"))
        P.emit("sync", lambda e: e.wait_ge(P.sem["sV"][0], P.val("sV")))
        P.emit("sync", lambda e: e.wait_ge(P.sem["sS"][0], P.val("sS")))
        P.emit("sync", lambda e: e.wait_ge(P.sem["sT"][0], P.val("sT")))
        return _finish(nc, P, es)

    esB.close()
    esC = ExitStack()
    sbC = lambda name, shape, dt: esC.enter_context(
        nc.sbuf_tensor(name, shape, dt))
    wdT = sbC("wdT", [128, 32 * 1024], BF16)
    hqT = sbC("hqT", [128, 32 * 512], BF16)
    x1g = sbC("x1g", [128, 4 * 1024], F32)
    outb = sbC("outb", [128, 2 * 1024], F32)

    wdl_toks = []
    for ic in range(32):
        wdl_toks.append(P.dmat("sync", wdT[:, ic * 1024:(ic + 1) * 1024],
                               wdq_h.ap()[:, ic * 128:(ic + 1) * 128],
                               dep=b1_done + [("sV", b1_v), ("sS", b1_s)]
                               if ic == 0 else []))
    wdl_done = [(ln, max(t for l2, t in wdl_toks if l2 == ln))
                for ln in {ln for ln, _ in wdl_toks}]

    out_store = {}
    lpC_free = [("sV", 0)]
    for g in range(4):
        hl_toks = []
        for ic in range(32):
            hl_toks.append(P.dmat("sync", hqT[:, ic * 512:(ic + 1) * 512],
                                  hq_h.ap()[g * 512:(g + 1) * 512,
                                            ic * 128:(ic + 1) * 128],
                                  dep=wdl_done
                                  if (g == 0 and ic == 0) else []))
        hl_done = [(ln, max(t for l2, t in hl_toks if l2 == ln))
                   for ln in {ln for ln, _ in hl_toks}]
        xg_toks = []
        for a_ in range(4):
            xg_toks.append(P.dma(
                "sync", x1g[:, a_ * 1024:(a_ + 1) * 1024],
                x1_h.ap()[g * 512 + a_ * 128:g * 512 + (a_ + 1) * 128, :],
                dep=hl_done if a_ == 0 else []))
        xg = [(ln, max(t for l2, t in xg_toks if l2 == ln))
              for ln in {ln for ln, _ in xg_toks}]
        for tl in range(4):
            ti = g * 4 + tl
            t_dn = None
            for ni in range(2):
                for ic in range(32):
                    dd = (xg + lpC_free) \
                        if (ic == 0 and ni == 0) else ()
                    t_dn = matmul(
                        lp[:, ni * 512:(ni + 1) * 512],
                        hqT[:, ic * 512 + tl * 128:
                            ic * 512 + tl * 128 + 128],
                        wdT[:, ic * 1024 + ni * 512:
                            ic * 1024 + (ni + 1) * 512],
                        start=(ic == 0), stop=(ic == 31), dep=dd,
                        inc="sT" if (ni == 1 and ic == 31) else None)
            vt4 = ln_chain(lp[:, 0:1024], lns[:],
                           dep_v=[("sT", t_dn)], dep_s=[("sT", t_dn)],
                           psum=True)
            po = tl % 2
            dep_out = [("sV", vt4), out_store.get(ti - 2)]
            vfin = P.emit("vector", lambda e, tl=tl, po=po: e.tensor_add(
                outb[:, po * 1024:(po + 1) * 1024],
                x1g[:, tl * 1024:(tl + 1) * 1024], lns[:]),
                dep=dep_out, inc="sV")
            lpC_free = [("sV", vfin)]
            out_store[ti] = P.dma("sync",
                                  out_h.ap()[ti * 128:(ti + 1) * 128, :],
                                  outb[:, po * 1024:(po + 1) * 1024],
                                  dep=[("sV", vfin)], sem="sO")
    esC.close()

    P._w("sync", P.fence("sO"))

    return _finish(nc, P, es)


def _finish(nc, P, es):
    with nc.Block() as block:
        @block.sync
        def _(sync):
            for fn in P.q["sync"]:
                fn(sync)

        @block.gpsimd
        def _(gpsimd):
            for fn in P.q["gpsimd"]:
                fn(gpsimd)

        @block.vector
        def _(vector):
            for fn in P.q["vector"]:
                fn(vector)

        @block.scalar
        def _(scalar):
            for fn in P.q["scalar"]:
                fn(scalar)

        @block.tensor
        def _(tensor):
            for fn in P.q["tensor"]:
                fn(tensor)

    return nc


def _host_inputs(core, x, wqkv, wo, wg, wu, wd):
    b, par = core // 2, core % 2
    js = JS[par]
    rows = np.concatenate([np.arange(512 * j, 512 * j + 512) for j in js])
    xb = np.ascontiguousarray(x[b])
    xq = np.ascontiguousarray(xb[rows])
    masks = np.zeros((4, 8, 128, 512), dtype=np.float32)
    for s_, j in enumerate(js):
        cap = CAPS[s_]
        q0 = 512 * j
        for kbrel in range(8):
            kb = cap - 8 + kbrel
            kabs = kb * 128 + np.arange(128)[:, None]
            qabs = q0 + np.arange(512)[None, :]
            masks[s_, kbrel] = (kabs <= qabs)
    masks = masks.transpose(2, 0, 1, 3).reshape(128, 4 * 8 * 512)
    masks = np.ascontiguousarray(masks).astype(ml_dtypes.bfloat16)
    pck = (np.arange(NKB)[None, :] * 128
           + np.arange(128)[:, None]).astype(np.float32)
    tq_tiles = np.concatenate([np.arange(4 * j, 4 * j + 4) for j in js])
    pcq = (tq_tiles[None, :] * 128
           + np.arange(128)[:, None]).astype(np.float32)
    fracb = np.ascontiguousarray(np.broadcast_to(
        (2.0 * np.arange(32) / 64.0).astype(np.float32), (128, 32)))
    ident = np.eye(128, dtype=np.float32).astype(ml_dtypes.bfloat16)
    return dict(xkv=xb, xq=xq, wqkv=wqkv, wo=wo, wg=wg, wu=wu, wd=wd,
                masks=masks, poscolk=pck, poscolq=pcq, fracb=fracb,
                ident=ident), rows


def kernel(L0_attn_x, L0_attn_wqkv, L0_attn_wo, L0_ffn_w_gate, L0_ffn_w_up,
           L0_ffn_w_down):
    x = np.ascontiguousarray(np.asarray(L0_attn_x, dtype=np.float32))
    wqkv = np.ascontiguousarray(np.asarray(L0_attn_wqkv, dtype=np.float32))
    wo = np.ascontiguousarray(np.asarray(L0_attn_wo, dtype=np.float32))
    wg = np.ascontiguousarray(np.asarray(L0_ffn_w_gate, dtype=np.float32))
    wu = np.ascontiguousarray(np.asarray(L0_ffn_w_up, dtype=np.float32))
    wd = np.ascontiguousarray(np.asarray(L0_ffn_w_down, dtype=np.float32))

    nc = build_nc()
    in_maps, rowmaps = [], []
    for core in range(8):
        im, rows = _host_inputs(core, x, wqkv, wo, wg, wu, wd)
        in_maps.append(im)
        rowmaps.append(rows)
    res = run_bass_kernel_spmd(nc, in_maps, core_ids=list(range(8)))
    out = np.empty((B, S, DM), dtype=np.float32)
    for core in range(8):
        b = core // 2
        out[b][rowmaps[core]] = np.asarray(res.results[core]["out"],
                                           dtype=np.float32)
    return out


# version 04
# speedup vs baseline: 1.4147x; 1.4147x over previous
import numpy as np
import ml_dtypes
from contextlib import ExitStack

import concourse.bass as bass
import concourse.mybir as mybir
from concourse.bass_utils import run_bass_kernel_spmd

F32 = mybir.dt.float32
BF16 = mybir.dt.bfloat16
I32 = mybir.dt.int32
AF = mybir.ActivationFunctionType
ALU = mybir.AluOpType
AX = mybir.AxisListType.X

B, S, DM, H, DH, INNER = 4, 4096, 1024, 16, 64, 4096
NT = 2048
CAPS = [8, 16, 24, 32]
JS = {0: [0, 3, 4, 7], 1: [1, 2, 5, 6]}
NKB = S // 128
EPS = 1e-5
EXPB = -10.0
RC = 256.0
DEBUG_TAPS = False
PHASE_LIMIT = 99


class Prog:
    ENGS = ("sync", "gpsimd", "vector", "scalar", "tensor")
    LANES = {"sL": 4, "sO": 2, "sGD": 2}

    def __init__(self, nc, es):
        self.nc = nc
        self.q = {e: [] for e in self.ENGS}
        self.sem = {}
        for name in ("sG", "sV", "sS", "sT"):
            self.sem[name] = [es.enter_context(nc.semaphore(name)), 0]
        for g, n in self.LANES.items():
            for i in range(n):
                nm = "%s%d" % (g, i)
                self.sem[nm] = [es.enter_context(nc.semaphore(nm)), 0]
        self.rr = {g: 0 for g in self.LANES}
        self.waited = {e: {} for e in self.ENGS}
        self.lastdrain = {e: False for e in self.ENGS}

    def _w(self, eng, dep):
        for d in dep:
            if not d:
                continue
            sn, val = d
            if val <= 0:
                continue
            if self.waited[eng].get(sn, 0) >= val:
                continue
            self.waited[eng][sn] = val
            h = self.sem[sn][0]
            self.q[eng].append(lambda e, h=h, v=val: e.wait_ge(h, v))
            self.lastdrain[eng] = False

    AUTODRAIN = ("vector", "gpsimd", "scalar")

    def emit(self, eng, fn, dep=(), inc=None, n=1):
        self._w(eng, dep)
        if eng in self.AUTODRAIN and not self.lastdrain[eng]:
            self.q[eng].append(lambda e: e.drain())
        self.lastdrain[eng] = False
        if inc is None:
            self.q[eng].append(fn)
            return None
        self.sem[inc][1] += n
        tgt = self.sem[inc][1]
        h = self.sem[inc][0]

        def run(e, fn=fn, h=h, n=n):
            fn(e).then_inc(h, n)
        self.q[eng].append(run)
        return tgt

    def drain(self, eng):
        if not self.lastdrain[eng]:
            self.q[eng].append(lambda e: e.drain())
            self.lastdrain[eng] = True

    def val(self, semname):
        return self.sem[semname][1]

    def fence(self, group):
        return [("%s%d" % (group, i), self.sem["%s%d" % (group, i)][1])
                for i in range(self.LANES[group])
                if self.sem["%s%d" % (group, i)][1] > 0]

    def _dma(self, eng, fn, dep, group):
        i = self.rr[group]
        self.rr[group] = (i + 1) % self.LANES[group]
        lane = "%s%d" % (group, i)
        prev = self.sem[lane][1]
        dep = list(dep) + [(lane, prev)]
        tgt = self.emit(eng, fn, dep=dep, inc=lane, n=16)
        return (lane, tgt)

    def dma(self, eng, dst, src, dep=(), sem=None, **kw):
        group = sem or ("sL" if eng == "sync" else "sGD")
        return self._dma(eng, lambda e: e.dma_start(dst, src, **kw),
                         dep, group)

    def dmat(self, eng, dst, src, dep=(), sem="sL"):
        return self._dma(eng, lambda e: e.dma_start_transpose(dst, src),
                         dep, group=sem)


def build_nc():
    nc = bass.Bass("TRN2", target_bir_lowering=False)
    es = ExitStack()

    xkv_h = nc.dram_tensor("xkv", [S, DM], F32, kind="ExternalInput")
    xq_h = nc.dram_tensor("xq", [NT, DM], F32, kind="ExternalInput")
    wqkv_h = nc.dram_tensor("wqkv", [DM, 3 * DM], F32, kind="ExternalInput")
    wo_h = nc.dram_tensor("wo", [DM, DM], F32, kind="ExternalInput")
    wg_h = nc.dram_tensor("wg", [INNER, DM], F32, kind="ExternalInput")
    wu_h = nc.dram_tensor("wu", [INNER, DM], F32, kind="ExternalInput")
    wd_h = nc.dram_tensor("wd", [DM, INNER], F32, kind="ExternalInput")
    mask_h = nc.dram_tensor("masks", [128, 4 * 8 * 512], BF16,
                            kind="ExternalInput")
    pck_h = nc.dram_tensor("poscolk", [128, NKB], F32, kind="ExternalInput")
    pcq_h = nc.dram_tensor("poscolq", [128, 16], F32, kind="ExternalInput")
    frac_h = nc.dram_tensor("fracb", [128, 32], F32, kind="ExternalInput")
    idn_h = nc.dram_tensor("ident", [128, 128], BF16, kind="ExternalInput")
    out_h = nc.dram_tensor("out", [NT, DM], F32, kind="ExternalOutput")

    kd = "ExternalOutput" if DEBUG_TAPS else "Internal"
    wgq_h = nc.dram_tensor("wgq", [INNER, DM], BF16, kind=kd)
    wuq_h = nc.dram_tensor("wuq", [INNER, DM], BF16, kind=kd)
    wdq_h = nc.dram_tensor("wdq", [DM, INNER], BF16, kind=kd)
    kt2_h = nc.dram_tensor("kt2", [128, NKB * 512], BF16, kind=kd)
    v2_h = nc.dram_tensor("v2", [128, NKB * 520], BF16, kind=kd)
    ctxA_h = nc.dram_tensor("ctxA", [64, 4 * 8 * 512], BF16, kind=kd)
    x1_h = nc.dram_tensor("x1", [NT, DM], F32, kind=kd)
    xqq_h = nc.dram_tensor("xqq", [NT, DM], BF16, kind=kd)
    hq_h = nc.dram_tensor("hqs", [NT, INNER], BF16, kind=kd)

    lp = es.enter_context(nc.psum_tensor("lp", [128, 2048], F32))
    cp = es.enter_context(nc.psum_tensor("cp", [128, 1024], F32))
    db = es.enter_context(nc.psum_tensor("db", [128, 512], F32))
    tp = es.enter_context(nc.psum_tensor("tp", [128, 1024], BF16))

    sb = lambda name, shape, dt: es.enter_context(
        nc.sbuf_tensor(name, shape, dt))
    idn = sb("idn", [128, 128], BF16)
    onesf = sb("onesf", [128, 128], F32)
    cset = sb("cset", [128, 12], F32)
    pck = sb("pck", [128, NKB], F32)
    pcq = sb("pcq", [128, 16], F32)
    frac = sb("frac", [128, 32], F32)
    stats = sb("stats", [128, 16], F32)
    acc3 = sb("acc3", [128, 4], F32)
    thr3 = sb("thr3", [128, 4], F32)
    nthr3 = sb("nthr3", [128, 4], F32)
    wmean3 = sb("wmean3", [128, 4], F32)
    sxinv = sb("sxinv", [128, 16], F32)
    sq = sb("sq", [128, 1024], F32)
    ra = sb("ra", [128, 512], F32)
    rb = sb("rb", [128, 512], F32)
    lns = sb("lns", [128, 1024], F32)
    qt_ = sb("qt", [128, 1024], F32)
    qi_ = sb("qi", [128, 1024], I32)
    qd_ = sb("qd", [128, 1024], F32)
    qb_ = sb("qb", [128, 1024], BF16)

    P = Prog(nc, es)

    def matmul(out, lhsT, rhs, start, stop, dep=(), inc=None):
        return P.emit("tensor",
                      lambda e: e.matmul(out, lhsT, rhs, start=start,
                                         stop=stop),
                      dep=dep, inc=inc)

    def ln_chain(src_ap, dst_ap, dep_v, dep_s, psum=False):
        if psum:
            cv = P.emit("vector",
                        lambda e, o=dst_ap, i=src_ap: e.tensor_copy(o, i),
                        dep=dep_v, inc="sV")
            dep_v = [("sV", cv)]
            dep_s = [("sV", cv)]
            src_ap = dst_ap
        P.emit("vector", lambda e: e.reduce_sum(stats[:, 0:1], src_ap,
                                                axis=AX), dep=dep_v)
        s1 = P.emit("scalar",
                    lambda e: e.activation(sq[:], src_ap, AF.Square,
                                           accum_out=stats[:, 1:2]),
                    dep=dep_s, inc="sS")
        P.drain("vector")
        P.emit("vector", lambda e: e.tensor_scalar(
            stats[:, 2:3], stats[:, 0:1], 1.0 / DM, None, ALU.mult))
        P.emit("vector", lambda e: e.tensor_scalar(
            stats[:, 3:4], stats[:, 1:2], 1.0 / DM, EPS, ALU.mult, ALU.add),
            dep=[("sS", s1)])
        P.drain("vector")
        P.emit("vector", lambda e: e.tensor_mul(
            stats[:, 4:5], stats[:, 2:3], stats[:, 2:3]))
        P.drain("vector")
        P.emit("vector", lambda e: e.tensor_sub(
            stats[:, 5:6], stats[:, 3:4], stats[:, 4:5]))
        P.drain("vector")
        v2t = P.emit("vector", lambda e: e.reciprocal(stats[:, 6:7],
                                                      stats[:, 5:6]),
                     inc="sV")
        s2 = P.emit("scalar", lambda e: e.activation(
            stats[:, 7:8], stats[:, 6:7], AF.Sqrt), dep=[("sV", v2t)],
            inc="sS")
        vt = P.emit("vector", lambda e: e.tensor_scalar(
            dst_ap, src_ap, stats[:, 2:3], stats[:, 7:8],
            ALU.subtract, ALU.mult), dep=[("sS", s2)], inc="sV")
        return vt

    def transpose8(src_sb, dep_t, dep_v, out_ap, in_ap):
        tt = None
        for j in range(8):
            tt = P.emit("tensor", lambda e, j=j: e.transpose(
                tp[:, j * 128:(j + 1) * 128],
                src_sb[:, j * 128:(j + 1) * 128], idn[:]),
                dep=(dep_t if j == 0 else ()),
                inc="sT" if j == 7 else None)
        vt = P.emit("vector", lambda e: e.tensor_copy(out_ap, in_ap),
                    dep=[("sT", tt)] + list(dep_v), inc="sV")
        return vt

    aq_state = {"g": 0}

    def act_quant_chain(src_ap, n, s_ap, dep_v):
        t1 = P.emit("vector", lambda e: e.tensor_scalar(
            qt_[:, 0:n], src_ap, s_ap, RC, ALU.mult, ALU.add),
            dep=list(dep_v) + [("sG", aq_state["g"])], inc="sV")
        g1 = P.emit("gpsimd", lambda e: e.tensor_copy(qi_[:, 0:n],
                                                      qt_[:, 0:n]),
                    dep=[("sV", t1)], inc="sG")
        aq_state["g"] = g1
        P.emit("vector", lambda e: e.tensor_sub(
            qd_[:, 0:n], qt_[:, 0:n], qi_[:, 0:n]), dep=[("sG", g1)])
        P.emit("vector", lambda e: e.scalar_tensor_tensor(
            qt_[:, 0:n], qd_[:, 0:n], 0.5, qi_[:, 0:n], ALU.is_ge, ALU.add))
        P.drain("vector")
        v3 = P.emit("vector", lambda e: e.tensor_scalar(
            qb_[:, 0:n], qt_[:, 0:n], RC, None, ALU.subtract), inc="sV")
        return v3

    def rope_block(src_psum_off, cos_sb, sin_sb, toff, tw, dst_sb, dep_first):
        in1 = bass.AP(lp, src_psum_off, [[2048, 128], [64, 16], [1, 32]])
        in2 = bass.AP(lp, src_psum_off + 32, [[2048, 128], [64, 16], [1, 32]])
        cs = bass.AP(cos_sb, toff * 32, [[tw * 32, 128], [0, 16], [1, 32]])
        sn = bass.AP(sin_sb, toff * 32, [[tw * 32, 128], [0, 16], [1, 32]])
        o1 = bass.AP(dst_sb, 0, [[1024, 128], [64, 16], [1, 32]])
        o2 = bass.AP(dst_sb, 32, [[1024, 128], [64, 16], [1, 32]])
        P.emit("vector", lambda e: e.tensor_tensor(ra[:], in1, cs, ALU.mult),
               dep=dep_first)
        P.emit("vector", lambda e: e.tensor_tensor(rb[:], in2, sn, ALU.mult))
        P.drain("vector")
        P.emit("vector", lambda e: e.tensor_sub(o1, ra[:], rb[:]))
        P.emit("vector", lambda e: e.tensor_tensor(ra[:], in2, cs, ALU.mult))
        P.emit("vector", lambda e: e.tensor_tensor(rb[:], in1, sn, ALU.mult))
        P.drain("vector")
        return P.emit("vector", lambda e: e.tensor_add(o2, ra[:], rb[:]),
                      inc="sV")

    esA = ExitStack()
    sbA = lambda name, shape, dt: esA.enter_context(
        nc.sbuf_tensor(name, shape, dt))
    kTc = sbA("kTc", [128, NKB * 512], BF16)
    vc = sbA("vc", [128, NKB * 520], BF16)
    xin = sbA("xin", [128, 2 * 1024], F32)
    qT = [sbA("qT%d" % s_, [128, 8 * 512], BF16) for s_ in range(4)]
    esA2 = ExitStack()
    sbA2 = lambda name, shape, dt: esA2.enter_context(
        nc.sbuf_tensor(name, shape, dt))
    wqq = sbA2("wqq", [128, 8 * 1024], BF16)
    xn = sbA2("xn", [128, 2 * 1024], BF16)
    xnT = sbA2("xnT", [128, 2 * 1024], BF16)
    ktm = sbA2("ktm", [128, 1024], BF16)
    cosq = sbA2("cosq", [128, 16 * 32], F32)
    sinq = sbA2("sinq", [128, 16 * 32], F32)
    esA1 = ExitStack()
    sbA1 = lambda name, shape, dt: esA1.enter_context(
        nc.sbuf_tensor(name, shape, dt))
    wkv = sbA1("wkv", [128, 8 * 2048], BF16)
    kstg = sbA1("kstg", [128, 512], BF16)
    vstg = sbA1("vstg", [128, 520], BF16)
    wst = sbA1("wst", [128, 1024], F32)
    cosk = sbA1("cosk", [128, NKB * 32], BF16)
    sink = sbA1("sink", [128, NKB * 32], BF16)

    lid = P.dma("sync", idn[:], idn_h.ap())
    lpk = P.dma("sync", pck[:], pck_h.ap())
    lpq = P.dma("sync", pcq[:], pcq_h.ap())
    lfr = P.dma("sync", frac[:], frac_h.ap())

    P.emit("gpsimd", lambda e: e.memset(onesf[:], 1.0))
    P.emit("gpsimd", lambda e: e.memset(cset[:, 0:1], -1.0))
    P.emit("gpsimd", lambda e: e.memset(cset[:, 5:6], EXPB))
    P.emit("gpsimd", lambda e: e.memset(cset[:, 6:7], 0.125))
    P.emit("gpsimd", lambda e: e.memset(sq[:, 0:1], 10000.0))
    P.emit("gpsimd", lambda e: e.memset(sq[:, 1:2], 1.0))
    P.emit("gpsimd", lambda e: e.memset(acc3[:], 0.0))
    P.drain("gpsimd")
    g0 = P.emit("gpsimd", lambda e: e.engine_nop(), inc="sG")

    s_ln = P.emit("scalar", lambda e: e.activation(
        cset[:, 1:2], sq[:, 0:1], AF.Ln), dep=[("sG", g0)], inc="sS")
    s_at = P.emit("scalar", lambda e: e.activation(
        cset[:, 7:8], sq[:, 1:2], AF.Arctan), inc="sS")
    P.emit("vector", lambda e: e.tensor_scalar(
        cset[:, 2:3], cset[:, 7:8], 2.0, None, ALU.mult), dep=[("sS", s_at)])
    P.emit("vector", lambda e: e.tensor_scalar(
        cset[:, 3:4], cset[:, 7:8], 8.0, None, ALU.mult))
    P.emit("vector", lambda e: e.tensor_scalar(
        cset[:, 8:9], cset[:, 7:8], 4.0, None, ALU.mult))
    P.emit("vector", lambda e: e.tensor_scalar(
        cset[:, 9:10], cset[:, 7:8], -4.0, None, ALU.mult))
    P.emit("vector", lambda e: e.tensor_scalar(
        cset[:, 10:11], cset[:, 7:8], -8.0, None, ALU.mult))
    v_2p = P.emit("vector", lambda e: e.reciprocal(cset[:, 4:5],
                                                   cset[:, 3:4]), inc="sV")
    v_fl = P.emit("vector", lambda e: e.tensor_scalar(
        sq[:, 32:64], frac[:], cset[:, 1:2], None, ALU.mult),
        dep=[lfr, ("sS", s_ln), ("sV", v_2p)], inc="sV")
    s_it = P.emit("scalar", lambda e: e.activation(
        frac[:], sq[:, 32:64], AF.Exp, scale=cset[:, 0:1]),
        dep=[("sV", v_fl)], inc="sS")

    def build_tables(n_tiles, pc_sb, cos_sb, sin_sb, dep0):
        deps = list(dep0)
        st = 0
        for t in range(n_tiles):
            P.emit("vector", lambda e, t=t: e.tensor_scalar(
                ra[:, 0:32], frac[:], pc_sb[:, t:t + 1], None, ALU.mult),
                dep=deps + [("sS", st)])
            deps = []
            P.drain("vector")
            P.emit("vector", lambda e: e.tensor_scalar(
                ra[:, 32:64], ra[:, 0:32], 1.0, cset[:, 2:3],
                ALU.mult, ALU.add))
            P.drain("vector")
            vm = P.emit("vector", lambda e: e.tensor_scalar(
                rb[:, 0:64], ra[:, 0:64], cset[:, 4:5], None, ALU.mult),
                inc="sV")
            gk = P.emit("gpsimd", lambda e: e.tensor_copy(
                qi_[:, 0:64], rb[:, 0:64]), dep=[("sV", vm)], inc="sG")
            P.emit("vector", lambda e: e.scalar_tensor_tensor(
                rb[:, 0:64], qi_[:, 0:64], cset[:, 10:11], ra[:, 0:64],
                ALU.mult, ALU.add), dep=[("sG", gk)])
            P.emit("vector", lambda e: e.tensor_scalar(
                rb[:, 64:128], rb[:, 0:64], cset[:, 8:9], None, ALU.is_gt))
            P.emit("vector", lambda e: e.scalar_tensor_tensor(
                rb[:, 0:64], rb[:, 64:128], cset[:, 10:11], rb[:, 0:64],
                ALU.mult, ALU.add))
            P.emit("vector", lambda e: e.tensor_scalar(
                rb[:, 64:128], rb[:, 0:64], cset[:, 9:10], None, ALU.is_lt))
            vr = P.emit("vector", lambda e: e.scalar_tensor_tensor(
                rb[:, 0:64], rb[:, 64:128], cset[:, 3:4], rb[:, 0:64],
                ALU.mult, ALU.add), inc="sV")
            P.emit("scalar", lambda e, t=t: e.activation(
                sin_sb[:, t * 32:(t + 1) * 32], rb[:, 0:32], AF.Sin),
                dep=[("sV", vr)])
            st = P.emit("scalar", lambda e, t=t: e.activation(
                cos_sb[:, t * 32:(t + 1) * 32], rb[:, 32:64], AF.Sin),
                inc="sS")
        return st

    s_tabk = build_tables(NKB, pck, cosk, sink,
                          [("sS", s_it), lpk, lpq, lid])
    s_tabq = build_tables(16, pcq, cosq, sinq, [('sS', s_tabk)])


    for c in range(8):
        P.dma("gpsimd", wkv[:, c * 2048:(c + 1) * 2048],
              wqkv_h.ap()[c * 128:(c + 1) * 128, 1024:3072])
        P.dma("gpsimd", wqq[:, c * 1024:(c + 1) * 1024],
              wqkv_h.ap()[c * 128:(c + 1) * 128, 0:1024])
    gw_done = P.fence("sGD")

    P.emit("gpsimd", lambda e: e.memset(
        bass.AP(vc, 64, [[NKB * 520, 128], [520, NKB], [65, 8]]), 1.0))
    P.emit("gpsimd", lambda e: e.memset(
        bass.AP(vstg, 64, [[520, 128], [65, 8]]), 1.0))
    P.drain("gpsimd")
    g_ones = P.emit("gpsimd", lambda e: e.engine_nop(), inc="sG")

    last_abs = 0
    last_ga = g_ones
    for wh, wi in ((wg_h, 0), (wu_h, 1), (wd_h, 2)):
        wre = wh.ap().rearrange("(a p) f -> p a f", p=128)
        n_a = wre.shape[1]
        n_f = wre.shape[2] // 1024
        chunks = [(ai, fj) for ai in range(n_a) for fj in range(n_f)]
        for ch, (ai, fj) in enumerate(chunks):
            ld = P.dma("sync", wst[:],
                       wre[:, ai, fj * 1024:(fj + 1) * 1024],
                       dep=[("sS", last_abs)])
            last_abs = P.emit("scalar", lambda e: e.activation(
                sq[:], wst[:], AF.Abs,
                accum_out=stats[:, 8:9]), dep=[ld, ("sG", last_ga)],
                inc="sS")
            last_ga = P.emit("gpsimd", lambda e, wi=wi: e.tensor_tensor(
                acc3[:, wi:wi + 1], acc3[:, wi:wi + 1], stats[:, 8:9],
                ALU.add), dep=[("sS", last_abs), ("sG", last_ga)], inc="sG")
            P.drain("gpsimd")
    mt = matmul(db[0:1, 0:4], onesf[:, 0:1], acc3[:], start=True, stop=True,
                dep=[("sG", last_ga)], inc="sT")
    P.emit("vector", lambda e: e.tensor_scalar(
        stats[0:1, 8:12], db[0:1, 0:4], 1.0 / (INNER * DM), 1e-5,
        ALU.mult, ALU.max), dep=[("sT", mt)])
    P.drain("vector")
    v_thr = P.emit("vector", lambda e: e.tensor_scalar(
        stats[0:1, 12:16], stats[0:1, 8:12], 0.5, None, ALU.mult), inc="sV")
    m_b1 = matmul(db[:, 0:4], onesf[0:1, 0:128], stats[0:1, 12:16],
                  start=True, stop=True, dep=[("sV", v_thr)], inc="sT")
    vb1 = P.emit("vector", lambda e: e.tensor_copy(thr3[:], db[:, 0:4]),
                 dep=[("sT", m_b1)], inc="sV")
    m_b2 = matmul(db[:, 0:4], onesf[0:1, 0:128], stats[0:1, 8:12],
                  start=True, stop=True, dep=[("sV", vb1)], inc="sT")
    vb2 = P.emit("vector", lambda e: e.tensor_copy(wmean3[:], db[:, 0:4]),
                 dep=[("sT", m_b2)], inc="sV")
    P.drain("vector")
    v_thrb = P.emit("vector", lambda e: e.tensor_scalar(
        nthr3[:], thr3[:], -1.0, None, ALU.mult), inc="sV")

    last_store = None
    last_vq = 0
    for wh, wq_out, wi in ((wg_h, wgq_h, 0), (wu_h, wuq_h, 1),
                           (wd_h, wdq_h, 2)):
        wre = wh.ap().rearrange("(a p) f -> p a f", p=128)
        wqre = wq_out.ap().rearrange("(a p) f -> p a f", p=128)
        n_a = wre.shape[1]
        n_f = wre.shape[2] // 1024
        chunks = [(ai, fj) for ai in range(n_a) for fj in range(n_f)]
        for ch, (ai, fj) in enumerate(chunks):
            ld = P.dma("sync", wst[:],
                       wre[:, ai, fj * 1024:(fj + 1) * 1024],
                       dep=[("sV", last_vq), ("sS", last_abs)])
            P.emit("vector", lambda e, wi=wi: e.tensor_scalar(
                qd_[:, 0:1024], wst[:],
                nthr3[:, wi:wi + 1], None, ALU.is_le),
                dep=[ld, ("sV", v_thrb), last_store])
            last_vq = P.emit("vector", lambda e, wi=wi:
                             e.scalar_tensor_tensor(
                                 qb_[:, 0:1024],
                                 wst[:],
                                 thr3[:, wi:wi + 1], qd_[:, 0:1024],
                                 ALU.is_ge, ALU.subtract),
                             inc="sV")
            last_store = P.dma("sync",
                               wqre[:, ai, fj * 1024:(fj + 1) * 1024],
                               qb_[:, 0:1024], dep=[("sV", last_vq)],
                               sem="sO")
    wq_ready = last_store
    if PHASE_LIMIT <= 2:
        P._w("sync", P.fence("sO") + P.fence("sL") + P.fence("sGD"))
        return _finish(nc, P, es)

    tp_free_v = 0
    lp_free = [("sV", 0)]
    ln_prev = 0
    st_k_hist = {}
    st_v_hist = {}
    for kb in range(NKB):
        pa = kb % 2
        ld = P.dma("sync", xin[:, pa * 1024:(pa + 1) * 1024],
                   xkv_h.ap()[kb * 128:(kb + 1) * 128, :],
                   dep=[("sV", ln_prev)])
        vt = ln_chain(xin[:, pa * 1024:(pa + 1) * 1024],
                      xn[:, pa * 1024:(pa + 1) * 1024],
                      dep_v=[ld], dep_s=[ld])
        ln_prev = vt
        vtr = transpose8(xn[:, pa * 1024:(pa + 1) * 1024],
                         dep_t=[("sV", vt), ("sV", tp_free_v)] + gw_done,
                         dep_v=[],
                         out_ap=xnT[:, pa * 1024:(pa + 1) * 1024],
                         in_ap=tp[:])
        tp_free_v = vtr
        t_kv = None
        for ni in range(4):
            for c in range(8):
                dd = ([("sV", vtr)] + lp_free) if (ni == 0 and c == 0) else ()
                t_kv = matmul(
                    lp[:, ni * 512:(ni + 1) * 512],
                    xnT[:, pa * 1024 + c * 128:pa * 1024 + (c + 1) * 128],
                    wkv[:, c * 2048 + ni * 512:c * 2048 + (ni + 1) * 512],
                    start=(c == 0), stop=(c == 7), dep=dd,
                    inc="sT" if (ni == 3 and c == 7) else None)
        v_rope = rope_block(0, cosk, sink, kb, NKB, ktm,
                            [("sT", t_kv), ("sS", s_tabk)])
        v_lo_in = bass.AP(lp, 1024, [[2048, 128], [64, 8], [1, 64]])
        v_hi_in = bass.AP(lp, 1024 + 512, [[2048, 128], [64, 8], [1, 64]])
        v_lo_out = bass.AP(vc, kb * 520, [[NKB * 520, 128], [65, 8], [1, 64]])
        v_hi_out = bass.AP(vstg, 0, [[520, 128], [65, 8], [1, 64]])
        dep_vstg = [st_v_hist.get(kb - 1)]
        P.emit("vector", lambda e, i=v_lo_in, o=v_lo_out: e.tensor_copy(o, i),
               dep=[("sG", g_ones)])
        v_ev = P.emit("vector",
                      lambda e, i=v_hi_in, o=v_hi_out: e.tensor_copy(o, i),
                      dep=dep_vstg, inc="sV")
        lp_free = [("sV", v_ev)]
        vtr2 = transpose8(ktm, dep_t=[("sV", v_rope), ("sV", tp_free_v)],
                          dep_v=[],
                          out_ap=bass.AP(kTc, kb * 512,
                                         [[NKB * 512, 128], [1, 512]]),
                          in_ap=tp[:, 0:512])
        v_k2 = P.emit("vector", lambda e: e.tensor_copy(
            kstg[:], tp[:, 512:1024]),
            dep=[st_k_hist.get(kb - 1)], inc="sV")
        tp_free_v = v_k2
        st_k_hist[kb] = P.dma("sync", kt2_h.ap()[:, kb * 512:(kb + 1) * 512],
                              kstg[:],
                              dep=[("sV", v_k2)], sem="sO")
        st_v_hist[kb] = P.dma("sync", v2_h.ap()[:, kb * 520:(kb + 1) * 520],
                              vstg[:],
                              dep=[("sV", v_ev)], sem="sO")
    p1_done_o = P.fence("sO")
    p1_v = P.val("sV")
    esA1.close()
    if PHASE_LIMIT <= 3:
        P._w("sync", P.fence("sO") + P.fence("sL") + P.fence("sGD"))
        return _finish(nc, P, es)


    lp_free = [("sV", 0)]
    ln_prev2 = p1_v
    for s_ in range(4):
        for tq in range(4):
            ti = s_ * 4 + tq
            pa = ti % 2
            ld = P.dma("sync", xin[:, pa * 1024:(pa + 1) * 1024],
                       xq_h.ap()[ti * 128:(ti + 1) * 128, :],
                       dep=[("sV", ln_prev2)])
            vt = ln_chain(xin[:, pa * 1024:(pa + 1) * 1024],
                          xn[:, pa * 1024:(pa + 1) * 1024],
                          dep_v=[ld], dep_s=[ld])
            ln_prev2 = vt
            vtr = transpose8(xn[:, pa * 1024:(pa + 1) * 1024],
                             dep_t=[("sV", vt), ("sV", tp_free_v)], dep_v=[],
                             out_ap=xnT[:, pa * 1024:(pa + 1) * 1024],
                             in_ap=tp[:])
            tp_free_v = vtr
            t_q = None
            for ni in range(2):
                for c in range(8):
                    dd = ([("sV", vtr)] + lp_free) if (ni == 0 and c == 0) \
                        else ()
                    t_q = matmul(
                        lp[:, ni * 512:(ni + 1) * 512],
                        xnT[:, pa * 1024 + c * 128:
                            pa * 1024 + (c + 1) * 128],
                        wqq[:, c * 1024 + ni * 512:
                            c * 1024 + (ni + 1) * 512],
                        start=(c == 0), stop=(c == 7), dep=dd,
                        inc="sT" if (ni == 1 and c == 7) else None)
            v_rope = rope_block(0, cosq, sinq, ti, 16, ktm,
                                [("sT", t_q), ("sS", s_tabq)])
            lp_free = [("sV", v_rope)]
            vq = transpose8(ktm, dep_t=[("sV", v_rope), ("sV", tp_free_v)],
                            dep_v=[],
                            out_ap=bass.AP(qT[s_], tq * 128,
                                           [[8 * 512, 128], [512, 8],
                                            [1, 128]]),
                            in_ap=bass.AP(tp, 0,
                                          [[1024, 128], [128, 8], [1, 128]]))
            tp_free_v = vq
    qT_done = P.val("sV")
    esA2.close()
    if PHASE_LIMIT <= 4:
        P._w("sync", P.fence("sO") + P.fence("sL") + P.fence("sGD"))
        P.emit("sync", lambda e: e.wait_ge(P.sem["sV"][0], P.val("sV")))
        return _finish(nc, P, es)
    esAtt = ExitStack()
    sbT = lambda name, shape, dt: esAtt.enter_context(
        nc.sbuf_tensor(name, shape, dt))
    maskb = sbT("maskb", [128, 8 * 512], BF16)
    ctxT = sbT("ctxT", [64, 16 * 512], BF16)
    den = sbT("den", [128, 1024], F32)
    rcp = sbT("rcp", [128, 1024], F32)
    pt = sbT("pt", [128, 2 * 1024], BF16)
    x1t = sbT("x1t", [128, 1024], F32)
    woS = sbT("woS", [64, 16 * 1024], BF16)

    gwo = 0
    arena_guard = ([("sT", P.val("sT")), ("sV", P.val("sV")),
                    ("sS", P.val("sS"))]
                   + P.fence("sO") + P.fence("sL"))
    for hh in range(16):
        P.dma("gpsimd", woS[0:64, hh * 1024:(hh + 1) * 1024],
              wo_h.ap()[hh * 64:(hh + 1) * 64, :],
              dep=arena_guard if hh == 0 else ())
    gwo = P.fence("sGD")

    att = {
        "last_exp": {0: 0, 1: 0},
        "last_pv": {0: 0, 1: 0},
        "cp_free": [("sV", 0), ("sS", 0)],
        "mask_free": 0,
        "evac_extra": [],
    }

    def attn_slot(stage, s_, kt_sb, v_sb, dep_mask):
        cap = CAPS[s_]
        mld = P.dma("sync", maskb[:],
                    mask_h.ap()[:, s_ * 4096:(s_ + 1) * 4096],
                    dep=dep_mask + [("sV", att["mask_free"])])
        last_mask_v = 0
        for hpl in range(4):
            hp = stage * 4 + hpl

            def emit_pv(kb, dep_pv):
                pa = kb % 2
                matmul(cp[0:65, 0:512],
                       v_sb[:, kb * 520 + (2 * hpl) * 65:
                            kb * 520 + (2 * hpl) * 65 + 65],
                       pt[:, pa * 1024:pa * 1024 + 512],
                       start=(kb == 0), stop=(kb == cap - 1), dep=dep_pv)
                tpv = matmul(cp[0:65, 512:1024],
                             v_sb[:, kb * 520 + (2 * hpl + 1) * 65:
                                  kb * 520 + (2 * hpl + 1) * 65 + 65],
                             pt[:, pa * 1024 + 512:(pa + 1) * 1024],
                             start=(kb == 0), stop=(kb == cap - 1), inc="sT")
                att["last_pv"][pa] = tpv

            pend = None
            for kb in range(cap):
                pa = kb % 2
                depq = [mld, ("sV", qT_done)]
                if att["last_exp"][pa]:
                    depq.append(("sS", att["last_exp"][pa]))
                depq += lp_free
                matmul(lp[:, pa * 1024:pa * 1024 + 512],
                       kt_sb[0:64, kb * 512 + hpl * 128:
                             kb * 512 + (hpl) * 128 + 128],
                       qT[s_][0:64, hp * 512:(hp + 1) * 512],
                       start=True, stop=True, dep=depq)
                tqk = matmul(lp[:, pa * 1024 + 512:pa * 1024 + 1024],
                             kt_sb[64:128, kb * 512 + hpl * 128:
                                   kb * 512 + hpl * 128 + 128],
                             qT[s_][64:128, hp * 512:(hp + 1) * 512],
                             start=True, stop=True, inc="sT")
                deps = [("sT", tqk)]
                if att["last_pv"][pa]:
                    deps.append(("sT", att["last_pv"][pa]))
                sexp = P.emit("scalar", lambda e, pa=pa: e.activation(
                    pt[:, pa * 1024:(pa + 1) * 1024],
                    lp[:, pa * 1024:(pa + 1) * 1024],
                    AF.Exp, bias=cset[:, 5:6], scale=cset[:, 6:7]),
                    dep=deps, inc="sS")
                att["last_exp"][pa] = sexp
                dep_pv = [("sS", sexp)]
                if kb >= cap - 8:
                    moff = (kb - (cap - 8)) * 512
                    pt_pair = bass.AP(pt, pa * 1024,
                                      [[2048, 128], [512, 2], [1, 512]])
                    m_b = bass.AP(maskb, moff,
                                  [[8 * 512, 128], [0, 2], [1, 512]])
                    vm = P.emit("vector", lambda e, o=pt_pair, m=m_b:
                                e.tensor_tensor(o, o, m, ALU.mult),
                                dep=[("sS", sexp)], inc="sV")
                    last_mask_v = vm
                    dep_pv = [("sV", vm)]
                if kb == 0:
                    dep_pv = dep_pv + att["cp_free"]
                if pend is not None:
                    emit_pv(*pend)
                pend = (kb, dep_pv)
            emit_pv(*pend)
            tdone = att["last_pv"][(cap - 1) % 2]
            vden = P.emit("vector", lambda e: e.reciprocal(
                den[64:65, 0:1024], cp[64:65, 0:1024]),
                dep=[("sT", tdone)], inc="sV")
            mb1 = matmul(db[:], onesf[64:65, 0:128], den[64:65, 0:512],
                         start=True, stop=True, dep=[("sV", vden)], inc="sT")
            vc1 = P.emit("vector", lambda e: e.tensor_copy(
                rcp[:, 0:512], db[:]), dep=[("sT", mb1)], inc="sV")
            mb2 = matmul(db[:], onesf[64:65, 0:128], den[64:65, 512:1024],
                         start=True, stop=True, dep=[("sV", vc1)], inc="sT")
            vc2 = P.emit("vector", lambda e: e.tensor_copy(
                rcp[:, 512:1024], db[:]), dep=[("sT", mb2)], inc="sV")
            hA, hB = 2 * hp, 2 * hp + 1
            extra = att["evac_extra"]
            att["evac_extra"] = []
            P.emit("vector", lambda e, hA=hA: e.tensor_tensor(
                ctxT[0:64, hA * 512:(hA + 1) * 512], cp[0:64, 0:512],
                rcp[0:64, 0:512], ALU.mult), dep=extra)
            vev = P.emit("vector", lambda e, hB=hB: e.tensor_tensor(
                ctxT[0:64, hB * 512:(hB + 1) * 512], cp[0:64, 512:1024],
                rcp[0:64, 512:1024], ALU.mult), dep=[("sV", vc2)], inc="sV")
            att["cp_free"] = [("sV", vev)]
        att["mask_free"] = last_mask_v if last_mask_v else att["mask_free"]

    ctxA_store = 0
    dep0 = [("sV", qT_done)]
    for s_ in range(4):
        attn_slot(0, s_, kTc, vc, dep0)
        dep0 = []
        ctxA_store = P.dma("sync", ctxA_h.ap()[:, s_ * 4096:(s_ + 1) * 4096],
                           ctxT[0:64, 0:4096], dep=[("sV", P.val("sV"))],
                           sem="sO")
        att["evac_extra"] = [ctxA_store]

    if PHASE_LIMIT <= 5:
        P._w("sync", P.fence("sO") + P.fence("sL") + P.fence("sGD"))
        P.emit("sync", lambda e: e.wait_ge(P.sem["sV"][0], P.val("sV")))
        P.emit("sync", lambda e: e.wait_ge(P.sem["sS"][0], P.val("sS")))
        P.emit("sync", lambda e: e.wait_ge(P.sem["sT"][0], P.val("sT")))
        return _finish(nc, P, es)
    rk = P.dma("sync", kTc[:], kt2_h.ap()[:],
               dep=[ctxA_store] + p1_done_o +
                    [("sS", att["last_exp"][0]), ("sS", att["last_exp"][1]),
                     ("sT", att["last_pv"][0]), ("sT", att["last_pv"][1])])
    rv = P.dma("sync", vc[:], v2_h.ap()[:])
    if PHASE_LIMIT == 50:
        P._w("sync", P.fence("sO") + P.fence("sL") + P.fence("sGD"))
        return _finish(nc, P, es)

    st_x1_prev = None
    st_xq_prev = None
    out_dep_first = list(gwo)
    t_op_prev = 0
    v_x1 = 0
    for s_ in range(4):
        if 53 <= PHASE_LIMIT <= 56 and s_ >= PHASE_LIMIT - 52:
            break
        attn_slot(1, s_, kTc, vc, [rv, rk])
        if PHASE_LIMIT == 51 or 53 <= PHASE_LIMIT <= 56:
            att["cp_free"] = [("sV", P.val("sV")), ("sS", P.val("sS"))]
            continue
        rc_ld = P.dma("sync", ctxT[0:64, 0:4096],
                      ctxA_h.ap()[:, s_ * 4096:(s_ + 1) * 4096],
                      dep=[("sV", P.val("sV")), ("sT", t_op_prev)])
        if PHASE_LIMIT == 520:
            att["cp_free"] = [("sV", P.val("sV")), ("sS", P.val("sS"))]
            continue
        t_op = 0
        vt2 = 0
        for tq in range(4):
            ti = s_ * 4 + tq
            ldx = P.dma("sync", xin[:, 0:1024],
                        xq_h.ap()[ti * 128:(ti + 1) * 128, :],
                        dep=[("sV", ln_prev2 if ti == 0 else v_x1)])
            for ni in range(2):
                for hh in range(16):
                    dd = ([rc_ld, ("sV", vt2)] + out_dep_first
                          + att["cp_free"]) \
                        if (ni == 0 and hh == 0) else ()
                    t_op = matmul(
                        cp[:, ni * 512:(ni + 1) * 512],
                        ctxT[0:64, hh * 512 + tq * 128:
                             hh * 512 + tq * 128 + 128],
                        woS[0:64, hh * 1024 + ni * 512:
                            hh * 1024 + (ni + 1) * 512],
                        start=(hh == 0), stop=(hh == 15), dep=dd,
                        inc="sT" if (ni == 1 and hh == 15) else None)
            out_dep_first = []
            if PHASE_LIMIT == 521:
                vt2 = 0
                continue
            vt2 = ln_chain(cp[:, 0:1024], lns[:],
                           dep_v=[("sT", t_op)], dep_s=[("sT", t_op)],
                           psum=True)
            if PHASE_LIMIT == 522:
                continue
            P.drain("vector")
            v_x1 = P.emit("vector", lambda e: e.tensor_add(
                x1t[:], xin[:, 0:1024], lns[:]),
                dep=[("sV", vt2), ldx, st_x1_prev],
                inc="sV")
            st_x1_prev = P.dma("sync",
                               x1_h.ap()[ti * 128:(ti + 1) * 128, :],
                               x1t[:], dep=[("sV", v_x1)], sem="sO")
            if PHASE_LIMIT == 52:
                continue
            vt3 = ln_chain(x1t[:], lns[:], dep_v=[("sV", v_x1)],
                           dep_s=[("sV", v_x1)])
            P.emit("vector", lambda e: e.reduce_max(
                stats[:, 9:10], lns[:], axis=AX, apply_absolute_value=True),
                dep=[("sV", vt3)])
            P.drain("vector")
            P.emit("vector", lambda e: e.tensor_scalar(
                stats[:, 10:11], stats[:, 9:10], 1e-5, None, ALU.max))
            P.drain("vector")
            P.emit("vector", lambda e, ti=ti: e.tensor_scalar(
                sxinv[:, ti:ti + 1], stats[:, 10:11], 1.0 / 127.0, None,
                ALU.mult))
            P.emit("vector", lambda e: e.reciprocal(
                stats[:, 11:12], stats[:, 10:11]))
            P.drain("vector")
            vsc = P.emit("vector", lambda e: e.tensor_scalar(
                stats[:, 12:13], stats[:, 11:12], 127.0, None, ALU.mult),
                inc="sV")
            vq = act_quant_chain(lns[:], 1024, stats[:, 12:13],
                                 dep_v=[("sV", vsc), st_xq_prev])
            st_xq_prev = P.dma("sync",
                               xqq_h.ap()[ti * 128:(ti + 1) * 128, :],
                               qb_[:, 0:1024], dep=[("sV", vq)], sem="sO")
        att["cp_free"] = [("sV", P.val("sV")), ("sS", P.val("sS"))]
        att["evac_extra"] = [("sT", t_op)]
        t_op_prev = t_op
    p2_done = P.fence("sO")
    if PHASE_LIMIT <= 6 or PHASE_LIMIT in (50, 51, 52, 53, 54, 55, 56, 520, 521, 522):
        P._w("sync", P.fence("sO") + P.fence("sL") + P.fence("sGD"))
        P.emit("sync", lambda e: e.wait_ge(P.sem["sV"][0], P.val("sV")))
        P.emit("sync", lambda e: e.wait_ge(P.sem["sS"][0], P.val("sS")))
        P.emit("sync", lambda e: e.wait_ge(P.sem["sT"][0], P.val("sT")))
        return _finish(nc, P, es)

    esAtt.close()
    esA.close()
    esB = ExitStack()
    sbB = lambda name, shape, dt: esB.enter_context(
        nc.sbuf_tensor(name, shape, dt))
    wgT = sbB("wgT", [128, 8 * 4096], BF16)
    wuT = sbB("wuT", [128, 8 * 4096], BF16)
    xqT = sbB("xqT", [128, 8 * 512], BF16)
    hbuf = sbB("hbuf", [128, 4096], F32)
    hqb = sbB("hqb", [128, 4096], BF16)
    sgu = sbB("sgu", [128, 4], F32)

    wl_toks = []
    for c in range(8):
        wl_toks.append(P.dmat("sync", wgT[:, c * 4096:(c + 1) * 4096],
                              wgq_h.ap()[:, c * 128:(c + 1) * 128],
                              dep=p2_done if c == 0 else []))
        wl_toks.append(P.dmat("sync", wuT[:, c * 4096:(c + 1) * 4096],
                              wuq_h.ap()[:, c * 128:(c + 1) * 128]))
    wl_done = [max((t for ln2, t in wl_toks if ln2 == ln), default=0)
               for ln in {ln for ln, _ in wl_toks}]
    wl_done = [(ln, max(t for l2, t in wl_toks if l2 == ln))
               for ln in {ln for ln, _ in wl_toks}]

    hq_store = {}
    lpB_free = [("sS", 0)]
    silu_guard = {0: 0, 1: 0}
    for g in range(4):
        xl_toks = []
        for c in range(8):
            xl_toks.append(P.dmat("sync", xqT[:, c * 512:(c + 1) * 512],
                                  xqq_h.ap()[g * 512:(g + 1) * 512,
                                             c * 128:(c + 1) * 128],
                                  dep=wl_done
                                  if (g == 0 and c == 0) else []))
        xl_done = [(ln, max(t for l2, t in xl_toks if l2 == ln))
                   for ln in {ln for ln, _ in xl_toks}]
        for tl in range(4):
            ti = g * 4 + tl
            P.emit("vector", lambda e, ti=ti: e.tensor_tensor(
                sgu[:, 0:1], sxinv[:, ti:ti + 1], wmean3[:, 0:1], ALU.mult),
                dep=xl_done + [("sS", P.val("sS"))])
            P.drain("vector")
            vsg = P.emit("vector", lambda e, ti=ti: e.tensor_tensor(
                sgu[:, 1:2], sxinv[:, ti:ti + 1], wmean3[:, 1:2], ALU.mult),
                inc="sV")
            h_written = {}
            for nci in range(8):
                pa = nci % 2
                dd0 = [("sV", vsg)] + lpB_free
                if nci >= 2:
                    dd0 = [("sV", h_written[nci - 2])]
                for c in range(8):
                    matmul(lp[:, pa * 512:(pa + 1) * 512],
                           xqT[:, c * 512 + tl * 128:c * 512 + tl * 128 + 128],
                           wgT[:, c * 4096 + nci * 512:
                               c * 4096 + (nci + 1) * 512],
                           start=(c == 0), stop=(c == 7),
                           dep=dd0 if c == 0 else ())
                tgu = None
                for c in range(8):
                    tgu = matmul(
                        lp[:, 1024 + pa * 512:1024 + (pa + 1) * 512],
                        xqT[:, c * 512 + tl * 128:c * 512 + tl * 128 + 128],
                        wuT[:, c * 4096 + nci * 512:
                            c * 4096 + (nci + 1) * 512],
                        start=(c == 0), stop=(c == 7),
                        inc="sT" if c == 7 else None)
                ssdep = [("sT", tgu)]
                if silu_guard[pa]:
                    ssdep.append(("sV", silu_guard[pa]))
                ss = P.emit("scalar", lambda e, pa=pa: e.activation(
                    sq[:, pa * 512:(pa + 1) * 512],
                    lp[:, pa * 512:(pa + 1) * 512], AF.Silu,
                    scale=sgu[:, 0:1]), dep=ssdep, inc="sS")
                vh = P.emit("vector", lambda e, nci=nci, pa=pa:
                            e.scalar_tensor_tensor(
                                hbuf[:, nci * 512:(nci + 1) * 512],
                                lp[:, 1024 + pa * 512:1024 + (pa + 1) * 512],
                                sgu[:, 1:2], sq[:, pa * 512:(pa + 1) * 512],
                                ALU.mult, ALU.mult),
                            dep=[("sS", ss)], inc="sV")
                h_written[nci] = vh
                silu_guard[pa] = vh
            P.emit("vector", lambda e: e.reduce_max(
                stats[:, 9:10], hbuf[:], axis=AX, apply_absolute_value=True),
                dep=[("sV", h_written[7])])
            P.drain("vector")
            P.emit("vector", lambda e: e.tensor_scalar(
                stats[:, 10:11], stats[:, 9:10], 1e-5, None, ALU.max))
            P.drain("vector")
            P.emit("vector", lambda e: e.reciprocal(
                stats[:, 11:12], stats[:, 10:11]))
            P.drain("vector")
            vsc = P.emit("vector", lambda e: e.tensor_scalar(
                stats[:, 12:13], stats[:, 11:12], 127.0, None, ALU.mult),
                inc="sV")
            lastq = vsc
            for qh in range(4):
                vq = act_quant_chain(hbuf[:, qh * 1024:(qh + 1) * 1024],
                                     1024, stats[:, 12:13],
                                     dep_v=[("sV", lastq),
                                            hq_store.get(ti - 1)])
                lastq = P.emit("vector", lambda e, qh=qh: e.tensor_copy(
                    hqb[:, qh * 1024:(qh + 1) * 1024], qb_[:, 0:1024]),
                    dep=[("sV", vq)], inc="sV")
            lpB_free = [("sV", lastq)]
            hq_store[ti] = P.dma("sync", hq_h.ap()[ti * 128:(ti + 1) * 128, :],
                                 hqb[:], dep=[("sV", lastq)], sem="sO")
    b1_done = P.fence("sO")
    b1_v = P.val("sV")
    b1_s = P.val("sS")
    if PHASE_LIMIT <= 7:
        P._w("sync", P.fence("sO") + P.fence("sL") + P.fence("sGD"))
        P.emit("sync", lambda e: e.wait_ge(P.sem["sV"][0], P.val("sV")))
        P.emit("sync", lambda e: e.wait_ge(P.sem["sS"][0], P.val("sS")))
        P.emit("sync", lambda e: e.wait_ge(P.sem["sT"][0], P.val("sT")))
        return _finish(nc, P, es)

    esB.close()
    esC = ExitStack()
    sbC = lambda name, shape, dt: esC.enter_context(
        nc.sbuf_tensor(name, shape, dt))
    wdT = sbC("wdT", [128, 32 * 1024], BF16)
    hqT = sbC("hqT", [128, 32 * 512], BF16)
    x1g = sbC("x1g", [128, 4 * 1024], F32)
    outb = sbC("outb", [128, 2 * 1024], F32)

    wdl_toks = []
    for ic in range(32):
        wdl_toks.append(P.dmat("sync", wdT[:, ic * 1024:(ic + 1) * 1024],
                               wdq_h.ap()[:, ic * 128:(ic + 1) * 128],
                               dep=b1_done + [("sV", b1_v), ("sS", b1_s)]
                               if ic == 0 else []))
    wdl_done = [(ln, max(t for l2, t in wdl_toks if l2 == ln))
                for ln in {ln for ln, _ in wdl_toks}]

    out_store = {}
    lpC_free = [("sV", 0)]
    for g in range(4):
        hl_toks = []
        for ic in range(32):
            hl_toks.append(P.dmat("sync", hqT[:, ic * 512:(ic + 1) * 512],
                                  hq_h.ap()[g * 512:(g + 1) * 512,
                                            ic * 128:(ic + 1) * 128],
                                  dep=wdl_done
                                  if (g == 0 and ic == 0) else []))
        hl_done = [(ln, max(t for l2, t in hl_toks if l2 == ln))
                   for ln in {ln for ln, _ in hl_toks}]
        xg_toks = []
        for a_ in range(4):
            xg_toks.append(P.dma(
                "sync", x1g[:, a_ * 1024:(a_ + 1) * 1024],
                x1_h.ap()[g * 512 + a_ * 128:g * 512 + (a_ + 1) * 128, :],
                dep=hl_done if a_ == 0 else []))
        xg = [(ln, max(t for l2, t in xg_toks if l2 == ln))
              for ln in {ln for ln, _ in xg_toks}]
        for tl in range(4):
            ti = g * 4 + tl
            t_dn = None
            for ni in range(2):
                for ic in range(32):
                    dd = (xg + lpC_free) \
                        if (ic == 0 and ni == 0) else ()
                    t_dn = matmul(
                        lp[:, ni * 512:(ni + 1) * 512],
                        hqT[:, ic * 512 + tl * 128:
                            ic * 512 + tl * 128 + 128],
                        wdT[:, ic * 1024 + ni * 512:
                            ic * 1024 + (ni + 1) * 512],
                        start=(ic == 0), stop=(ic == 31), dep=dd,
                        inc="sT" if (ni == 1 and ic == 31) else None)
            vt4 = ln_chain(lp[:, 0:1024], lns[:],
                           dep_v=[("sT", t_dn)], dep_s=[("sT", t_dn)],
                           psum=True)
            po = tl % 2
            dep_out = [("sV", vt4), out_store.get(ti - 2)]
            vfin = P.emit("vector", lambda e, tl=tl, po=po: e.tensor_add(
                outb[:, po * 1024:(po + 1) * 1024],
                x1g[:, tl * 1024:(tl + 1) * 1024], lns[:]),
                dep=dep_out, inc="sV")
            lpC_free = [("sV", vfin)]
            out_store[ti] = P.dma("sync",
                                  out_h.ap()[ti * 128:(ti + 1) * 128, :],
                                  outb[:, po * 1024:(po + 1) * 1024],
                                  dep=[("sV", vfin)], sem="sO")
    esC.close()

    P._w("sync", P.fence("sO"))

    return _finish(nc, P, es)


def _finish(nc, P, es):
    with nc.Block() as block:
        @block.sync
        def _(sync):
            for fn in P.q["sync"]:
                fn(sync)

        @block.gpsimd
        def _(gpsimd):
            for fn in P.q["gpsimd"]:
                fn(gpsimd)

        @block.vector
        def _(vector):
            for fn in P.q["vector"]:
                fn(vector)

        @block.scalar
        def _(scalar):
            for fn in P.q["scalar"]:
                fn(scalar)

        @block.tensor
        def _(tensor):
            for fn in P.q["tensor"]:
                fn(tensor)

    return nc


def _host_inputs(core, x, wqkv, wo, wg, wu, wd):
    b, par = core // 2, core % 2
    js = JS[par]
    rows = np.concatenate([np.arange(512 * j, 512 * j + 512) for j in js])
    xb = np.ascontiguousarray(x[b])
    xq = np.ascontiguousarray(xb[rows])
    masks = np.zeros((4, 8, 128, 512), dtype=np.float32)
    for s_, j in enumerate(js):
        cap = CAPS[s_]
        q0 = 512 * j
        for kbrel in range(8):
            kb = cap - 8 + kbrel
            kabs = kb * 128 + np.arange(128)[:, None]
            qabs = q0 + np.arange(512)[None, :]
            masks[s_, kbrel] = (kabs <= qabs)
    masks = masks.transpose(2, 0, 1, 3).reshape(128, 4 * 8 * 512)
    masks = np.ascontiguousarray(masks).astype(ml_dtypes.bfloat16)
    pck = (np.arange(NKB)[None, :] * 128
           + np.arange(128)[:, None]).astype(np.float32)
    tq_tiles = np.concatenate([np.arange(4 * j, 4 * j + 4) for j in js])
    pcq = (tq_tiles[None, :] * 128
           + np.arange(128)[:, None]).astype(np.float32)
    fracb = np.ascontiguousarray(np.broadcast_to(
        (2.0 * np.arange(32) / 64.0).astype(np.float32), (128, 32)))
    ident = np.eye(128, dtype=np.float32).astype(ml_dtypes.bfloat16)
    return dict(xkv=xb, xq=xq, wqkv=wqkv, wo=wo, wg=wg, wu=wu, wd=wd,
                masks=masks, poscolk=pck, poscolq=pcq, fracb=fracb,
                ident=ident), rows


def kernel(L0_attn_x, L0_attn_wqkv, L0_attn_wo, L0_ffn_w_gate, L0_ffn_w_up,
           L0_ffn_w_down):
    x = np.ascontiguousarray(np.asarray(L0_attn_x, dtype=np.float32))
    wqkv = np.ascontiguousarray(np.asarray(L0_attn_wqkv, dtype=np.float32))
    wo = np.ascontiguousarray(np.asarray(L0_attn_wo, dtype=np.float32))
    wg = np.ascontiguousarray(np.asarray(L0_ffn_w_gate, dtype=np.float32))
    wu = np.ascontiguousarray(np.asarray(L0_ffn_w_up, dtype=np.float32))
    wd = np.ascontiguousarray(np.asarray(L0_ffn_w_down, dtype=np.float32))

    nc = build_nc()
    in_maps, rowmaps = [], []
    for core in range(8):
        im, rows = _host_inputs(core, x, wqkv, wo, wg, wu, wd)
        in_maps.append(im)
        rowmaps.append(rows)
    res = run_bass_kernel_spmd(nc, in_maps, core_ids=list(range(8)))
    out = np.empty((B, S, DM), dtype=np.float32)
    for core in range(8):
        b = core // 2
        out[b][rowmaps[core]] = np.asarray(res.results[core]["out"],
                                           dtype=np.float32)
    return out


# version 05
# speedup vs baseline: 1.4492x; 1.0244x over previous
import numpy as np
import ml_dtypes
from contextlib import ExitStack

import concourse.bass as bass
import concourse.mybir as mybir
from concourse.bass_utils import run_bass_kernel_spmd

F32 = mybir.dt.float32
BF16 = mybir.dt.bfloat16
I32 = mybir.dt.int32
AF = mybir.ActivationFunctionType
ALU = mybir.AluOpType
AX = mybir.AxisListType.X

B, S, DM, H, DH, INNER = 4, 4096, 1024, 16, 64, 4096
NT = 2048
CAPS = [8, 16, 24, 32]
JS = {0: [0, 3, 4, 7], 1: [1, 2, 5, 6]}
NKB = S // 128
EPS = 1e-5
EXPB = -10.0
RC = 256.0
DEBUG_TAPS = False
PHASE_LIMIT = 99


class Prog:
    ENGS = ("sync", "gpsimd", "vector", "scalar", "tensor")
    LANES = {"sL": 4, "sO": 2, "sGD": 2}

    def __init__(self, nc, es):
        self.nc = nc
        self.q = {e: [] for e in self.ENGS}
        self.sem = {}
        for name in ("sG", "sV", "sS", "sT"):
            self.sem[name] = [es.enter_context(nc.semaphore(name)), 0]
        for g, n in self.LANES.items():
            for i in range(n):
                nm = "%s%d" % (g, i)
                self.sem[nm] = [es.enter_context(nc.semaphore(nm)), 0]
        self.rr = {g: 0 for g in self.LANES}
        self.waited = {e: {} for e in self.ENGS}
        self.lastdrain = {e: False for e in self.ENGS}

    def _w(self, eng, dep):
        for d in dep:
            if not d:
                continue
            sn, val = d
            if val <= 0:
                continue
            if self.waited[eng].get(sn, 0) >= val:
                continue
            self.waited[eng][sn] = val
            h = self.sem[sn][0]
            self.q[eng].append(lambda e, h=h, v=val: e.wait_ge(h, v))
            self.lastdrain[eng] = False

    AUTODRAIN = ("vector", "gpsimd", "scalar")

    def emit(self, eng, fn, dep=(), inc=None, n=1):
        self._w(eng, dep)
        if eng in self.AUTODRAIN and not self.lastdrain[eng]:
            self.q[eng].append(lambda e: e.drain())
        self.lastdrain[eng] = False
        if inc is None:
            self.q[eng].append(fn)
            return None
        self.sem[inc][1] += n
        tgt = self.sem[inc][1]
        h = self.sem[inc][0]

        def run(e, fn=fn, h=h, n=n):
            fn(e).then_inc(h, n)
        self.q[eng].append(run)
        return tgt

    def drain(self, eng):
        if not self.lastdrain[eng]:
            self.q[eng].append(lambda e: e.drain())
            self.lastdrain[eng] = True

    def val(self, semname):
        return self.sem[semname][1]

    def fence(self, group):
        return [("%s%d" % (group, i), self.sem["%s%d" % (group, i)][1])
                for i in range(self.LANES[group])
                if self.sem["%s%d" % (group, i)][1] > 0]

    def _dma(self, eng, fn, dep, group):
        i = self.rr[group]
        self.rr[group] = (i + 1) % self.LANES[group]
        lane = "%s%d" % (group, i)
        prev = self.sem[lane][1]
        dep = list(dep) + [(lane, prev)]
        tgt = self.emit(eng, fn, dep=dep, inc=lane, n=16)
        return (lane, tgt)

    def dma(self, eng, dst, src, dep=(), sem=None, **kw):
        group = sem or ("sL" if eng == "sync" else "sGD")
        return self._dma(eng, lambda e: e.dma_start(dst, src, **kw),
                         dep, group)

    def dmat(self, eng, dst, src, dep=(), sem="sL"):
        return self._dma(eng, lambda e: e.dma_start_transpose(dst, src),
                         dep, group=sem)


def build_nc():
    nc = bass.Bass("TRN2", target_bir_lowering=False)
    es = ExitStack()

    xkv_h = nc.dram_tensor("xkv", [S, DM], F32, kind="ExternalInput")
    xq_h = nc.dram_tensor("xq", [NT, DM], F32, kind="ExternalInput")
    wqkv_h = nc.dram_tensor("wqkv", [DM, 3 * DM], F32, kind="ExternalInput")
    wo_h = nc.dram_tensor("wo", [DM, DM], F32, kind="ExternalInput")
    wg_h = nc.dram_tensor("wg", [INNER, DM], F32, kind="ExternalInput")
    wu_h = nc.dram_tensor("wu", [INNER, DM], F32, kind="ExternalInput")
    wd_h = nc.dram_tensor("wd", [DM, INNER], F32, kind="ExternalInput")
    mask_h = nc.dram_tensor("masks", [128, 4 * 8 * 512], BF16,
                            kind="ExternalInput")
    pck_h = nc.dram_tensor("poscolk", [128, NKB], F32, kind="ExternalInput")
    pcq_h = nc.dram_tensor("poscolq", [128, 16], F32, kind="ExternalInput")
    frac_h = nc.dram_tensor("fracb", [128, 32], F32, kind="ExternalInput")
    idn_h = nc.dram_tensor("ident", [128, 128], BF16, kind="ExternalInput")
    out_h = nc.dram_tensor("out", [NT, DM], F32, kind="ExternalOutput")

    kd = "ExternalOutput" if DEBUG_TAPS else "Internal"
    wgq_h = nc.dram_tensor("wgq", [INNER, DM], BF16, kind=kd)
    wuq_h = nc.dram_tensor("wuq", [INNER, DM], BF16, kind=kd)
    wdq_h = nc.dram_tensor("wdq", [DM, INNER], BF16, kind=kd)
    kt2_h = nc.dram_tensor("kt2", [128, NKB * 512], BF16, kind=kd)
    v2_h = nc.dram_tensor("v2", [128, NKB * 520], BF16, kind=kd)
    ctxA_h = nc.dram_tensor("ctxA", [64, 4 * 8 * 512], BF16, kind=kd)
    x1_h = nc.dram_tensor("x1", [NT, DM], F32, kind=kd)
    xqq_h = nc.dram_tensor("xqq", [NT, DM], BF16, kind=kd)
    hq_h = nc.dram_tensor("hqs", [NT, INNER], BF16, kind=kd)

    lp = es.enter_context(nc.psum_tensor("lp", [128, 2048], F32))
    cp = es.enter_context(nc.psum_tensor("cp", [128, 1024], F32))
    db = es.enter_context(nc.psum_tensor("db", [128, 512], F32))
    tp = es.enter_context(nc.psum_tensor("tp", [128, 1024], BF16))

    sb = lambda name, shape, dt: es.enter_context(
        nc.sbuf_tensor(name, shape, dt))
    idn = sb("idn", [128, 128], BF16)
    onesf = sb("onesf", [128, 128], F32)
    cset = sb("cset", [128, 12], F32)
    pck = sb("pck", [128, NKB], F32)
    pcq = sb("pcq", [128, 16], F32)
    frac = sb("frac", [128, 32], F32)
    stats = sb("stats", [128, 16], F32)
    acc3 = sb("acc3", [128, 4], F32)
    thr3 = sb("thr3", [128, 4], F32)
    nthr3 = sb("nthr3", [128, 4], F32)
    wmean3 = sb("wmean3", [128, 4], F32)
    sxinv = sb("sxinv", [128, 16], F32)
    sq = sb("sq", [128, 1024], F32)
    ra = sb("ra", [128, 512], F32)
    rb = sb("rb", [128, 512], F32)
    lns = sb("lns", [128, 1024], F32)
    qt_ = sb("qt", [128, 1024], F32)
    qi_ = sb("qi", [128, 1024], I32)
    qd_ = sb("qd", [128, 1024], F32)
    qb_ = sb("qb", [128, 1024], BF16)

    P = Prog(nc, es)

    def matmul(out, lhsT, rhs, start, stop, dep=(), inc=None):
        return P.emit("tensor",
                      lambda e: e.matmul(out, lhsT, rhs, start=start,
                                         stop=stop),
                      dep=dep, inc=inc)

    def ln_chain(src_ap, dst_ap, dep_v, dep_s, psum=False):
        if psum:
            cv = P.emit("vector",
                        lambda e, o=dst_ap, i=src_ap: e.tensor_copy(o, i),
                        dep=dep_v, inc="sV")
            dep_v = [("sV", cv)]
            dep_s = [("sV", cv)]
            src_ap = dst_ap
        P.emit("vector", lambda e: e.reduce_sum(stats[:, 0:1], src_ap,
                                                axis=AX), dep=dep_v)
        s1 = P.emit("scalar",
                    lambda e: e.activation(sq[:], src_ap, AF.Square,
                                           accum_out=stats[:, 1:2]),
                    dep=dep_s, inc="sS")
        P.drain("vector")
        P.emit("vector", lambda e: e.tensor_scalar(
            stats[:, 2:3], stats[:, 0:1], 1.0 / DM, None, ALU.mult))
        P.emit("vector", lambda e: e.tensor_scalar(
            stats[:, 3:4], stats[:, 1:2], 1.0 / DM, EPS, ALU.mult, ALU.add),
            dep=[("sS", s1)])
        P.drain("vector")
        P.emit("vector", lambda e: e.tensor_mul(
            stats[:, 4:5], stats[:, 2:3], stats[:, 2:3]))
        P.drain("vector")
        P.emit("vector", lambda e: e.tensor_sub(
            stats[:, 5:6], stats[:, 3:4], stats[:, 4:5]))
        P.drain("vector")
        v2t = P.emit("vector", lambda e: e.reciprocal(stats[:, 6:7],
                                                      stats[:, 5:6]),
                     inc="sV")
        s2 = P.emit("scalar", lambda e: e.activation(
            stats[:, 7:8], stats[:, 6:7], AF.Sqrt), dep=[("sV", v2t)],
            inc="sS")
        vt = P.emit("vector", lambda e: e.tensor_scalar(
            dst_ap, src_ap, stats[:, 2:3], stats[:, 7:8],
            ALU.subtract, ALU.mult), dep=[("sS", s2)], inc="sV")
        return vt

    def transpose8(src_sb, dep_t, dep_v, out_ap, in_ap):
        tt = None
        for j in range(8):
            tt = P.emit("tensor", lambda e, j=j: e.transpose(
                tp[:, j * 128:(j + 1) * 128],
                src_sb[:, j * 128:(j + 1) * 128], idn[:]),
                dep=(dep_t if j == 0 else ()),
                inc="sT" if j == 7 else None)
        vt = P.emit("vector", lambda e: e.tensor_copy(out_ap, in_ap),
                    dep=[("sT", tt)] + list(dep_v), inc="sV")
        return vt

    aq_state = {"g": 0}

    def act_quant_chain(src_ap, n, s_ap, dep_v):
        t1 = P.emit("vector", lambda e: e.tensor_scalar(
            qt_[:, 0:n], src_ap, s_ap, RC, ALU.mult, ALU.add),
            dep=list(dep_v) + [("sG", aq_state["g"])], inc="sV")
        g1 = P.emit("gpsimd", lambda e: e.tensor_copy(qi_[:, 0:n],
                                                      qt_[:, 0:n]),
                    dep=[("sV", t1)], inc="sG")
        aq_state["g"] = g1
        P.emit("vector", lambda e: e.tensor_sub(
            qd_[:, 0:n], qt_[:, 0:n], qi_[:, 0:n]), dep=[("sG", g1)])
        P.emit("vector", lambda e: e.scalar_tensor_tensor(
            qt_[:, 0:n], qd_[:, 0:n], 0.5, qi_[:, 0:n], ALU.is_ge, ALU.add))
        P.drain("vector")
        v3 = P.emit("vector", lambda e: e.tensor_scalar(
            qb_[:, 0:n], qt_[:, 0:n], RC, None, ALU.subtract), inc="sV")
        return v3

    def rope_block(src_psum_off, cos_sb, sin_sb, toff, tw, dst_sb, dep_first):
        in1 = bass.AP(lp, src_psum_off, [[2048, 128], [64, 16], [1, 32]])
        in2 = bass.AP(lp, src_psum_off + 32, [[2048, 128], [64, 16], [1, 32]])
        cs = bass.AP(cos_sb, toff * 32, [[tw * 32, 128], [0, 16], [1, 32]])
        sn = bass.AP(sin_sb, toff * 32, [[tw * 32, 128], [0, 16], [1, 32]])
        o1 = bass.AP(dst_sb, 0, [[1024, 128], [64, 16], [1, 32]])
        o2 = bass.AP(dst_sb, 32, [[1024, 128], [64, 16], [1, 32]])
        P.emit("vector", lambda e: e.tensor_tensor(ra[:], in1, cs, ALU.mult),
               dep=dep_first)
        P.emit("vector", lambda e: e.tensor_tensor(rb[:], in2, sn, ALU.mult))
        P.drain("vector")
        P.emit("vector", lambda e: e.tensor_sub(o1, ra[:], rb[:]))
        P.emit("vector", lambda e: e.tensor_tensor(ra[:], in2, cs, ALU.mult))
        P.emit("vector", lambda e: e.tensor_tensor(rb[:], in1, sn, ALU.mult))
        P.drain("vector")
        return P.emit("vector", lambda e: e.tensor_add(o2, ra[:], rb[:]),
                      inc="sV")

    esA = ExitStack()
    sbA = lambda name, shape, dt: esA.enter_context(
        nc.sbuf_tensor(name, shape, dt))
    kTc = sbA("kTc", [128, NKB * 512], BF16)
    vc = sbA("vc", [128, NKB * 520], BF16)
    xin = sbA("xin", [128, 2 * 1024], F32)
    qT = [sbA("qT%d" % s_, [128, 8 * 512], BF16) for s_ in range(4)]
    esA2 = ExitStack()
    sbA2 = lambda name, shape, dt: esA2.enter_context(
        nc.sbuf_tensor(name, shape, dt))
    wqq = sbA2("wqq", [128, 8 * 1024], BF16)
    xn = sbA2("xn", [128, 2 * 1024], BF16)
    xnT = sbA2("xnT", [128, 2 * 1024], BF16)
    ktm = sbA2("ktm", [128, 1024], BF16)
    cosq = sbA2("cosq", [128, 16 * 32], F32)
    sinq = sbA2("sinq", [128, 16 * 32], F32)
    esA1 = ExitStack()
    sbA1 = lambda name, shape, dt: esA1.enter_context(
        nc.sbuf_tensor(name, shape, dt))
    wkv = sbA1("wkv", [128, 8 * 2048], BF16)
    kstg = sbA1("kstg", [128, 512], BF16)
    vstg = sbA1("vstg", [128, 520], BF16)
    wst = sbA1("wst", [128, 1024], F32)
    cosk = sbA1("cosk", [128, NKB * 32], BF16)
    sink = sbA1("sink", [128, NKB * 32], BF16)

    lid = P.dma("sync", idn[:], idn_h.ap())
    lpk = P.dma("sync", pck[:], pck_h.ap())
    lpq = P.dma("sync", pcq[:], pcq_h.ap())
    lfr = P.dma("sync", frac[:], frac_h.ap())

    P.emit("gpsimd", lambda e: e.memset(onesf[:], 1.0))
    P.emit("gpsimd", lambda e: e.memset(cset[:, 0:1], -1.0))
    P.emit("gpsimd", lambda e: e.memset(cset[:, 5:6], EXPB))
    P.emit("gpsimd", lambda e: e.memset(cset[:, 6:7], 0.125))
    P.emit("gpsimd", lambda e: e.memset(sq[:, 0:1], 10000.0))
    P.emit("gpsimd", lambda e: e.memset(sq[:, 1:2], 1.0))
    P.emit("gpsimd", lambda e: e.memset(acc3[:], 0.0))
    P.drain("gpsimd")
    g0 = P.emit("gpsimd", lambda e: e.engine_nop(), inc="sG")

    s_ln = P.emit("scalar", lambda e: e.activation(
        cset[:, 1:2], sq[:, 0:1], AF.Ln), dep=[("sG", g0)], inc="sS")
    s_at = P.emit("scalar", lambda e: e.activation(
        cset[:, 7:8], sq[:, 1:2], AF.Arctan), inc="sS")
    P.emit("vector", lambda e: e.tensor_scalar(
        cset[:, 2:3], cset[:, 7:8], 2.0, None, ALU.mult), dep=[("sS", s_at)])
    P.emit("vector", lambda e: e.tensor_scalar(
        cset[:, 3:4], cset[:, 7:8], 8.0, None, ALU.mult))
    P.emit("vector", lambda e: e.tensor_scalar(
        cset[:, 8:9], cset[:, 7:8], 4.0, None, ALU.mult))
    P.emit("vector", lambda e: e.tensor_scalar(
        cset[:, 9:10], cset[:, 7:8], -4.0, None, ALU.mult))
    P.emit("vector", lambda e: e.tensor_scalar(
        cset[:, 10:11], cset[:, 7:8], -8.0, None, ALU.mult))
    v_2p = P.emit("vector", lambda e: e.reciprocal(cset[:, 4:5],
                                                   cset[:, 3:4]), inc="sV")
    v_fl = P.emit("vector", lambda e: e.tensor_scalar(
        sq[:, 32:64], frac[:], cset[:, 1:2], None, ALU.mult),
        dep=[lfr, ("sS", s_ln), ("sV", v_2p)], inc="sV")
    s_it = P.emit("scalar", lambda e: e.activation(
        frac[:], sq[:, 32:64], AF.Exp, scale=cset[:, 0:1]),
        dep=[("sV", v_fl)], inc="sS")

    def build_tables(n_tiles, pc_sb, cos_sb, sin_sb, dep0):
        deps = list(dep0)
        st = 0
        for t in range(n_tiles):
            P.emit("vector", lambda e, t=t: e.tensor_scalar(
                ra[:, 0:32], frac[:], pc_sb[:, t:t + 1], None, ALU.mult),
                dep=deps + [("sS", st)])
            deps = []
            P.drain("vector")
            P.emit("vector", lambda e: e.tensor_scalar(
                ra[:, 32:64], ra[:, 0:32], 1.0, cset[:, 2:3],
                ALU.mult, ALU.add))
            P.drain("vector")
            vm = P.emit("vector", lambda e: e.tensor_scalar(
                rb[:, 0:64], ra[:, 0:64], cset[:, 4:5], None, ALU.mult),
                inc="sV")
            gk = P.emit("gpsimd", lambda e: e.tensor_copy(
                qi_[:, 0:64], rb[:, 0:64]), dep=[("sV", vm)], inc="sG")
            P.emit("vector", lambda e: e.scalar_tensor_tensor(
                rb[:, 0:64], qi_[:, 0:64], cset[:, 10:11], ra[:, 0:64],
                ALU.mult, ALU.add), dep=[("sG", gk)])
            P.emit("vector", lambda e: e.tensor_scalar(
                rb[:, 64:128], rb[:, 0:64], cset[:, 8:9], None, ALU.is_gt))
            P.emit("vector", lambda e: e.scalar_tensor_tensor(
                rb[:, 0:64], rb[:, 64:128], cset[:, 10:11], rb[:, 0:64],
                ALU.mult, ALU.add))
            P.emit("vector", lambda e: e.tensor_scalar(
                rb[:, 64:128], rb[:, 0:64], cset[:, 9:10], None, ALU.is_lt))
            vr = P.emit("vector", lambda e: e.scalar_tensor_tensor(
                rb[:, 0:64], rb[:, 64:128], cset[:, 3:4], rb[:, 0:64],
                ALU.mult, ALU.add), inc="sV")
            P.emit("scalar", lambda e, t=t: e.activation(
                sin_sb[:, t * 32:(t + 1) * 32], rb[:, 0:32], AF.Sin),
                dep=[("sV", vr)])
            st = P.emit("scalar", lambda e, t=t: e.activation(
                cos_sb[:, t * 32:(t + 1) * 32], rb[:, 32:64], AF.Sin),
                inc="sS")
        return st

    s_tabk = build_tables(NKB, pck, cosk, sink,
                          [("sS", s_it), lpk, lpq, lid])
    s_tabq = build_tables(16, pcq, cosq, sinq, [('sS', s_tabk)])


    for c in range(8):
        P.dma("gpsimd", wkv[:, c * 2048:(c + 1) * 2048],
              wqkv_h.ap()[c * 128:(c + 1) * 128, 1024:3072])
        P.dma("gpsimd", wqq[:, c * 1024:(c + 1) * 1024],
              wqkv_h.ap()[c * 128:(c + 1) * 128, 0:1024])
    gw_done = P.fence("sGD")

    P.emit("gpsimd", lambda e: e.memset(
        bass.AP(vc, 64, [[NKB * 520, 128], [520, NKB], [65, 8]]), 1.0))
    P.emit("gpsimd", lambda e: e.memset(
        bass.AP(vstg, 64, [[520, 128], [65, 8]]), 1.0))
    P.drain("gpsimd")
    g_ones = P.emit("gpsimd", lambda e: e.engine_nop(), inc="sG")

    wstate = {"abs": 0, "ga": g_ones, "vq": 0, "store": None, "thrb": 0}
    wsteps = []

    def w1_step(wh, wi, ai, fj):
        wre = wh.ap().rearrange("(a p) f -> p a f", p=128)
        ld = P.dma("sync", wst[:],
                   wre[:, ai, fj * 1024:(fj + 1) * 1024],
                   dep=[("sS", wstate["abs"])])
        wstate["abs"] = P.emit("scalar", lambda e: e.activation(
            sq[:], wst[:], AF.Abs, accum_out=stats[:, 8:9]),
            dep=[ld, ("sG", wstate["ga"])], inc="sS")
        wstate["ga"] = P.emit("gpsimd", lambda e, wi=wi: e.tensor_tensor(
            acc3[:, wi:wi + 1], acc3[:, wi:wi + 1], stats[:, 8:9],
            ALU.add), dep=[("sS", wstate["abs"]), ("sG", wstate["ga"])],
            inc="sG")

    def w_thr_step():
        mt = matmul(db[0:1, 0:4], onesf[:, 0:1], acc3[:], start=True,
                    stop=True, dep=[("sG", wstate["ga"])], inc="sT")
        P.emit("vector", lambda e: e.tensor_scalar(
            stats[0:1, 8:12], db[0:1, 0:4], 1.0 / (INNER * DM), 1e-5,
            ALU.mult, ALU.max), dep=[("sT", mt)])
        v_thr = P.emit("vector", lambda e: e.tensor_scalar(
            stats[0:1, 12:16], stats[0:1, 8:12], 0.5, None, ALU.mult),
            inc="sV")
        m_b1 = matmul(db[:, 0:4], onesf[0:1, 0:128], stats[0:1, 12:16],
                      start=True, stop=True, dep=[("sV", v_thr)], inc="sT")
        vb1 = P.emit("vector", lambda e: e.tensor_copy(thr3[:], db[:, 0:4]),
                     dep=[("sT", m_b1)], inc="sV")
        m_b2 = matmul(db[:, 0:4], onesf[0:1, 0:128], stats[0:1, 8:12],
                      start=True, stop=True, dep=[("sV", vb1)], inc="sT")
        P.emit("vector", lambda e: e.tensor_copy(wmean3[:], db[:, 0:4]),
               dep=[("sT", m_b2)])
        wstate["thrb"] = P.emit("vector", lambda e: e.tensor_scalar(
            nthr3[:], thr3[:], -1.0, None, ALU.mult), inc="sV")

    def w2_step(wh, wq_out, wi, ai, fj):
        wre = wh.ap().rearrange("(a p) f -> p a f", p=128)
        wqre = wq_out.ap().rearrange("(a p) f -> p a f", p=128)
        ld = P.dma("sync", wst[:],
                   wre[:, ai, fj * 1024:(fj + 1) * 1024],
                   dep=[("sV", wstate["vq"]), ("sS", wstate["abs"])])
        P.emit("vector", lambda e, wi=wi: e.tensor_scalar(
            qd_[:, 0:1024], wst[:], nthr3[:, wi:wi + 1], None, ALU.is_le),
            dep=[ld, ("sV", wstate["thrb"]), wstate["store"]])
        wstate["vq"] = P.emit("vector", lambda e, wi=wi:
                              e.scalar_tensor_tensor(
                                  qb_[:, 0:1024], wst[:],
                                  thr3[:, wi:wi + 1], qd_[:, 0:1024],
                                  ALU.is_ge, ALU.subtract), inc="sV")
        wstate["store"] = P.dma("sync",
                                wqre[:, ai, fj * 1024:(fj + 1) * 1024],
                                qb_[:, 0:1024], dep=[("sV", wstate["vq"])],
                                sem="sO")

    for wh, wi in ((wg_h, 0), (wu_h, 1), (wd_h, 2)):
        n_a = wh.ap().shape[0] // 128
        n_f = wh.ap().shape[1] // 1024
        for ai in range(n_a):
            for fj in range(n_f):
                wsteps.append(lambda wh=wh, wi=wi, ai=ai, fj=fj:
                              w1_step(wh, wi, ai, fj))
    wsteps.append(w_thr_step)
    for wh, wq_out, wi in ((wg_h, wgq_h, 0), (wu_h, wuq_h, 1),
                           (wd_h, wdq_h, 2)):
        n_a = wh.ap().shape[0] // 128
        n_f = wh.ap().shape[1] // 1024
        for ai in range(n_a):
            for fj in range(n_f):
                wsteps.append(lambda wh=wh, wq_out=wq_out, wi=wi, ai=ai,
                              fj=fj: w2_step(wh, wq_out, wi, ai, fj))
    if PHASE_LIMIT <= 2:
        for st_fn in wsteps:
            st_fn()
        P._w("sync", P.fence("sO") + P.fence("sL") + P.fence("sGD"))
        return _finish(nc, P, es)

    tp_free_v = 0
    lp_free = [("sV", 0)]
    ln_prev = 0
    st_k_hist = {}
    st_v_hist = {}
    for kb in range(NKB):
        pa = kb % 2
        ld = P.dma("sync", xin[:, pa * 1024:(pa + 1) * 1024],
                   xkv_h.ap()[kb * 128:(kb + 1) * 128, :],
                   dep=[("sV", ln_prev)])
        vt = ln_chain(xin[:, pa * 1024:(pa + 1) * 1024],
                      xn[:, pa * 1024:(pa + 1) * 1024],
                      dep_v=[ld], dep_s=[ld])
        ln_prev = vt
        vtr = transpose8(xn[:, pa * 1024:(pa + 1) * 1024],
                         dep_t=[("sV", vt), ("sV", tp_free_v)] + gw_done,
                         dep_v=[],
                         out_ap=xnT[:, pa * 1024:(pa + 1) * 1024],
                         in_ap=tp[:])
        tp_free_v = vtr
        t_kv = None
        for ni in range(4):
            for c in range(8):
                dd = ([("sV", vtr)] + lp_free) if (ni == 0 and c == 0) else ()
                t_kv = matmul(
                    lp[:, ni * 512:(ni + 1) * 512],
                    xnT[:, pa * 1024 + c * 128:pa * 1024 + (c + 1) * 128],
                    wkv[:, c * 2048 + ni * 512:c * 2048 + (ni + 1) * 512],
                    start=(c == 0), stop=(c == 7), dep=dd,
                    inc="sT" if (ni == 3 and c == 7) else None)
        v_rope = rope_block(0, cosk, sink, kb, NKB, ktm,
                            [("sT", t_kv), ("sS", s_tabk)])
        v_lo_in = bass.AP(lp, 1024, [[2048, 128], [64, 8], [1, 64]])
        v_hi_in = bass.AP(lp, 1024 + 512, [[2048, 128], [64, 8], [1, 64]])
        v_lo_out = bass.AP(vc, kb * 520, [[NKB * 520, 128], [65, 8], [1, 64]])
        v_hi_out = bass.AP(vstg, 0, [[520, 128], [65, 8], [1, 64]])
        dep_vstg = [st_v_hist.get(kb - 1)]
        P.emit("vector", lambda e, i=v_lo_in, o=v_lo_out: e.tensor_copy(o, i),
               dep=[("sG", g_ones)])
        v_ev = P.emit("vector",
                      lambda e, i=v_hi_in, o=v_hi_out: e.tensor_copy(o, i),
                      dep=dep_vstg, inc="sV")
        lp_free = [("sV", v_ev)]
        vtr2 = transpose8(ktm, dep_t=[("sV", v_rope), ("sV", tp_free_v)],
                          dep_v=[],
                          out_ap=bass.AP(kTc, kb * 512,
                                         [[NKB * 512, 128], [1, 512]]),
                          in_ap=tp[:, 0:512])
        v_k2 = P.emit("vector", lambda e: e.tensor_copy(
            kstg[:], tp[:, 512:1024]),
            dep=[st_k_hist.get(kb - 1)], inc="sV")
        tp_free_v = v_k2
        st_k_hist[kb] = P.dma("sync", kt2_h.ap()[:, kb * 512:(kb + 1) * 512],
                              kstg[:],
                              dep=[("sV", v_k2)], sem="sO")
        st_v_hist[kb] = P.dma("sync", v2_h.ap()[:, kb * 520:(kb + 1) * 520],
                              vstg[:],
                              dep=[("sV", v_ev)], sem="sO")
        for _ in range(7):
            if wsteps:
                wsteps.pop(0)()
    while wsteps:
        wsteps.pop(0)()
    p1_done_o = P.fence("sO")
    p1_v = P.val("sV")
    esA1.close()
    if PHASE_LIMIT <= 3:
        P._w("sync", P.fence("sO") + P.fence("sL") + P.fence("sGD"))
        return _finish(nc, P, es)


    lp_free = [("sV", 0)]
    ln_prev2 = p1_v
    for s_ in range(4):
        for tq in range(4):
            ti = s_ * 4 + tq
            pa = ti % 2
            ld = P.dma("sync", xin[:, pa * 1024:(pa + 1) * 1024],
                       xq_h.ap()[ti * 128:(ti + 1) * 128, :],
                       dep=[("sV", ln_prev2)])
            vt = ln_chain(xin[:, pa * 1024:(pa + 1) * 1024],
                          xn[:, pa * 1024:(pa + 1) * 1024],
                          dep_v=[ld], dep_s=[ld])
            ln_prev2 = vt
            vtr = transpose8(xn[:, pa * 1024:(pa + 1) * 1024],
                             dep_t=[("sV", vt), ("sV", tp_free_v)], dep_v=[],
                             out_ap=xnT[:, pa * 1024:(pa + 1) * 1024],
                             in_ap=tp[:])
            tp_free_v = vtr
            t_q = None
            for ni in range(2):
                for c in range(8):
                    dd = ([("sV", vtr)] + lp_free) if (ni == 0 and c == 0) \
                        else ()
                    t_q = matmul(
                        lp[:, ni * 512:(ni + 1) * 512],
                        xnT[:, pa * 1024 + c * 128:
                            pa * 1024 + (c + 1) * 128],
                        wqq[:, c * 1024 + ni * 512:
                            c * 1024 + (ni + 1) * 512],
                        start=(c == 0), stop=(c == 7), dep=dd,
                        inc="sT" if (ni == 1 and c == 7) else None)
            v_rope = rope_block(0, cosq, sinq, ti, 16, ktm,
                                [("sT", t_q), ("sS", s_tabq)])
            lp_free = [("sV", v_rope)]
            vq = transpose8(ktm, dep_t=[("sV", v_rope), ("sV", tp_free_v)],
                            dep_v=[],
                            out_ap=bass.AP(qT[s_], tq * 128,
                                           [[8 * 512, 128], [512, 8],
                                            [1, 128]]),
                            in_ap=bass.AP(tp, 0,
                                          [[1024, 128], [128, 8], [1, 128]]))
            tp_free_v = vq
    qT_done = P.val("sV")
    esA2.close()
    if PHASE_LIMIT <= 4:
        P._w("sync", P.fence("sO") + P.fence("sL") + P.fence("sGD"))
        P.emit("sync", lambda e: e.wait_ge(P.sem["sV"][0], P.val("sV")))
        return _finish(nc, P, es)
    esAtt = ExitStack()
    sbT = lambda name, shape, dt: esAtt.enter_context(
        nc.sbuf_tensor(name, shape, dt))
    maskb = sbT("maskb", [128, 8 * 512], BF16)
    ctxT = sbT("ctxT", [64, 16 * 512], BF16)
    den = sbT("den", [128, 1024], F32)
    rcp = sbT("rcp", [128, 1024], F32)
    pt = sbT("pt", [128, 2 * 1024], BF16)
    x1t = sbT("x1t", [128, 1024], F32)
    woS = sbT("woS", [64, 16 * 1024], BF16)

    gwo = 0
    arena_guard = ([("sT", P.val("sT")), ("sV", P.val("sV")),
                    ("sS", P.val("sS"))]
                   + P.fence("sO") + P.fence("sL"))
    for hh in range(16):
        P.dma("gpsimd", woS[0:64, hh * 1024:(hh + 1) * 1024],
              wo_h.ap()[hh * 64:(hh + 1) * 64, :],
              dep=arena_guard if hh == 0 else ())
    gwo = P.fence("sGD")

    att = {
        "last_exp": {0: 0, 1: 0},
        "last_pv": {0: 0, 1: 0},
        "cp_free": [("sV", 0), ("sS", 0)],
        "mask_free": 0,
        "evac_extra": [],
    }

    def attn_slot(stage, s_, kt_sb, v_sb, dep_mask):
        cap = CAPS[s_]
        mld = P.dma("sync", maskb[:],
                    mask_h.ap()[:, s_ * 4096:(s_ + 1) * 4096],
                    dep=dep_mask + [("sV", att["mask_free"])])
        last_mask_v = 0
        for hpl in range(4):
            hp = stage * 4 + hpl

            def emit_pv(kb, dep_pv):
                pa = kb % 2
                matmul(cp[0:65, 0:512],
                       v_sb[:, kb * 520 + (2 * hpl) * 65:
                            kb * 520 + (2 * hpl) * 65 + 65],
                       pt[:, pa * 1024:pa * 1024 + 512],
                       start=(kb == 0), stop=(kb == cap - 1), dep=dep_pv)
                tpv = matmul(cp[0:65, 512:1024],
                             v_sb[:, kb * 520 + (2 * hpl + 1) * 65:
                                  kb * 520 + (2 * hpl + 1) * 65 + 65],
                             pt[:, pa * 1024 + 512:(pa + 1) * 1024],
                             start=(kb == 0), stop=(kb == cap - 1), inc="sT")
                att["last_pv"][pa] = tpv

            pend = None
            for kb in range(cap):
                pa = kb % 2
                depq = [mld, ("sV", qT_done)]
                if att["last_exp"][pa]:
                    depq.append(("sS", att["last_exp"][pa]))
                depq += lp_free
                matmul(lp[:, pa * 1024:pa * 1024 + 512],
                       kt_sb[0:64, kb * 512 + hpl * 128:
                             kb * 512 + (hpl) * 128 + 128],
                       qT[s_][0:64, hp * 512:(hp + 1) * 512],
                       start=True, stop=True, dep=depq)
                tqk = matmul(lp[:, pa * 1024 + 512:pa * 1024 + 1024],
                             kt_sb[64:128, kb * 512 + hpl * 128:
                                   kb * 512 + hpl * 128 + 128],
                             qT[s_][64:128, hp * 512:(hp + 1) * 512],
                             start=True, stop=True, inc="sT")
                deps = [("sT", tqk)]
                if att["last_pv"][pa]:
                    deps.append(("sT", att["last_pv"][pa]))
                sexp = P.emit("scalar", lambda e, pa=pa: e.activation(
                    pt[:, pa * 1024:(pa + 1) * 1024],
                    lp[:, pa * 1024:(pa + 1) * 1024],
                    AF.Exp, bias=cset[:, 5:6], scale=cset[:, 6:7]),
                    dep=deps, inc="sS")
                att["last_exp"][pa] = sexp
                dep_pv = [("sS", sexp)]
                if kb >= cap - 8:
                    moff = (kb - (cap - 8)) * 512
                    pt_pair = bass.AP(pt, pa * 1024,
                                      [[2048, 128], [512, 2], [1, 512]])
                    m_b = bass.AP(maskb, moff,
                                  [[8 * 512, 128], [0, 2], [1, 512]])
                    vm = P.emit("vector", lambda e, o=pt_pair, m=m_b:
                                e.tensor_tensor(o, o, m, ALU.mult),
                                dep=[("sS", sexp)], inc="sV")
                    last_mask_v = vm
                    dep_pv = [("sV", vm)]
                if kb == 0:
                    dep_pv = dep_pv + att["cp_free"]
                if pend is not None:
                    emit_pv(*pend)
                pend = (kb, dep_pv)
            emit_pv(*pend)
            tdone = att["last_pv"][(cap - 1) % 2]
            vden = P.emit("vector", lambda e: e.reciprocal(
                den[64:65, 0:1024], cp[64:65, 0:1024]),
                dep=[("sT", tdone)], inc="sV")
            mb1 = matmul(db[:], onesf[64:65, 0:128], den[64:65, 0:512],
                         start=True, stop=True, dep=[("sV", vden)], inc="sT")
            vc1 = P.emit("vector", lambda e: e.tensor_copy(
                rcp[:, 0:512], db[:]), dep=[("sT", mb1)], inc="sV")
            mb2 = matmul(db[:], onesf[64:65, 0:128], den[64:65, 512:1024],
                         start=True, stop=True, dep=[("sV", vc1)], inc="sT")
            vc2 = P.emit("vector", lambda e: e.tensor_copy(
                rcp[:, 512:1024], db[:]), dep=[("sT", mb2)], inc="sV")
            hA, hB = 2 * hp, 2 * hp + 1
            extra = att["evac_extra"]
            att["evac_extra"] = []
            P.emit("vector", lambda e, hA=hA: e.tensor_tensor(
                ctxT[0:64, hA * 512:(hA + 1) * 512], cp[0:64, 0:512],
                rcp[0:64, 0:512], ALU.mult), dep=extra)
            vev = P.emit("vector", lambda e, hB=hB: e.tensor_tensor(
                ctxT[0:64, hB * 512:(hB + 1) * 512], cp[0:64, 512:1024],
                rcp[0:64, 512:1024], ALU.mult), dep=[("sV", vc2)], inc="sV")
            att["cp_free"] = [("sV", vev)]
        att["mask_free"] = last_mask_v if last_mask_v else att["mask_free"]

    ctxA_store = 0
    dep0 = [("sV", qT_done)]
    for s_ in range(4):
        attn_slot(0, s_, kTc, vc, dep0)
        dep0 = []
        ctxA_store = P.dma("sync", ctxA_h.ap()[:, s_ * 4096:(s_ + 1) * 4096],
                           ctxT[0:64, 0:4096], dep=[("sV", P.val("sV"))],
                           sem="sO")
        att["evac_extra"] = [ctxA_store]

    if PHASE_LIMIT <= 5:
        P._w("sync", P.fence("sO") + P.fence("sL") + P.fence("sGD"))
        P.emit("sync", lambda e: e.wait_ge(P.sem["sV"][0], P.val("sV")))
        P.emit("sync", lambda e: e.wait_ge(P.sem["sS"][0], P.val("sS")))
        P.emit("sync", lambda e: e.wait_ge(P.sem["sT"][0], P.val("sT")))
        return _finish(nc, P, es)
    rk = P.dma("sync", kTc[:], kt2_h.ap()[:],
               dep=[ctxA_store] + p1_done_o +
                    [("sS", att["last_exp"][0]), ("sS", att["last_exp"][1]),
                     ("sT", att["last_pv"][0]), ("sT", att["last_pv"][1])])
    rv = P.dma("sync", vc[:], v2_h.ap()[:])
    if PHASE_LIMIT == 50:
        P._w("sync", P.fence("sO") + P.fence("sL") + P.fence("sGD"))
        return _finish(nc, P, es)

    st_x1_prev = None
    st_xq_prev = None
    out_dep_first = list(gwo)
    t_op_prev = 0
    v_x1 = 0
    for s_ in range(4):
        if 53 <= PHASE_LIMIT <= 56 and s_ >= PHASE_LIMIT - 52:
            break
        attn_slot(1, s_, kTc, vc, [rv, rk])
        if PHASE_LIMIT == 51 or 53 <= PHASE_LIMIT <= 56:
            att["cp_free"] = [("sV", P.val("sV")), ("sS", P.val("sS"))]
            continue
        rc_ld = P.dma("sync", ctxT[0:64, 0:4096],
                      ctxA_h.ap()[:, s_ * 4096:(s_ + 1) * 4096],
                      dep=[("sV", P.val("sV")), ("sT", t_op_prev)])
        if PHASE_LIMIT == 520:
            att["cp_free"] = [("sV", P.val("sV")), ("sS", P.val("sS"))]
            continue
        t_op = 0
        vt2 = 0
        for tq in range(4):
            ti = s_ * 4 + tq
            ldx = P.dma("sync", xin[:, 0:1024],
                        xq_h.ap()[ti * 128:(ti + 1) * 128, :],
                        dep=[("sV", ln_prev2 if ti == 0 else v_x1)])
            for ni in range(2):
                for hh in range(16):
                    dd = ([rc_ld, ("sV", vt2)] + out_dep_first
                          + att["cp_free"]) \
                        if (ni == 0 and hh == 0) else ()
                    t_op = matmul(
                        cp[:, ni * 512:(ni + 1) * 512],
                        ctxT[0:64, hh * 512 + tq * 128:
                             hh * 512 + tq * 128 + 128],
                        woS[0:64, hh * 1024 + ni * 512:
                            hh * 1024 + (ni + 1) * 512],
                        start=(hh == 0), stop=(hh == 15), dep=dd,
                        inc="sT" if (ni == 1 and hh == 15) else None)
            out_dep_first = []
            if PHASE_LIMIT == 521:
                vt2 = 0
                continue
            vt2 = ln_chain(cp[:, 0:1024], lns[:],
                           dep_v=[("sT", t_op)], dep_s=[("sT", t_op)],
                           psum=True)
            if PHASE_LIMIT == 522:
                continue
            P.drain("vector")
            v_x1 = P.emit("vector", lambda e: e.tensor_add(
                x1t[:], xin[:, 0:1024], lns[:]),
                dep=[("sV", vt2), ldx, st_x1_prev],
                inc="sV")
            st_x1_prev = P.dma("sync",
                               x1_h.ap()[ti * 128:(ti + 1) * 128, :],
                               x1t[:], dep=[("sV", v_x1)], sem="sO")
            if PHASE_LIMIT == 52:
                continue
            vt3 = ln_chain(x1t[:], lns[:], dep_v=[("sV", v_x1)],
                           dep_s=[("sV", v_x1)])
            P.emit("vector", lambda e: e.reduce_max(
                stats[:, 9:10], lns[:], axis=AX, apply_absolute_value=True),
                dep=[("sV", vt3)])
            P.drain("vector")
            P.emit("vector", lambda e: e.tensor_scalar(
                stats[:, 10:11], stats[:, 9:10], 1e-5, None, ALU.max))
            P.drain("vector")
            P.emit("vector", lambda e, ti=ti: e.tensor_scalar(
                sxinv[:, ti:ti + 1], stats[:, 10:11], 1.0 / 127.0, None,
                ALU.mult))
            P.emit("vector", lambda e: e.reciprocal(
                stats[:, 11:12], stats[:, 10:11]))
            P.drain("vector")
            vsc = P.emit("vector", lambda e: e.tensor_scalar(
                stats[:, 12:13], stats[:, 11:12], 127.0, None, ALU.mult),
                inc="sV")
            vq = act_quant_chain(lns[:], 1024, stats[:, 12:13],
                                 dep_v=[("sV", vsc), st_xq_prev])
            st_xq_prev = P.dma("sync",
                               xqq_h.ap()[ti * 128:(ti + 1) * 128, :],
                               qb_[:, 0:1024], dep=[("sV", vq)], sem="sO")
        att["cp_free"] = [("sV", P.val("sV")), ("sS", P.val("sS"))]
        att["evac_extra"] = [("sT", t_op)]
        t_op_prev = t_op
    p2_done = P.fence("sO")
    if PHASE_LIMIT <= 6 or PHASE_LIMIT in (50, 51, 52, 53, 54, 55, 56, 520, 521, 522):
        P._w("sync", P.fence("sO") + P.fence("sL") + P.fence("sGD"))
        P.emit("sync", lambda e: e.wait_ge(P.sem["sV"][0], P.val("sV")))
        P.emit("sync", lambda e: e.wait_ge(P.sem["sS"][0], P.val("sS")))
        P.emit("sync", lambda e: e.wait_ge(P.sem["sT"][0], P.val("sT")))
        return _finish(nc, P, es)

    esAtt.close()
    esA.close()
    esB = ExitStack()
    sbB = lambda name, shape, dt: esB.enter_context(
        nc.sbuf_tensor(name, shape, dt))
    wgT = sbB("wgT", [128, 8 * 4096], BF16)
    wuT = sbB("wuT", [128, 8 * 4096], BF16)
    xqT = sbB("xqT", [128, 8 * 512], BF16)
    hbuf = sbB("hbuf", [128, 4096], F32)
    hqb = sbB("hqb", [128, 4096], BF16)
    sgu = sbB("sgu", [128, 4], F32)

    wl_toks = []
    for c in range(8):
        wl_toks.append(P.dmat("sync", wgT[:, c * 4096:(c + 1) * 4096],
                              wgq_h.ap()[:, c * 128:(c + 1) * 128],
                              dep=p2_done if c == 0 else []))
        wl_toks.append(P.dmat("sync", wuT[:, c * 4096:(c + 1) * 4096],
                              wuq_h.ap()[:, c * 128:(c + 1) * 128]))
    wl_done = [max((t for ln2, t in wl_toks if ln2 == ln), default=0)
               for ln in {ln for ln, _ in wl_toks}]
    wl_done = [(ln, max(t for l2, t in wl_toks if l2 == ln))
               for ln in {ln for ln, _ in wl_toks}]

    hq_store = {}
    lpB_free = [("sS", 0)]
    silu_guard = {0: 0, 1: 0}
    for g in range(4):
        xl_toks = []
        for c in range(8):
            xl_toks.append(P.dmat("sync", xqT[:, c * 512:(c + 1) * 512],
                                  xqq_h.ap()[g * 512:(g + 1) * 512,
                                             c * 128:(c + 1) * 128],
                                  dep=wl_done
                                  if (g == 0 and c == 0) else []))
        xl_done = [(ln, max(t for l2, t in xl_toks if l2 == ln))
                   for ln in {ln for ln, _ in xl_toks}]
        for tl in range(4):
            ti = g * 4 + tl
            P.emit("vector", lambda e, ti=ti: e.tensor_tensor(
                sgu[:, 0:1], sxinv[:, ti:ti + 1], wmean3[:, 0:1], ALU.mult),
                dep=xl_done + [("sS", P.val("sS"))])
            P.drain("vector")
            vsg = P.emit("vector", lambda e, ti=ti: e.tensor_tensor(
                sgu[:, 1:2], sxinv[:, ti:ti + 1], wmean3[:, 1:2], ALU.mult),
                inc="sV")
            h_written = {}
            for nci in range(8):
                pa = nci % 2
                dd0 = [("sV", vsg)] + lpB_free
                if nci >= 2:
                    dd0 = [("sV", h_written[nci - 2])]
                for c in range(8):
                    matmul(lp[:, pa * 512:(pa + 1) * 512],
                           xqT[:, c * 512 + tl * 128:c * 512 + tl * 128 + 128],
                           wgT[:, c * 4096 + nci * 512:
                               c * 4096 + (nci + 1) * 512],
                           start=(c == 0), stop=(c == 7),
                           dep=dd0 if c == 0 else ())
                tgu = None
                for c in range(8):
                    tgu = matmul(
                        lp[:, 1024 + pa * 512:1024 + (pa + 1) * 512],
                        xqT[:, c * 512 + tl * 128:c * 512 + tl * 128 + 128],
                        wuT[:, c * 4096 + nci * 512:
                            c * 4096 + (nci + 1) * 512],
                        start=(c == 0), stop=(c == 7),
                        inc="sT" if c == 7 else None)
                ssdep = [("sT", tgu)]
                if silu_guard[pa]:
                    ssdep.append(("sV", silu_guard[pa]))
                ss = P.emit("scalar", lambda e, pa=pa: e.activation(
                    sq[:, pa * 512:(pa + 1) * 512],
                    lp[:, pa * 512:(pa + 1) * 512], AF.Silu,
                    scale=sgu[:, 0:1]), dep=ssdep, inc="sS")
                vh = P.emit("vector", lambda e, nci=nci, pa=pa:
                            e.scalar_tensor_tensor(
                                hbuf[:, nci * 512:(nci + 1) * 512],
                                lp[:, 1024 + pa * 512:1024 + (pa + 1) * 512],
                                sgu[:, 1:2], sq[:, pa * 512:(pa + 1) * 512],
                                ALU.mult, ALU.mult),
                            dep=[("sS", ss)], inc="sV")
                h_written[nci] = vh
                silu_guard[pa] = vh
            P.emit("vector", lambda e: e.reduce_max(
                stats[:, 9:10], hbuf[:], axis=AX, apply_absolute_value=True),
                dep=[("sV", h_written[7])])
            P.drain("vector")
            P.emit("vector", lambda e: e.tensor_scalar(
                stats[:, 10:11], stats[:, 9:10], 1e-5, None, ALU.max))
            P.drain("vector")
            P.emit("vector", lambda e: e.reciprocal(
                stats[:, 11:12], stats[:, 10:11]))
            P.drain("vector")
            vsc = P.emit("vector", lambda e: e.tensor_scalar(
                stats[:, 12:13], stats[:, 11:12], 127.0, None, ALU.mult),
                inc="sV")
            lastq = vsc
            for qh in range(4):
                vq = act_quant_chain(hbuf[:, qh * 1024:(qh + 1) * 1024],
                                     1024, stats[:, 12:13],
                                     dep_v=[("sV", lastq),
                                            hq_store.get(ti - 1)])
                lastq = P.emit("vector", lambda e, qh=qh: e.tensor_copy(
                    hqb[:, qh * 1024:(qh + 1) * 1024], qb_[:, 0:1024]),
                    dep=[("sV", vq)], inc="sV")
            lpB_free = [("sV", lastq)]
            hq_store[ti] = P.dma("sync", hq_h.ap()[ti * 128:(ti + 1) * 128, :],
                                 hqb[:], dep=[("sV", lastq)], sem="sO")
    b1_done = P.fence("sO")
    b1_v = P.val("sV")
    b1_s = P.val("sS")
    if PHASE_LIMIT <= 7:
        P._w("sync", P.fence("sO") + P.fence("sL") + P.fence("sGD"))
        P.emit("sync", lambda e: e.wait_ge(P.sem["sV"][0], P.val("sV")))
        P.emit("sync", lambda e: e.wait_ge(P.sem["sS"][0], P.val("sS")))
        P.emit("sync", lambda e: e.wait_ge(P.sem["sT"][0], P.val("sT")))
        return _finish(nc, P, es)

    esB.close()
    esC = ExitStack()
    sbC = lambda name, shape, dt: esC.enter_context(
        nc.sbuf_tensor(name, shape, dt))
    wdT = sbC("wdT", [128, 32 * 1024], BF16)
    hqT = sbC("hqT", [128, 32 * 512], BF16)
    x1g = sbC("x1g", [128, 4 * 1024], F32)
    outb = sbC("outb", [128, 2 * 1024], F32)

    wdl_toks = []
    for ic in range(32):
        wdl_toks.append(P.dmat("sync", wdT[:, ic * 1024:(ic + 1) * 1024],
                               wdq_h.ap()[:, ic * 128:(ic + 1) * 128],
                               dep=b1_done + [("sV", b1_v), ("sS", b1_s)]
                               if ic == 0 else []))
    wdl_done = [(ln, max(t for l2, t in wdl_toks if l2 == ln))
                for ln in {ln for ln, _ in wdl_toks}]

    out_store = {}
    lpC_free = [("sV", 0)]
    for g in range(4):
        hl_toks = []
        for ic in range(32):
            hl_toks.append(P.dmat("sync", hqT[:, ic * 512:(ic + 1) * 512],
                                  hq_h.ap()[g * 512:(g + 1) * 512,
                                            ic * 128:(ic + 1) * 128],
                                  dep=wdl_done
                                  if (g == 0 and ic == 0) else []))
        hl_done = [(ln, max(t for l2, t in hl_toks if l2 == ln))
                   for ln in {ln for ln, _ in hl_toks}]
        xg_toks = []
        for a_ in range(4):
            xg_toks.append(P.dma(
                "sync", x1g[:, a_ * 1024:(a_ + 1) * 1024],
                x1_h.ap()[g * 512 + a_ * 128:g * 512 + (a_ + 1) * 128, :],
                dep=hl_done if a_ == 0 else []))
        xg = [(ln, max(t for l2, t in xg_toks if l2 == ln))
              for ln in {ln for ln, _ in xg_toks}]
        for tl in range(4):
            ti = g * 4 + tl
            t_dn = None
            for ni in range(2):
                for ic in range(32):
                    dd = (xg + lpC_free) \
                        if (ic == 0 and ni == 0) else ()
                    t_dn = matmul(
                        lp[:, ni * 512:(ni + 1) * 512],
                        hqT[:, ic * 512 + tl * 128:
                            ic * 512 + tl * 128 + 128],
                        wdT[:, ic * 1024 + ni * 512:
                            ic * 1024 + (ni + 1) * 512],
                        start=(ic == 0), stop=(ic == 31), dep=dd,
                        inc="sT" if (ni == 1 and ic == 31) else None)
            vt4 = ln_chain(lp[:, 0:1024], lns[:],
                           dep_v=[("sT", t_dn)], dep_s=[("sT", t_dn)],
                           psum=True)
            po = tl % 2
            dep_out = [("sV", vt4), out_store.get(ti - 2)]
            vfin = P.emit("vector", lambda e, tl=tl, po=po: e.tensor_add(
                outb[:, po * 1024:(po + 1) * 1024],
                x1g[:, tl * 1024:(tl + 1) * 1024], lns[:]),
                dep=dep_out, inc="sV")
            lpC_free = [("sV", vfin)]
            out_store[ti] = P.dma("sync",
                                  out_h.ap()[ti * 128:(ti + 1) * 128, :],
                                  outb[:, po * 1024:(po + 1) * 1024],
                                  dep=[("sV", vfin)], sem="sO")
    esC.close()

    P._w("sync", P.fence("sO"))

    return _finish(nc, P, es)


def _finish(nc, P, es):
    with nc.Block() as block:
        @block.sync
        def _(sync):
            for fn in P.q["sync"]:
                fn(sync)

        @block.gpsimd
        def _(gpsimd):
            for fn in P.q["gpsimd"]:
                fn(gpsimd)

        @block.vector
        def _(vector):
            for fn in P.q["vector"]:
                fn(vector)

        @block.scalar
        def _(scalar):
            for fn in P.q["scalar"]:
                fn(scalar)

        @block.tensor
        def _(tensor):
            for fn in P.q["tensor"]:
                fn(tensor)

    return nc


def _host_inputs(core, x, wqkv, wo, wg, wu, wd):
    b, par = core // 2, core % 2
    js = JS[par]
    rows = np.concatenate([np.arange(512 * j, 512 * j + 512) for j in js])
    xb = np.ascontiguousarray(x[b])
    xq = np.ascontiguousarray(xb[rows])
    masks = np.zeros((4, 8, 128, 512), dtype=np.float32)
    for s_, j in enumerate(js):
        cap = CAPS[s_]
        q0 = 512 * j
        for kbrel in range(8):
            kb = cap - 8 + kbrel
            kabs = kb * 128 + np.arange(128)[:, None]
            qabs = q0 + np.arange(512)[None, :]
            masks[s_, kbrel] = (kabs <= qabs)
    masks = masks.transpose(2, 0, 1, 3).reshape(128, 4 * 8 * 512)
    masks = np.ascontiguousarray(masks).astype(ml_dtypes.bfloat16)
    pck = (np.arange(NKB)[None, :] * 128
           + np.arange(128)[:, None]).astype(np.float32)
    tq_tiles = np.concatenate([np.arange(4 * j, 4 * j + 4) for j in js])
    pcq = (tq_tiles[None, :] * 128
           + np.arange(128)[:, None]).astype(np.float32)
    fracb = np.ascontiguousarray(np.broadcast_to(
        (2.0 * np.arange(32) / 64.0).astype(np.float32), (128, 32)))
    ident = np.eye(128, dtype=np.float32).astype(ml_dtypes.bfloat16)
    return dict(xkv=xb, xq=xq, wqkv=wqkv, wo=wo, wg=wg, wu=wu, wd=wd,
                masks=masks, poscolk=pck, poscolq=pcq, fracb=fracb,
                ident=ident), rows


def kernel(L0_attn_x, L0_attn_wqkv, L0_attn_wo, L0_ffn_w_gate, L0_ffn_w_up,
           L0_ffn_w_down):
    x = np.ascontiguousarray(np.asarray(L0_attn_x, dtype=np.float32))
    wqkv = np.ascontiguousarray(np.asarray(L0_attn_wqkv, dtype=np.float32))
    wo = np.ascontiguousarray(np.asarray(L0_attn_wo, dtype=np.float32))
    wg = np.ascontiguousarray(np.asarray(L0_ffn_w_gate, dtype=np.float32))
    wu = np.ascontiguousarray(np.asarray(L0_ffn_w_up, dtype=np.float32))
    wd = np.ascontiguousarray(np.asarray(L0_ffn_w_down, dtype=np.float32))

    nc = build_nc()
    in_maps, rowmaps = [], []
    for core in range(8):
        im, rows = _host_inputs(core, x, wqkv, wo, wg, wu, wd)
        in_maps.append(im)
        rowmaps.append(rows)
    res = run_bass_kernel_spmd(nc, in_maps, core_ids=list(range(8)))
    out = np.empty((B, S, DM), dtype=np.float32)
    for core in range(8):
        b = core // 2
        out[b][rowmaps[core]] = np.asarray(res.results[core]["out"],
                                           dtype=np.float32)
    return out


# version 06
# speedup vs baseline: 1.4638x; 1.0101x over previous
import numpy as np
import ml_dtypes
from contextlib import ExitStack

import concourse.bass as bass
import concourse.mybir as mybir
from concourse.bass_utils import run_bass_kernel_spmd

F32 = mybir.dt.float32
BF16 = mybir.dt.bfloat16
I32 = mybir.dt.int32
AF = mybir.ActivationFunctionType
ALU = mybir.AluOpType
AX = mybir.AxisListType.X

B, S, DM, H, DH, INNER = 4, 4096, 1024, 16, 64, 4096
NT = 2048
CAPS = [8, 16, 24, 32]
JS = {0: [0, 3, 4, 7], 1: [1, 2, 5, 6]}
NKB = S // 128
EPS = 1e-5
EXPB = -10.0
RC = 256.0
DEBUG_TAPS = False
PHASE_LIMIT = 99


class Prog:
    ENGS = ("sync", "gpsimd", "vector", "scalar", "tensor")
    LANES = {"sL": 4, "sO": 2, "sGD": 2}

    def __init__(self, nc, es):
        self.nc = nc
        self.q = {e: [] for e in self.ENGS}
        self.sem = {}
        for name in ("sG", "sV", "sS", "sT"):
            self.sem[name] = [es.enter_context(nc.semaphore(name)), 0]
        for g, n in self.LANES.items():
            for i in range(n):
                nm = "%s%d" % (g, i)
                self.sem[nm] = [es.enter_context(nc.semaphore(nm)), 0]
        self.rr = {g: 0 for g in self.LANES}
        self.waited = {e: {} for e in self.ENGS}
        self.lastdrain = {e: False for e in self.ENGS}

    def _w(self, eng, dep):
        for d in dep:
            if not d:
                continue
            sn, val = d
            if val <= 0:
                continue
            if self.waited[eng].get(sn, 0) >= val:
                continue
            self.waited[eng][sn] = val
            h = self.sem[sn][0]
            self.q[eng].append(lambda e, h=h, v=val: e.wait_ge(h, v))
            self.lastdrain[eng] = False

    AUTODRAIN = ("vector", "gpsimd", "scalar")

    def emit(self, eng, fn, dep=(), inc=None, n=1):
        self._w(eng, dep)
        if eng in self.AUTODRAIN and not self.lastdrain[eng]:
            self.q[eng].append(lambda e: e.drain())
        self.lastdrain[eng] = False
        if inc is None:
            self.q[eng].append(fn)
            return None
        self.sem[inc][1] += n
        tgt = self.sem[inc][1]
        h = self.sem[inc][0]

        def run(e, fn=fn, h=h, n=n):
            fn(e).then_inc(h, n)
        self.q[eng].append(run)
        return tgt

    def drain(self, eng):
        if not self.lastdrain[eng]:
            self.q[eng].append(lambda e: e.drain())
            self.lastdrain[eng] = True

    def val(self, semname):
        return self.sem[semname][1]

    def fence(self, group):
        return [("%s%d" % (group, i), self.sem["%s%d" % (group, i)][1])
                for i in range(self.LANES[group])
                if self.sem["%s%d" % (group, i)][1] > 0]

    def _dma(self, eng, fn, dep, group):
        i = self.rr[group]
        self.rr[group] = (i + 1) % self.LANES[group]
        lane = "%s%d" % (group, i)
        prev = self.sem[lane][1]
        dep = list(dep) + [(lane, prev)]
        tgt = self.emit(eng, fn, dep=dep, inc=lane, n=16)
        return (lane, tgt)

    def dma(self, eng, dst, src, dep=(), sem=None, **kw):
        group = sem or ("sL" if eng == "sync" else "sGD")
        return self._dma(eng, lambda e: e.dma_start(dst, src, **kw),
                         dep, group)

    def dmat(self, eng, dst, src, dep=(), sem="sL"):
        return self._dma(eng, lambda e: e.dma_start_transpose(dst, src),
                         dep, group=sem)


def build_nc():
    nc = bass.Bass("TRN2", target_bir_lowering=False)
    es = ExitStack()

    xkv_h = nc.dram_tensor("xkv", [S, DM], F32, kind="ExternalInput")
    xq_h = nc.dram_tensor("xq", [NT, DM], F32, kind="ExternalInput")
    wqkv_h = nc.dram_tensor("wqkv", [DM, 3 * DM], F32, kind="ExternalInput")
    wo_h = nc.dram_tensor("wo", [DM, DM], F32, kind="ExternalInput")
    wg_h = nc.dram_tensor("wg", [INNER, DM], F32, kind="ExternalInput")
    wu_h = nc.dram_tensor("wu", [INNER, DM], F32, kind="ExternalInput")
    wd_h = nc.dram_tensor("wd", [DM, INNER], F32, kind="ExternalInput")
    mask_h = nc.dram_tensor("masks", [128, 4 * 8 * 512], BF16,
                            kind="ExternalInput")
    pck_h = nc.dram_tensor("poscolk", [128, NKB], F32, kind="ExternalInput")
    pcq_h = nc.dram_tensor("poscolq", [128, 16], F32, kind="ExternalInput")
    frac_h = nc.dram_tensor("fracb", [128, 32], F32, kind="ExternalInput")
    idn_h = nc.dram_tensor("ident", [128, 128], BF16, kind="ExternalInput")
    out_h = nc.dram_tensor("out", [NT, DM], F32, kind="ExternalOutput")

    kd = "ExternalOutput" if DEBUG_TAPS else "Internal"
    wgq_h = nc.dram_tensor("wgq", [INNER, DM], BF16, kind=kd)
    wuq_h = nc.dram_tensor("wuq", [INNER, DM], BF16, kind=kd)
    wdq_h = nc.dram_tensor("wdq", [DM, INNER], BF16, kind=kd)
    kt2_h = nc.dram_tensor("kt2", [128, NKB * 512], BF16, kind=kd)
    v2_h = nc.dram_tensor("v2", [128, NKB * 520], BF16, kind=kd)
    ctxA_h = nc.dram_tensor("ctxA", [64, 4 * 8 * 512], BF16, kind=kd)
    x1_h = nc.dram_tensor("x1", [NT, DM], F32, kind=kd)
    xqq_h = nc.dram_tensor("xqq", [NT, DM], BF16, kind=kd)
    hq_h = nc.dram_tensor("hqs", [NT, INNER], BF16, kind=kd)

    lp = es.enter_context(nc.psum_tensor("lp", [128, 2048], F32))
    cp = es.enter_context(nc.psum_tensor("cp", [128, 1024], F32))
    db = es.enter_context(nc.psum_tensor("db", [128, 512], F32))
    tp = es.enter_context(nc.psum_tensor("tp", [128, 1024], BF16))

    sb = lambda name, shape, dt: es.enter_context(
        nc.sbuf_tensor(name, shape, dt))
    idn = sb("idn", [128, 128], BF16)
    onesf = sb("onesf", [128, 128], F32)
    cset = sb("cset", [128, 12], F32)
    stats = sb("stats", [128, 16], F32)
    acc3 = sb("acc3", [128, 4], F32)
    thr3 = sb("thr3", [128, 4], F32)
    nthr3 = sb("nthr3", [128, 4], F32)
    wmean3 = sb("wmean3", [128, 4], F32)
    sxinv = sb("sxinv", [128, 16], F32)
    sq = sb("sq", [128, 1024], F32)
    ra = sb("ra", [128, 512], F32)
    rb = sb("rb", [128, 512], F32)
    lns = sb("lns", [128, 1024], F32)
    qt_ = sb("qt", [128, 1024], F32)
    qi_ = sb("qi", [128, 1024], I32)
    qd_ = sb("qd", [128, 1024], F32)
    qb_ = sb("qb", [128, 1024], BF16)

    P = Prog(nc, es)

    def matmul(out, lhsT, rhs, start, stop, dep=(), inc=None):
        return P.emit("tensor",
                      lambda e: e.matmul(out, lhsT, rhs, start=start,
                                         stop=stop),
                      dep=dep, inc=inc)

    def ln_chain(src_ap, dst_ap, dep_v, dep_s, psum=False):
        if psum:
            cv = P.emit("vector",
                        lambda e, o=dst_ap, i=src_ap: e.tensor_copy(o, i),
                        dep=dep_v, inc="sV")
            dep_v = [("sV", cv)]
            dep_s = [("sV", cv)]
            src_ap = dst_ap
        P.emit("vector", lambda e: e.reduce_sum(stats[:, 0:1], src_ap,
                                                axis=AX), dep=dep_v)
        s1 = P.emit("scalar",
                    lambda e: e.activation(sq[:], src_ap, AF.Square,
                                           accum_out=stats[:, 1:2]),
                    dep=dep_s, inc="sS")
        P.drain("vector")
        P.emit("vector", lambda e: e.tensor_scalar(
            stats[:, 2:3], stats[:, 0:1], 1.0 / DM, None, ALU.mult))
        P.emit("vector", lambda e: e.tensor_scalar(
            stats[:, 3:4], stats[:, 1:2], 1.0 / DM, EPS, ALU.mult, ALU.add),
            dep=[("sS", s1)])
        P.drain("vector")
        P.emit("vector", lambda e: e.tensor_mul(
            stats[:, 4:5], stats[:, 2:3], stats[:, 2:3]))
        P.drain("vector")
        P.emit("vector", lambda e: e.tensor_sub(
            stats[:, 5:6], stats[:, 3:4], stats[:, 4:5]))
        P.drain("vector")
        v2t = P.emit("vector", lambda e: e.reciprocal(stats[:, 6:7],
                                                      stats[:, 5:6]),
                     inc="sV")
        s2 = P.emit("scalar", lambda e: e.activation(
            stats[:, 7:8], stats[:, 6:7], AF.Sqrt), dep=[("sV", v2t)],
            inc="sS")
        vt = P.emit("vector", lambda e: e.tensor_scalar(
            dst_ap, src_ap, stats[:, 2:3], stats[:, 7:8],
            ALU.subtract, ALU.mult), dep=[("sS", s2)], inc="sV")
        return vt

    def transpose8(src_sb, dep_t, dep_v, out_ap, in_ap):
        tt = None
        for j in range(8):
            tt = P.emit("tensor", lambda e, j=j: e.transpose(
                tp[:, j * 128:(j + 1) * 128],
                src_sb[:, j * 128:(j + 1) * 128], idn[:]),
                dep=(dep_t if j == 0 else ()),
                inc="sT" if j == 7 else None)
        vt = P.emit("vector", lambda e: e.tensor_copy(out_ap, in_ap),
                    dep=[("sT", tt)] + list(dep_v), inc="sV")
        return vt

    aq_state = {"g": 0}

    def act_quant_chain(src_ap, n, s_ap, dep_v):
        t1 = P.emit("vector", lambda e: e.tensor_scalar(
            qt_[:, 0:n], src_ap, s_ap, RC, ALU.mult, ALU.add),
            dep=list(dep_v) + [("sG", aq_state["g"])], inc="sV")
        g1 = P.emit("gpsimd", lambda e: e.tensor_copy(qi_[:, 0:n],
                                                      qt_[:, 0:n]),
                    dep=[("sV", t1)], inc="sG")
        aq_state["g"] = g1
        P.emit("vector", lambda e: e.tensor_sub(
            qd_[:, 0:n], qt_[:, 0:n], qi_[:, 0:n]), dep=[("sG", g1)])
        P.emit("vector", lambda e: e.scalar_tensor_tensor(
            qt_[:, 0:n], qd_[:, 0:n], 0.5, qi_[:, 0:n], ALU.is_ge, ALU.add))
        P.drain("vector")
        v3 = P.emit("vector", lambda e: e.tensor_scalar(
            qb_[:, 0:n], qt_[:, 0:n], RC, None, ALU.subtract), inc="sV")
        return v3

    def rope_block(src_psum_off, cos_sb, sin_sb, toff, tw, dst_sb, dep_first):
        in1 = bass.AP(lp, src_psum_off, [[2048, 128], [64, 16], [1, 32]])
        in2 = bass.AP(lp, src_psum_off + 32, [[2048, 128], [64, 16], [1, 32]])
        cs = bass.AP(cos_sb, toff * 32, [[tw * 32, 128], [0, 16], [1, 32]])
        sn = bass.AP(sin_sb, toff * 32, [[tw * 32, 128], [0, 16], [1, 32]])
        o1 = bass.AP(dst_sb, 0, [[1024, 128], [64, 16], [1, 32]])
        o2 = bass.AP(dst_sb, 32, [[1024, 128], [64, 16], [1, 32]])
        P.emit("vector", lambda e: e.tensor_tensor(ra[:], in1, cs, ALU.mult),
               dep=dep_first)
        P.emit("vector", lambda e: e.tensor_tensor(rb[:], in2, sn, ALU.mult))
        P.drain("vector")
        P.emit("vector", lambda e: e.tensor_sub(o1, ra[:], rb[:]))
        P.emit("vector", lambda e: e.tensor_tensor(ra[:], in2, cs, ALU.mult))
        P.emit("vector", lambda e: e.tensor_tensor(rb[:], in1, sn, ALU.mult))
        P.drain("vector")
        return P.emit("vector", lambda e: e.tensor_add(o2, ra[:], rb[:]),
                      inc="sV")

    esA = ExitStack()
    sbA = lambda name, shape, dt: esA.enter_context(
        nc.sbuf_tensor(name, shape, dt))
    kTc = sbA("kTc", [128, NKB * 512], BF16)
    vc = sbA("vc", [128, NKB * 520], BF16)
    xin = sbA("xin", [128, 2 * 1024], F32)
    qT = [sbA("qT%d" % s_, [128, 8 * 512], BF16) for s_ in range(4)]
    esA2 = ExitStack()
    sbA2 = lambda name, shape, dt: esA2.enter_context(
        nc.sbuf_tensor(name, shape, dt))
    wqq = sbA2("wqq", [128, 8 * 1024], BF16)
    xn = sbA2("xn", [128, 2 * 1024], BF16)
    xnT = sbA2("xnT", [128, 2 * 1024], BF16)
    ktm = sbA2("ktm", [128, 1024], BF16)
    cosq = sbA2("cosq", [128, 16 * 32], F32)
    sinq = sbA2("sinq", [128, 16 * 32], F32)
    pck = sbA2("pck", [128, NKB], F32)
    pcq = sbA2("pcq", [128, 16], F32)
    frac = sbA2("frac", [128, 32], F32)
    esA1 = ExitStack()
    sbA1 = lambda name, shape, dt: esA1.enter_context(
        nc.sbuf_tensor(name, shape, dt))
    wkv = sbA1("wkv", [128, 8 * 2048], BF16)
    kstg = sbA1("kstg", [128, 512], BF16)
    vstg = sbA1("vstg", [128, 520], BF16)
    wst = sbA1("wst", [128, 1024], F32)
    cosk = sbA1("cosk", [128, NKB * 32], BF16)
    sink = sbA1("sink", [128, NKB * 32], BF16)

    lid = P.dma("sync", idn[:], idn_h.ap())
    lpk = P.dma("sync", pck[:], pck_h.ap())
    lpq = P.dma("sync", pcq[:], pcq_h.ap())
    lfr = P.dma("sync", frac[:], frac_h.ap())

    P.emit("gpsimd", lambda e: e.memset(onesf[:], 1.0))
    P.emit("gpsimd", lambda e: e.memset(cset[:, 0:1], -1.0))
    P.emit("gpsimd", lambda e: e.memset(cset[:, 5:6], EXPB))
    P.emit("gpsimd", lambda e: e.memset(cset[:, 6:7], 0.125))
    P.emit("gpsimd", lambda e: e.memset(sq[:, 0:1], 10000.0))
    P.emit("gpsimd", lambda e: e.memset(sq[:, 1:2], 1.0))
    P.emit("gpsimd", lambda e: e.memset(acc3[:], 0.0))
    P.drain("gpsimd")
    g0 = P.emit("gpsimd", lambda e: e.engine_nop(), inc="sG")

    s_ln = P.emit("scalar", lambda e: e.activation(
        cset[:, 1:2], sq[:, 0:1], AF.Ln), dep=[("sG", g0)], inc="sS")
    s_at = P.emit("scalar", lambda e: e.activation(
        cset[:, 7:8], sq[:, 1:2], AF.Arctan), inc="sS")
    P.emit("vector", lambda e: e.tensor_scalar(
        cset[:, 2:3], cset[:, 7:8], 2.0, None, ALU.mult), dep=[("sS", s_at)])
    P.emit("vector", lambda e: e.tensor_scalar(
        cset[:, 3:4], cset[:, 7:8], 8.0, None, ALU.mult))
    P.emit("vector", lambda e: e.tensor_scalar(
        cset[:, 8:9], cset[:, 7:8], 4.0, None, ALU.mult))
    P.emit("vector", lambda e: e.tensor_scalar(
        cset[:, 9:10], cset[:, 7:8], -4.0, None, ALU.mult))
    P.emit("vector", lambda e: e.tensor_scalar(
        cset[:, 10:11], cset[:, 7:8], -8.0, None, ALU.mult))
    v_2p = P.emit("vector", lambda e: e.reciprocal(cset[:, 4:5],
                                                   cset[:, 3:4]), inc="sV")
    v_fl = P.emit("vector", lambda e: e.tensor_scalar(
        sq[:, 32:64], frac[:], cset[:, 1:2], None, ALU.mult),
        dep=[lfr, ("sS", s_ln), ("sV", v_2p)], inc="sV")
    s_it = P.emit("scalar", lambda e: e.activation(
        frac[:], sq[:, 32:64], AF.Exp, scale=cset[:, 0:1]),
        dep=[("sV", v_fl)], inc="sS")

    def build_tables(n_tiles, pc_sb, cos_sb, sin_sb, dep0):
        deps = list(dep0)
        st = 0
        for t in range(n_tiles):
            P.emit("vector", lambda e, t=t: e.tensor_scalar(
                ra[:, 0:32], frac[:], pc_sb[:, t:t + 1], None, ALU.mult),
                dep=deps + [("sS", st)])
            deps = []
            P.drain("vector")
            P.emit("vector", lambda e: e.tensor_scalar(
                ra[:, 32:64], ra[:, 0:32], 1.0, cset[:, 2:3],
                ALU.mult, ALU.add))
            P.drain("vector")
            vm = P.emit("vector", lambda e: e.tensor_scalar(
                rb[:, 0:64], ra[:, 0:64], cset[:, 4:5], None, ALU.mult),
                inc="sV")
            gk = P.emit("gpsimd", lambda e: e.tensor_copy(
                qi_[:, 0:64], rb[:, 0:64]), dep=[("sV", vm)], inc="sG")
            P.emit("vector", lambda e: e.scalar_tensor_tensor(
                rb[:, 0:64], qi_[:, 0:64], cset[:, 10:11], ra[:, 0:64],
                ALU.mult, ALU.add), dep=[("sG", gk)])
            P.emit("vector", lambda e: e.tensor_scalar(
                rb[:, 64:128], rb[:, 0:64], cset[:, 8:9], None, ALU.is_gt))
            P.emit("vector", lambda e: e.scalar_tensor_tensor(
                rb[:, 0:64], rb[:, 64:128], cset[:, 10:11], rb[:, 0:64],
                ALU.mult, ALU.add))
            P.emit("vector", lambda e: e.tensor_scalar(
                rb[:, 64:128], rb[:, 0:64], cset[:, 9:10], None, ALU.is_lt))
            vr = P.emit("vector", lambda e: e.scalar_tensor_tensor(
                rb[:, 0:64], rb[:, 64:128], cset[:, 3:4], rb[:, 0:64],
                ALU.mult, ALU.add), inc="sV")
            P.emit("scalar", lambda e, t=t: e.activation(
                sin_sb[:, t * 32:(t + 1) * 32], rb[:, 0:32], AF.Sin),
                dep=[("sV", vr)])
            st = P.emit("scalar", lambda e, t=t: e.activation(
                cos_sb[:, t * 32:(t + 1) * 32], rb[:, 32:64], AF.Sin),
                inc="sS")
        return st

    s_tabk = build_tables(NKB, pck, cosk, sink,
                          [("sS", s_it), lpk, lpq, lid])
    s_tabq = build_tables(16, pcq, cosq, sinq, [('sS', s_tabk)])


    for c in range(8):
        P.dma("gpsimd", wkv[:, c * 2048:(c + 1) * 2048],
              wqkv_h.ap()[c * 128:(c + 1) * 128, 1024:3072])
        P.dma("gpsimd", wqq[:, c * 1024:(c + 1) * 1024],
              wqkv_h.ap()[c * 128:(c + 1) * 128, 0:1024])
    gw_done = P.fence("sGD")

    P.emit("gpsimd", lambda e: e.memset(
        bass.AP(vc, 64, [[NKB * 520, 128], [520, NKB], [65, 8]]), 1.0))
    P.emit("gpsimd", lambda e: e.memset(
        bass.AP(vstg, 64, [[520, 128], [65, 8]]), 1.0))
    P.drain("gpsimd")
    g_ones = P.emit("gpsimd", lambda e: e.engine_nop(), inc="sG")

    wstate = {"abs": [0, 0], "ga": g_ones, "vq": [0, 0], "store": None,
              "thrb": 0, "i": 0}
    wsteps = []

    def w1_step(wh, wi, ai, fj):
        pa = 0
        wstate["i"] += 1
        wre = wh.ap().rearrange("(a p) f -> p a f", p=128)
        ld = P.dma("gpsimd", wst[:],
                   wre[:, ai, fj * 1024:(fj + 1) * 1024],
                   dep=[("sS", wstate["abs"][pa])])
        wstate["abs"][pa] = P.emit("scalar", lambda e, pa=pa: e.activation(
            sq[:], wst[:], AF.Abs,
            accum_out=stats[:, 8:9]),
            dep=[ld, ("sG", wstate["ga"])], inc="sS")
        wstate["ga"] = P.emit("gpsimd", lambda e, wi=wi: e.tensor_tensor(
            acc3[:, wi:wi + 1], acc3[:, wi:wi + 1], stats[:, 8:9],
            ALU.add), dep=[("sS", wstate["abs"][pa]), ("sG", wstate["ga"])],
            inc="sG")

    def w_thr_step():
        mt = matmul(db[0:1, 0:4], onesf[:, 0:1], acc3[:], start=True,
                    stop=True, dep=[("sG", wstate["ga"])], inc="sT")
        P.emit("vector", lambda e: e.tensor_scalar(
            stats[0:1, 8:12], db[0:1, 0:4], 1.0 / (INNER * DM), 1e-5,
            ALU.mult, ALU.max), dep=[("sT", mt)])
        v_thr = P.emit("vector", lambda e: e.tensor_scalar(
            stats[0:1, 12:16], stats[0:1, 8:12], 0.5, None, ALU.mult),
            inc="sV")
        m_b1 = matmul(db[:, 0:4], onesf[0:1, 0:128], stats[0:1, 12:16],
                      start=True, stop=True, dep=[("sV", v_thr)], inc="sT")
        vb1 = P.emit("vector", lambda e: e.tensor_copy(thr3[:], db[:, 0:4]),
                     dep=[("sT", m_b1)], inc="sV")
        m_b2 = matmul(db[:, 0:4], onesf[0:1, 0:128], stats[0:1, 8:12],
                      start=True, stop=True, dep=[("sV", vb1)], inc="sT")
        P.emit("vector", lambda e: e.tensor_copy(wmean3[:], db[:, 0:4]),
               dep=[("sT", m_b2)])
        wstate["thrb"] = P.emit("vector", lambda e: e.tensor_scalar(
            nthr3[:], thr3[:], -1.0, None, ALU.mult), inc="sV")

    def w2_step(wh, wq_out, wi, ai, fj):
        pa = 0
        wstate["i"] += 1
        wre = wh.ap().rearrange("(a p) f -> p a f", p=128)
        wqre = wq_out.ap().rearrange("(a p) f -> p a f", p=128)
        ld = P.dma("gpsimd", wst[:],
                   wre[:, ai, fj * 1024:(fj + 1) * 1024],
                   dep=[("sV", wstate["vq"][pa]),
                        ("sS", wstate["abs"][pa])])
        P.emit("vector", lambda e, wi=wi, pa=pa: e.tensor_scalar(
            qd_[:, 0:1024], wst[:],
            nthr3[:, wi:wi + 1], None, ALU.is_le),
            dep=[ld, ("sV", wstate["thrb"]), wstate["store"]])
        wstate["vq"][pa] = P.emit("vector", lambda e, wi=wi, pa=pa:
                                  e.scalar_tensor_tensor(
                                      qb_[:, 0:1024],
                                      wst[:],
                                      thr3[:, wi:wi + 1], qd_[:, 0:1024],
                                      ALU.is_ge, ALU.subtract), inc="sV")
        wstate["store"] = P.dma("gpsimd",
                                wqre[:, ai, fj * 1024:(fj + 1) * 1024],
                                qb_[:, 0:1024], dep=[("sV", wstate["vq"][pa])],
                                sem="sGD")

    for wh, wi in ((wg_h, 0), (wu_h, 1), (wd_h, 2)):
        n_a = wh.ap().shape[0] // 128
        n_f = wh.ap().shape[1] // 1024
        for ai in range(n_a):
            for fj in range(n_f):
                wsteps.append(lambda wh=wh, wi=wi, ai=ai, fj=fj:
                              w1_step(wh, wi, ai, fj))
    wsteps.append(w_thr_step)
    for wh, wq_out, wi in ((wg_h, wgq_h, 0), (wu_h, wuq_h, 1),
                           (wd_h, wdq_h, 2)):
        n_a = wh.ap().shape[0] // 128
        n_f = wh.ap().shape[1] // 1024
        for ai in range(n_a):
            for fj in range(n_f):
                wsteps.append(lambda wh=wh, wq_out=wq_out, wi=wi, ai=ai,
                              fj=fj: w2_step(wh, wq_out, wi, ai, fj))
    if PHASE_LIMIT <= 2:
        for st_fn in wsteps:
            st_fn()
        P._w("sync", P.fence("sO") + P.fence("sL") + P.fence("sGD"))
        return _finish(nc, P, es)

    tp_free_v = 0
    lp_free = [("sV", 0)]
    ln_prev = 0
    st_k_hist = {}
    st_v_hist = {}
    for kb in range(NKB):
        pa = kb % 2
        ld = P.dma("sync", xin[:, pa * 1024:(pa + 1) * 1024],
                   xkv_h.ap()[kb * 128:(kb + 1) * 128, :],
                   dep=[("sV", ln_prev)])
        vt = ln_chain(xin[:, pa * 1024:(pa + 1) * 1024],
                      xn[:, pa * 1024:(pa + 1) * 1024],
                      dep_v=[ld], dep_s=[ld])
        ln_prev = vt
        vtr = transpose8(xn[:, pa * 1024:(pa + 1) * 1024],
                         dep_t=[("sV", vt), ("sV", tp_free_v)] + gw_done,
                         dep_v=[],
                         out_ap=xnT[:, pa * 1024:(pa + 1) * 1024],
                         in_ap=tp[:])
        tp_free_v = vtr
        t_kv = None
        for ni in range(4):
            for c in range(8):
                dd = ([("sV", vtr)] + lp_free) if (ni == 0 and c == 0) else ()
                t_kv = matmul(
                    lp[:, ni * 512:(ni + 1) * 512],
                    xnT[:, pa * 1024 + c * 128:pa * 1024 + (c + 1) * 128],
                    wkv[:, c * 2048 + ni * 512:c * 2048 + (ni + 1) * 512],
                    start=(c == 0), stop=(c == 7), dep=dd,
                    inc="sT" if (ni == 3 and c == 7) else None)
        v_rope = rope_block(0, cosk, sink, kb, NKB, ktm,
                            [("sT", t_kv), ("sS", s_tabk)])
        v_lo_in = bass.AP(lp, 1024, [[2048, 128], [64, 8], [1, 64]])
        v_hi_in = bass.AP(lp, 1024 + 512, [[2048, 128], [64, 8], [1, 64]])
        v_lo_out = bass.AP(vc, kb * 520, [[NKB * 520, 128], [65, 8], [1, 64]])
        v_hi_out = bass.AP(vstg, 0, [[520, 128], [65, 8], [1, 64]])
        dep_vstg = [st_v_hist.get(kb - 1)]
        P.emit("vector", lambda e, i=v_lo_in, o=v_lo_out: e.tensor_copy(o, i),
               dep=[("sG", g_ones)])
        v_ev = P.emit("vector",
                      lambda e, i=v_hi_in, o=v_hi_out: e.tensor_copy(o, i),
                      dep=dep_vstg, inc="sV")
        lp_free = [("sV", v_ev)]
        vtr2 = transpose8(ktm, dep_t=[("sV", v_rope), ("sV", tp_free_v)],
                          dep_v=[],
                          out_ap=bass.AP(kTc, kb * 512,
                                         [[NKB * 512, 128], [1, 512]]),
                          in_ap=tp[:, 0:512])
        v_k2 = P.emit("vector", lambda e: e.tensor_copy(
            kstg[:], tp[:, 512:1024]),
            dep=[st_k_hist.get(kb - 1)], inc="sV")
        tp_free_v = v_k2
        st_k_hist[kb] = P.dma("sync", kt2_h.ap()[:, kb * 512:(kb + 1) * 512],
                              kstg[:],
                              dep=[("sV", v_k2)], sem="sO")
        st_v_hist[kb] = P.dma("sync", v2_h.ap()[:, kb * 520:(kb + 1) * 520],
                              vstg[:],
                              dep=[("sV", v_ev)], sem="sO")
        for _ in range(7):
            if wsteps:
                wsteps.pop(0)()
    while wsteps:
        wsteps.pop(0)()
    p1_done_o = P.fence("sO")
    p1_v = P.val("sV")
    esA1.close()
    if PHASE_LIMIT <= 3:
        P._w("sync", P.fence("sO") + P.fence("sL") + P.fence("sGD"))
        return _finish(nc, P, es)


    lp_free = [("sV", 0)]
    ln_prev2 = p1_v
    for s_ in range(4):
        for tq in range(4):
            ti = s_ * 4 + tq
            pa = ti % 2
            ld = P.dma("sync", xin[:, pa * 1024:(pa + 1) * 1024],
                       xq_h.ap()[ti * 128:(ti + 1) * 128, :],
                       dep=[("sV", ln_prev2)])
            vt = ln_chain(xin[:, pa * 1024:(pa + 1) * 1024],
                          xn[:, pa * 1024:(pa + 1) * 1024],
                          dep_v=[ld], dep_s=[ld])
            ln_prev2 = vt
            vtr = transpose8(xn[:, pa * 1024:(pa + 1) * 1024],
                             dep_t=[("sV", vt), ("sV", tp_free_v)], dep_v=[],
                             out_ap=xnT[:, pa * 1024:(pa + 1) * 1024],
                             in_ap=tp[:])
            tp_free_v = vtr
            t_q = None
            for ni in range(2):
                for c in range(8):
                    dd = ([("sV", vtr)] + lp_free) if (ni == 0 and c == 0) \
                        else ()
                    t_q = matmul(
                        lp[:, ni * 512:(ni + 1) * 512],
                        xnT[:, pa * 1024 + c * 128:
                            pa * 1024 + (c + 1) * 128],
                        wqq[:, c * 1024 + ni * 512:
                            c * 1024 + (ni + 1) * 512],
                        start=(c == 0), stop=(c == 7), dep=dd,
                        inc="sT" if (ni == 1 and c == 7) else None)
            v_rope = rope_block(0, cosq, sinq, ti, 16, ktm,
                                [("sT", t_q), ("sS", s_tabq)])
            lp_free = [("sV", v_rope)]
            vq = transpose8(ktm, dep_t=[("sV", v_rope), ("sV", tp_free_v)],
                            dep_v=[],
                            out_ap=bass.AP(qT[s_], tq * 128,
                                           [[8 * 512, 128], [512, 8],
                                            [1, 128]]),
                            in_ap=bass.AP(tp, 0,
                                          [[1024, 128], [128, 8], [1, 128]]))
            tp_free_v = vq
    qT_done = P.val("sV")
    esA2.close()
    if PHASE_LIMIT <= 4:
        P._w("sync", P.fence("sO") + P.fence("sL") + P.fence("sGD"))
        P.emit("sync", lambda e: e.wait_ge(P.sem["sV"][0], P.val("sV")))
        return _finish(nc, P, es)
    esAtt = ExitStack()
    sbT = lambda name, shape, dt: esAtt.enter_context(
        nc.sbuf_tensor(name, shape, dt))
    maskb = sbT("maskb", [128, 8 * 512], BF16)
    ctxT = sbT("ctxT", [64, 16 * 512], BF16)
    den = sbT("den", [128, 1024], F32)
    rcp = sbT("rcp", [128, 1024], F32)
    pt = sbT("pt", [128, 2 * 1024], BF16)
    x1t = sbT("x1t", [128, 1024], F32)
    woS = sbT("woS", [64, 16 * 1024], BF16)

    gwo = 0
    arena_guard = ([("sT", P.val("sT")), ("sV", P.val("sV")),
                    ("sS", P.val("sS"))]
                   + P.fence("sO") + P.fence("sL"))
    for hh in range(16):
        P.dma("gpsimd", woS[0:64, hh * 1024:(hh + 1) * 1024],
              wo_h.ap()[hh * 64:(hh + 1) * 64, :],
              dep=arena_guard if hh == 0 else ())
    gwo = P.fence("sGD")

    att = {
        "last_exp": {0: 0, 1: 0},
        "last_pv": {0: 0, 1: 0},
        "cp_free": [("sV", 0), ("sS", 0)],
        "mask_free": 0,
        "evac_extra": [],
    }

    def attn_slot(stage, s_, kt_sb, v_sb, dep_mask):
        cap = CAPS[s_]
        mld = P.dma("sync", maskb[:],
                    mask_h.ap()[:, s_ * 4096:(s_ + 1) * 4096],
                    dep=dep_mask + [("sV", att["mask_free"])])
        last_mask_v = 0
        for hpl in range(4):
            hp = stage * 4 + hpl

            def emit_pv(kb, dep_pv):
                pa = kb % 2
                matmul(cp[0:65, 0:512],
                       v_sb[:, kb * 520 + (2 * hpl) * 65:
                            kb * 520 + (2 * hpl) * 65 + 65],
                       pt[:, pa * 1024:pa * 1024 + 512],
                       start=(kb == 0), stop=(kb == cap - 1), dep=dep_pv)
                tpv = matmul(cp[0:65, 512:1024],
                             v_sb[:, kb * 520 + (2 * hpl + 1) * 65:
                                  kb * 520 + (2 * hpl + 1) * 65 + 65],
                             pt[:, pa * 1024 + 512:(pa + 1) * 1024],
                             start=(kb == 0), stop=(kb == cap - 1), inc="sT")
                att["last_pv"][pa] = tpv

            pend = None
            for kb in range(cap):
                pa = kb % 2
                depq = [mld, ("sV", qT_done)]
                if att["last_exp"][pa]:
                    depq.append(("sS", att["last_exp"][pa]))
                depq += lp_free
                matmul(lp[:, pa * 1024:pa * 1024 + 512],
                       kt_sb[0:64, kb * 512 + hpl * 128:
                             kb * 512 + (hpl) * 128 + 128],
                       qT[s_][0:64, hp * 512:(hp + 1) * 512],
                       start=True, stop=True, dep=depq)
                tqk = matmul(lp[:, pa * 1024 + 512:pa * 1024 + 1024],
                             kt_sb[64:128, kb * 512 + hpl * 128:
                                   kb * 512 + hpl * 128 + 128],
                             qT[s_][64:128, hp * 512:(hp + 1) * 512],
                             start=True, stop=True, inc="sT")
                deps = [("sT", tqk)]
                if att["last_pv"][pa]:
                    deps.append(("sT", att["last_pv"][pa]))
                sexp = P.emit("scalar", lambda e, pa=pa: e.activation(
                    pt[:, pa * 1024:(pa + 1) * 1024],
                    lp[:, pa * 1024:(pa + 1) * 1024],
                    AF.Exp, bias=cset[:, 5:6], scale=cset[:, 6:7]),
                    dep=deps, inc="sS")
                att["last_exp"][pa] = sexp
                dep_pv = [("sS", sexp)]
                if kb >= cap - 8:
                    moff = (kb - (cap - 8)) * 512
                    pt_pair = bass.AP(pt, pa * 1024,
                                      [[2048, 128], [512, 2], [1, 512]])
                    m_b = bass.AP(maskb, moff,
                                  [[8 * 512, 128], [0, 2], [1, 512]])
                    vm = P.emit("vector", lambda e, o=pt_pair, m=m_b:
                                e.tensor_tensor(o, o, m, ALU.mult),
                                dep=[("sS", sexp)], inc="sV")
                    last_mask_v = vm
                    dep_pv = [("sV", vm)]
                if kb == 0:
                    dep_pv = dep_pv + att["cp_free"]
                if pend is not None:
                    emit_pv(*pend)
                pend = (kb, dep_pv)
            emit_pv(*pend)
            tdone = att["last_pv"][(cap - 1) % 2]
            vden = P.emit("vector", lambda e: e.reciprocal(
                den[64:65, 0:1024], cp[64:65, 0:1024]),
                dep=[("sT", tdone)], inc="sV")
            mb1 = matmul(db[:], onesf[64:65, 0:128], den[64:65, 0:512],
                         start=True, stop=True, dep=[("sV", vden)], inc="sT")
            vc1 = P.emit("vector", lambda e: e.tensor_copy(
                rcp[:, 0:512], db[:]), dep=[("sT", mb1)], inc="sV")
            mb2 = matmul(db[:], onesf[64:65, 0:128], den[64:65, 512:1024],
                         start=True, stop=True, dep=[("sV", vc1)], inc="sT")
            vc2 = P.emit("vector", lambda e: e.tensor_copy(
                rcp[:, 512:1024], db[:]), dep=[("sT", mb2)], inc="sV")
            hA, hB = 2 * hp, 2 * hp + 1
            extra = att["evac_extra"]
            att["evac_extra"] = []
            P.emit("vector", lambda e, hA=hA: e.tensor_tensor(
                ctxT[0:64, hA * 512:(hA + 1) * 512], cp[0:64, 0:512],
                rcp[0:64, 0:512], ALU.mult), dep=extra)
            vev = P.emit("vector", lambda e, hB=hB: e.tensor_tensor(
                ctxT[0:64, hB * 512:(hB + 1) * 512], cp[0:64, 512:1024],
                rcp[0:64, 512:1024], ALU.mult), dep=[("sV", vc2)], inc="sV")
            att["cp_free"] = [("sV", vev)]
        att["mask_free"] = last_mask_v if last_mask_v else att["mask_free"]

    ctxA_store = 0
    dep0 = [("sV", qT_done)]
    for s_ in range(4):
        attn_slot(0, s_, kTc, vc, dep0)
        dep0 = []
        ctxA_store = P.dma("sync", ctxA_h.ap()[:, s_ * 4096:(s_ + 1) * 4096],
                           ctxT[0:64, 0:4096], dep=[("sV", P.val("sV"))],
                           sem="sO")
        att["evac_extra"] = [ctxA_store]

    if PHASE_LIMIT <= 5:
        P._w("sync", P.fence("sO") + P.fence("sL") + P.fence("sGD"))
        P.emit("sync", lambda e: e.wait_ge(P.sem["sV"][0], P.val("sV")))
        P.emit("sync", lambda e: e.wait_ge(P.sem["sS"][0], P.val("sS")))
        P.emit("sync", lambda e: e.wait_ge(P.sem["sT"][0], P.val("sT")))
        return _finish(nc, P, es)
    rk = P.dma("sync", kTc[:], kt2_h.ap()[:],
               dep=[ctxA_store] + p1_done_o +
                    [("sS", att["last_exp"][0]), ("sS", att["last_exp"][1]),
                     ("sT", att["last_pv"][0]), ("sT", att["last_pv"][1])])
    rv = P.dma("sync", vc[:], v2_h.ap()[:])
    if PHASE_LIMIT == 50:
        P._w("sync", P.fence("sO") + P.fence("sL") + P.fence("sGD"))
        return _finish(nc, P, es)

    st_x1_prev = None
    st_xq_prev = None
    out_dep_first = list(gwo)
    t_op_prev = 0
    v_x1 = 0
    for s_ in range(4):
        if 53 <= PHASE_LIMIT <= 56 and s_ >= PHASE_LIMIT - 52:
            break
        attn_slot(1, s_, kTc, vc, [rv, rk])
        if PHASE_LIMIT == 51 or 53 <= PHASE_LIMIT <= 56:
            att["cp_free"] = [("sV", P.val("sV")), ("sS", P.val("sS"))]
            continue
        rc_ld = P.dma("sync", ctxT[0:64, 0:4096],
                      ctxA_h.ap()[:, s_ * 4096:(s_ + 1) * 4096],
                      dep=[("sV", P.val("sV")), ("sT", t_op_prev)])
        if PHASE_LIMIT == 520:
            att["cp_free"] = [("sV", P.val("sV")), ("sS", P.val("sS"))]
            continue
        t_op = 0
        vt2 = 0
        for tq in range(4):
            ti = s_ * 4 + tq
            ldx = P.dma("sync", xin[:, 0:1024],
                        xq_h.ap()[ti * 128:(ti + 1) * 128, :],
                        dep=[("sV", ln_prev2 if ti == 0 else v_x1)])
            for ni in range(2):
                for hh in range(16):
                    dd = ([rc_ld, ("sV", vt2)] + out_dep_first
                          + att["cp_free"]) \
                        if (ni == 0 and hh == 0) else ()
                    t_op = matmul(
                        cp[:, ni * 512:(ni + 1) * 512],
                        ctxT[0:64, hh * 512 + tq * 128:
                             hh * 512 + tq * 128 + 128],
                        woS[0:64, hh * 1024 + ni * 512:
                            hh * 1024 + (ni + 1) * 512],
                        start=(hh == 0), stop=(hh == 15), dep=dd,
                        inc="sT" if (ni == 1 and hh == 15) else None)
            out_dep_first = []
            if PHASE_LIMIT == 521:
                vt2 = 0
                continue
            vt2 = ln_chain(cp[:, 0:1024], lns[:],
                           dep_v=[("sT", t_op)], dep_s=[("sT", t_op)],
                           psum=True)
            if PHASE_LIMIT == 522:
                continue
            P.drain("vector")
            v_x1 = P.emit("vector", lambda e: e.tensor_add(
                x1t[:], xin[:, 0:1024], lns[:]),
                dep=[("sV", vt2), ldx, st_x1_prev],
                inc="sV")
            st_x1_prev = P.dma("sync",
                               x1_h.ap()[ti * 128:(ti + 1) * 128, :],
                               x1t[:], dep=[("sV", v_x1)], sem="sO")
            if PHASE_LIMIT == 52:
                continue
            vt3 = ln_chain(x1t[:], lns[:], dep_v=[("sV", v_x1)],
                           dep_s=[("sV", v_x1)])
            P.emit("vector", lambda e: e.reduce_max(
                stats[:, 9:10], lns[:], axis=AX, apply_absolute_value=True),
                dep=[("sV", vt3)])
            P.drain("vector")
            P.emit("vector", lambda e: e.tensor_scalar(
                stats[:, 10:11], stats[:, 9:10], 1e-5, None, ALU.max))
            P.drain("vector")
            P.emit("vector", lambda e, ti=ti: e.tensor_scalar(
                sxinv[:, ti:ti + 1], stats[:, 10:11], 1.0 / 127.0, None,
                ALU.mult))
            P.emit("vector", lambda e: e.reciprocal(
                stats[:, 11:12], stats[:, 10:11]))
            P.drain("vector")
            vsc = P.emit("vector", lambda e: e.tensor_scalar(
                stats[:, 12:13], stats[:, 11:12], 127.0, None, ALU.mult),
                inc="sV")
            vq = act_quant_chain(lns[:], 1024, stats[:, 12:13],
                                 dep_v=[("sV", vsc), st_xq_prev])
            st_xq_prev = P.dma("sync",
                               xqq_h.ap()[ti * 128:(ti + 1) * 128, :],
                               qb_[:, 0:1024], dep=[("sV", vq)], sem="sO")
        att["cp_free"] = [("sV", P.val("sV")), ("sS", P.val("sS"))]
        att["evac_extra"] = [("sT", t_op)]
        t_op_prev = t_op
    p2_done = P.fence("sO")
    if PHASE_LIMIT <= 6 or PHASE_LIMIT in (50, 51, 52, 53, 54, 55, 56, 520, 521, 522):
        P._w("sync", P.fence("sO") + P.fence("sL") + P.fence("sGD"))
        P.emit("sync", lambda e: e.wait_ge(P.sem["sV"][0], P.val("sV")))
        P.emit("sync", lambda e: e.wait_ge(P.sem["sS"][0], P.val("sS")))
        P.emit("sync", lambda e: e.wait_ge(P.sem["sT"][0], P.val("sT")))
        return _finish(nc, P, es)

    esAtt.close()
    esA.close()
    esB = ExitStack()
    sbB = lambda name, shape, dt: esB.enter_context(
        nc.sbuf_tensor(name, shape, dt))
    wgT = sbB("wgT", [128, 8 * 4096], BF16)
    wuT = sbB("wuT", [128, 8 * 4096], BF16)
    xqT = sbB("xqT", [128, 8 * 512], BF16)
    hbuf = sbB("hbuf", [128, 4096], F32)
    hqb = sbB("hqb", [128, 4096], BF16)
    sgu = sbB("sgu", [128, 4], F32)

    wl_toks = []
    for c in range(8):
        wl_toks.append(P.dmat("sync", wgT[:, c * 4096:(c + 1) * 4096],
                              wgq_h.ap()[:, c * 128:(c + 1) * 128],
                              dep=(p2_done + P.fence("sGD"))
                              if c == 0 else []))
        wl_toks.append(P.dmat("sync", wuT[:, c * 4096:(c + 1) * 4096],
                              wuq_h.ap()[:, c * 128:(c + 1) * 128]))
    wl_done = [max((t for ln2, t in wl_toks if ln2 == ln), default=0)
               for ln in {ln for ln, _ in wl_toks}]
    wl_done = [(ln, max(t for l2, t in wl_toks if l2 == ln))
               for ln in {ln for ln, _ in wl_toks}]

    hq_store = {}
    lpB_free = [("sS", 0)]
    silu_guard = {0: 0, 1: 0}
    for g in range(4):
        xl_toks = []
        for c in range(8):
            xl_toks.append(P.dmat("sync", xqT[:, c * 512:(c + 1) * 512],
                                  xqq_h.ap()[g * 512:(g + 1) * 512,
                                             c * 128:(c + 1) * 128],
                                  dep=wl_done
                                  if (g == 0 and c == 0) else []))
        xl_done = [(ln, max(t for l2, t in xl_toks if l2 == ln))
                   for ln in {ln for ln, _ in xl_toks}]
        for tl in range(4):
            ti = g * 4 + tl
            P.emit("vector", lambda e, ti=ti: e.tensor_tensor(
                sgu[:, 0:1], sxinv[:, ti:ti + 1], wmean3[:, 0:1], ALU.mult),
                dep=xl_done + [("sS", P.val("sS"))])
            P.drain("vector")
            vsg = P.emit("vector", lambda e, ti=ti: e.tensor_tensor(
                sgu[:, 1:2], sxinv[:, ti:ti + 1], wmean3[:, 1:2], ALU.mult),
                inc="sV")
            h_written = {}
            for nci in range(8):
                pa = nci % 2
                dd0 = [("sV", vsg)] + lpB_free
                if nci >= 2:
                    dd0 = [("sV", h_written[nci - 2])]
                for c in range(8):
                    matmul(lp[:, pa * 512:(pa + 1) * 512],
                           xqT[:, c * 512 + tl * 128:c * 512 + tl * 128 + 128],
                           wgT[:, c * 4096 + nci * 512:
                               c * 4096 + (nci + 1) * 512],
                           start=(c == 0), stop=(c == 7),
                           dep=dd0 if c == 0 else ())
                tgu = None
                for c in range(8):
                    tgu = matmul(
                        lp[:, 1024 + pa * 512:1024 + (pa + 1) * 512],
                        xqT[:, c * 512 + tl * 128:c * 512 + tl * 128 + 128],
                        wuT[:, c * 4096 + nci * 512:
                            c * 4096 + (nci + 1) * 512],
                        start=(c == 0), stop=(c == 7),
                        inc="sT" if c == 7 else None)
                ssdep = [("sT", tgu)]
                if silu_guard[pa]:
                    ssdep.append(("sV", silu_guard[pa]))
                ss = P.emit("scalar", lambda e, pa=pa: e.activation(
                    sq[:, pa * 512:(pa + 1) * 512],
                    lp[:, pa * 512:(pa + 1) * 512], AF.Silu,
                    scale=sgu[:, 0:1]), dep=ssdep, inc="sS")
                vh = P.emit("vector", lambda e, nci=nci, pa=pa:
                            e.scalar_tensor_tensor(
                                hbuf[:, nci * 512:(nci + 1) * 512],
                                lp[:, 1024 + pa * 512:1024 + (pa + 1) * 512],
                                sgu[:, 1:2], sq[:, pa * 512:(pa + 1) * 512],
                                ALU.mult, ALU.mult),
                            dep=[("sS", ss)], inc="sV")
                h_written[nci] = vh
                silu_guard[pa] = vh
            P.emit("vector", lambda e: e.reduce_max(
                stats[:, 9:10], hbuf[:], axis=AX, apply_absolute_value=True),
                dep=[("sV", h_written[7])])
            P.drain("vector")
            P.emit("vector", lambda e: e.tensor_scalar(
                stats[:, 10:11], stats[:, 9:10], 1e-5, None, ALU.max))
            P.drain("vector")
            P.emit("vector", lambda e: e.reciprocal(
                stats[:, 11:12], stats[:, 10:11]))
            P.drain("vector")
            vsc = P.emit("vector", lambda e: e.tensor_scalar(
                stats[:, 12:13], stats[:, 11:12], 127.0, None, ALU.mult),
                inc="sV")
            lastq = vsc
            for qh in range(4):
                vq = act_quant_chain(hbuf[:, qh * 1024:(qh + 1) * 1024],
                                     1024, stats[:, 12:13],
                                     dep_v=[("sV", lastq),
                                            hq_store.get(ti - 1)])
                lastq = P.emit("vector", lambda e, qh=qh: e.tensor_copy(
                    hqb[:, qh * 1024:(qh + 1) * 1024], qb_[:, 0:1024]),
                    dep=[("sV", vq)], inc="sV")
            lpB_free = [("sV", lastq)]
            hq_store[ti] = P.dma("sync", hq_h.ap()[ti * 128:(ti + 1) * 128, :],
                                 hqb[:], dep=[("sV", lastq)], sem="sO")
    b1_done = P.fence("sO")
    b1_v = P.val("sV")
    b1_s = P.val("sS")
    if PHASE_LIMIT <= 7:
        P._w("sync", P.fence("sO") + P.fence("sL") + P.fence("sGD"))
        P.emit("sync", lambda e: e.wait_ge(P.sem["sV"][0], P.val("sV")))
        P.emit("sync", lambda e: e.wait_ge(P.sem["sS"][0], P.val("sS")))
        P.emit("sync", lambda e: e.wait_ge(P.sem["sT"][0], P.val("sT")))
        return _finish(nc, P, es)

    esB.close()
    esC = ExitStack()
    sbC = lambda name, shape, dt: esC.enter_context(
        nc.sbuf_tensor(name, shape, dt))
    wdT = sbC("wdT", [128, 32 * 1024], BF16)
    hqT = sbC("hqT", [128, 32 * 512], BF16)
    x1g = sbC("x1g", [128, 4 * 1024], F32)
    outb = sbC("outb", [128, 2 * 1024], F32)

    wdl_toks = []
    for ic in range(32):
        wdl_toks.append(P.dmat("sync", wdT[:, ic * 1024:(ic + 1) * 1024],
                               wdq_h.ap()[:, ic * 128:(ic + 1) * 128],
                               dep=b1_done + [("sV", b1_v), ("sS", b1_s)]
                               if ic == 0 else []))
    wdl_done = [(ln, max(t for l2, t in wdl_toks if l2 == ln))
                for ln in {ln for ln, _ in wdl_toks}]

    out_store = {}
    lpC_free = [("sV", 0)]
    for g in range(4):
        hl_toks = []
        for ic in range(32):
            hl_toks.append(P.dmat("sync", hqT[:, ic * 512:(ic + 1) * 512],
                                  hq_h.ap()[g * 512:(g + 1) * 512,
                                            ic * 128:(ic + 1) * 128],
                                  dep=wdl_done
                                  if (g == 0 and ic == 0) else []))
        hl_done = [(ln, max(t for l2, t in hl_toks if l2 == ln))
                   for ln in {ln for ln, _ in hl_toks}]
        xg_toks = []
        for a_ in range(4):
            xg_toks.append(P.dma(
                "sync", x1g[:, a_ * 1024:(a_ + 1) * 1024],
                x1_h.ap()[g * 512 + a_ * 128:g * 512 + (a_ + 1) * 128, :],
                dep=hl_done if a_ == 0 else []))
        xg = [(ln, max(t for l2, t in xg_toks if l2 == ln))
              for ln in {ln for ln, _ in xg_toks}]
        for tl in range(4):
            ti = g * 4 + tl
            t_dn = None
            for ni in range(2):
                for ic in range(32):
                    dd = (xg + lpC_free) \
                        if (ic == 0 and ni == 0) else ()
                    t_dn = matmul(
                        lp[:, ni * 512:(ni + 1) * 512],
                        hqT[:, ic * 512 + tl * 128:
                            ic * 512 + tl * 128 + 128],
                        wdT[:, ic * 1024 + ni * 512:
                            ic * 1024 + (ni + 1) * 512],
                        start=(ic == 0), stop=(ic == 31), dep=dd,
                        inc="sT" if (ni == 1 and ic == 31) else None)
            vt4 = ln_chain(lp[:, 0:1024], lns[:],
                           dep_v=[("sT", t_dn)], dep_s=[("sT", t_dn)],
                           psum=True)
            po = tl % 2
            dep_out = [("sV", vt4), out_store.get(ti - 2)]
            vfin = P.emit("vector", lambda e, tl=tl, po=po: e.tensor_add(
                outb[:, po * 1024:(po + 1) * 1024],
                x1g[:, tl * 1024:(tl + 1) * 1024], lns[:]),
                dep=dep_out, inc="sV")
            lpC_free = [("sV", vfin)]
            out_store[ti] = P.dma("sync",
                                  out_h.ap()[ti * 128:(ti + 1) * 128, :],
                                  outb[:, po * 1024:(po + 1) * 1024],
                                  dep=[("sV", vfin)], sem="sO")
    esC.close()

    P._w("sync", P.fence("sO"))

    return _finish(nc, P, es)


def _finish(nc, P, es):
    with nc.Block() as block:
        @block.sync
        def _(sync):
            for fn in P.q["sync"]:
                fn(sync)

        @block.gpsimd
        def _(gpsimd):
            for fn in P.q["gpsimd"]:
                fn(gpsimd)

        @block.vector
        def _(vector):
            for fn in P.q["vector"]:
                fn(vector)

        @block.scalar
        def _(scalar):
            for fn in P.q["scalar"]:
                fn(scalar)

        @block.tensor
        def _(tensor):
            for fn in P.q["tensor"]:
                fn(tensor)

    return nc


def _host_inputs(core, x, wqkv, wo, wg, wu, wd):
    b, par = core // 2, core % 2
    js = JS[par]
    rows = np.concatenate([np.arange(512 * j, 512 * j + 512) for j in js])
    xb = np.ascontiguousarray(x[b])
    xq = np.ascontiguousarray(xb[rows])
    masks = np.zeros((4, 8, 128, 512), dtype=np.float32)
    for s_, j in enumerate(js):
        cap = CAPS[s_]
        q0 = 512 * j
        for kbrel in range(8):
            kb = cap - 8 + kbrel
            kabs = kb * 128 + np.arange(128)[:, None]
            qabs = q0 + np.arange(512)[None, :]
            masks[s_, kbrel] = (kabs <= qabs)
    masks = masks.transpose(2, 0, 1, 3).reshape(128, 4 * 8 * 512)
    masks = np.ascontiguousarray(masks).astype(ml_dtypes.bfloat16)
    pck = (np.arange(NKB)[None, :] * 128
           + np.arange(128)[:, None]).astype(np.float32)
    tq_tiles = np.concatenate([np.arange(4 * j, 4 * j + 4) for j in js])
    pcq = (tq_tiles[None, :] * 128
           + np.arange(128)[:, None]).astype(np.float32)
    fracb = np.ascontiguousarray(np.broadcast_to(
        (2.0 * np.arange(32) / 64.0).astype(np.float32), (128, 32)))
    ident = np.eye(128, dtype=np.float32).astype(ml_dtypes.bfloat16)
    return dict(xkv=xb, xq=xq, wqkv=wqkv, wo=wo, wg=wg, wu=wu, wd=wd,
                masks=masks, poscolk=pck, poscolq=pcq, fracb=fracb,
                ident=ident), rows


def kernel(L0_attn_x, L0_attn_wqkv, L0_attn_wo, L0_ffn_w_gate, L0_ffn_w_up,
           L0_ffn_w_down):
    x = np.ascontiguousarray(np.asarray(L0_attn_x, dtype=np.float32))
    wqkv = np.ascontiguousarray(np.asarray(L0_attn_wqkv, dtype=np.float32))
    wo = np.ascontiguousarray(np.asarray(L0_attn_wo, dtype=np.float32))
    wg = np.ascontiguousarray(np.asarray(L0_ffn_w_gate, dtype=np.float32))
    wu = np.ascontiguousarray(np.asarray(L0_ffn_w_up, dtype=np.float32))
    wd = np.ascontiguousarray(np.asarray(L0_ffn_w_down, dtype=np.float32))

    nc = build_nc()
    in_maps, rowmaps = [], []
    for core in range(8):
        im, rows = _host_inputs(core, x, wqkv, wo, wg, wu, wd)
        in_maps.append(im)
        rowmaps.append(rows)
    res = run_bass_kernel_spmd(nc, in_maps, core_ids=list(range(8)))
    out = np.empty((B, S, DM), dtype=np.float32)
    for core in range(8):
        b = core // 2
        out[b][rowmaps[core]] = np.asarray(res.results[core]["out"],
                                           dtype=np.float32)
    return out


# version 08
# speedup vs baseline: 1.4782x; 1.0098x over previous
import numpy as np
import ml_dtypes
from contextlib import ExitStack

import concourse.bass as bass
import concourse.mybir as mybir
from concourse.bass_utils import run_bass_kernel_spmd

F32 = mybir.dt.float32
BF16 = mybir.dt.bfloat16
I32 = mybir.dt.int32
AF = mybir.ActivationFunctionType
ALU = mybir.AluOpType
AX = mybir.AxisListType.X

B, S, DM, H, DH, INNER = 4, 4096, 1024, 16, 64, 4096
NT = 2048
CAPS = [8, 16, 24, 32]
JS = {0: [0, 3, 4, 7], 1: [1, 2, 5, 6]}
NKB = S // 128
EPS = 1e-5
EXPB = -10.0
RC = 256.0
DEBUG_TAPS = False
PHASE_LIMIT = 99


class Prog:
    ENGS = ("sync", "gpsimd", "vector", "scalar", "tensor")
    LANES = {"sL": 4, "sO": 2, "sGD": 2}

    def __init__(self, nc, es):
        self.nc = nc
        self.q = {e: [] for e in self.ENGS}
        self.sem = {}
        for name in ("sG", "sV", "sS", "sT"):
            self.sem[name] = [es.enter_context(nc.semaphore(name)), 0]
        for g, n in self.LANES.items():
            for i in range(n):
                nm = "%s%d" % (g, i)
                self.sem[nm] = [es.enter_context(nc.semaphore(nm)), 0]
        self.rr = {g: 0 for g in self.LANES}
        self.waited = {e: {} for e in self.ENGS}
        self.lastdrain = {e: False for e in self.ENGS}

    def _w(self, eng, dep):
        for d in dep:
            if not d:
                continue
            sn, val = d
            if val <= 0:
                continue
            if self.waited[eng].get(sn, 0) >= val:
                continue
            self.waited[eng][sn] = val
            h = self.sem[sn][0]
            self.q[eng].append(lambda e, h=h, v=val: e.wait_ge(h, v))
            self.lastdrain[eng] = False

    AUTODRAIN = ("vector", "gpsimd", "scalar")

    def emit(self, eng, fn, dep=(), inc=None, n=1):
        self._w(eng, dep)
        if eng in self.AUTODRAIN and not self.lastdrain[eng]:
            self.q[eng].append(lambda e: e.drain())
        self.lastdrain[eng] = False
        if inc is None:
            self.q[eng].append(fn)
            return None
        self.sem[inc][1] += n
        tgt = self.sem[inc][1]
        h = self.sem[inc][0]

        def run(e, fn=fn, h=h, n=n):
            fn(e).then_inc(h, n)
        self.q[eng].append(run)
        return tgt

    def drain(self, eng):
        if not self.lastdrain[eng]:
            self.q[eng].append(lambda e: e.drain())
            self.lastdrain[eng] = True

    def val(self, semname):
        return self.sem[semname][1]

    def fence(self, group):
        return [("%s%d" % (group, i), self.sem["%s%d" % (group, i)][1])
                for i in range(self.LANES[group])
                if self.sem["%s%d" % (group, i)][1] > 0]

    def _dma(self, eng, fn, dep, group):
        i = self.rr[group]
        self.rr[group] = (i + 1) % self.LANES[group]
        lane = "%s%d" % (group, i)
        prev = self.sem[lane][1]
        dep = list(dep) + [(lane, prev)]
        tgt = self.emit(eng, fn, dep=dep, inc=lane, n=16)
        return (lane, tgt)

    def dma(self, eng, dst, src, dep=(), sem=None, **kw):
        group = sem or ("sL" if eng == "sync" else "sGD")
        return self._dma(eng, lambda e: e.dma_start(dst, src, **kw),
                         dep, group)

    def dmat(self, eng, dst, src, dep=(), sem="sL"):
        return self._dma(eng, lambda e: e.dma_start_transpose(dst, src),
                         dep, group=sem)


def build_nc():
    nc = bass.Bass("TRN2", target_bir_lowering=False)
    es = ExitStack()

    xkv_h = nc.dram_tensor("xkv", [S, DM], F32, kind="ExternalInput")
    xq_h = nc.dram_tensor("xq", [NT, DM], F32, kind="ExternalInput")
    wqkv_h = nc.dram_tensor("wqkv", [DM, 3 * DM], F32, kind="ExternalInput")
    wo_h = nc.dram_tensor("wo", [DM, DM], F32, kind="ExternalInput")
    wg_h = nc.dram_tensor("wg", [INNER, DM], F32, kind="ExternalInput")
    wu_h = nc.dram_tensor("wu", [INNER, DM], F32, kind="ExternalInput")
    wd_h = nc.dram_tensor("wd", [DM, INNER], F32, kind="ExternalInput")
    mask_h = nc.dram_tensor("masks", [128, 4 * 8 * 512], BF16,
                            kind="ExternalInput")
    pck_h = nc.dram_tensor("poscolk", [128, NKB], F32, kind="ExternalInput")
    pcq_h = nc.dram_tensor("poscolq", [128, 16], F32, kind="ExternalInput")
    frac_h = nc.dram_tensor("fracb", [128, 32], F32, kind="ExternalInput")
    idn_h = nc.dram_tensor("ident", [128, 128], BF16, kind="ExternalInput")
    out_h = nc.dram_tensor("out", [NT, DM], F32, kind="ExternalOutput")

    kd = "ExternalOutput" if DEBUG_TAPS else "Internal"
    wgq_h = nc.dram_tensor("wgq", [INNER, DM], BF16, kind=kd)
    wuq_h = nc.dram_tensor("wuq", [INNER, DM], BF16, kind=kd)
    wdq_h = nc.dram_tensor("wdq", [DM, INNER], BF16, kind=kd)
    kt2_h = nc.dram_tensor("kt2", [128, NKB * 512], BF16, kind=kd)
    v2_h = nc.dram_tensor("v2", [128, NKB * 520], BF16, kind=kd)
    ctxA_h = nc.dram_tensor("ctxA", [64, 4 * 8 * 512], BF16, kind=kd)
    x1_h = nc.dram_tensor("x1", [NT, DM], F32, kind=kd)
    xqq_h = nc.dram_tensor("xqq", [NT, DM], BF16, kind=kd)
    hq_h = nc.dram_tensor("hqs", [NT, INNER], BF16, kind=kd)

    lp = es.enter_context(nc.psum_tensor("lp", [128, 2048], F32))
    cp = es.enter_context(nc.psum_tensor("cp", [128, 1024], F32))
    db = es.enter_context(nc.psum_tensor("db", [128, 512], F32))
    tp = es.enter_context(nc.psum_tensor("tp", [128, 1024], BF16))

    sb = lambda name, shape, dt: es.enter_context(
        nc.sbuf_tensor(name, shape, dt))
    idn = sb("idn", [128, 128], BF16)
    onesf = sb("onesf", [128, 128], F32)
    cset = sb("cset", [128, 12], F32)
    stats = sb("stats", [128, 16], F32)
    acc3 = sb("acc3", [128, 4], F32)
    thr3 = sb("thr3", [128, 4], F32)
    nthr3 = sb("nthr3", [128, 4], F32)
    wmean3 = sb("wmean3", [128, 4], F32)
    sxinv = sb("sxinv", [128, 16], F32)
    sq = sb("sq", [128, 1024], F32)
    ra = sb("ra", [128, 512], F32)
    rb = sb("rb", [128, 512], F32)
    lns = sb("lns", [128, 1024], F32)
    qt_ = sb("qt", [128, 1024], F32)
    qi_ = sb("qi", [128, 1024], I32)
    qd_ = sb("qd", [128, 1024], F32)
    qb_ = sb("qb", [128, 1024], BF16)

    P = Prog(nc, es)

    def matmul(out, lhsT, rhs, start, stop, dep=(), inc=None):
        return P.emit("tensor",
                      lambda e: e.matmul(out, lhsT, rhs, start=start,
                                         stop=stop),
                      dep=dep, inc=inc)

    def ln_chain(src_ap, dst_ap, dep_v, dep_s, psum=False):
        cv = None
        if psum:
            cv = P.emit("vector",
                        lambda e, o=dst_ap, i=src_ap: e.tensor_copy(o, i),
                        dep=dep_v, inc="sV")
            dep_v = [("sV", cv)]
            dep_s = [("sV", cv)]
            src_ap = dst_ap
        P.emit("vector", lambda e: e.reduce_sum(stats[:, 0:1], src_ap,
                                                axis=AX), dep=dep_v)
        s1 = P.emit("scalar",
                    lambda e: e.activation(sq[:], src_ap, AF.Square,
                                           accum_out=stats[:, 1:2]),
                    dep=dep_s, inc="sS")
        P.drain("vector")
        P.emit("vector", lambda e: e.tensor_scalar(
            stats[:, 2:3], stats[:, 0:1], 1.0 / DM, None, ALU.mult))
        P.emit("vector", lambda e: e.tensor_scalar(
            stats[:, 3:4], stats[:, 1:2], 1.0 / DM, EPS, ALU.mult, ALU.add),
            dep=[("sS", s1)])
        P.drain("vector")
        P.emit("vector", lambda e: e.tensor_mul(
            stats[:, 4:5], stats[:, 2:3], stats[:, 2:3]))
        P.drain("vector")
        P.emit("vector", lambda e: e.tensor_sub(
            stats[:, 5:6], stats[:, 3:4], stats[:, 4:5]))
        P.drain("vector")
        v2t = P.emit("vector", lambda e: e.reciprocal(stats[:, 6:7],
                                                      stats[:, 5:6]),
                     inc="sV")
        s2 = P.emit("scalar", lambda e: e.activation(
            stats[:, 7:8], stats[:, 6:7], AF.Sqrt), dep=[("sV", v2t)],
            inc="sS")
        vt = P.emit("vector", lambda e: e.tensor_scalar(
            dst_ap, src_ap, stats[:, 2:3], stats[:, 7:8],
            ALU.subtract, ALU.mult), dep=[("sS", s2)], inc="sV")
        return (vt, cv) if psum else vt

    def transpose8(src_sb, dep_t, dep_v, out_ap, in_ap):
        tt = None
        for j in range(8):
            tt = P.emit("tensor", lambda e, j=j: e.transpose(
                tp[:, j * 128:(j + 1) * 128],
                src_sb[:, j * 128:(j + 1) * 128], idn[:]),
                dep=(dep_t if j == 0 else ()),
                inc="sT" if j == 7 else None)
        vt = P.emit("vector", lambda e: e.tensor_copy(out_ap, in_ap),
                    dep=[("sT", tt)] + list(dep_v), inc="sV")
        return vt

    aq_state = {"g": 0}

    def act_quant_chain(src_ap, n, s_ap, dep_v):
        t1 = P.emit("vector", lambda e: e.tensor_scalar(
            qt_[:, 0:n], src_ap, s_ap, RC, ALU.mult, ALU.add),
            dep=list(dep_v) + [("sG", aq_state["g"])], inc="sV")
        g1 = P.emit("gpsimd", lambda e: e.tensor_copy(qi_[:, 0:n],
                                                      qt_[:, 0:n]),
                    dep=[("sV", t1)], inc="sG")
        aq_state["g"] = g1
        P.emit("vector", lambda e: e.tensor_sub(
            qd_[:, 0:n], qt_[:, 0:n], qi_[:, 0:n]), dep=[("sG", g1)])
        P.emit("vector", lambda e: e.scalar_tensor_tensor(
            qt_[:, 0:n], qd_[:, 0:n], 0.5, qi_[:, 0:n], ALU.is_ge, ALU.add))
        P.drain("vector")
        v3 = P.emit("vector", lambda e: e.tensor_scalar(
            qb_[:, 0:n], qt_[:, 0:n], RC, None, ALU.subtract), inc="sV")
        return v3

    def rope_block(src_psum_off, cos_sb, sin_sb, toff, tw, dst_sb, dep_first):
        in1 = bass.AP(lp, src_psum_off, [[2048, 128], [64, 16], [1, 32]])
        in2 = bass.AP(lp, src_psum_off + 32, [[2048, 128], [64, 16], [1, 32]])
        cs = bass.AP(cos_sb, toff * 32, [[tw * 32, 128], [0, 16], [1, 32]])
        sn = bass.AP(sin_sb, toff * 32, [[tw * 32, 128], [0, 16], [1, 32]])
        o1 = bass.AP(dst_sb, 0, [[1024, 128], [64, 16], [1, 32]])
        o2 = bass.AP(dst_sb, 32, [[1024, 128], [64, 16], [1, 32]])
        P.emit("vector", lambda e: e.tensor_tensor(ra[:], in1, cs, ALU.mult),
               dep=dep_first)
        P.emit("vector", lambda e: e.tensor_tensor(rb[:], in2, sn, ALU.mult))
        P.drain("vector")
        P.emit("vector", lambda e: e.tensor_sub(o1, ra[:], rb[:]))
        P.emit("vector", lambda e: e.tensor_tensor(ra[:], in2, cs, ALU.mult))
        P.emit("vector", lambda e: e.tensor_tensor(rb[:], in1, sn, ALU.mult))
        P.drain("vector")
        return P.emit("vector", lambda e: e.tensor_add(o2, ra[:], rb[:]),
                      inc="sV")

    esA = ExitStack()
    sbA = lambda name, shape, dt: esA.enter_context(
        nc.sbuf_tensor(name, shape, dt))
    kTc = sbA("kTc", [128, NKB * 512], BF16)
    vc = sbA("vc", [128, NKB * 520], BF16)
    xin = sbA("xin", [128, 2 * 1024], F32)
    qT = [sbA("qT%d" % s_, [128, 8 * 512], BF16) for s_ in range(4)]
    esA2 = ExitStack()
    sbA2 = lambda name, shape, dt: esA2.enter_context(
        nc.sbuf_tensor(name, shape, dt))
    wqq = sbA2("wqq", [128, 8 * 1024], BF16)
    xn = sbA2("xn", [128, 2 * 1024], BF16)
    xnT = sbA2("xnT", [128, 2 * 1024], BF16)
    ktm = sbA2("ktm", [128, 1024], BF16)
    cosq = sbA2("cosq", [128, 16 * 32], F32)
    sinq = sbA2("sinq", [128, 16 * 32], F32)
    pck = sbA2("pck", [128, NKB], F32)
    pcq = sbA2("pcq", [128, 16], F32)
    frac = sbA2("frac", [128, 32], F32)
    esA1 = ExitStack()
    sbA1 = lambda name, shape, dt: esA1.enter_context(
        nc.sbuf_tensor(name, shape, dt))
    wkv = sbA1("wkv", [128, 8 * 2048], BF16)
    kstg = sbA1("kstg", [128, 512], BF16)
    vstg = sbA1("vstg", [128, 520], BF16)
    wst = sbA1("wst", [128, 1024], F32)
    cosk = sbA1("cosk", [128, NKB * 32], BF16)
    sink = sbA1("sink", [128, NKB * 32], BF16)

    lid = P.dma("sync", idn[:], idn_h.ap())
    lpk = P.dma("sync", pck[:], pck_h.ap())
    lpq = P.dma("sync", pcq[:], pcq_h.ap())
    lfr = P.dma("sync", frac[:], frac_h.ap())

    P.emit("gpsimd", lambda e: e.memset(onesf[:], 1.0))
    P.emit("gpsimd", lambda e: e.memset(cset[:, 0:1], -1.0))
    P.emit("gpsimd", lambda e: e.memset(cset[:, 5:6], EXPB))
    P.emit("gpsimd", lambda e: e.memset(cset[:, 6:7], 0.125))
    P.emit("gpsimd", lambda e: e.memset(sq[:, 0:1], 10000.0))
    P.emit("gpsimd", lambda e: e.memset(sq[:, 1:2], 1.0))
    P.emit("gpsimd", lambda e: e.memset(acc3[:], 0.0))
    P.drain("gpsimd")
    g0 = P.emit("gpsimd", lambda e: e.engine_nop(), inc="sG")

    s_ln = P.emit("scalar", lambda e: e.activation(
        cset[:, 1:2], sq[:, 0:1], AF.Ln), dep=[("sG", g0)], inc="sS")
    s_at = P.emit("scalar", lambda e: e.activation(
        cset[:, 7:8], sq[:, 1:2], AF.Arctan), inc="sS")
    P.emit("vector", lambda e: e.tensor_scalar(
        cset[:, 2:3], cset[:, 7:8], 2.0, None, ALU.mult), dep=[("sS", s_at)])
    P.emit("vector", lambda e: e.tensor_scalar(
        cset[:, 3:4], cset[:, 7:8], 8.0, None, ALU.mult))
    P.emit("vector", lambda e: e.tensor_scalar(
        cset[:, 8:9], cset[:, 7:8], 4.0, None, ALU.mult))
    P.emit("vector", lambda e: e.tensor_scalar(
        cset[:, 9:10], cset[:, 7:8], -4.0, None, ALU.mult))
    P.emit("vector", lambda e: e.tensor_scalar(
        cset[:, 10:11], cset[:, 7:8], -8.0, None, ALU.mult))
    v_2p = P.emit("vector", lambda e: e.reciprocal(cset[:, 4:5],
                                                   cset[:, 3:4]), inc="sV")
    v_fl = P.emit("vector", lambda e: e.tensor_scalar(
        sq[:, 32:64], frac[:], cset[:, 1:2], None, ALU.mult),
        dep=[lfr, ("sS", s_ln), ("sV", v_2p)], inc="sV")
    s_it = P.emit("scalar", lambda e: e.activation(
        frac[:], sq[:, 32:64], AF.Exp, scale=cset[:, 0:1]),
        dep=[("sV", v_fl)], inc="sS")

    def build_tables(n_tiles, pc_sb, cos_sb, sin_sb, dep0):
        deps = list(dep0)
        st = 0
        for t in range(n_tiles):
            P.emit("vector", lambda e, t=t: e.tensor_scalar(
                ra[:, 0:32], frac[:], pc_sb[:, t:t + 1], None, ALU.mult),
                dep=deps + [("sS", st)])
            deps = []
            P.drain("vector")
            P.emit("vector", lambda e: e.tensor_scalar(
                ra[:, 32:64], ra[:, 0:32], 1.0, cset[:, 2:3],
                ALU.mult, ALU.add))
            P.drain("vector")
            vm = P.emit("vector", lambda e: e.tensor_scalar(
                rb[:, 0:64], ra[:, 0:64], cset[:, 4:5], None, ALU.mult),
                inc="sV")
            gk = P.emit("gpsimd", lambda e: e.tensor_copy(
                qi_[:, 0:64], rb[:, 0:64]), dep=[("sV", vm)], inc="sG")
            P.emit("vector", lambda e: e.scalar_tensor_tensor(
                rb[:, 0:64], qi_[:, 0:64], cset[:, 10:11], ra[:, 0:64],
                ALU.mult, ALU.add), dep=[("sG", gk)])
            P.emit("vector", lambda e: e.tensor_scalar(
                rb[:, 64:128], rb[:, 0:64], cset[:, 8:9], None, ALU.is_gt))
            P.emit("vector", lambda e: e.scalar_tensor_tensor(
                rb[:, 0:64], rb[:, 64:128], cset[:, 10:11], rb[:, 0:64],
                ALU.mult, ALU.add))
            P.emit("vector", lambda e: e.tensor_scalar(
                rb[:, 64:128], rb[:, 0:64], cset[:, 9:10], None, ALU.is_lt))
            vr = P.emit("vector", lambda e: e.scalar_tensor_tensor(
                rb[:, 0:64], rb[:, 64:128], cset[:, 3:4], rb[:, 0:64],
                ALU.mult, ALU.add), inc="sV")
            P.emit("scalar", lambda e, t=t: e.activation(
                sin_sb[:, t * 32:(t + 1) * 32], rb[:, 0:32], AF.Sin),
                dep=[("sV", vr)])
            st = P.emit("scalar", lambda e, t=t: e.activation(
                cos_sb[:, t * 32:(t + 1) * 32], rb[:, 32:64], AF.Sin),
                inc="sS")
        return st

    s_tabk = build_tables(NKB, pck, cosk, sink,
                          [("sS", s_it), lpk, lpq, lid])
    s_tabq = build_tables(16, pcq, cosq, sinq, [('sS', s_tabk)])


    for c in range(8):
        P.dma("gpsimd", wkv[:, c * 2048:(c + 1) * 2048],
              wqkv_h.ap()[c * 128:(c + 1) * 128, 1024:3072])
        P.dma("gpsimd", wqq[:, c * 1024:(c + 1) * 1024],
              wqkv_h.ap()[c * 128:(c + 1) * 128, 0:1024])
    gw_done = P.fence("sGD")

    P.emit("gpsimd", lambda e: e.memset(
        bass.AP(vc, 64, [[NKB * 520, 128], [520, NKB], [65, 8]]), 1.0))
    P.emit("gpsimd", lambda e: e.memset(
        bass.AP(vstg, 64, [[520, 128], [65, 8]]), 1.0))
    P.drain("gpsimd")
    g_ones = P.emit("gpsimd", lambda e: e.engine_nop(), inc="sG")

    wstate = {"abs": [0, 0], "ga": g_ones, "vq": [0, 0], "store": None,
              "thrb": 0, "i": 0}
    wsteps = []

    def w1_step(wh, wi, ai, fj):
        pa = 0
        wstate["i"] += 1
        wre = wh.ap().rearrange("(a p) f -> p a f", p=128)
        ld = P.dma("gpsimd", wst[:],
                   wre[:, ai, fj * 1024:(fj + 1) * 1024],
                   dep=[("sS", wstate["abs"][pa])])
        wstate["abs"][pa] = P.emit("scalar", lambda e, pa=pa: e.activation(
            sq[:], wst[:], AF.Abs,
            accum_out=stats[:, 8:9]),
            dep=[ld, ("sG", wstate["ga"])], inc="sS")
        wstate["ga"] = P.emit("gpsimd", lambda e, wi=wi: e.tensor_tensor(
            acc3[:, wi:wi + 1], acc3[:, wi:wi + 1], stats[:, 8:9],
            ALU.add), dep=[("sS", wstate["abs"][pa]), ("sG", wstate["ga"])],
            inc="sG")

    def w_thr_step():
        mt = matmul(db[0:1, 0:4], onesf[:, 0:1], acc3[:], start=True,
                    stop=True, dep=[("sG", wstate["ga"])], inc="sT")
        P.emit("vector", lambda e: e.tensor_scalar(
            stats[0:1, 8:12], db[0:1, 0:4], 1.0 / (INNER * DM), 1e-5,
            ALU.mult, ALU.max), dep=[("sT", mt)])
        v_thr = P.emit("vector", lambda e: e.tensor_scalar(
            stats[0:1, 12:16], stats[0:1, 8:12], 0.5, None, ALU.mult),
            inc="sV")
        m_b1 = matmul(db[:, 0:4], onesf[0:1, 0:128], stats[0:1, 12:16],
                      start=True, stop=True, dep=[("sV", v_thr)], inc="sT")
        vb1 = P.emit("vector", lambda e: e.tensor_copy(thr3[:], db[:, 0:4]),
                     dep=[("sT", m_b1)], inc="sV")
        m_b2 = matmul(db[:, 0:4], onesf[0:1, 0:128], stats[0:1, 8:12],
                      start=True, stop=True, dep=[("sV", vb1)], inc="sT")
        P.emit("vector", lambda e: e.tensor_copy(wmean3[:], db[:, 0:4]),
               dep=[("sT", m_b2)])
        wstate["thrb"] = P.emit("vector", lambda e: e.tensor_scalar(
            nthr3[:], thr3[:], -1.0, None, ALU.mult), inc="sV")

    def w2_step(wh, wq_out, wi, ai, fj):
        pa = 0
        wstate["i"] += 1
        wre = wh.ap().rearrange("(a p) f -> p a f", p=128)
        wqre = wq_out.ap().rearrange("(a p) f -> p a f", p=128)
        ld = P.dma("gpsimd", wst[:],
                   wre[:, ai, fj * 1024:(fj + 1) * 1024],
                   dep=[("sV", wstate["vq"][pa]),
                        ("sS", wstate["abs"][pa])])
        P.emit("vector", lambda e, wi=wi, pa=pa: e.tensor_scalar(
            qd_[:, 0:1024], wst[:],
            nthr3[:, wi:wi + 1], None, ALU.is_le),
            dep=[ld, ("sV", wstate["thrb"]), wstate["store"]])
        wstate["vq"][pa] = P.emit("vector", lambda e, wi=wi, pa=pa:
                                  e.scalar_tensor_tensor(
                                      qb_[:, 0:1024],
                                      wst[:],
                                      thr3[:, wi:wi + 1], qd_[:, 0:1024],
                                      ALU.is_ge, ALU.subtract), inc="sV")
        wstate["store"] = P.dma("gpsimd",
                                wqre[:, ai, fj * 1024:(fj + 1) * 1024],
                                qb_[:, 0:1024], dep=[("sV", wstate["vq"][pa])],
                                sem="sGD")

    for wh, wi in ((wg_h, 0), (wu_h, 1), (wd_h, 2)):
        n_a = wh.ap().shape[0] // 128
        n_f = wh.ap().shape[1] // 1024
        for ai in range(n_a):
            for fj in range(n_f):
                wsteps.append(lambda wh=wh, wi=wi, ai=ai, fj=fj:
                              w1_step(wh, wi, ai, fj))
    wsteps.append(w_thr_step)
    for wh, wq_out, wi in ((wg_h, wgq_h, 0), (wu_h, wuq_h, 1),
                           (wd_h, wdq_h, 2)):
        n_a = wh.ap().shape[0] // 128
        n_f = wh.ap().shape[1] // 1024
        for ai in range(n_a):
            for fj in range(n_f):
                wsteps.append(lambda wh=wh, wq_out=wq_out, wi=wi, ai=ai,
                              fj=fj: w2_step(wh, wq_out, wi, ai, fj))
    if PHASE_LIMIT <= 2:
        for st_fn in wsteps:
            st_fn()
        P._w("sync", P.fence("sO") + P.fence("sL") + P.fence("sGD"))
        return _finish(nc, P, es)

    tp_free_v = 0
    lp_free = [("sV", 0)]
    ln_prev = {0: 0, 1: 0}
    st_k_hist = {}
    st_v_hist = {}
    for kb in range(NKB):
        pa = kb % 2
        ld = P.dma("sync", xin[:, pa * 1024:(pa + 1) * 1024],
                   xkv_h.ap()[kb * 128:(kb + 1) * 128, :],
                   dep=[("sV", ln_prev[pa])])
        vt = ln_chain(xin[:, pa * 1024:(pa + 1) * 1024],
                      xn[:, pa * 1024:(pa + 1) * 1024],
                      dep_v=[ld], dep_s=[ld])
        ln_prev[pa] = vt
        vtr = transpose8(xn[:, pa * 1024:(pa + 1) * 1024],
                         dep_t=[("sV", vt), ("sV", tp_free_v)] + gw_done,
                         dep_v=[],
                         out_ap=xnT[:, pa * 1024:(pa + 1) * 1024],
                         in_ap=tp[:])
        tp_free_v = vtr
        t_kv = None
        for ni in range(4):
            for c in range(8):
                dd = ([("sV", vtr)] + lp_free) if (ni == 0 and c == 0) else ()
                t_kv = matmul(
                    lp[:, ni * 512:(ni + 1) * 512],
                    xnT[:, pa * 1024 + c * 128:pa * 1024 + (c + 1) * 128],
                    wkv[:, c * 2048 + ni * 512:c * 2048 + (ni + 1) * 512],
                    start=(c == 0), stop=(c == 7), dep=dd,
                    inc="sT" if (ni == 3 and c == 7) else None)
        v_rope = rope_block(0, cosk, sink, kb, NKB, ktm,
                            [("sT", t_kv), ("sS", s_tabk)])
        v_lo_in = bass.AP(lp, 1024, [[2048, 128], [64, 8], [1, 64]])
        v_hi_in = bass.AP(lp, 1024 + 512, [[2048, 128], [64, 8], [1, 64]])
        v_lo_out = bass.AP(vc, kb * 520, [[NKB * 520, 128], [65, 8], [1, 64]])
        v_hi_out = bass.AP(vstg, 0, [[520, 128], [65, 8], [1, 64]])
        dep_vstg = [st_v_hist.get(kb - 1)]
        P.emit("vector", lambda e, i=v_lo_in, o=v_lo_out: e.tensor_copy(o, i),
               dep=[("sG", g_ones)])
        v_ev = P.emit("vector",
                      lambda e, i=v_hi_in, o=v_hi_out: e.tensor_copy(o, i),
                      dep=dep_vstg, inc="sV")
        lp_free = [("sV", v_ev)]
        vtr2 = transpose8(ktm, dep_t=[("sV", v_rope), ("sV", tp_free_v)],
                          dep_v=[],
                          out_ap=bass.AP(kTc, kb * 512,
                                         [[NKB * 512, 128], [1, 512]]),
                          in_ap=tp[:, 0:512])
        v_k2 = P.emit("vector", lambda e: e.tensor_copy(
            kstg[:], tp[:, 512:1024]),
            dep=[st_k_hist.get(kb - 1)], inc="sV")
        tp_free_v = v_k2
        st_k_hist[kb] = P.dma("sync", kt2_h.ap()[:, kb * 512:(kb + 1) * 512],
                              kstg[:],
                              dep=[("sV", v_k2)], sem="sO")
        st_v_hist[kb] = P.dma("sync", v2_h.ap()[:, kb * 520:(kb + 1) * 520],
                              vstg[:],
                              dep=[("sV", v_ev)], sem="sO")
        for _ in range(7):
            if wsteps:
                wsteps.pop(0)()
    while wsteps:
        wsteps.pop(0)()
    p1_done_o = P.fence("sO")
    p1_v = P.val("sV")
    esA1.close()
    if PHASE_LIMIT <= 3:
        P._w("sync", P.fence("sO") + P.fence("sL") + P.fence("sGD"))
        return _finish(nc, P, es)


    lp_free = [("sV", 0)]
    ln_prev2 = {0: p1_v, 1: p1_v}
    for s_ in range(4):
        for tq in range(4):
            ti = s_ * 4 + tq
            pa = ti % 2
            ld = P.dma("sync", xin[:, pa * 1024:(pa + 1) * 1024],
                       xq_h.ap()[ti * 128:(ti + 1) * 128, :],
                       dep=[("sV", ln_prev2[pa])])
            vt = ln_chain(xin[:, pa * 1024:(pa + 1) * 1024],
                          xn[:, pa * 1024:(pa + 1) * 1024],
                          dep_v=[ld], dep_s=[ld])
            ln_prev2[pa] = vt
            vtr = transpose8(xn[:, pa * 1024:(pa + 1) * 1024],
                             dep_t=[("sV", vt), ("sV", tp_free_v)], dep_v=[],
                             out_ap=xnT[:, pa * 1024:(pa + 1) * 1024],
                             in_ap=tp[:])
            tp_free_v = vtr
            t_q = None
            for ni in range(2):
                for c in range(8):
                    dd = ([("sV", vtr)] + lp_free) if (ni == 0 and c == 0) \
                        else ()
                    t_q = matmul(
                        lp[:, ni * 512:(ni + 1) * 512],
                        xnT[:, pa * 1024 + c * 128:
                            pa * 1024 + (c + 1) * 128],
                        wqq[:, c * 1024 + ni * 512:
                            c * 1024 + (ni + 1) * 512],
                        start=(c == 0), stop=(c == 7), dep=dd,
                        inc="sT" if (ni == 1 and c == 7) else None)
            v_rope = rope_block(0, cosq, sinq, ti, 16, ktm,
                                [("sT", t_q), ("sS", s_tabq)])
            lp_free = [("sV", v_rope)]
            vq = transpose8(ktm, dep_t=[("sV", v_rope), ("sV", tp_free_v)],
                            dep_v=[],
                            out_ap=bass.AP(qT[s_], tq * 128,
                                           [[8 * 512, 128], [512, 8],
                                            [1, 128]]),
                            in_ap=bass.AP(tp, 0,
                                          [[1024, 128], [128, 8], [1, 128]]))
            tp_free_v = vq
    qT_done = P.val("sV")
    esA2.close()
    if PHASE_LIMIT <= 4:
        P._w("sync", P.fence("sO") + P.fence("sL") + P.fence("sGD"))
        P.emit("sync", lambda e: e.wait_ge(P.sem["sV"][0], P.val("sV")))
        return _finish(nc, P, es)
    esAtt = ExitStack()
    sbT = lambda name, shape, dt: esAtt.enter_context(
        nc.sbuf_tensor(name, shape, dt))
    maskb = sbT("maskb", [128, 8 * 512], BF16)
    ctxT = sbT("ctxT", [64, 16 * 512], BF16)
    den = sbT("den", [128, 1024], F32)
    rcp = sbT("rcp", [128, 1024], F32)
    pt = sbT("pt", [128, 2 * 1024], BF16)
    x1t = sbT("x1t", [128, 1024], F32)
    woS = sbT("woS", [64, 16 * 1024], BF16)

    gwo = 0
    arena_guard = ([("sT", P.val("sT")), ("sV", P.val("sV")),
                    ("sS", P.val("sS"))]
                   + P.fence("sO") + P.fence("sL"))
    for hh in range(16):
        P.dma("gpsimd", woS[0:64, hh * 1024:(hh + 1) * 1024],
              wo_h.ap()[hh * 64:(hh + 1) * 64, :],
              dep=arena_guard if hh == 0 else ())
    gwo = P.fence("sGD")

    att = {
        "last_exp": {0: 0, 1: 0},
        "last_pv": {0: 0, 1: 0},
        "cp_free": [("sV", 0), ("sS", 0)],
        "mask_free": 0,
        "evac_extra": [],
        "pend_den": None,
    }


    def attn_slot(stage, s_, kt_sb, v_sb, dep_mask):
        cap = CAPS[s_]
        mld = P.dma("sync", maskb[:],
                    mask_h.ap()[:, s_ * 4096:(s_ + 1) * 4096],
                    dep=dep_mask + [("sV", att["mask_free"])])
        last_mask_v = 0
        for hpl in range(4):
            hp = stage * 4 + hpl

            def emit_pv(kb, dep_pv):
                pa = kb % 2
                matmul(cp[0:65, 0:512],
                       v_sb[:, kb * 520 + (2 * hpl) * 65:
                            kb * 520 + (2 * hpl) * 65 + 65],
                       pt[:, pa * 1024:pa * 1024 + 512],
                       start=(kb == 0), stop=(kb == cap - 1), dep=dep_pv)
                tpv = matmul(cp[0:65, 512:1024],
                             v_sb[:, kb * 520 + (2 * hpl + 1) * 65:
                                  kb * 520 + (2 * hpl + 1) * 65 + 65],
                             pt[:, pa * 1024 + 512:(pa + 1) * 1024],
                             start=(kb == 0), stop=(kb == cap - 1), inc="sT")
                att["last_pv"][pa] = tpv

            pend = None
            for kb in range(cap):
                pa = kb % 2
                depq = [mld, ("sV", qT_done)]
                if att["last_exp"][pa]:
                    depq.append(("sS", att["last_exp"][pa]))
                depq += lp_free
                matmul(lp[:, pa * 1024:pa * 1024 + 512],
                       kt_sb[0:64, kb * 512 + hpl * 128:
                             kb * 512 + (hpl) * 128 + 128],
                       qT[s_][0:64, hp * 512:(hp + 1) * 512],
                       start=True, stop=True, dep=depq)
                tqk = matmul(lp[:, pa * 1024 + 512:pa * 1024 + 1024],
                             kt_sb[64:128, kb * 512 + hpl * 128:
                                   kb * 512 + hpl * 128 + 128],
                             qT[s_][64:128, hp * 512:(hp + 1) * 512],
                             start=True, stop=True, inc="sT")
                deps = [("sT", tqk)]
                if att["last_pv"][pa]:
                    deps.append(("sT", att["last_pv"][pa]))
                sexp = P.emit("scalar", lambda e, pa=pa: e.activation(
                    pt[:, pa * 1024:(pa + 1) * 1024],
                    lp[:, pa * 1024:(pa + 1) * 1024],
                    AF.Exp, bias=cset[:, 5:6], scale=cset[:, 6:7]),
                    dep=deps, inc="sS")
                att["last_exp"][pa] = sexp
                dep_pv = [("sS", sexp)]
                if kb >= cap - 8:
                    moff = (kb - (cap - 8)) * 512
                    pt_pair = bass.AP(pt, pa * 1024,
                                      [[2048, 128], [512, 2], [1, 512]])
                    m_b = bass.AP(maskb, moff,
                                  [[8 * 512, 128], [0, 2], [1, 512]])
                    vm = P.emit("vector", lambda e, o=pt_pair, m=m_b:
                                e.tensor_tensor(o, o, m, ALU.mult),
                                dep=[("sS", sexp)], inc="sV")
                    last_mask_v = vm
                    dep_pv = [("sV", vm)]
                if att["pend_den"] is not None and kb == 1:
                    att["pend_den"]()
                    att["pend_den"] = None
                if kb == 0:
                    dep_pv = dep_pv + att["cp_free"]
                elif kb == 1 and pend is not None:
                    pend = (pend[0], pend[1] + att["cp_free"])
                if pend is not None:
                    emit_pv(*pend)
                pend = (kb, dep_pv)
            emit_pv(*pend)
            tdone = att["last_pv"][(cap - 1) % 2]
            vden = P.emit("vector", lambda e: e.reciprocal(
                den[64:65, 0:1024], cp[64:65, 0:1024]),
                dep=[("sT", tdone)], inc="sV")
            mb1 = matmul(db[:], onesf[64:65, 0:128], den[64:65, 0:512],
                         start=True, stop=True, dep=[("sV", vden)], inc="sT")

            def den_part2(hp=hp):
                vc1 = P.emit("vector", lambda e: e.tensor_copy(
                    rcp[:, 0:512], db[:]), dep=[("sT", mb1)], inc="sV")
                mb2 = matmul(db[:], onesf[64:65, 0:128],
                             den[64:65, 512:1024],
                             start=True, stop=True, dep=[("sV", vc1)],
                             inc="sT")
                vc2 = P.emit("vector", lambda e: e.tensor_copy(
                    rcp[:, 512:1024], db[:]), dep=[("sT", mb2)], inc="sV")
                hA, hB = 2 * hp, 2 * hp + 1
                extra = att["evac_extra"]
                att["evac_extra"] = []
                P.emit("vector", lambda e, hA=hA: e.tensor_tensor(
                    ctxT[0:64, hA * 512:(hA + 1) * 512], cp[0:64, 0:512],
                    rcp[0:64, 0:512], ALU.mult), dep=extra)
                vev = P.emit("vector", lambda e, hB=hB: e.tensor_tensor(
                    ctxT[0:64, hB * 512:(hB + 1) * 512], cp[0:64, 512:1024],
                    rcp[0:64, 512:1024], ALU.mult), dep=[("sV", vc2)],
                    inc="sV")
                att["cp_free"] = [("sV", vev)]
            att["pend_den"] = den_part2
        if att["pend_den"] is not None:
            att["pend_den"]()
            att["pend_den"] = None
        att["mask_free"] = last_mask_v if last_mask_v else att["mask_free"]

    ctxA_store = 0
    dep0 = [("sV", qT_done)]
    for s_ in range(4):
        attn_slot(0, s_, kTc, vc, dep0)
        dep0 = []
        ctxA_store = P.dma("sync", ctxA_h.ap()[:, s_ * 4096:(s_ + 1) * 4096],
                           ctxT[0:64, 0:4096], dep=[("sV", P.val("sV"))],
                           sem="sO")
        att["evac_extra"] = [ctxA_store]

    if PHASE_LIMIT <= 5:
        P._w("sync", P.fence("sO") + P.fence("sL") + P.fence("sGD"))
        P.emit("sync", lambda e: e.wait_ge(P.sem["sV"][0], P.val("sV")))
        P.emit("sync", lambda e: e.wait_ge(P.sem["sS"][0], P.val("sS")))
        P.emit("sync", lambda e: e.wait_ge(P.sem["sT"][0], P.val("sT")))
        return _finish(nc, P, es)
    rk = P.dma("sync", kTc[:], kt2_h.ap()[:],
               dep=[ctxA_store] + p1_done_o +
                    [("sS", att["last_exp"][0]), ("sS", att["last_exp"][1]),
                     ("sT", att["last_pv"][0]), ("sT", att["last_pv"][1])])
    rv = P.dma("sync", vc[:], v2_h.ap()[:])
    if PHASE_LIMIT == 50:
        P._w("sync", P.fence("sO") + P.fence("sL") + P.fence("sGD"))
        return _finish(nc, P, es)

    st_x1_prev = None
    st_xq_prev = None
    out_dep_first = list(gwo)
    t_op_prev = 0
    v_x1 = 0
    for s_ in range(4):
        if 53 <= PHASE_LIMIT <= 56 and s_ >= PHASE_LIMIT - 52:
            break
        attn_slot(1, s_, kTc, vc, [rv, rk])
        if PHASE_LIMIT == 51 or 53 <= PHASE_LIMIT <= 56:
            att["cp_free"] = [("sV", P.val("sV")), ("sS", P.val("sS"))]
            continue
        rc_ld = P.dma("sync", ctxT[0:64, 0:4096],
                      ctxA_h.ap()[:, s_ * 4096:(s_ + 1) * 4096],
                      dep=[("sV", P.val("sV")), ("sT", t_op_prev)])
        if PHASE_LIMIT == 520:
            att["cp_free"] = [("sV", P.val("sV")), ("sS", P.val("sS"))]
            continue
        t_op = 0
        cv2 = 0
        for tq in range(4):
            ti = s_ * 4 + tq
            ldx = P.dma("sync", xin[:, 0:1024],
                        xq_h.ap()[ti * 128:(ti + 1) * 128, :],
                        dep=[("sV", max(ln_prev2.values()) if ti == 0 else v_x1)])
            for ni in range(2):
                for hh in range(16):
                    dd = ([rc_ld, ("sV", cv2)] + out_dep_first
                          + att["cp_free"]) \
                        if (ni == 0 and hh == 0) else ()
                    t_op = matmul(
                        cp[:, ni * 512:(ni + 1) * 512],
                        ctxT[0:64, hh * 512 + tq * 128:
                             hh * 512 + tq * 128 + 128],
                        woS[0:64, hh * 1024 + ni * 512:
                            hh * 1024 + (ni + 1) * 512],
                        start=(hh == 0), stop=(hh == 15), dep=dd,
                        inc="sT" if (ni == 1 and hh == 15) else None)
            out_dep_first = []
            if PHASE_LIMIT == 521:
                vt2 = 0
                continue
            vt2, cv2 = ln_chain(cp[:, 0:1024], lns[:],
                                dep_v=[("sT", t_op)], dep_s=[("sT", t_op)],
                                psum=True)
            if PHASE_LIMIT == 522:
                continue
            P.drain("vector")
            v_x1 = P.emit("vector", lambda e: e.tensor_add(
                x1t[:], xin[:, 0:1024], lns[:]),
                dep=[("sV", vt2), ldx, st_x1_prev],
                inc="sV")
            st_x1_prev = P.dma("sync",
                               x1_h.ap()[ti * 128:(ti + 1) * 128, :],
                               x1t[:], dep=[("sV", v_x1)], sem="sO")
            if PHASE_LIMIT == 52:
                continue
            vt3 = ln_chain(x1t[:], lns[:], dep_v=[("sV", v_x1)],
                           dep_s=[("sV", v_x1)])
            P.emit("vector", lambda e: e.reduce_max(
                stats[:, 9:10], lns[:], axis=AX, apply_absolute_value=True),
                dep=[("sV", vt3)])
            P.drain("vector")
            P.emit("vector", lambda e: e.tensor_scalar(
                stats[:, 10:11], stats[:, 9:10], 1e-5, None, ALU.max))
            P.drain("vector")
            P.emit("vector", lambda e, ti=ti: e.tensor_scalar(
                sxinv[:, ti:ti + 1], stats[:, 10:11], 1.0 / 127.0, None,
                ALU.mult))
            P.emit("vector", lambda e: e.reciprocal(
                stats[:, 11:12], stats[:, 10:11]))
            P.drain("vector")
            vsc = P.emit("vector", lambda e: e.tensor_scalar(
                stats[:, 12:13], stats[:, 11:12], 127.0, None, ALU.mult),
                inc="sV")
            vq = act_quant_chain(lns[:], 1024, stats[:, 12:13],
                                 dep_v=[("sV", vsc), st_xq_prev])
            st_xq_prev = P.dma("sync",
                               xqq_h.ap()[ti * 128:(ti + 1) * 128, :],
                               qb_[:, 0:1024], dep=[("sV", vq)], sem="sO")
        att["cp_free"] = [("sV", P.val("sV")), ("sS", P.val("sS"))]
        att["evac_extra"] = [("sT", t_op)]
        t_op_prev = t_op
    p2_done = P.fence("sO")
    if PHASE_LIMIT <= 6 or PHASE_LIMIT in (50, 51, 52, 53, 54, 55, 56, 520, 521, 522):
        P._w("sync", P.fence("sO") + P.fence("sL") + P.fence("sGD"))
        P.emit("sync", lambda e: e.wait_ge(P.sem["sV"][0], P.val("sV")))
        P.emit("sync", lambda e: e.wait_ge(P.sem["sS"][0], P.val("sS")))
        P.emit("sync", lambda e: e.wait_ge(P.sem["sT"][0], P.val("sT")))
        return _finish(nc, P, es)

    esAtt.close()
    esA.close()
    esB = ExitStack()
    sbB = lambda name, shape, dt: esB.enter_context(
        nc.sbuf_tensor(name, shape, dt))
    wgT = sbB("wgT", [128, 8 * 4096], BF16)
    wuT = sbB("wuT", [128, 8 * 4096], BF16)
    xqT = sbB("xqT", [128, 8 * 512], BF16)
    hbuf = sbB("hbuf", [128, 4096], F32)
    hqb = sbB("hqb", [128, 4096], BF16)
    sgu = sbB("sgu", [128, 4], F32)

    wl_toks = []
    for c in range(8):
        wl_toks.append(P.dmat("sync", wgT[:, c * 4096:(c + 1) * 4096],
                              wgq_h.ap()[:, c * 128:(c + 1) * 128],
                              dep=(p2_done + P.fence("sGD"))
                              if c == 0 else []))
        wl_toks.append(P.dmat("sync", wuT[:, c * 4096:(c + 1) * 4096],
                              wuq_h.ap()[:, c * 128:(c + 1) * 128]))
    wl_done = [max((t for ln2, t in wl_toks if ln2 == ln), default=0)
               for ln in {ln for ln, _ in wl_toks}]
    wl_done = [(ln, max(t for l2, t in wl_toks if l2 == ln))
               for ln in {ln for ln, _ in wl_toks}]

    hq_store = {}
    lpB_free = [("sS", 0)]
    silu_guard = {0: 0, 1: 0}
    for g in range(4):
        xl_toks = []
        for c in range(8):
            xl_toks.append(P.dmat("sync", xqT[:, c * 512:(c + 1) * 512],
                                  xqq_h.ap()[g * 512:(g + 1) * 512,
                                             c * 128:(c + 1) * 128],
                                  dep=wl_done
                                  if (g == 0 and c == 0) else []))
        xl_done = [(ln, max(t for l2, t in xl_toks if l2 == ln))
                   for ln in {ln for ln, _ in xl_toks}]
        for tl in range(4):
            ti = g * 4 + tl
            P.emit("vector", lambda e, ti=ti: e.tensor_tensor(
                sgu[:, 0:1], sxinv[:, ti:ti + 1], wmean3[:, 0:1], ALU.mult),
                dep=xl_done + [("sS", P.val("sS"))])
            P.drain("vector")
            vsg = P.emit("vector", lambda e, ti=ti: e.tensor_tensor(
                sgu[:, 1:2], sxinv[:, ti:ti + 1], wmean3[:, 1:2], ALU.mult),
                inc="sV")
            h_written = {}
            for nci in range(8):
                pa = nci % 2
                dd0 = [("sV", vsg)] + lpB_free
                if nci >= 2:
                    dd0 = [("sV", h_written[nci - 2])]
                for c in range(8):
                    matmul(lp[:, pa * 512:(pa + 1) * 512],
                           xqT[:, c * 512 + tl * 128:c * 512 + tl * 128 + 128],
                           wgT[:, c * 4096 + nci * 512:
                               c * 4096 + (nci + 1) * 512],
                           start=(c == 0), stop=(c == 7),
                           dep=dd0 if c == 0 else ())
                tgu = None
                for c in range(8):
                    tgu = matmul(
                        lp[:, 1024 + pa * 512:1024 + (pa + 1) * 512],
                        xqT[:, c * 512 + tl * 128:c * 512 + tl * 128 + 128],
                        wuT[:, c * 4096 + nci * 512:
                            c * 4096 + (nci + 1) * 512],
                        start=(c == 0), stop=(c == 7),
                        inc="sT" if c == 7 else None)
                ssdep = [("sT", tgu)]
                if silu_guard[pa]:
                    ssdep.append(("sV", silu_guard[pa]))
                ss = P.emit("scalar", lambda e, pa=pa: e.activation(
                    sq[:, pa * 512:(pa + 1) * 512],
                    lp[:, pa * 512:(pa + 1) * 512], AF.Silu,
                    scale=sgu[:, 0:1]), dep=ssdep, inc="sS")
                vh = P.emit("vector", lambda e, nci=nci, pa=pa:
                            e.scalar_tensor_tensor(
                                hbuf[:, nci * 512:(nci + 1) * 512],
                                lp[:, 1024 + pa * 512:1024 + (pa + 1) * 512],
                                sgu[:, 1:2], sq[:, pa * 512:(pa + 1) * 512],
                                ALU.mult, ALU.mult),
                            dep=[("sS", ss)], inc="sV")
                h_written[nci] = vh
                silu_guard[pa] = vh
            P.emit("vector", lambda e: e.reduce_max(
                stats[:, 9:10], hbuf[:], axis=AX, apply_absolute_value=True),
                dep=[("sV", h_written[7])])
            P.drain("vector")
            P.emit("vector", lambda e: e.tensor_scalar(
                stats[:, 10:11], stats[:, 9:10], 1e-5, None, ALU.max))
            P.drain("vector")
            P.emit("vector", lambda e: e.reciprocal(
                stats[:, 11:12], stats[:, 10:11]))
            P.drain("vector")
            vsc = P.emit("vector", lambda e: e.tensor_scalar(
                stats[:, 12:13], stats[:, 11:12], 127.0, None, ALU.mult),
                inc="sV")
            lastq = vsc
            for qh in range(4):
                vq = act_quant_chain(hbuf[:, qh * 1024:(qh + 1) * 1024],
                                     1024, stats[:, 12:13],
                                     dep_v=[("sV", lastq),
                                            hq_store.get(ti - 1)])
                lastq = P.emit("vector", lambda e, qh=qh: e.tensor_copy(
                    hqb[:, qh * 1024:(qh + 1) * 1024], qb_[:, 0:1024]),
                    dep=[("sV", vq)], inc="sV")
            lpB_free = [("sV", lastq)]
            hq_store[ti] = P.dma("sync", hq_h.ap()[ti * 128:(ti + 1) * 128, :],
                                 hqb[:], dep=[("sV", lastq)], sem="sO")
    b1_done = P.fence("sO")
    b1_v = P.val("sV")
    b1_s = P.val("sS")
    if PHASE_LIMIT <= 7:
        P._w("sync", P.fence("sO") + P.fence("sL") + P.fence("sGD"))
        P.emit("sync", lambda e: e.wait_ge(P.sem["sV"][0], P.val("sV")))
        P.emit("sync", lambda e: e.wait_ge(P.sem["sS"][0], P.val("sS")))
        P.emit("sync", lambda e: e.wait_ge(P.sem["sT"][0], P.val("sT")))
        return _finish(nc, P, es)

    esB.close()
    esC = ExitStack()
    sbC = lambda name, shape, dt: esC.enter_context(
        nc.sbuf_tensor(name, shape, dt))
    wdT = sbC("wdT", [128, 32 * 1024], BF16)
    hqT = sbC("hqT", [128, 32 * 512], BF16)
    x1g = sbC("x1g", [128, 4 * 1024], F32)
    outb = sbC("outb", [128, 2 * 1024], F32)

    wdl_toks = []
    for ic in range(32):
        wdl_toks.append(P.dmat("sync", wdT[:, ic * 1024:(ic + 1) * 1024],
                               wdq_h.ap()[:, ic * 128:(ic + 1) * 128],
                               dep=b1_done + [("sV", b1_v), ("sS", b1_s)]
                               if ic == 0 else []))
    wdl_done = [(ln, max(t for l2, t in wdl_toks if l2 == ln))
                for ln in {ln for ln, _ in wdl_toks}]

    out_store = {}
    lpC_free = [("sV", 0)]
    for g in range(4):
        hl_toks = []
        for ic in range(32):
            hl_toks.append(P.dmat("sync", hqT[:, ic * 512:(ic + 1) * 512],
                                  hq_h.ap()[g * 512:(g + 1) * 512,
                                            ic * 128:(ic + 1) * 128],
                                  dep=wdl_done
                                  if (g == 0 and ic == 0) else []))
        hl_done = [(ln, max(t for l2, t in hl_toks if l2 == ln))
                   for ln in {ln for ln, _ in hl_toks}]
        xg_toks = []
        for a_ in range(4):
            xg_toks.append(P.dma(
                "sync", x1g[:, a_ * 1024:(a_ + 1) * 1024],
                x1_h.ap()[g * 512 + a_ * 128:g * 512 + (a_ + 1) * 128, :],
                dep=hl_done if a_ == 0 else []))
        xg = [(ln, max(t for l2, t in xg_toks if l2 == ln))
              for ln in {ln for ln, _ in xg_toks}]
        for tl in range(4):
            ti = g * 4 + tl
            t_dn = None
            for ni in range(2):
                for ic in range(32):
                    dd = (xg + lpC_free) \
                        if (ic == 0 and ni == 0) else ()
                    t_dn = matmul(
                        lp[:, ni * 512:(ni + 1) * 512],
                        hqT[:, ic * 512 + tl * 128:
                            ic * 512 + tl * 128 + 128],
                        wdT[:, ic * 1024 + ni * 512:
                            ic * 1024 + (ni + 1) * 512],
                        start=(ic == 0), stop=(ic == 31), dep=dd,
                        inc="sT" if (ni == 1 and ic == 31) else None)
            vt4, cv4 = ln_chain(lp[:, 0:1024], lns[:],
                                dep_v=[("sT", t_dn)], dep_s=[("sT", t_dn)],
                                psum=True)
            po = tl % 2
            dep_out = [("sV", vt4), out_store.get(ti - 2)]
            vfin = P.emit("vector", lambda e, tl=tl, po=po: e.tensor_add(
                outb[:, po * 1024:(po + 1) * 1024],
                x1g[:, tl * 1024:(tl + 1) * 1024], lns[:]),
                dep=dep_out, inc="sV")
            lpC_free = [("sV", cv4)]
            out_store[ti] = P.dma("sync",
                                  out_h.ap()[ti * 128:(ti + 1) * 128, :],
                                  outb[:, po * 1024:(po + 1) * 1024],
                                  dep=[("sV", vfin)], sem="sO")
    esC.close()

    P._w("sync", P.fence("sO"))

    return _finish(nc, P, es)


def _finish(nc, P, es):
    with nc.Block() as block:
        @block.sync
        def _(sync):
            for fn in P.q["sync"]:
                fn(sync)

        @block.gpsimd
        def _(gpsimd):
            for fn in P.q["gpsimd"]:
                fn(gpsimd)

        @block.vector
        def _(vector):
            for fn in P.q["vector"]:
                fn(vector)

        @block.scalar
        def _(scalar):
            for fn in P.q["scalar"]:
                fn(scalar)

        @block.tensor
        def _(tensor):
            for fn in P.q["tensor"]:
                fn(tensor)

    return nc


def _host_inputs(core, x, wqkv, wo, wg, wu, wd):
    b, par = core // 2, core % 2
    js = JS[par]
    rows = np.concatenate([np.arange(512 * j, 512 * j + 512) for j in js])
    xb = np.ascontiguousarray(x[b])
    xq = np.ascontiguousarray(xb[rows])
    masks = np.zeros((4, 8, 128, 512), dtype=np.float32)
    for s_, j in enumerate(js):
        cap = CAPS[s_]
        q0 = 512 * j
        for kbrel in range(8):
            kb = cap - 8 + kbrel
            kabs = kb * 128 + np.arange(128)[:, None]
            qabs = q0 + np.arange(512)[None, :]
            masks[s_, kbrel] = (kabs <= qabs)
    masks = masks.transpose(2, 0, 1, 3).reshape(128, 4 * 8 * 512)
    masks = np.ascontiguousarray(masks).astype(ml_dtypes.bfloat16)
    pck = (np.arange(NKB)[None, :] * 128
           + np.arange(128)[:, None]).astype(np.float32)
    tq_tiles = np.concatenate([np.arange(4 * j, 4 * j + 4) for j in js])
    pcq = (tq_tiles[None, :] * 128
           + np.arange(128)[:, None]).astype(np.float32)
    fracb = np.ascontiguousarray(np.broadcast_to(
        (2.0 * np.arange(32) / 64.0).astype(np.float32), (128, 32)))
    ident = np.eye(128, dtype=np.float32).astype(ml_dtypes.bfloat16)
    return dict(xkv=xb, xq=xq, wqkv=wqkv, wo=wo, wg=wg, wu=wu, wd=wd,
                masks=masks, poscolk=pck, poscolq=pcq, fracb=fracb,
                ident=ident), rows


def kernel(L0_attn_x, L0_attn_wqkv, L0_attn_wo, L0_ffn_w_gate, L0_ffn_w_up,
           L0_ffn_w_down):
    x = np.ascontiguousarray(np.asarray(L0_attn_x, dtype=np.float32))
    wqkv = np.ascontiguousarray(np.asarray(L0_attn_wqkv, dtype=np.float32))
    wo = np.ascontiguousarray(np.asarray(L0_attn_wo, dtype=np.float32))
    wg = np.ascontiguousarray(np.asarray(L0_ffn_w_gate, dtype=np.float32))
    wu = np.ascontiguousarray(np.asarray(L0_ffn_w_up, dtype=np.float32))
    wd = np.ascontiguousarray(np.asarray(L0_ffn_w_down, dtype=np.float32))

    nc = build_nc()
    in_maps, rowmaps = [], []
    for core in range(8):
        im, rows = _host_inputs(core, x, wqkv, wo, wg, wu, wd)
        in_maps.append(im)
        rowmaps.append(rows)
    res = run_bass_kernel_spmd(nc, in_maps, core_ids=list(range(8)))
    out = np.empty((B, S, DM), dtype=np.float32)
    for core in range(8):
        b = core // 2
        out[b][rowmaps[core]] = np.asarray(res.results[core]["out"],
                                           dtype=np.float32)
    return out


# version 11
# speedup vs baseline: 1.5415x; 1.0428x over previous
import numpy as np
import ml_dtypes
from contextlib import ExitStack

import concourse.bass as bass
import concourse.mybir as mybir
from concourse.bass_utils import run_bass_kernel_spmd

F32 = mybir.dt.float32
BF16 = mybir.dt.bfloat16
I32 = mybir.dt.int32
AF = mybir.ActivationFunctionType
ALU = mybir.AluOpType
AX = mybir.AxisListType.X

B, S, DM, H, DH, INNER = 4, 4096, 1024, 16, 64, 4096
NT = 2048
CAPS = [8, 16, 24, 32]
JS = {0: [0, 3, 4, 7], 1: [1, 2, 5, 6]}
NKB = S // 128
EPS = 1e-5
EXPB = -10.0
RC = 256.0
DEBUG_TAPS = False
PHASE_LIMIT = 99


class Prog:
    ENGS = ("sync", "gpsimd", "vector", "scalar", "tensor")
    LANES = {"sL": 4, "sO": 2, "sGD": 4, "sGO": 2, "sSC": 4}

    def __init__(self, nc, es):
        self.nc = nc
        self.q = {e: [] for e in self.ENGS}
        self.sem = {}
        for name in ("sG", "sV", "sS", "sT"):
            self.sem[name] = [es.enter_context(nc.semaphore(name)), 0]
        for g, n in self.LANES.items():
            for i in range(n):
                nm = "%s%d" % (g, i)
                self.sem[nm] = [es.enter_context(nc.semaphore(nm)), 0]
        self.rr = {g: 0 for g in self.LANES}
        self.waited = {e: {} for e in self.ENGS}
        self.lastdrain = {e: False for e in self.ENGS}

    def _w(self, eng, dep):
        for d in dep:
            if not d:
                continue
            sn, val = d
            if val <= 0:
                continue
            if self.waited[eng].get(sn, 0) >= val:
                continue
            self.waited[eng][sn] = val
            h = self.sem[sn][0]
            self.q[eng].append(lambda e, h=h, v=val: e.wait_ge(h, v))
            self.lastdrain[eng] = False

    AUTODRAIN = ("vector", "gpsimd", "scalar")

    def emit(self, eng, fn, dep=(), inc=None, n=1):
        self._w(eng, dep)
        if eng in self.AUTODRAIN and not self.lastdrain[eng]:
            self.q[eng].append(lambda e: e.drain())
        self.lastdrain[eng] = False
        if inc is None:
            self.q[eng].append(fn)
            return None
        self.sem[inc][1] += n
        tgt = self.sem[inc][1]
        h = self.sem[inc][0]

        def run(e, fn=fn, h=h, n=n):
            fn(e).then_inc(h, n)
        self.q[eng].append(run)
        return tgt

    def drain(self, eng):
        if not self.lastdrain[eng]:
            self.q[eng].append(lambda e: e.drain())
            self.lastdrain[eng] = True

    def val(self, semname):
        return self.sem[semname][1]

    def fence(self, group):
        return [("%s%d" % (group, i), self.sem["%s%d" % (group, i)][1])
                for i in range(self.LANES[group])
                if self.sem["%s%d" % (group, i)][1] > 0]

    def _dma(self, eng, fn, dep, group):
        i = self.rr[group]
        self.rr[group] = (i + 1) % self.LANES[group]
        lane = "%s%d" % (group, i)
        prev = self.sem[lane][1]
        dep = list(dep) + [(lane, prev)]
        tgt = self.emit(eng, fn, dep=dep, inc=lane, n=16)
        return (lane, tgt)

    def dma(self, eng, dst, src, dep=(), sem=None, **kw):
        group = sem or ("sL" if eng == "sync" else "sGD")
        return self._dma(eng, lambda e: e.dma_start(dst, src, **kw),
                         dep, group)

    def dmat(self, eng, dst, src, dep=(), sem="sL"):
        return self._dma(eng, lambda e: e.dma_start_transpose(dst, src),
                         dep, group=sem)


def build_nc():
    nc = bass.Bass("TRN2", target_bir_lowering=False)
    es = ExitStack()

    xkv_h = nc.dram_tensor("xkv", [S, DM], F32, kind="ExternalInput")
    xq_h = nc.dram_tensor("xq", [NT, DM], F32, kind="ExternalInput")
    wqkv_h = nc.dram_tensor("wqkv", [DM, 3 * DM], F32, kind="ExternalInput")
    wo_h = nc.dram_tensor("wo", [DM, DM], F32, kind="ExternalInput")
    wg_h = nc.dram_tensor("wg", [INNER, DM], F32, kind="ExternalInput")
    wu_h = nc.dram_tensor("wu", [INNER, DM], F32, kind="ExternalInput")
    wd_h = nc.dram_tensor("wd", [DM, INNER], F32, kind="ExternalInput")
    mask_h = nc.dram_tensor("masks", [128, 4 * 8 * 512], BF16,
                            kind="ExternalInput")
    pck_h = nc.dram_tensor("poscolk", [128, NKB], F32, kind="ExternalInput")
    pcq_h = nc.dram_tensor("poscolq", [128, 16], F32, kind="ExternalInput")
    frac_h = nc.dram_tensor("fracb", [128, 32], F32, kind="ExternalInput")
    idn_h = nc.dram_tensor("ident", [128, 128], BF16, kind="ExternalInput")
    out_h = nc.dram_tensor("out", [NT, DM], F32, kind="ExternalOutput")

    kd = "ExternalOutput" if DEBUG_TAPS else "Internal"
    wgq_h = nc.dram_tensor("wgq", [INNER, DM], BF16, kind=kd)
    wuq_h = nc.dram_tensor("wuq", [INNER, DM], BF16, kind=kd)
    wdq_h = nc.dram_tensor("wdq", [DM, INNER], BF16, kind=kd)
    kt2_h = nc.dram_tensor("kt2", [128, NKB * 512], BF16, kind=kd)
    v2_h = nc.dram_tensor("v2", [128, NKB * 520], BF16, kind=kd)
    ctxA_h = nc.dram_tensor("ctxA", [64, 4 * 8 * 512], BF16, kind=kd)
    x1_h = nc.dram_tensor("x1", [NT, DM], F32, kind=kd)
    xqq_h = nc.dram_tensor("xqq", [NT, DM], BF16, kind=kd)
    hq_h = nc.dram_tensor("hqs", [NT, INNER], BF16, kind=kd)

    lp = es.enter_context(nc.psum_tensor("lp", [128, 2048], F32))
    cp = es.enter_context(nc.psum_tensor("cp", [128, 1024], F32))
    db = es.enter_context(nc.psum_tensor("db", [128, 512], F32))
    tp = es.enter_context(nc.psum_tensor("tp", [128, 1024], BF16))

    sb = lambda name, shape, dt: es.enter_context(
        nc.sbuf_tensor(name, shape, dt))
    idn = sb("idn", [128, 128], BF16)
    onesf = sb("onesf", [128, 128], F32)
    cset = sb("cset", [128, 12], F32)
    stats = sb("stats", [128, 16], F32)
    acc3 = sb("acc3", [128, 4], F32)
    thr3 = sb("thr3", [128, 4], F32)
    nthr3 = sb("nthr3", [128, 4], F32)
    wmean3 = sb("wmean3", [128, 4], F32)
    sxinv = sb("sxinv", [128, 16], F32)
    sq = sb("sq", [128, 1024], F32)
    ra = sb("ra", [128, 512], F32)
    rb = sb("rb", [128, 512], F32)
    lns = sb("lns", [128, 1024], F32)
    qt_ = sb("qt", [128, 1024], F32)
    qi_ = sb("qi", [128, 1024], I32)
    qd_ = sb("qd", [128, 1024], F32)
    qb_ = sb("qb", [128, 1024], BF16)

    P = Prog(nc, es)

    def matmul(out, lhsT, rhs, start, stop, dep=(), inc=None):
        return P.emit("tensor",
                      lambda e: e.matmul(out, lhsT, rhs, start=start,
                                         stop=stop),
                      dep=dep, inc=inc)

    def ln_chain(src_ap, dst_ap, dep_v, dep_s, psum=False):
        cv = None
        if psum:
            cv = P.emit("vector",
                        lambda e, o=dst_ap, i=src_ap: e.tensor_copy(o, i),
                        dep=dep_v, inc="sV")
            dep_v = [("sV", cv)]
            dep_s = [("sV", cv)]
            src_ap = dst_ap
        P.emit("vector", lambda e: e.reduce_sum(stats[:, 0:1], src_ap,
                                                axis=AX), dep=dep_v)
        s1 = P.emit("scalar",
                    lambda e: e.activation(sq[:], src_ap, AF.Square,
                                           accum_out=stats[:, 1:2]),
                    dep=dep_s, inc="sS")
        P.drain("vector")
        P.emit("vector", lambda e: e.tensor_scalar(
            stats[:, 2:3], stats[:, 0:1], 1.0 / DM, None, ALU.mult))
        P.emit("vector", lambda e: e.tensor_scalar(
            stats[:, 3:4], stats[:, 1:2], 1.0 / DM, EPS, ALU.mult, ALU.add),
            dep=[("sS", s1)])
        P.drain("vector")
        P.emit("vector", lambda e: e.tensor_mul(
            stats[:, 4:5], stats[:, 2:3], stats[:, 2:3]))
        P.drain("vector")
        P.emit("vector", lambda e: e.tensor_sub(
            stats[:, 5:6], stats[:, 3:4], stats[:, 4:5]))
        P.drain("vector")
        v2t = P.emit("vector", lambda e: e.reciprocal(stats[:, 6:7],
                                                      stats[:, 5:6]),
                     inc="sV")
        s2 = P.emit("scalar", lambda e: e.activation(
            stats[:, 7:8], stats[:, 6:7], AF.Sqrt), dep=[("sV", v2t)],
            inc="sS")
        vt = P.emit("vector", lambda e: e.tensor_scalar(
            dst_ap, src_ap, stats[:, 2:3], stats[:, 7:8],
            ALU.subtract, ALU.mult), dep=[("sS", s2)], inc="sV")
        return (vt, cv) if psum else vt

    def transpose8(src_sb, dep_t, dep_v, out_ap, in_ap):
        tt = None
        for j in range(8):
            tt = P.emit("tensor", lambda e, j=j: e.transpose(
                tp[:, j * 128:(j + 1) * 128],
                src_sb[:, j * 128:(j + 1) * 128], idn[:]),
                dep=(dep_t if j == 0 else ()),
                inc="sT" if j == 7 else None)
        vt = P.emit("vector", lambda e: e.tensor_copy(out_ap, in_ap),
                    dep=[("sT", tt)] + list(dep_v), inc="sV")
        return vt

    aq_state = {"g": 0}

    def act_quant_chain(src_ap, n, s_ap, dep_v, ob=None):
        if ob is None:
            ob = qb_
        t1 = P.emit("vector", lambda e: e.tensor_scalar(
            qt_[:, 0:n], src_ap, s_ap, RC, ALU.mult, ALU.add),
            dep=list(dep_v) + [("sG", aq_state["g"])], inc="sV")
        g1 = P.emit("gpsimd", lambda e: e.tensor_copy(qi_[:, 0:n],
                                                      qt_[:, 0:n]),
                    dep=[("sV", t1)], inc="sG")
        aq_state["g"] = g1
        P.emit("vector", lambda e: e.tensor_sub(
            qd_[:, 0:n], qt_[:, 0:n], qi_[:, 0:n]), dep=[("sG", g1)])
        P.emit("vector", lambda e: e.scalar_tensor_tensor(
            qt_[:, 0:n], qd_[:, 0:n], 0.5, qi_[:, 0:n], ALU.is_ge, ALU.add))
        P.drain("vector")
        v3 = P.emit("vector", lambda e, ob=ob: e.tensor_scalar(
            ob[:, 0:n], qt_[:, 0:n], RC, None, ALU.subtract), inc="sV")
        return v3

    def rope_block(src_psum_off, cos_sb, sin_sb, toff, tw, dst_sb, dep_first):
        in1 = bass.AP(lp, src_psum_off, [[2048, 128], [64, 16], [1, 32]])
        in2 = bass.AP(lp, src_psum_off + 32, [[2048, 128], [64, 16], [1, 32]])
        cs = bass.AP(cos_sb, toff * 32, [[tw * 32, 128], [0, 16], [1, 32]])
        sn = bass.AP(sin_sb, toff * 32, [[tw * 32, 128], [0, 16], [1, 32]])
        o1 = bass.AP(dst_sb, 0, [[1024, 128], [64, 16], [1, 32]])
        o2 = bass.AP(dst_sb, 32, [[1024, 128], [64, 16], [1, 32]])
        P.emit("vector", lambda e: e.tensor_tensor(ra[:], in1, cs, ALU.mult),
               dep=dep_first)
        P.emit("vector", lambda e: e.tensor_tensor(rb[:], in2, sn, ALU.mult))
        P.drain("vector")
        P.emit("vector", lambda e: e.tensor_sub(o1, ra[:], rb[:]))
        P.emit("vector", lambda e: e.tensor_tensor(ra[:], in2, cs, ALU.mult))
        P.emit("vector", lambda e: e.tensor_tensor(rb[:], in1, sn, ALU.mult))
        P.drain("vector")
        return P.emit("vector", lambda e: e.tensor_add(o2, ra[:], rb[:]),
                      inc="sV")

    esA = ExitStack()
    sbA = lambda name, shape, dt: esA.enter_context(
        nc.sbuf_tensor(name, shape, dt))
    kTc = sbA("kTc", [128, NKB * 512], BF16)
    vc = sbA("vc", [128, NKB * 520], BF16)
    xin = sbA("xin", [128, 2 * 1024], F32)
    qT = [sbA("qT%d" % s_, [128, 8 * 512], BF16) for s_ in range(4)]
    esA2 = ExitStack()
    sbA2 = lambda name, shape, dt: esA2.enter_context(
        nc.sbuf_tensor(name, shape, dt))
    wqq = sbA2("wqq", [128, 8 * 1024], BF16)
    xn = sbA2("xn", [128, 2 * 1024], BF16)
    xnT = sbA2("xnT", [128, 2 * 1024], BF16)
    ktm = sbA2("ktm", [128, 1024], BF16)
    cosq = sbA2("cosq", [128, 16 * 32], F32)
    sinq = sbA2("sinq", [128, 16 * 32], F32)
    pck = sbA2("pck", [128, NKB], F32)
    pcq = sbA2("pcq", [128, 16], F32)
    frac = sbA2("frac", [128, 32], F32)
    esA1 = ExitStack()
    sbA1 = lambda name, shape, dt: esA1.enter_context(
        nc.sbuf_tensor(name, shape, dt))
    wkv = sbA1("wkv", [128, 8 * 2048], BF16)
    kstg = sbA1("kstg", [128, 512], BF16)
    vstg = sbA1("vstg", [128, 520], BF16)
    wst = sbA1("wst", [128, 1024], F32)
    wqb2 = sbA1("wqb2", [128, 1024], BF16)
    cosk = sbA1("cosk", [128, NKB * 32], BF16)
    sink = sbA1("sink", [128, NKB * 32], BF16)

    lid = P.dma("sync", idn[:], idn_h.ap())
    lpk = P.dma("sync", pck[:], pck_h.ap())
    lpq = P.dma("sync", pcq[:], pcq_h.ap())
    lfr = P.dma("sync", frac[:], frac_h.ap())

    P.emit("gpsimd", lambda e: e.memset(onesf[:], 1.0))
    P.emit("gpsimd", lambda e: e.memset(cset[:, 0:1], -1.0))
    P.emit("gpsimd", lambda e: e.memset(cset[:, 5:6], EXPB))
    P.emit("gpsimd", lambda e: e.memset(cset[:, 6:7], 0.125))
    P.emit("gpsimd", lambda e: e.memset(sq[:, 0:1], 10000.0))
    P.emit("gpsimd", lambda e: e.memset(sq[:, 1:2], 1.0))
    P.emit("gpsimd", lambda e: e.memset(acc3[:], 0.0))
    P.drain("gpsimd")
    g0 = P.emit("gpsimd", lambda e: e.engine_nop(), inc="sG")

    s_ln = P.emit("scalar", lambda e: e.activation(
        cset[:, 1:2], sq[:, 0:1], AF.Ln), dep=[("sG", g0)], inc="sS")
    s_at = P.emit("scalar", lambda e: e.activation(
        cset[:, 7:8], sq[:, 1:2], AF.Arctan), inc="sS")
    P.emit("vector", lambda e: e.tensor_scalar(
        cset[:, 2:3], cset[:, 7:8], 2.0, None, ALU.mult), dep=[("sS", s_at)])
    P.emit("vector", lambda e: e.tensor_scalar(
        cset[:, 3:4], cset[:, 7:8], 8.0, None, ALU.mult))
    P.emit("vector", lambda e: e.tensor_scalar(
        cset[:, 8:9], cset[:, 7:8], 4.0, None, ALU.mult))
    P.emit("vector", lambda e: e.tensor_scalar(
        cset[:, 9:10], cset[:, 7:8], -4.0, None, ALU.mult))
    P.emit("vector", lambda e: e.tensor_scalar(
        cset[:, 10:11], cset[:, 7:8], -8.0, None, ALU.mult))
    v_2p = P.emit("vector", lambda e: e.reciprocal(cset[:, 4:5],
                                                   cset[:, 3:4]), inc="sV")
    v_fl = P.emit("vector", lambda e: e.tensor_scalar(
        sq[:, 32:64], frac[:], cset[:, 1:2], None, ALU.mult),
        dep=[lfr, ("sS", s_ln), ("sV", v_2p)], inc="sV")
    s_it = P.emit("scalar", lambda e: e.activation(
        frac[:], sq[:, 32:64], AF.Exp, scale=cset[:, 0:1]),
        dep=[("sV", v_fl)], inc="sS")

    def build_tables(n_tiles, pc_sb, cos_sb, sin_sb, dep0):
        deps = list(dep0)
        st = 0
        for t in range(n_tiles):
            P.emit("vector", lambda e, t=t: e.tensor_scalar(
                ra[:, 0:32], frac[:], pc_sb[:, t:t + 1], None, ALU.mult),
                dep=deps + [("sS", st)])
            deps = []
            P.drain("vector")
            P.emit("vector", lambda e: e.tensor_scalar(
                ra[:, 32:64], ra[:, 0:32], 1.0, cset[:, 2:3],
                ALU.mult, ALU.add))
            P.drain("vector")
            vm = P.emit("vector", lambda e: e.tensor_scalar(
                rb[:, 0:64], ra[:, 0:64], cset[:, 4:5], None, ALU.mult),
                inc="sV")
            gk = P.emit("gpsimd", lambda e: e.tensor_copy(
                qi_[:, 0:64], rb[:, 0:64]), dep=[("sV", vm)], inc="sG")
            P.emit("vector", lambda e: e.scalar_tensor_tensor(
                rb[:, 0:64], qi_[:, 0:64], cset[:, 10:11], ra[:, 0:64],
                ALU.mult, ALU.add), dep=[("sG", gk)])
            P.emit("vector", lambda e: e.tensor_scalar(
                rb[:, 64:128], rb[:, 0:64], cset[:, 8:9], None, ALU.is_gt))
            P.emit("vector", lambda e: e.scalar_tensor_tensor(
                rb[:, 0:64], rb[:, 64:128], cset[:, 10:11], rb[:, 0:64],
                ALU.mult, ALU.add))
            P.emit("vector", lambda e: e.tensor_scalar(
                rb[:, 64:128], rb[:, 0:64], cset[:, 9:10], None, ALU.is_lt))
            vr = P.emit("vector", lambda e: e.scalar_tensor_tensor(
                rb[:, 0:64], rb[:, 64:128], cset[:, 3:4], rb[:, 0:64],
                ALU.mult, ALU.add), inc="sV")
            P.emit("scalar", lambda e, t=t: e.activation(
                sin_sb[:, t * 32:(t + 1) * 32], rb[:, 0:32], AF.Sin),
                dep=[("sV", vr)])
            st = P.emit("scalar", lambda e, t=t: e.activation(
                cos_sb[:, t * 32:(t + 1) * 32], rb[:, 32:64], AF.Sin),
                inc="sS")
        return st

    s_tabk = build_tables(NKB, pck, cosk, sink,
                          [("sS", s_it), lpk, lpq, lid])
    s_tabq = build_tables(16, pcq, cosq, sinq, [('sS', s_tabk)])


    for c in range(8):
        P.dma("gpsimd", wkv[:, c * 2048:(c + 1) * 2048],
              wqkv_h.ap()[c * 128:(c + 1) * 128, 1024:3072])
        P.dma("gpsimd", wqq[:, c * 1024:(c + 1) * 1024],
              wqkv_h.ap()[c * 128:(c + 1) * 128, 0:1024])
    gw_done = P.fence("sGD")

    P.emit("gpsimd", lambda e: e.memset(
        bass.AP(vc, 64, [[NKB * 520, 128], [520, NKB], [65, 8]]), 1.0))
    P.emit("gpsimd", lambda e: e.memset(
        bass.AP(vstg, 64, [[520, 128], [65, 8]]), 1.0))
    P.drain("gpsimd")
    g_ones = P.emit("gpsimd", lambda e: e.engine_nop(), inc="sG")

    wstate = {"abs": [0, 0], "ga": g_ones, "gah": [g_ones, g_ones],
              "vq": [0, 0], "store": [None, None], "thrb": 0, "i": 0}
    wsteps = []

    def w1_step(wh, wi, ai, fj):
        pa = wstate["i"] % 2
        wstate["i"] += 1
        buf = wst if pa == 0 else lns
        slot = 8 if pa == 0 else 13
        wre = wh.ap().rearrange("(a p) f -> p a f", p=128)
        ld = P.dma("gpsimd", buf[:],
                   wre[:, ai, fj * 1024:(fj + 1) * 1024],
                   dep=[("sS", wstate["abs"][pa])])
        wstate["abs"][pa] = P.emit(
            "scalar", lambda e, buf=buf, slot=slot: e.activation(
                sq[:], buf[:], AF.Abs,
                accum_out=stats[:, slot:slot + 1]),
            dep=[ld, ("sG", wstate["gah"][pa])], inc="sS")
        wstate["gah"][pa] = wstate["ga"] = P.emit(
            "gpsimd", lambda e, wi=wi, slot=slot: e.tensor_tensor(
                acc3[:, wi:wi + 1], acc3[:, wi:wi + 1],
                stats[:, slot:slot + 1], ALU.add),
            dep=[("sS", wstate["abs"][pa]), ("sG", wstate["ga"])],
            inc="sG")

    def w_thr_step():
        mt = matmul(db[0:1, 0:4], onesf[:, 0:1], acc3[:], start=True,
                    stop=True, dep=[("sG", wstate["ga"])], inc="sT")
        P.emit("vector", lambda e: e.tensor_scalar(
            stats[0:1, 8:12], db[0:1, 0:4], 1.0 / (INNER * DM), 1e-5,
            ALU.mult, ALU.max), dep=[("sT", mt)])
        v_thr = P.emit("vector", lambda e: e.tensor_scalar(
            stats[0:1, 12:16], stats[0:1, 8:12], 0.5, None, ALU.mult),
            inc="sV")
        m_b1 = matmul(db[:, 0:4], onesf[0:1, 0:128], stats[0:1, 12:16],
                      start=True, stop=True, dep=[("sV", v_thr)], inc="sT")
        vb1 = P.emit("vector", lambda e: e.tensor_copy(thr3[:], db[:, 0:4]),
                     dep=[("sT", m_b1)], inc="sV")
        m_b2 = matmul(db[:, 0:4], onesf[0:1, 0:128], stats[0:1, 8:12],
                      start=True, stop=True, dep=[("sV", vb1)], inc="sT")
        P.emit("vector", lambda e: e.tensor_copy(wmean3[:], db[:, 0:4]),
               dep=[("sT", m_b2)])
        wstate["thrb"] = P.emit("vector", lambda e: e.tensor_scalar(
            nthr3[:], thr3[:], -1.0, None, ALU.mult), inc="sV")

    def w2_step(wh, wq_out, wi, ai, fj):
        pa = wstate["i"] % 2
        wstate["i"] += 1
        buf = wst if pa == 0 else lns
        mid = qd_ if pa == 0 else qt_
        ob = qb_ if pa == 0 else wqb2
        wre = wh.ap().rearrange("(a p) f -> p a f", p=128)
        wqre = wq_out.ap().rearrange("(a p) f -> p a f", p=128)
        ld = P.dma("gpsimd", buf[:],
                   wre[:, ai, fj * 1024:(fj + 1) * 1024],
                   dep=[("sV", wstate["vq"][pa]),
                        ("sS", wstate["abs"][pa])])
        P.emit("vector", lambda e, wi=wi, buf=buf, mid=mid: e.tensor_scalar(
            mid[:, 0:1024], buf[:],
            nthr3[:, wi:wi + 1], None, ALU.is_le),
            dep=[ld, ("sV", wstate["thrb"])])
        wstate["vq"][pa] = P.emit(
            "vector", lambda e, wi=wi, buf=buf, mid=mid, ob=ob:
            e.scalar_tensor_tensor(
                ob[:, 0:1024],
                buf[:],
                thr3[:, wi:wi + 1], mid[:, 0:1024],
                ALU.is_ge, ALU.subtract),
            dep=[wstate["store"][pa]], inc="sV")
        wstate["store"][pa] = P.dma("sync",
                                    wqre[:, ai, fj * 1024:(fj + 1) * 1024],
                                    ob[:, 0:1024],
                                    dep=[("sV", wstate["vq"][pa])],
                                    sem="sO")

    for wh, wi in ((wg_h, 0), (wu_h, 1), (wd_h, 2)):
        n_a = wh.ap().shape[0] // 128
        n_f = wh.ap().shape[1] // 1024
        for ai in range(n_a):
            for fj in range(n_f):
                wsteps.append(lambda wh=wh, wi=wi, ai=ai, fj=fj:
                              w1_step(wh, wi, ai, fj))
    wsteps.append(w_thr_step)
    for wh, wq_out, wi in ((wg_h, wgq_h, 0), (wu_h, wuq_h, 1),
                           (wd_h, wdq_h, 2)):
        n_a = wh.ap().shape[0] // 128
        n_f = wh.ap().shape[1] // 1024
        for ai in range(n_a):
            for fj in range(n_f):
                wsteps.append(lambda wh=wh, wq_out=wq_out, wi=wi, ai=ai,
                              fj=fj: w2_step(wh, wq_out, wi, ai, fj))
    if PHASE_LIMIT <= 2:
        for st_fn in wsteps:
            st_fn()
        P._w("sync", P.fence("sO") + P.fence("sL") + P.fence("sGD"))
        return _finish(nc, P, es)

    tp_free_v = 0
    lp_free = [("sV", 0)]
    ln_prev = {0: 0, 1: 0}
    st_k_hist = {}
    st_v_hist = {}
    for kb in range(NKB):
        pa = kb % 2
        ld = P.dma("sync", xin[:, pa * 1024:(pa + 1) * 1024],
                   xkv_h.ap()[kb * 128:(kb + 1) * 128, :],
                   dep=[("sV", ln_prev[pa])])
        vt = ln_chain(xin[:, pa * 1024:(pa + 1) * 1024],
                      xn[:, pa * 1024:(pa + 1) * 1024],
                      dep_v=[ld], dep_s=[ld])
        ln_prev[pa] = vt
        vtr = transpose8(xn[:, pa * 1024:(pa + 1) * 1024],
                         dep_t=[("sV", vt), ("sV", tp_free_v)] + gw_done,
                         dep_v=[],
                         out_ap=xnT[:, pa * 1024:(pa + 1) * 1024],
                         in_ap=tp[:])
        tp_free_v = vtr
        t_kv = None
        for ni in range(4):
            for c in range(8):
                dd = ([("sV", vtr)] + lp_free) if (ni == 0 and c == 0) else ()
                t_kv = matmul(
                    lp[:, ni * 512:(ni + 1) * 512],
                    xnT[:, pa * 1024 + c * 128:pa * 1024 + (c + 1) * 128],
                    wkv[:, c * 2048 + ni * 512:c * 2048 + (ni + 1) * 512],
                    start=(c == 0), stop=(c == 7), dep=dd,
                    inc="sT" if (ni == 3 and c == 7) else None)
        v_rope = rope_block(0, cosk, sink, kb, NKB, ktm,
                            [("sT", t_kv), ("sS", s_tabk)])
        v_lo_in = bass.AP(lp, 1024, [[2048, 128], [64, 8], [1, 64]])
        v_hi_in = bass.AP(lp, 1024 + 512, [[2048, 128], [64, 8], [1, 64]])
        v_lo_out = bass.AP(vc, kb * 520, [[NKB * 520, 128], [65, 8], [1, 64]])
        v_hi_out = bass.AP(vstg, 0, [[520, 128], [65, 8], [1, 64]])
        dep_vstg = [st_v_hist.get(kb - 1)]
        P.emit("vector", lambda e, i=v_lo_in, o=v_lo_out: e.tensor_copy(o, i),
               dep=[("sG", g_ones)])
        v_ev = P.emit("vector",
                      lambda e, i=v_hi_in, o=v_hi_out: e.tensor_copy(o, i),
                      dep=dep_vstg, inc="sV")
        lp_free = [("sV", v_ev)]
        vtr2 = transpose8(ktm, dep_t=[("sV", v_rope), ("sV", tp_free_v)],
                          dep_v=[],
                          out_ap=bass.AP(kTc, kb * 512,
                                         [[NKB * 512, 128], [1, 512]]),
                          in_ap=tp[:, 0:512])
        v_k2 = P.emit("vector", lambda e: e.tensor_copy(
            kstg[:], tp[:, 512:1024]),
            dep=[st_k_hist.get(kb - 1)], inc="sV")
        tp_free_v = v_k2
        st_k_hist[kb] = P.dma("gpsimd",
                              kt2_h.ap()[:, kb * 512:(kb + 1) * 512],
                              kstg[:],
                              dep=[("sV", v_k2)], sem="sGO")
        st_v_hist[kb] = P.dma("gpsimd",
                              v2_h.ap()[:, kb * 520:(kb + 1) * 520],
                              vstg[:],
                              dep=[("sV", v_ev)], sem="sGO")
        for _ in range(7):
            if wsteps:
                wsteps.pop(0)()
    while wsteps:
        wsteps.pop(0)()
    p1_done_o = P.fence("sGO")
    p1_v = P.val("sV")
    esA1.close()
    if PHASE_LIMIT <= 3:
        P._w("sync", P.fence("sO") + P.fence("sL") + P.fence("sGD"))
        return _finish(nc, P, es)


    lp_free = [("sV", 0)]
    ln_prev2 = {0: p1_v, 1: p1_v}
    for s_ in range(4):
        for tq in range(4):
            ti = s_ * 4 + tq
            pa = ti % 2
            ld = P.dma("sync", xin[:, pa * 1024:(pa + 1) * 1024],
                       xq_h.ap()[ti * 128:(ti + 1) * 128, :],
                       dep=[("sV", ln_prev2[pa])])
            vt = ln_chain(xin[:, pa * 1024:(pa + 1) * 1024],
                          xn[:, pa * 1024:(pa + 1) * 1024],
                          dep_v=[ld], dep_s=[ld])
            ln_prev2[pa] = vt
            vtr = transpose8(xn[:, pa * 1024:(pa + 1) * 1024],
                             dep_t=[("sV", vt), ("sV", tp_free_v)], dep_v=[],
                             out_ap=xnT[:, pa * 1024:(pa + 1) * 1024],
                             in_ap=tp[:])
            tp_free_v = vtr
            t_q = None
            for ni in range(2):
                for c in range(8):
                    dd = ([("sV", vtr)] + lp_free) if (ni == 0 and c == 0) \
                        else ()
                    t_q = matmul(
                        lp[:, ni * 512:(ni + 1) * 512],
                        xnT[:, pa * 1024 + c * 128:
                            pa * 1024 + (c + 1) * 128],
                        wqq[:, c * 1024 + ni * 512:
                            c * 1024 + (ni + 1) * 512],
                        start=(c == 0), stop=(c == 7), dep=dd,
                        inc="sT" if (ni == 1 and c == 7) else None)
            v_rope = rope_block(0, cosq, sinq, ti, 16, ktm,
                                [("sT", t_q), ("sS", s_tabq)])
            lp_free = [("sV", v_rope)]
            vq = transpose8(ktm, dep_t=[("sV", v_rope), ("sV", tp_free_v)],
                            dep_v=[],
                            out_ap=bass.AP(qT[s_], tq * 128,
                                           [[8 * 512, 128], [512, 8],
                                            [1, 128]]),
                            in_ap=bass.AP(tp, 0,
                                          [[1024, 128], [128, 8], [1, 128]]))
            tp_free_v = vq
    qT_done = P.val("sV")
    esA2.close()
    if PHASE_LIMIT <= 4:
        P._w("sync", P.fence("sO") + P.fence("sL") + P.fence("sGD"))
        P.emit("sync", lambda e: e.wait_ge(P.sem["sV"][0], P.val("sV")))
        return _finish(nc, P, es)
    esAtt = ExitStack()
    sbT = lambda name, shape, dt: esAtt.enter_context(
        nc.sbuf_tensor(name, shape, dt))
    maskb = sbT("maskb", [128, 8 * 512], BF16)
    ctxT = sbT("ctxT", [64, 16 * 512], BF16)
    den = sbT("den", [128, 1024], F32)
    rcp = sbT("rcp", [128, 1024], F32)
    pt = sbT("pt", [128, 2 * 1024], BF16)
    x1t = sbT("x1t", [128, 1024], F32)
    woS = sbT("woS", [64, 16 * 1024], BF16)

    gwo = 0
    arena_guard = ([("sT", P.val("sT")), ("sV", P.val("sV")),
                    ("sS", P.val("sS"))]
                   + P.fence("sO") + P.fence("sL") + P.fence("sGO"))
    for hh in range(16):
        P.dma("gpsimd", woS[0:64, hh * 1024:(hh + 1) * 1024],
              wo_h.ap()[hh * 64:(hh + 1) * 64, :],
              dep=arena_guard if hh == 0 else ())
    gwo = P.fence("sGD")

    att = {
        "last_exp": {0: 0, 1: 0},
        "last_pv": {0: 0, 1: 0},
        "cp_free": [("sV", 0), ("sS", 0)],
        "mask_free": 0,
        "evac_extra": [],
        "pend_den": None,
    }


    def attn_slot(stage, s_, kt_sb, v_sb, dep_mask):
        cap = CAPS[s_]
        mld = P.dma("sync", maskb[:],
                    mask_h.ap()[:, s_ * 4096:(s_ + 1) * 4096],
                    dep=dep_mask + [("sV", att["mask_free"])])
        last_mask_v = 0
        for hpl in range(4):
            hp = stage * 4 + hpl

            def emit_pv(kb, dep_pv):
                pa = kb % 2
                matmul(cp[0:65, 0:512],
                       v_sb[:, kb * 520 + (2 * hpl) * 65:
                            kb * 520 + (2 * hpl) * 65 + 65],
                       pt[:, pa * 1024:pa * 1024 + 512],
                       start=(kb == 0), stop=(kb == cap - 1), dep=dep_pv)
                tpv = matmul(cp[0:65, 512:1024],
                             v_sb[:, kb * 520 + (2 * hpl + 1) * 65:
                                  kb * 520 + (2 * hpl + 1) * 65 + 65],
                             pt[:, pa * 1024 + 512:(pa + 1) * 1024],
                             start=(kb == 0), stop=(kb == cap - 1), inc="sT")
                att["last_pv"][pa] = tpv

            pend = None
            for kb in range(cap):
                pa = kb % 2
                depq = [mld, ("sV", qT_done)]
                if att["last_exp"][pa]:
                    depq.append(("sS", att["last_exp"][pa]))
                depq += lp_free
                matmul(lp[:, pa * 1024:pa * 1024 + 512],
                       kt_sb[0:64, kb * 512 + hpl * 128:
                             kb * 512 + (hpl) * 128 + 128],
                       qT[s_][0:64, hp * 512:(hp + 1) * 512],
                       start=True, stop=True, dep=depq)
                tqk = matmul(lp[:, pa * 1024 + 512:pa * 1024 + 1024],
                             kt_sb[64:128, kb * 512 + hpl * 128:
                                   kb * 512 + hpl * 128 + 128],
                             qT[s_][64:128, hp * 512:(hp + 1) * 512],
                             start=True, stop=True, inc="sT")
                deps = [("sT", tqk)]
                if att["last_pv"][pa]:
                    deps.append(("sT", att["last_pv"][pa]))
                sexp = P.emit("scalar", lambda e, pa=pa: e.activation(
                    pt[:, pa * 1024:(pa + 1) * 1024],
                    lp[:, pa * 1024:(pa + 1) * 1024],
                    AF.Exp, bias=cset[:, 5:6], scale=cset[:, 6:7]),
                    dep=deps, inc="sS")
                att["last_exp"][pa] = sexp
                dep_pv = [("sS", sexp)]
                if kb >= cap - 8:
                    moff = (kb - (cap - 8)) * 512
                    pt_pair = bass.AP(pt, pa * 1024,
                                      [[2048, 128], [512, 2], [1, 512]])
                    m_b = bass.AP(maskb, moff,
                                  [[8 * 512, 128], [0, 2], [1, 512]])
                    vm = P.emit("vector", lambda e, o=pt_pair, m=m_b:
                                e.tensor_tensor(o, o, m, ALU.mult),
                                dep=[("sS", sexp)], inc="sV")
                    last_mask_v = vm
                    dep_pv = [("sV", vm)]
                if att["pend_den"] is not None and kb == 1:
                    att["pend_den"]()
                    att["pend_den"] = None
                if kb == 0:
                    dep_pv = dep_pv + att["cp_free"]
                elif kb == 1 and pend is not None:
                    pend = (pend[0], pend[1] + att["cp_free"])
                if pend is not None:
                    emit_pv(*pend)
                pend = (kb, dep_pv)
            emit_pv(*pend)
            tdone = att["last_pv"][(cap - 1) % 2]
            vden = P.emit("vector", lambda e: e.reciprocal(
                den[64:65, 0:1024], cp[64:65, 0:1024]),
                dep=[("sT", tdone)], inc="sV")
            mb1 = matmul(db[:], onesf[64:65, 0:128], den[64:65, 0:512],
                         start=True, stop=True, dep=[("sV", vden)], inc="sT")

            def den_part2(hp=hp):
                vc1 = P.emit("vector", lambda e: e.tensor_copy(
                    rcp[:, 0:512], db[:]), dep=[("sT", mb1)], inc="sV")
                mb2 = matmul(db[:], onesf[64:65, 0:128],
                             den[64:65, 512:1024],
                             start=True, stop=True, dep=[("sV", vc1)],
                             inc="sT")
                vc2 = P.emit("vector", lambda e: e.tensor_copy(
                    rcp[:, 512:1024], db[:]), dep=[("sT", mb2)], inc="sV")
                hA, hB = 2 * hp, 2 * hp + 1
                extra = att["evac_extra"]
                att["evac_extra"] = []
                P.emit("vector", lambda e, hA=hA: e.tensor_tensor(
                    ctxT[0:64, hA * 512:(hA + 1) * 512], cp[0:64, 0:512],
                    rcp[0:64, 0:512], ALU.mult), dep=extra)
                vev = P.emit("vector", lambda e, hB=hB: e.tensor_tensor(
                    ctxT[0:64, hB * 512:(hB + 1) * 512], cp[0:64, 512:1024],
                    rcp[0:64, 512:1024], ALU.mult), dep=[("sV", vc2)],
                    inc="sV")
                att["cp_free"] = [("sV", vev)]
            att["pend_den"] = den_part2
        if att["pend_den"] is not None:
            att["pend_den"]()
            att["pend_den"] = None
        att["mask_free"] = last_mask_v if last_mask_v else att["mask_free"]

    ctxA_store = 0
    dep0 = [("sV", qT_done)]
    for s_ in range(4):
        attn_slot(0, s_, kTc, vc, dep0)
        dep0 = []
        ctxA_store = P.dma("gpsimd",
                           ctxA_h.ap()[:, s_ * 4096:(s_ + 1) * 4096],
                           ctxT[0:64, 0:4096], dep=[("sV", P.val("sV"))],
                           sem="sGO")
        att["evac_extra"] = [ctxA_store]

    if PHASE_LIMIT <= 5:
        P._w("sync", P.fence("sO") + P.fence("sL") + P.fence("sGD"))
        P.emit("sync", lambda e: e.wait_ge(P.sem["sV"][0], P.val("sV")))
        P.emit("sync", lambda e: e.wait_ge(P.sem["sS"][0], P.val("sS")))
        P.emit("sync", lambda e: e.wait_ge(P.sem["sT"][0], P.val("sT")))
        return _finish(nc, P, es)
    rk = P.dma("sync", kTc[:], kt2_h.ap()[:],
               dep=[ctxA_store] + p1_done_o +
                    [("sS", att["last_exp"][0]), ("sS", att["last_exp"][1]),
                     ("sT", att["last_pv"][0]), ("sT", att["last_pv"][1])])
    rv = P.dma("sync", vc[:], v2_h.ap()[:])
    if PHASE_LIMIT == 50:
        P._w("sync", P.fence("sO") + P.fence("sL") + P.fence("sGD"))
        return _finish(nc, P, es)

    st_x1_prev = None
    st_xq_prev = None
    out_dep_first = list(gwo)
    t_op_prev = 0
    v_x1 = 0
    for s_ in range(4):
        if 53 <= PHASE_LIMIT <= 56 and s_ >= PHASE_LIMIT - 52:
            break
        attn_slot(1, s_, kTc, vc, [rv, rk])
        if PHASE_LIMIT == 51 or 53 <= PHASE_LIMIT <= 56:
            att["cp_free"] = [("sV", P.val("sV")), ("sS", P.val("sS"))]
            continue
        rc_ld = P.dma("sync", ctxT[0:64, 0:4096],
                      ctxA_h.ap()[:, s_ * 4096:(s_ + 1) * 4096],
                      dep=[("sV", P.val("sV")), ("sT", t_op_prev)])
        if PHASE_LIMIT == 520:
            att["cp_free"] = [("sV", P.val("sV")), ("sS", P.val("sS"))]
            continue
        t_op = 0
        cv2 = 0
        for tq in range(4):
            ti = s_ * 4 + tq
            ldx = P.dma("sync", xin[:, 0:1024],
                        xq_h.ap()[ti * 128:(ti + 1) * 128, :],
                        dep=[("sV", max(ln_prev2.values()) if ti == 0 else v_x1)])
            for ni in range(2):
                for hh in range(16):
                    dd = ([rc_ld, ("sV", cv2)] + out_dep_first
                          + att["cp_free"]) \
                        if (ni == 0 and hh == 0) else ()
                    t_op = matmul(
                        cp[:, ni * 512:(ni + 1) * 512],
                        ctxT[0:64, hh * 512 + tq * 128:
                             hh * 512 + tq * 128 + 128],
                        woS[0:64, hh * 1024 + ni * 512:
                            hh * 1024 + (ni + 1) * 512],
                        start=(hh == 0), stop=(hh == 15), dep=dd,
                        inc="sT" if (ni == 1 and hh == 15) else None)
            out_dep_first = []
            if PHASE_LIMIT == 521:
                vt2 = 0
                continue
            vt2, cv2 = ln_chain(cp[:, 0:1024], lns[:],
                                dep_v=[("sT", t_op)], dep_s=[("sT", t_op)],
                                psum=True)
            if PHASE_LIMIT == 522:
                continue
            P.drain("vector")
            v_x1 = P.emit("vector", lambda e: e.tensor_add(
                x1t[:], xin[:, 0:1024], lns[:]),
                dep=[("sV", vt2), ldx, st_x1_prev],
                inc="sV")
            st_x1_prev = P.dma("scalar",
                               x1_h.ap()[ti * 128:(ti + 1) * 128, :],
                               x1t[:], dep=[("sV", v_x1)], sem="sSC")
            if PHASE_LIMIT == 52:
                continue
            vt3 = ln_chain(x1t[:], lns[:], dep_v=[("sV", v_x1)],
                           dep_s=[("sV", v_x1)])
            P.emit("vector", lambda e: e.reduce_max(
                stats[:, 9:10], lns[:], axis=AX, apply_absolute_value=True),
                dep=[("sV", vt3)])
            P.drain("vector")
            P.emit("vector", lambda e: e.tensor_scalar(
                stats[:, 10:11], stats[:, 9:10], 1e-5, None, ALU.max))
            P.drain("vector")
            P.emit("vector", lambda e, ti=ti: e.tensor_scalar(
                sxinv[:, ti:ti + 1], stats[:, 10:11], 1.0 / 127.0, None,
                ALU.mult))
            P.emit("vector", lambda e: e.reciprocal(
                stats[:, 11:12], stats[:, 10:11]))
            P.drain("vector")
            vsc = P.emit("vector", lambda e: e.tensor_scalar(
                stats[:, 12:13], stats[:, 11:12], 127.0, None, ALU.mult),
                inc="sV")
            vq = act_quant_chain(lns[:], 1024, stats[:, 12:13],
                                 dep_v=[("sV", vsc), st_xq_prev])
            st_xq_prev = P.dma("scalar",
                               xqq_h.ap()[ti * 128:(ti + 1) * 128, :],
                               qb_[:, 0:1024], dep=[("sV", vq)], sem="sSC")
        att["cp_free"] = [("sV", P.val("sV")), ("sS", P.val("sS"))]
        att["evac_extra"] = [("sT", t_op)]
        t_op_prev = t_op
    p2_done = P.fence("sSC")
    if PHASE_LIMIT <= 6 or PHASE_LIMIT in (50, 51, 52, 53, 54, 55, 56, 520, 521, 522):
        P._w("sync", P.fence("sO") + P.fence("sL") + P.fence("sGD"))
        P.emit("sync", lambda e: e.wait_ge(P.sem["sV"][0], P.val("sV")))
        P.emit("sync", lambda e: e.wait_ge(P.sem["sS"][0], P.val("sS")))
        P.emit("sync", lambda e: e.wait_ge(P.sem["sT"][0], P.val("sT")))
        return _finish(nc, P, es)

    esAtt.close()
    esA.close()
    esB = ExitStack()
    sbB = lambda name, shape, dt: esB.enter_context(
        nc.sbuf_tensor(name, shape, dt))
    wgT = sbB("wgT", [128, 8 * 4096], BF16)
    wuT = sbB("wuT", [128, 8 * 4096], BF16)
    xqT = sbB("xqT", [128, 2 * 8 * 512], BF16)
    hbuf = sbB("hbuf", [128, 2 * 4096], F32)
    hqb1 = sbB("hqb1", [128, 1024], BF16)
    sgu = sbB("sgu", [128, 4], F32)

    wl_toks = []
    for c in range(8):
        wl_toks.append(P.dmat("sync", wgT[:, c * 4096:(c + 1) * 4096],
                              wgq_h.ap()[:, c * 128:(c + 1) * 128],
                              dep=(p2_done + P.fence("sGD"))
                              if c == 0 else []))
        wl_toks.append(P.dmat("sync", wuT[:, c * 4096:(c + 1) * 4096],
                              wuq_h.ap()[:, c * 128:(c + 1) * 128]))
    wl_done = [max((t for ln2, t in wl_toks if ln2 == ln), default=0)
               for ln in {ln for ln, _ in wl_toks}]
    wl_done = [(ln, max(t for l2, t in wl_toks if l2 == ln))
               for ln in {ln for ln, _ in wl_toks}]

    hq_st_hist = {}
    lpB_free = [("sS", 0)]
    silu_guard = {0: 0, 1: 0}
    h_written = {}
    xq_free = {}
    for g in range(4):
        xo = (g % 2) * 4096
        if g == 0:
            xl_dep0 = wl_done
        elif g >= 2:
            xl_dep0 = [xq_free[g - 2]]
        else:
            xl_dep0 = []
        xl_toks = []
        for c in range(8):
            xl_toks.append(P.dmat("sync",
                                  xqT[:, xo + c * 512:xo + (c + 1) * 512],
                                  xqq_h.ap()[g * 512:(g + 1) * 512,
                                             c * 128:(c + 1) * 128],
                                  dep=xl_dep0 if c == 0 else []))
        xl_done = [(ln, max(t for l2, t in xl_toks if l2 == ln))
                   for ln in {ln for ln, _ in xl_toks}]
        for tl in range(4):
            ti = g * 4 + tl
            hb = (ti % 2) * 4096
            P.emit("vector", lambda e, ti=ti: e.tensor_tensor(
                sgu[:, 0:1], sxinv[:, ti:ti + 1], wmean3[:, 0:1], ALU.mult),
                dep=xl_done + [("sS", P.val("sS"))])
            P.drain("vector")
            vsg = P.emit("vector", lambda e, ti=ti: e.tensor_tensor(
                sgu[:, 1:2], sxinv[:, ti:ti + 1], wmean3[:, 1:2], ALU.mult),
                inc="sV")
            for nci in range(8):
                pa = nci % 2
                dd0 = [("sV", vsg)] + lpB_free
                if nci >= 2:
                    dd0 = [("sV", h_written[nci - 2])]
                for c in range(8):
                    matmul(lp[:, pa * 512:(pa + 1) * 512],
                           xqT[:, xo + c * 512 + tl * 128:
                               xo + c * 512 + tl * 128 + 128],
                           wgT[:, c * 4096 + nci * 512:
                               c * 4096 + (nci + 1) * 512],
                           start=(c == 0), stop=(c == 7),
                           dep=dd0 if c == 0 else ())
                tgu = None
                for c in range(8):
                    tgu = matmul(
                        lp[:, 1024 + pa * 512:1024 + (pa + 1) * 512],
                        xqT[:, xo + c * 512 + tl * 128:
                            xo + c * 512 + tl * 128 + 128],
                        wuT[:, c * 4096 + nci * 512:
                            c * 4096 + (nci + 1) * 512],
                        start=(c == 0), stop=(c == 7),
                        inc="sT" if c == 7 else None)
                ssdep = [("sT", tgu)]
                if silu_guard[pa]:
                    ssdep.append(("sV", silu_guard[pa]))
                ss = P.emit("scalar", lambda e, pa=pa: e.activation(
                    sq[:, pa * 512:(pa + 1) * 512],
                    lp[:, pa * 512:(pa + 1) * 512], AF.Silu,
                    scale=sgu[:, 0:1]), dep=ssdep, inc="sS")
                vh = P.emit("vector", lambda e, nci=nci, pa=pa, hb=hb:
                            e.scalar_tensor_tensor(
                                hbuf[:, hb + nci * 512:hb + (nci + 1) * 512],
                                lp[:, 1024 + pa * 512:1024 + (pa + 1) * 512],
                                sgu[:, 1:2], sq[:, pa * 512:(pa + 1) * 512],
                                ALU.mult, ALU.mult),
                            dep=[("sS", ss)], inc="sV")
                h_written[nci] = vh
                silu_guard[pa] = vh
            P.emit("vector", lambda e, hb=hb: e.reduce_max(
                stats[:, 9:10], hbuf[:, hb:hb + 4096], axis=AX,
                apply_absolute_value=True),
                dep=[("sV", h_written[7])])
            P.drain("vector")
            P.emit("vector", lambda e: e.tensor_scalar(
                stats[:, 10:11], stats[:, 9:10], 1e-5, None, ALU.max))
            P.drain("vector")
            P.emit("vector", lambda e: e.reciprocal(
                stats[:, 11:12], stats[:, 10:11]))
            P.drain("vector")
            vsc = P.emit("vector", lambda e: e.tensor_scalar(
                stats[:, 12:13], stats[:, 11:12], 127.0, None, ALU.mult),
                inc="sV")
            lastq = vsc
            for qh in range(4):
                ob = qb_ if qh % 2 == 0 else hqb1
                st_prev = (hq_st_hist.get((ti, qh - 2))
                           if qh >= 2 else hq_st_hist.get((ti - 1, qh + 2)))
                vq = act_quant_chain(hbuf[:, hb + qh * 1024:
                                          hb + (qh + 1) * 1024],
                                     1024, stats[:, 12:13],
                                     dep_v=[("sV", lastq), st_prev],
                                     ob=ob)
                hq_st_hist[(ti, qh)] = P.dma(
                    "scalar",
                    hq_h.ap()[ti * 128:(ti + 1) * 128,
                              qh * 1024:(qh + 1) * 1024],
                    ob[:, 0:1024], dep=[("sV", vq)], sem="sSC")
                lastq = vq
            lpB_free = [("sV", h_written[6]), ("sV", h_written[7])]
        xq_free[g] = ("sT", tgu)
    b1_done = P.fence("sSC")
    b1_v = P.val("sV")
    b1_s = P.val("sS")
    if PHASE_LIMIT <= 7:
        P._w("sync", P.fence("sO") + P.fence("sL") + P.fence("sGD"))
        P.emit("sync", lambda e: e.wait_ge(P.sem["sV"][0], P.val("sV")))
        P.emit("sync", lambda e: e.wait_ge(P.sem["sS"][0], P.val("sS")))
        P.emit("sync", lambda e: e.wait_ge(P.sem["sT"][0], P.val("sT")))
        return _finish(nc, P, es)

    esB.close()
    esC = ExitStack()
    sbC = lambda name, shape, dt: esC.enter_context(
        nc.sbuf_tensor(name, shape, dt))
    wdT = sbC("wdT", [128, 32 * 1024], BF16)
    hqT = sbC("hqT", [128, 2 * 32 * 512], BF16)
    x1g = sbC("x1g", [128, 2 * 4 * 1024], F32)
    outb = sbC("outb", [128, 2 * 1024], F32)

    wdl_toks = []
    for ic in range(32):
        wdl_toks.append(P.dmat("sync", wdT[:, ic * 1024:(ic + 1) * 1024],
                               wdq_h.ap()[:, ic * 128:(ic + 1) * 128],
                               dep=b1_done + [("sV", b1_v), ("sS", b1_s)]
                               if ic == 0 else []))
    wdl_done = [(ln, max(t for l2, t in wdl_toks if l2 == ln))
                for ln in {ln for ln, _ in wdl_toks}]

    out_store = {}
    lpC_free = [("sV", 0)]
    hq_free = {}
    xg_free = {}
    for g in range(4):
        ho = (g % 2) * 16384
        xgo = (g % 2) * 4096
        if g == 0:
            hl_dep0 = wdl_done
        elif g >= 2:
            hl_dep0 = [hq_free[g - 2]]
        else:
            hl_dep0 = []
        hl_toks = []
        for ic in range(32):
            hl_toks.append(P.dmat("sync",
                                  hqT[:, ho + ic * 512:ho + (ic + 1) * 512],
                                  hq_h.ap()[g * 512:(g + 1) * 512,
                                            ic * 128:(ic + 1) * 128],
                                  dep=hl_dep0 if ic == 0 else []))
        hl_done = [(ln, max(t for l2, t in hl_toks if l2 == ln))
                   for ln in {ln for ln, _ in hl_toks}]
        if g >= 2:
            xg_dep0 = hl_done + [xg_free[g - 2]]
        else:
            xg_dep0 = hl_done
        xg_toks = []
        for a_ in range(4):
            xg_toks.append(P.dma(
                "sync", x1g[:, xgo + a_ * 1024:xgo + (a_ + 1) * 1024],
                x1_h.ap()[g * 512 + a_ * 128:g * 512 + (a_ + 1) * 128, :],
                dep=xg_dep0 if a_ == 0 else []))
        xg = [(ln, max(t for l2, t in xg_toks if l2 == ln))
              for ln in {ln for ln, _ in xg_toks}]
        for tl in range(4):
            ti = g * 4 + tl
            t_dn = None
            for ni in range(2):
                for ic in range(32):
                    dd = (xg + lpC_free) \
                        if (ic == 0 and ni == 0) else ()
                    t_dn = matmul(
                        lp[:, ni * 512:(ni + 1) * 512],
                        hqT[:, ho + ic * 512 + tl * 128:
                            ho + ic * 512 + tl * 128 + 128],
                        wdT[:, ic * 1024 + ni * 512:
                            ic * 1024 + (ni + 1) * 512],
                        start=(ic == 0), stop=(ic == 31), dep=dd,
                        inc="sT" if (ni == 1 and ic == 31) else None)
            vt4, cv4 = ln_chain(lp[:, 0:1024], lns[:],
                                dep_v=[("sT", t_dn)], dep_s=[("sT", t_dn)],
                                psum=True)
            po = tl % 2
            dep_out = [("sV", vt4), out_store.get(ti - 2)]
            vfin = P.emit("vector",
                          lambda e, tl=tl, po=po, xgo=xgo: e.tensor_add(
                              outb[:, po * 1024:(po + 1) * 1024],
                              x1g[:, xgo + tl * 1024:xgo + (tl + 1) * 1024],
                              lns[:]),
                          dep=dep_out, inc="sV")
            lpC_free = [("sV", cv4)]
            out_store[ti] = P.dma("scalar",
                                  out_h.ap()[ti * 128:(ti + 1) * 128, :],
                                  outb[:, po * 1024:(po + 1) * 1024],
                                  dep=[("sV", vfin)], sem="sSC")
        hq_free[g] = ("sT", t_dn)
        xg_free[g] = ("sV", vfin)
    esC.close()

    P._w("sync", P.fence("sSC") + P.fence("sGO") + P.fence("sO"))

    return _finish(nc, P, es)


def _finish(nc, P, es):
    with nc.Block() as block:
        @block.sync
        def _(sync):
            for fn in P.q["sync"]:
                fn(sync)

        @block.gpsimd
        def _(gpsimd):
            for fn in P.q["gpsimd"]:
                fn(gpsimd)

        @block.vector
        def _(vector):
            for fn in P.q["vector"]:
                fn(vector)

        @block.scalar
        def _(scalar):
            for fn in P.q["scalar"]:
                fn(scalar)

        @block.tensor
        def _(tensor):
            for fn in P.q["tensor"]:
                fn(tensor)

    return nc


def _host_inputs(core, x, wqkv, wo, wg, wu, wd):
    b, par = core // 2, core % 2
    js = JS[par]
    rows = np.concatenate([np.arange(512 * j, 512 * j + 512) for j in js])
    xb = np.ascontiguousarray(x[b])
    xq = np.ascontiguousarray(xb[rows])
    masks = np.zeros((4, 8, 128, 512), dtype=np.float32)
    for s_, j in enumerate(js):
        cap = CAPS[s_]
        q0 = 512 * j
        for kbrel in range(8):
            kb = cap - 8 + kbrel
            kabs = kb * 128 + np.arange(128)[:, None]
            qabs = q0 + np.arange(512)[None, :]
            masks[s_, kbrel] = (kabs <= qabs)
    masks = masks.transpose(2, 0, 1, 3).reshape(128, 4 * 8 * 512)
    masks = np.ascontiguousarray(masks).astype(ml_dtypes.bfloat16)
    pck = (np.arange(NKB)[None, :] * 128
           + np.arange(128)[:, None]).astype(np.float32)
    tq_tiles = np.concatenate([np.arange(4 * j, 4 * j + 4) for j in js])
    pcq = (tq_tiles[None, :] * 128
           + np.arange(128)[:, None]).astype(np.float32)
    fracb = np.ascontiguousarray(np.broadcast_to(
        (2.0 * np.arange(32) / 64.0).astype(np.float32), (128, 32)))
    ident = np.eye(128, dtype=np.float32).astype(ml_dtypes.bfloat16)
    return dict(xkv=xb, xq=xq, wqkv=wqkv, wo=wo, wg=wg, wu=wu, wd=wd,
                masks=masks, poscolk=pck, poscolq=pcq, fracb=fracb,
                ident=ident), rows


def kernel(L0_attn_x, L0_attn_wqkv, L0_attn_wo, L0_ffn_w_gate, L0_ffn_w_up,
           L0_ffn_w_down):
    x = np.ascontiguousarray(np.asarray(L0_attn_x, dtype=np.float32))
    wqkv = np.ascontiguousarray(np.asarray(L0_attn_wqkv, dtype=np.float32))
    wo = np.ascontiguousarray(np.asarray(L0_attn_wo, dtype=np.float32))
    wg = np.ascontiguousarray(np.asarray(L0_ffn_w_gate, dtype=np.float32))
    wu = np.ascontiguousarray(np.asarray(L0_ffn_w_up, dtype=np.float32))
    wd = np.ascontiguousarray(np.asarray(L0_ffn_w_down, dtype=np.float32))

    nc = build_nc()
    in_maps, rowmaps = [], []
    for core in range(8):
        im, rows = _host_inputs(core, x, wqkv, wo, wg, wu, wd)
        in_maps.append(im)
        rowmaps.append(rows)
    res = run_bass_kernel_spmd(nc, in_maps, core_ids=list(range(8)))
    out = np.empty((B, S, DM), dtype=np.float32)
    for core in range(8):
        b = core // 2
        out[b][rowmaps[core]] = np.asarray(res.results[core]["out"],
                                           dtype=np.float32)
    return out


# version 12 (final)
# speedup vs baseline: 1.5480x; 1.0042x over previous
import numpy as np
import ml_dtypes
from contextlib import ExitStack

import concourse.bass as bass
import concourse.mybir as mybir
from concourse.bass_utils import run_bass_kernel_spmd

F32 = mybir.dt.float32
BF16 = mybir.dt.bfloat16
I32 = mybir.dt.int32
AF = mybir.ActivationFunctionType
ALU = mybir.AluOpType
AX = mybir.AxisListType.X

B, S, DM, H, DH, INNER = 4, 4096, 1024, 16, 64, 4096
NT = 2048
CAPS = [8, 16, 24, 32]
JS = {0: [0, 3, 4, 7], 1: [1, 2, 5, 6]}
NKB = S // 128
EPS = 1e-5
EXPB = -10.0
RC = 256.0
DEBUG_TAPS = False
PHASE_LIMIT = 99


class Prog:
    ENGS = ("sync", "gpsimd", "vector", "scalar", "tensor")
    LANES = {"sL": 4, "sO": 2, "sGD": 4, "sGO": 2, "sSC": 4}

    def __init__(self, nc, es):
        self.nc = nc
        self.q = {e: [] for e in self.ENGS}
        self.sem = {}
        for name in ("sG", "sV", "sS", "sT"):
            self.sem[name] = [es.enter_context(nc.semaphore(name)), 0]
        for g, n in self.LANES.items():
            for i in range(n):
                nm = "%s%d" % (g, i)
                self.sem[nm] = [es.enter_context(nc.semaphore(nm)), 0]
        self.rr = {g: 0 for g in self.LANES}
        self.waited = {e: {} for e in self.ENGS}
        self.lastdrain = {e: False for e in self.ENGS}

    def _w(self, eng, dep):
        for d in dep:
            if not d:
                continue
            sn, val = d
            if val <= 0:
                continue
            if self.waited[eng].get(sn, 0) >= val:
                continue
            self.waited[eng][sn] = val
            h = self.sem[sn][0]
            self.q[eng].append(lambda e, h=h, v=val: e.wait_ge(h, v))
            self.lastdrain[eng] = False

    AUTODRAIN = ("vector", "gpsimd", "scalar")

    def emit(self, eng, fn, dep=(), inc=None, n=1):
        self._w(eng, dep)
        if eng in self.AUTODRAIN and not self.lastdrain[eng]:
            self.q[eng].append(lambda e: e.drain())
        self.lastdrain[eng] = False
        if inc is None:
            self.q[eng].append(fn)
            return None
        self.sem[inc][1] += n
        tgt = self.sem[inc][1]
        h = self.sem[inc][0]

        def run(e, fn=fn, h=h, n=n):
            fn(e).then_inc(h, n)
        self.q[eng].append(run)
        return tgt

    def drain(self, eng):
        if not self.lastdrain[eng]:
            self.q[eng].append(lambda e: e.drain())
            self.lastdrain[eng] = True

    def val(self, semname):
        return self.sem[semname][1]

    def fence(self, group):
        return [("%s%d" % (group, i), self.sem["%s%d" % (group, i)][1])
                for i in range(self.LANES[group])
                if self.sem["%s%d" % (group, i)][1] > 0]

    def _dma(self, eng, fn, dep, group):
        i = self.rr[group]
        self.rr[group] = (i + 1) % self.LANES[group]
        lane = "%s%d" % (group, i)
        prev = self.sem[lane][1]
        dep = list(dep) + [(lane, prev)]
        tgt = self.emit(eng, fn, dep=dep, inc=lane, n=16)
        return (lane, tgt)

    def dma(self, eng, dst, src, dep=(), sem=None, **kw):
        group = sem or ("sL" if eng == "sync" else "sGD")
        return self._dma(eng, lambda e: e.dma_start(dst, src, **kw),
                         dep, group)

    def dmat(self, eng, dst, src, dep=(), sem="sL"):
        return self._dma(eng, lambda e: e.dma_start_transpose(dst, src),
                         dep, group=sem)


def build_nc():
    nc = bass.Bass("TRN2", target_bir_lowering=False)
    es = ExitStack()

    xkv_h = nc.dram_tensor("xkv", [S, DM], F32, kind="ExternalInput")
    xq_h = nc.dram_tensor("xq", [NT, DM], F32, kind="ExternalInput")
    wqkv_h = nc.dram_tensor("wqkv", [DM, 3 * DM], F32, kind="ExternalInput")
    wo_h = nc.dram_tensor("wo", [DM, DM], F32, kind="ExternalInput")
    wg_h = nc.dram_tensor("wg", [INNER, DM], F32, kind="ExternalInput")
    wu_h = nc.dram_tensor("wu", [INNER, DM], F32, kind="ExternalInput")
    wd_h = nc.dram_tensor("wd", [DM, INNER], F32, kind="ExternalInput")
    mask_h = nc.dram_tensor("masks", [128, 4 * 8 * 512], BF16,
                            kind="ExternalInput")
    pck_h = nc.dram_tensor("poscolk", [128, NKB], F32, kind="ExternalInput")
    pcq_h = nc.dram_tensor("poscolq", [128, 16], F32, kind="ExternalInput")
    frac_h = nc.dram_tensor("fracb", [128, 32], F32, kind="ExternalInput")
    idn_h = nc.dram_tensor("ident", [128, 128], BF16, kind="ExternalInput")
    out_h = nc.dram_tensor("out", [NT, DM], F32, kind="ExternalOutput")

    kd = "ExternalOutput" if DEBUG_TAPS else "Internal"
    wgq_h = nc.dram_tensor("wgq", [INNER, DM], BF16, kind=kd)
    wuq_h = nc.dram_tensor("wuq", [INNER, DM], BF16, kind=kd)
    wdq_h = nc.dram_tensor("wdq", [DM, INNER], BF16, kind=kd)
    kt2_h = nc.dram_tensor("kt2", [128, NKB * 512], BF16, kind=kd)
    v2_h = nc.dram_tensor("v2", [128, NKB * 520], BF16, kind=kd)
    ctxA_h = nc.dram_tensor("ctxA", [64, 4 * 8 * 512], BF16, kind=kd)
    x1_h = nc.dram_tensor("x1", [NT, DM], F32, kind=kd)
    xqq_h = nc.dram_tensor("xqq", [NT, DM], BF16, kind=kd)
    hq_h = nc.dram_tensor("hqs", [NT, INNER], BF16, kind=kd)

    lp = es.enter_context(nc.psum_tensor("lp", [128, 2048], F32))
    cp = es.enter_context(nc.psum_tensor("cp", [128, 1024], F32))
    db = es.enter_context(nc.psum_tensor("db", [128, 512], F32))
    tp = es.enter_context(nc.psum_tensor("tp", [128, 1024], BF16))

    sb = lambda name, shape, dt: es.enter_context(
        nc.sbuf_tensor(name, shape, dt))
    idn = sb("idn", [128, 128], BF16)
    onesf = sb("onesf", [128, 128], F32)
    cset = sb("cset", [128, 12], F32)
    stats = sb("stats", [128, 16], F32)
    stats2 = sb("stats2", [128, 8], F32)
    acc3 = sb("acc3", [128, 4], F32)
    thr3 = sb("thr3", [128, 4], F32)
    nthr3 = sb("nthr3", [128, 4], F32)
    wmean3 = sb("wmean3", [128, 4], F32)
    sxinv = sb("sxinv", [128, 16], F32)
    sq = sb("sq", [128, 1024], F32)
    ra = sb("ra", [128, 512], F32)
    rb = sb("rb", [128, 512], F32)
    lns = sb("lns", [128, 1024], F32)
    qt_ = sb("qt", [128, 1024], F32)
    qi_ = sb("qi", [128, 1024], I32)
    qd_ = sb("qd", [128, 1024], F32)
    qb_ = sb("qb", [128, 1024], BF16)

    P = Prog(nc, es)

    def matmul(out, lhsT, rhs, start, stop, dep=(), inc=None):
        return P.emit("tensor",
                      lambda e: e.matmul(out, lhsT, rhs, start=start,
                                         stop=stop),
                      dep=dep, inc=inc)

    def ln_chain(src_ap, dst_ap, dep_v, dep_s, psum=False, st=None):
        if st is None:
            st = stats
        cv = None
        if psum:
            cv = P.emit("vector",
                        lambda e, o=dst_ap, i=src_ap: e.tensor_copy(o, i),
                        dep=dep_v, inc="sV")
            dep_v = [("sV", cv)]
            dep_s = [("sV", cv)]
            src_ap = dst_ap
        P.emit("vector", lambda e: e.reduce_sum(st[:, 0:1], src_ap,
                                                axis=AX), dep=dep_v)
        s1 = P.emit("scalar",
                    lambda e: e.activation(sq[:], src_ap, AF.Square,
                                           accum_out=st[:, 1:2]),
                    dep=dep_s, inc="sS")
        P.drain("vector")
        P.emit("vector", lambda e: e.tensor_scalar(
            st[:, 2:3], st[:, 0:1], 1.0 / DM, None, ALU.mult))
        P.emit("vector", lambda e: e.tensor_scalar(
            st[:, 3:4], st[:, 1:2], 1.0 / DM, EPS, ALU.mult, ALU.add),
            dep=[("sS", s1)])
        P.drain("vector")
        P.emit("vector", lambda e: e.tensor_mul(
            st[:, 4:5], st[:, 2:3], st[:, 2:3]))
        P.drain("vector")
        P.emit("vector", lambda e: e.tensor_sub(
            st[:, 5:6], st[:, 3:4], st[:, 4:5]))
        P.drain("vector")
        v2t = P.emit("vector", lambda e: e.reciprocal(st[:, 6:7],
                                                      st[:, 5:6]),
                     inc="sV")
        s2 = P.emit("scalar", lambda e: e.activation(
            st[:, 7:8], st[:, 6:7], AF.Sqrt), dep=[("sV", v2t)],
            inc="sS")
        vt = P.emit("vector", lambda e: e.tensor_scalar(
            dst_ap, src_ap, st[:, 2:3], st[:, 7:8],
            ALU.subtract, ALU.mult), dep=[("sS", s2)], inc="sV")
        return (vt, cv) if psum else vt

    def transpose8(src_sb, dep_t, dep_v, out_ap, in_ap):
        tt = None
        for j in range(8):
            tt = P.emit("tensor", lambda e, j=j: e.transpose(
                tp[:, j * 128:(j + 1) * 128],
                src_sb[:, j * 128:(j + 1) * 128], idn[:]),
                dep=(dep_t if j == 0 else ()),
                inc="sT" if j == 7 else None)
        vt = P.emit("vector", lambda e: e.tensor_copy(out_ap, in_ap),
                    dep=[("sT", tt)] + list(dep_v), inc="sV")
        return vt

    aq_state = {"g": 0}

    def act_quant_chain(src_ap, n, s_ap, dep_v, ob=None):
        if ob is None:
            ob = qb_
        t1 = P.emit("vector", lambda e: e.tensor_scalar(
            qt_[:, 0:n], src_ap, s_ap, RC, ALU.mult, ALU.add),
            dep=list(dep_v) + [("sG", aq_state["g"])], inc="sV")
        g1 = P.emit("gpsimd", lambda e: e.tensor_copy(qi_[:, 0:n],
                                                      qt_[:, 0:n]),
                    dep=[("sV", t1)], inc="sG")
        aq_state["g"] = g1
        P.emit("vector", lambda e: e.tensor_sub(
            qd_[:, 0:n], qt_[:, 0:n], qi_[:, 0:n]), dep=[("sG", g1)])
        P.emit("vector", lambda e: e.scalar_tensor_tensor(
            qt_[:, 0:n], qd_[:, 0:n], 0.5, qi_[:, 0:n], ALU.is_ge, ALU.add))
        P.drain("vector")
        v3 = P.emit("vector", lambda e, ob=ob: e.tensor_scalar(
            ob[:, 0:n], qt_[:, 0:n], RC, None, ALU.subtract), inc="sV")
        return v3

    def rope_block(src_psum_off, cos_sb, sin_sb, toff, tw, dst_sb, dep_first):
        in1 = bass.AP(lp, src_psum_off, [[2048, 128], [64, 16], [1, 32]])
        in2 = bass.AP(lp, src_psum_off + 32, [[2048, 128], [64, 16], [1, 32]])
        cs = bass.AP(cos_sb, toff * 32, [[tw * 32, 128], [0, 16], [1, 32]])
        sn = bass.AP(sin_sb, toff * 32, [[tw * 32, 128], [0, 16], [1, 32]])
        o1 = bass.AP(dst_sb, 0, [[1024, 128], [64, 16], [1, 32]])
        o2 = bass.AP(dst_sb, 32, [[1024, 128], [64, 16], [1, 32]])
        P.emit("vector", lambda e: e.tensor_tensor(ra[:], in1, cs, ALU.mult),
               dep=dep_first)
        P.emit("vector", lambda e: e.tensor_tensor(rb[:], in2, sn, ALU.mult))
        P.drain("vector")
        P.emit("vector", lambda e: e.tensor_sub(o1, ra[:], rb[:]))
        P.emit("vector", lambda e: e.tensor_tensor(ra[:], in2, cs, ALU.mult))
        P.emit("vector", lambda e: e.tensor_tensor(rb[:], in1, sn, ALU.mult))
        P.drain("vector")
        return P.emit("vector", lambda e: e.tensor_add(o2, ra[:], rb[:]),
                      inc="sV")

    esA = ExitStack()
    sbA = lambda name, shape, dt: esA.enter_context(
        nc.sbuf_tensor(name, shape, dt))
    kTc = sbA("kTc", [128, NKB * 512], BF16)
    vc = sbA("vc", [128, NKB * 520], BF16)
    xin = sbA("xin", [128, 2 * 1024], F32)
    qT = [sbA("qT%d" % s_, [128, 8 * 512], BF16) for s_ in range(4)]
    esA2 = ExitStack()
    sbA2 = lambda name, shape, dt: esA2.enter_context(
        nc.sbuf_tensor(name, shape, dt))
    wqq = sbA2("wqq", [128, 8 * 1024], BF16)
    xn = sbA2("xn", [128, 2 * 1024], BF16)
    xnT = sbA2("xnT", [128, 2 * 1024], BF16)
    ktm = sbA2("ktm", [128, 1024], BF16)
    cosq = sbA2("cosq", [128, 16 * 32], F32)
    sinq = sbA2("sinq", [128, 16 * 32], F32)
    pck = sbA2("pck", [128, NKB], F32)
    pcq = sbA2("pcq", [128, 16], F32)
    frac = sbA2("frac", [128, 32], F32)
    esA1 = ExitStack()
    sbA1 = lambda name, shape, dt: esA1.enter_context(
        nc.sbuf_tensor(name, shape, dt))
    wkv = sbA1("wkv", [128, 8 * 2048], BF16)
    kstg = sbA1("kstg", [128, 512], BF16)
    vstg = sbA1("vstg", [128, 520], BF16)
    wst = sbA1("wst", [128, 1024], F32)
    wqb2 = sbA1("wqb2", [128, 1024], BF16)
    cosk = sbA1("cosk", [128, NKB * 32], BF16)
    sink = sbA1("sink", [128, NKB * 32], BF16)

    lid = P.dma("sync", idn[:], idn_h.ap())
    lpk = P.dma("sync", pck[:], pck_h.ap())
    lpq = P.dma("sync", pcq[:], pcq_h.ap())
    lfr = P.dma("sync", frac[:], frac_h.ap())

    P.emit("gpsimd", lambda e: e.memset(onesf[:], 1.0))
    P.emit("gpsimd", lambda e: e.memset(cset[:, 0:1], -1.0))
    P.emit("gpsimd", lambda e: e.memset(cset[:, 5:6], EXPB))
    P.emit("gpsimd", lambda e: e.memset(cset[:, 6:7], 0.125))
    P.emit("gpsimd", lambda e: e.memset(sq[:, 0:1], 10000.0))
    P.emit("gpsimd", lambda e: e.memset(sq[:, 1:2], 1.0))
    P.emit("gpsimd", lambda e: e.memset(acc3[:], 0.0))
    P.drain("gpsimd")
    g0 = P.emit("gpsimd", lambda e: e.engine_nop(), inc="sG")

    s_ln = P.emit("scalar", lambda e: e.activation(
        cset[:, 1:2], sq[:, 0:1], AF.Ln), dep=[("sG", g0)], inc="sS")
    s_at = P.emit("scalar", lambda e: e.activation(
        cset[:, 7:8], sq[:, 1:2], AF.Arctan), inc="sS")
    P.emit("vector", lambda e: e.tensor_scalar(
        cset[:, 2:3], cset[:, 7:8], 2.0, None, ALU.mult), dep=[("sS", s_at)])
    P.emit("vector", lambda e: e.tensor_scalar(
        cset[:, 3:4], cset[:, 7:8], 8.0, None, ALU.mult))
    P.emit("vector", lambda e: e.tensor_scalar(
        cset[:, 8:9], cset[:, 7:8], 4.0, None, ALU.mult))
    P.emit("vector", lambda e: e.tensor_scalar(
        cset[:, 9:10], cset[:, 7:8], -4.0, None, ALU.mult))
    P.emit("vector", lambda e: e.tensor_scalar(
        cset[:, 10:11], cset[:, 7:8], -8.0, None, ALU.mult))
    v_2p = P.emit("vector", lambda e: e.reciprocal(cset[:, 4:5],
                                                   cset[:, 3:4]), inc="sV")
    v_fl = P.emit("vector", lambda e: e.tensor_scalar(
        sq[:, 32:64], frac[:], cset[:, 1:2], None, ALU.mult),
        dep=[lfr, ("sS", s_ln), ("sV", v_2p)], inc="sV")
    s_it = P.emit("scalar", lambda e: e.activation(
        frac[:], sq[:, 32:64], AF.Exp, scale=cset[:, 0:1]),
        dep=[("sV", v_fl)], inc="sS")

    def build_tables(n_tiles, pc_sb, cos_sb, sin_sb, dep0):
        deps = list(dep0)
        st = 0
        for t in range(n_tiles):
            P.emit("vector", lambda e, t=t: e.tensor_scalar(
                ra[:, 0:32], frac[:], pc_sb[:, t:t + 1], None, ALU.mult),
                dep=deps + [("sS", st)])
            deps = []
            P.drain("vector")
            P.emit("vector", lambda e: e.tensor_scalar(
                ra[:, 32:64], ra[:, 0:32], 1.0, cset[:, 2:3],
                ALU.mult, ALU.add))
            P.drain("vector")
            vm = P.emit("vector", lambda e: e.tensor_scalar(
                rb[:, 0:64], ra[:, 0:64], cset[:, 4:5], None, ALU.mult),
                inc="sV")
            gk = P.emit("gpsimd", lambda e: e.tensor_copy(
                qi_[:, 0:64], rb[:, 0:64]), dep=[("sV", vm)], inc="sG")
            P.emit("vector", lambda e: e.scalar_tensor_tensor(
                rb[:, 0:64], qi_[:, 0:64], cset[:, 10:11], ra[:, 0:64],
                ALU.mult, ALU.add), dep=[("sG", gk)])
            P.emit("vector", lambda e: e.tensor_scalar(
                rb[:, 64:128], rb[:, 0:64], cset[:, 8:9], None, ALU.is_gt))
            P.emit("vector", lambda e: e.scalar_tensor_tensor(
                rb[:, 0:64], rb[:, 64:128], cset[:, 10:11], rb[:, 0:64],
                ALU.mult, ALU.add))
            P.emit("vector", lambda e: e.tensor_scalar(
                rb[:, 64:128], rb[:, 0:64], cset[:, 9:10], None, ALU.is_lt))
            vr = P.emit("vector", lambda e: e.scalar_tensor_tensor(
                rb[:, 0:64], rb[:, 64:128], cset[:, 3:4], rb[:, 0:64],
                ALU.mult, ALU.add), inc="sV")
            P.emit("scalar", lambda e, t=t: e.activation(
                sin_sb[:, t * 32:(t + 1) * 32], rb[:, 0:32], AF.Sin),
                dep=[("sV", vr)])
            st = P.emit("scalar", lambda e, t=t: e.activation(
                cos_sb[:, t * 32:(t + 1) * 32], rb[:, 32:64], AF.Sin),
                inc="sS")
        return st

    s_tabk = build_tables(NKB, pck, cosk, sink,
                          [("sS", s_it), lpk, lpq, lid])
    s_tabq = build_tables(16, pcq, cosq, sinq, [('sS', s_tabk)])


    for c in range(8):
        P.dma("gpsimd", wkv[:, c * 2048:(c + 1) * 2048],
              wqkv_h.ap()[c * 128:(c + 1) * 128, 1024:3072])
        P.dma("gpsimd", wqq[:, c * 1024:(c + 1) * 1024],
              wqkv_h.ap()[c * 128:(c + 1) * 128, 0:1024])
    gw_done = P.fence("sGD")

    P.emit("gpsimd", lambda e: e.memset(
        bass.AP(vc, 64, [[NKB * 520, 128], [520, NKB], [65, 8]]), 1.0))
    P.emit("gpsimd", lambda e: e.memset(
        bass.AP(vstg, 64, [[520, 128], [65, 8]]), 1.0))
    P.drain("gpsimd")
    g_ones = P.emit("gpsimd", lambda e: e.engine_nop(), inc="sG")

    wstate = {"abs": [0, 0], "ga": g_ones, "gah": [g_ones, g_ones],
              "vq": [0, 0], "store": [None, None], "thrb": 0, "i": 0}
    wsteps = []

    def w1_step(wh, wi, ai, fj):
        pa = wstate["i"] % 2
        wstate["i"] += 1
        buf = wst if pa == 0 else lns
        slot = 8 if pa == 0 else 13
        wre = wh.ap().rearrange("(a p) f -> p a f", p=128)
        ld = P.dma("gpsimd", buf[:],
                   wre[:, ai, fj * 1024:(fj + 1) * 1024],
                   dep=[("sS", wstate["abs"][pa])])
        wstate["abs"][pa] = P.emit(
            "scalar", lambda e, buf=buf, slot=slot: e.activation(
                sq[:], buf[:], AF.Abs,
                accum_out=stats[:, slot:slot + 1]),
            dep=[ld, ("sG", wstate["gah"][pa])], inc="sS")
        wstate["gah"][pa] = wstate["ga"] = P.emit(
            "gpsimd", lambda e, wi=wi, slot=slot: e.tensor_tensor(
                acc3[:, wi:wi + 1], acc3[:, wi:wi + 1],
                stats[:, slot:slot + 1], ALU.add),
            dep=[("sS", wstate["abs"][pa]), ("sG", wstate["ga"])],
            inc="sG")

    def w_thr_step():
        mt = matmul(db[0:1, 0:4], onesf[:, 0:1], acc3[:], start=True,
                    stop=True, dep=[("sG", wstate["ga"])], inc="sT")
        P.emit("vector", lambda e: e.tensor_scalar(
            stats[0:1, 8:12], db[0:1, 0:4], 1.0 / (INNER * DM), 1e-5,
            ALU.mult, ALU.max), dep=[("sT", mt)])
        v_thr = P.emit("vector", lambda e: e.tensor_scalar(
            stats[0:1, 12:16], stats[0:1, 8:12], 0.5, None, ALU.mult),
            inc="sV")
        m_b1 = matmul(db[:, 0:4], onesf[0:1, 0:128], stats[0:1, 12:16],
                      start=True, stop=True, dep=[("sV", v_thr)], inc="sT")
        vb1 = P.emit("vector", lambda e: e.tensor_copy(thr3[:], db[:, 0:4]),
                     dep=[("sT", m_b1)], inc="sV")
        m_b2 = matmul(db[:, 0:4], onesf[0:1, 0:128], stats[0:1, 8:12],
                      start=True, stop=True, dep=[("sV", vb1)], inc="sT")
        P.emit("vector", lambda e: e.tensor_copy(wmean3[:], db[:, 0:4]),
               dep=[("sT", m_b2)])
        wstate["thrb"] = P.emit("vector", lambda e: e.tensor_scalar(
            nthr3[:], thr3[:], -1.0, None, ALU.mult), inc="sV")

    def w2_step(wh, wq_out, wi, ai, fj):
        pa = wstate["i"] % 2
        wstate["i"] += 1
        buf = wst if pa == 0 else lns
        mid = qd_ if pa == 0 else qt_
        ob = qb_ if pa == 0 else wqb2
        wre = wh.ap().rearrange("(a p) f -> p a f", p=128)
        wqre = wq_out.ap().rearrange("(a p) f -> p a f", p=128)
        ld = P.dma("gpsimd", buf[:],
                   wre[:, ai, fj * 1024:(fj + 1) * 1024],
                   dep=[("sV", wstate["vq"][pa]),
                        ("sS", wstate["abs"][pa])])
        P.emit("vector", lambda e, wi=wi, buf=buf, mid=mid: e.tensor_scalar(
            mid[:, 0:1024], buf[:],
            nthr3[:, wi:wi + 1], None, ALU.is_le),
            dep=[ld, ("sV", wstate["thrb"])])
        wstate["vq"][pa] = P.emit(
            "vector", lambda e, wi=wi, buf=buf, mid=mid, ob=ob:
            e.scalar_tensor_tensor(
                ob[:, 0:1024],
                buf[:],
                thr3[:, wi:wi + 1], mid[:, 0:1024],
                ALU.is_ge, ALU.subtract),
            dep=[wstate["store"][pa]], inc="sV")
        wstate["store"][pa] = P.dma("sync",
                                    wqre[:, ai, fj * 1024:(fj + 1) * 1024],
                                    ob[:, 0:1024],
                                    dep=[("sV", wstate["vq"][pa])],
                                    sem="sO")

    for wh, wi in ((wg_h, 0), (wu_h, 1), (wd_h, 2)):
        n_a = wh.ap().shape[0] // 128
        n_f = wh.ap().shape[1] // 1024
        for ai in range(n_a):
            for fj in range(n_f):
                wsteps.append(lambda wh=wh, wi=wi, ai=ai, fj=fj:
                              w1_step(wh, wi, ai, fj))
    wsteps.append(w_thr_step)
    for wh, wq_out, wi in ((wg_h, wgq_h, 0), (wu_h, wuq_h, 1),
                           (wd_h, wdq_h, 2)):
        n_a = wh.ap().shape[0] // 128
        n_f = wh.ap().shape[1] // 1024
        for ai in range(n_a):
            for fj in range(n_f):
                wsteps.append(lambda wh=wh, wq_out=wq_out, wi=wi, ai=ai,
                              fj=fj: w2_step(wh, wq_out, wi, ai, fj))
    if PHASE_LIMIT <= 2:
        for st_fn in wsteps:
            st_fn()
        P._w("sync", P.fence("sO") + P.fence("sL") + P.fence("sGD"))
        return _finish(nc, P, es)

    ln_prev = {0: 0, 1: 0}
    st_k_hist = {}
    st_v_hist = {}
    ln_vt = {}
    xn_guard = {0: 0, 1: 0}
    p1s = {"tp": 0, "lp": [("sV", 0)]}

    def p1_rest(kb):
        pa = kb % 2
        vtr = transpose8(xn[:, pa * 1024:(pa + 1) * 1024],
                         dep_t=[("sV", ln_vt[kb]), ("sV", p1s["tp"])]
                         + gw_done,
                         dep_v=[],
                         out_ap=xnT[:, pa * 1024:(pa + 1) * 1024],
                         in_ap=tp[:])
        xn_guard[pa] = vtr
        t_kv = None
        for ni in range(4):
            for c in range(8):
                dd = ([("sV", vtr)] + p1s["lp"])                     if (ni == 0 and c == 0) else ()
                t_kv = matmul(
                    lp[:, ni * 512:(ni + 1) * 512],
                    xnT[:, pa * 1024 + c * 128:pa * 1024 + (c + 1) * 128],
                    wkv[:, c * 2048 + ni * 512:c * 2048 + (ni + 1) * 512],
                    start=(c == 0), stop=(c == 7), dep=dd,
                    inc="sT" if (ni == 3 and c == 7) else None)
        v_rope = rope_block(0, cosk, sink, kb, NKB, ktm,
                            [("sT", t_kv), ("sS", s_tabk)])
        v_lo_in = bass.AP(lp, 1024, [[2048, 128], [64, 8], [1, 64]])
        v_hi_in = bass.AP(lp, 1024 + 512, [[2048, 128], [64, 8], [1, 64]])
        v_lo_out = bass.AP(vc, kb * 520, [[NKB * 520, 128], [65, 8],
                                          [1, 64]])
        v_hi_out = bass.AP(vstg, 0, [[520, 128], [65, 8], [1, 64]])
        dep_vstg = [st_v_hist.get(kb - 1)]
        P.emit("vector", lambda e, i=v_lo_in, o=v_lo_out: e.tensor_copy(o, i),
               dep=[("sG", g_ones)])
        v_ev = P.emit("vector",
                      lambda e, i=v_hi_in, o=v_hi_out: e.tensor_copy(o, i),
                      dep=dep_vstg, inc="sV")
        p1s["lp"] = [("sV", v_ev)]
        vtr2 = transpose8(ktm, dep_t=[("sV", v_rope), ("sV", vtr)],
                          dep_v=[],
                          out_ap=bass.AP(kTc, kb * 512,
                                         [[NKB * 512, 128], [1, 512]]),
                          in_ap=tp[:, 0:512])
        v_k2 = P.emit("vector", lambda e: e.tensor_copy(
            kstg[:], tp[:, 512:1024]),
            dep=[st_k_hist.get(kb - 1)], inc="sV")
        p1s["tp"] = v_k2
        st_k_hist[kb] = P.dma("gpsimd",
                              kt2_h.ap()[:, kb * 512:(kb + 1) * 512],
                              kstg[:],
                              dep=[("sV", v_k2)], sem="sGO")
        st_v_hist[kb] = P.dma("gpsimd",
                              v2_h.ap()[:, kb * 520:(kb + 1) * 520],
                              vstg[:],
                              dep=[("sV", v_ev)], sem="sGO")
        for _ in range(7):
            if wsteps:
                wsteps.pop(0)()

    for kb in range(NKB):
        pa = kb % 2
        ld = P.dma("sync", xin[:, pa * 1024:(pa + 1) * 1024],
                   xkv_h.ap()[kb * 128:(kb + 1) * 128, :],
                   dep=[("sV", ln_prev[pa]), ("sV", xn_guard[pa])])
        vt = ln_chain(xin[:, pa * 1024:(pa + 1) * 1024],
                      xn[:, pa * 1024:(pa + 1) * 1024],
                      dep_v=[ld], dep_s=[ld],
                      st=(stats if pa == 0 else stats2))
        ln_prev[pa] = vt
        ln_vt[kb] = vt
        if kb > 0:
            p1_rest(kb - 1)
    p1_rest(NKB - 1)
    while wsteps:
        wsteps.pop(0)()
    tp_free_v = p1s["tp"]
    lp_free = p1s["lp"]
    p1_done_o = P.fence("sGO")
    p1_v = P.val("sV")
    esA1.close()
    if PHASE_LIMIT <= 3:
        P._w("sync", P.fence("sO") + P.fence("sL") + P.fence("sGD"))
        return _finish(nc, P, es)


    lp_free = [("sV", 0)]
    ln_prev2 = {0: p1_v, 1: p1_v}
    for s_ in range(4):
        for tq in range(4):
            ti = s_ * 4 + tq
            pa = ti % 2
            ld = P.dma("sync", xin[:, pa * 1024:(pa + 1) * 1024],
                       xq_h.ap()[ti * 128:(ti + 1) * 128, :],
                       dep=[("sV", ln_prev2[pa])])
            vt = ln_chain(xin[:, pa * 1024:(pa + 1) * 1024],
                          xn[:, pa * 1024:(pa + 1) * 1024],
                          dep_v=[ld], dep_s=[ld])
            ln_prev2[pa] = vt
            vtr = transpose8(xn[:, pa * 1024:(pa + 1) * 1024],
                             dep_t=[("sV", vt), ("sV", tp_free_v)], dep_v=[],
                             out_ap=xnT[:, pa * 1024:(pa + 1) * 1024],
                             in_ap=tp[:])
            tp_free_v = vtr
            t_q = None
            for ni in range(2):
                for c in range(8):
                    dd = ([("sV", vtr)] + lp_free) if (ni == 0 and c == 0) \
                        else ()
                    t_q = matmul(
                        lp[:, ni * 512:(ni + 1) * 512],
                        xnT[:, pa * 1024 + c * 128:
                            pa * 1024 + (c + 1) * 128],
                        wqq[:, c * 1024 + ni * 512:
                            c * 1024 + (ni + 1) * 512],
                        start=(c == 0), stop=(c == 7), dep=dd,
                        inc="sT" if (ni == 1 and c == 7) else None)
            v_rope = rope_block(0, cosq, sinq, ti, 16, ktm,
                                [("sT", t_q), ("sS", s_tabq)])
            lp_free = [("sV", v_rope)]
            vq = transpose8(ktm, dep_t=[("sV", v_rope), ("sV", tp_free_v)],
                            dep_v=[],
                            out_ap=bass.AP(qT[s_], tq * 128,
                                           [[8 * 512, 128], [512, 8],
                                            [1, 128]]),
                            in_ap=bass.AP(tp, 0,
                                          [[1024, 128], [128, 8], [1, 128]]))
            tp_free_v = vq
    qT_done = P.val("sV")
    esA2.close()
    if PHASE_LIMIT <= 4:
        P._w("sync", P.fence("sO") + P.fence("sL") + P.fence("sGD"))
        P.emit("sync", lambda e: e.wait_ge(P.sem["sV"][0], P.val("sV")))
        return _finish(nc, P, es)
    esAtt = ExitStack()
    sbT = lambda name, shape, dt: esAtt.enter_context(
        nc.sbuf_tensor(name, shape, dt))
    maskb = sbT("maskb", [128, 8 * 512], BF16)
    ctxT = sbT("ctxT", [64, 16 * 512], BF16)
    den = sbT("den", [128, 1024], F32)
    rcp = sbT("rcp", [128, 1024], F32)
    pt = sbT("pt", [128, 2 * 1024], BF16)
    x1t = sbT("x1t", [128, 1024], F32)
    woS = sbT("woS", [64, 16 * 1024], BF16)

    gwo = 0
    arena_guard = ([("sT", P.val("sT")), ("sV", P.val("sV")),
                    ("sS", P.val("sS"))]
                   + P.fence("sO") + P.fence("sL") + P.fence("sGO"))
    for hh in range(16):
        P.dma("gpsimd", woS[0:64, hh * 1024:(hh + 1) * 1024],
              wo_h.ap()[hh * 64:(hh + 1) * 64, :],
              dep=arena_guard if hh == 0 else ())
    gwo = P.fence("sGD")

    att = {
        "last_exp": {0: 0, 1: 0},
        "last_pv": {0: 0, 1: 0},
        "cp_free": [("sV", 0), ("sS", 0)],
        "mask_free": 0,
        "evac_extra": [],
        "pend_den": None,
    }


    def attn_slot(stage, s_, kt_sb, v_sb, dep_mask):
        cap = CAPS[s_]
        mld = P.dma("sync", maskb[:],
                    mask_h.ap()[:, s_ * 4096:(s_ + 1) * 4096],
                    dep=dep_mask + [("sV", att["mask_free"])])
        last_mask_v = 0
        for hpl in range(4):
            hp = stage * 4 + hpl

            def emit_pv(kb, dep_pv):
                pa = kb % 2
                matmul(cp[0:65, 0:512],
                       v_sb[:, kb * 520 + (2 * hpl) * 65:
                            kb * 520 + (2 * hpl) * 65 + 65],
                       pt[:, pa * 1024:pa * 1024 + 512],
                       start=(kb == 0), stop=(kb == cap - 1), dep=dep_pv)
                tpv = matmul(cp[0:65, 512:1024],
                             v_sb[:, kb * 520 + (2 * hpl + 1) * 65:
                                  kb * 520 + (2 * hpl + 1) * 65 + 65],
                             pt[:, pa * 1024 + 512:(pa + 1) * 1024],
                             start=(kb == 0), stop=(kb == cap - 1), inc="sT")
                att["last_pv"][pa] = tpv

            pend = None
            for kb in range(cap):
                pa = kb % 2
                depq = [mld, ("sV", qT_done)]
                if att["last_exp"][pa]:
                    depq.append(("sS", att["last_exp"][pa]))
                depq += lp_free
                matmul(lp[:, pa * 1024:pa * 1024 + 512],
                       kt_sb[0:64, kb * 512 + hpl * 128:
                             kb * 512 + (hpl) * 128 + 128],
                       qT[s_][0:64, hp * 512:(hp + 1) * 512],
                       start=True, stop=True, dep=depq)
                tqk = matmul(lp[:, pa * 1024 + 512:pa * 1024 + 1024],
                             kt_sb[64:128, kb * 512 + hpl * 128:
                                   kb * 512 + hpl * 128 + 128],
                             qT[s_][64:128, hp * 512:(hp + 1) * 512],
                             start=True, stop=True, inc="sT")
                deps = [("sT", tqk)]
                if att["last_pv"][pa]:
                    deps.append(("sT", att["last_pv"][pa]))
                sexp = P.emit("scalar", lambda e, pa=pa: e.activation(
                    pt[:, pa * 1024:(pa + 1) * 1024],
                    lp[:, pa * 1024:(pa + 1) * 1024],
                    AF.Exp, bias=cset[:, 5:6], scale=cset[:, 6:7]),
                    dep=deps, inc="sS")
                att["last_exp"][pa] = sexp
                dep_pv = [("sS", sexp)]
                if kb >= cap - 8:
                    moff = (kb - (cap - 8)) * 512
                    pt_pair = bass.AP(pt, pa * 1024,
                                      [[2048, 128], [512, 2], [1, 512]])
                    m_b = bass.AP(maskb, moff,
                                  [[8 * 512, 128], [0, 2], [1, 512]])
                    vm = P.emit("vector", lambda e, o=pt_pair, m=m_b:
                                e.tensor_tensor(o, o, m, ALU.mult),
                                dep=[("sS", sexp)], inc="sV")
                    last_mask_v = vm
                    dep_pv = [("sV", vm)]
                if att["pend_den"] is not None and kb == 1:
                    att["pend_den"]()
                    att["pend_den"] = None
                if kb == 0:
                    dep_pv = dep_pv + att["cp_free"]
                elif kb == 1 and pend is not None:
                    pend = (pend[0], pend[1] + att["cp_free"])
                if pend is not None:
                    emit_pv(*pend)
                pend = (kb, dep_pv)
            emit_pv(*pend)
            tdone = att["last_pv"][(cap - 1) % 2]
            vden = P.emit("vector", lambda e: e.reciprocal(
                den[64:65, 0:1024], cp[64:65, 0:1024]),
                dep=[("sT", tdone)], inc="sV")
            mb1 = matmul(db[:], onesf[64:65, 0:128], den[64:65, 0:512],
                         start=True, stop=True, dep=[("sV", vden)], inc="sT")

            def den_part2(hp=hp):
                vc1 = P.emit("vector", lambda e: e.tensor_copy(
                    rcp[:, 0:512], db[:]), dep=[("sT", mb1)], inc="sV")
                mb2 = matmul(db[:], onesf[64:65, 0:128],
                             den[64:65, 512:1024],
                             start=True, stop=True, dep=[("sV", vc1)],
                             inc="sT")
                vc2 = P.emit("vector", lambda e: e.tensor_copy(
                    rcp[:, 512:1024], db[:]), dep=[("sT", mb2)], inc="sV")
                hA, hB = 2 * hp, 2 * hp + 1
                extra = att["evac_extra"]
                att["evac_extra"] = []
                P.emit("vector", lambda e, hA=hA: e.tensor_tensor(
                    ctxT[0:64, hA * 512:(hA + 1) * 512], cp[0:64, 0:512],
                    rcp[0:64, 0:512], ALU.mult), dep=extra)
                vev = P.emit("vector", lambda e, hB=hB: e.tensor_tensor(
                    ctxT[0:64, hB * 512:(hB + 1) * 512], cp[0:64, 512:1024],
                    rcp[0:64, 512:1024], ALU.mult), dep=[("sV", vc2)],
                    inc="sV")
                att["cp_free"] = [("sV", vev)]
            att["pend_den"] = den_part2
        if att["pend_den"] is not None:
            att["pend_den"]()
            att["pend_den"] = None
        att["mask_free"] = last_mask_v if last_mask_v else att["mask_free"]

    ctxA_store = 0
    dep0 = [("sV", qT_done)]
    for s_ in range(4):
        attn_slot(0, s_, kTc, vc, dep0)
        dep0 = []
        ctxA_store = P.dma("gpsimd",
                           ctxA_h.ap()[:, s_ * 4096:(s_ + 1) * 4096],
                           ctxT[0:64, 0:4096], dep=[("sV", P.val("sV"))],
                           sem="sGO")
        att["evac_extra"] = [ctxA_store]

    if PHASE_LIMIT <= 5:
        P._w("sync", P.fence("sO") + P.fence("sL") + P.fence("sGD"))
        P.emit("sync", lambda e: e.wait_ge(P.sem["sV"][0], P.val("sV")))
        P.emit("sync", lambda e: e.wait_ge(P.sem["sS"][0], P.val("sS")))
        P.emit("sync", lambda e: e.wait_ge(P.sem["sT"][0], P.val("sT")))
        return _finish(nc, P, es)
    rk = P.dma("sync", kTc[:], kt2_h.ap()[:],
               dep=[ctxA_store] + p1_done_o +
                    [("sS", att["last_exp"][0]), ("sS", att["last_exp"][1]),
                     ("sT", att["last_pv"][0]), ("sT", att["last_pv"][1])])
    rv = P.dma("sync", vc[:], v2_h.ap()[:])
    if PHASE_LIMIT == 50:
        P._w("sync", P.fence("sO") + P.fence("sL") + P.fence("sGD"))
        return _finish(nc, P, es)

    st_x1_prev = None
    st_xq_prev = None
    out_dep_first = list(gwo)
    t_op_prev = 0
    v_x1 = 0
    for s_ in range(4):
        if 53 <= PHASE_LIMIT <= 56 and s_ >= PHASE_LIMIT - 52:
            break
        attn_slot(1, s_, kTc, vc, [rv, rk])
        if PHASE_LIMIT == 51 or 53 <= PHASE_LIMIT <= 56:
            att["cp_free"] = [("sV", P.val("sV")), ("sS", P.val("sS"))]
            continue
        rc_ld = P.dma("sync", ctxT[0:64, 0:4096],
                      ctxA_h.ap()[:, s_ * 4096:(s_ + 1) * 4096],
                      dep=[("sV", P.val("sV")), ("sT", t_op_prev)])
        if PHASE_LIMIT == 520:
            att["cp_free"] = [("sV", P.val("sV")), ("sS", P.val("sS"))]
            continue
        t_op = 0
        cv2 = 0
        for tq in range(4):
            ti = s_ * 4 + tq
            ldx = P.dma("sync", xin[:, 0:1024],
                        xq_h.ap()[ti * 128:(ti + 1) * 128, :],
                        dep=[("sV", max(ln_prev2.values()) if ti == 0 else v_x1)])
            for ni in range(2):
                for hh in range(16):
                    dd = ([rc_ld, ("sV", cv2)] + out_dep_first
                          + att["cp_free"]) \
                        if (ni == 0 and hh == 0) else ()
                    t_op = matmul(
                        cp[:, ni * 512:(ni + 1) * 512],
                        ctxT[0:64, hh * 512 + tq * 128:
                             hh * 512 + tq * 128 + 128],
                        woS[0:64, hh * 1024 + ni * 512:
                            hh * 1024 + (ni + 1) * 512],
                        start=(hh == 0), stop=(hh == 15), dep=dd,
                        inc="sT" if (ni == 1 and hh == 15) else None)
            out_dep_first = []
            if PHASE_LIMIT == 521:
                vt2 = 0
                continue
            vt2, cv2 = ln_chain(cp[:, 0:1024], lns[:],
                                dep_v=[("sT", t_op)], dep_s=[("sT", t_op)],
                                psum=True)
            if PHASE_LIMIT == 522:
                continue
            P.drain("vector")
            v_x1 = P.emit("vector", lambda e: e.tensor_add(
                x1t[:], xin[:, 0:1024], lns[:]),
                dep=[("sV", vt2), ldx, st_x1_prev],
                inc="sV")
            st_x1_prev = P.dma("scalar",
                               x1_h.ap()[ti * 128:(ti + 1) * 128, :],
                               x1t[:], dep=[("sV", v_x1)], sem="sSC")
            if PHASE_LIMIT == 52:
                continue
            vt3 = ln_chain(x1t[:], lns[:], dep_v=[("sV", v_x1)],
                           dep_s=[("sV", v_x1)])
            P.emit("vector", lambda e: e.reduce_max(
                stats[:, 9:10], lns[:], axis=AX, apply_absolute_value=True),
                dep=[("sV", vt3)])
            P.drain("vector")
            P.emit("vector", lambda e: e.tensor_scalar(
                stats[:, 10:11], stats[:, 9:10], 1e-5, None, ALU.max))
            P.drain("vector")
            P.emit("vector", lambda e, ti=ti: e.tensor_scalar(
                sxinv[:, ti:ti + 1], stats[:, 10:11], 1.0 / 127.0, None,
                ALU.mult))
            P.emit("vector", lambda e: e.reciprocal(
                stats[:, 11:12], stats[:, 10:11]))
            P.drain("vector")
            vsc = P.emit("vector", lambda e: e.tensor_scalar(
                stats[:, 12:13], stats[:, 11:12], 127.0, None, ALU.mult),
                inc="sV")
            vq = act_quant_chain(lns[:], 1024, stats[:, 12:13],
                                 dep_v=[("sV", vsc), st_xq_prev])
            st_xq_prev = P.dma("scalar",
                               xqq_h.ap()[ti * 128:(ti + 1) * 128, :],
                               qb_[:, 0:1024], dep=[("sV", vq)], sem="sSC")
        att["cp_free"] = [("sV", P.val("sV")), ("sS", P.val("sS"))]
        att["evac_extra"] = [("sT", t_op)]
        t_op_prev = t_op
    p2_done = P.fence("sSC")
    if PHASE_LIMIT <= 6 or PHASE_LIMIT in (50, 51, 52, 53, 54, 55, 56, 520, 521, 522):
        P._w("sync", P.fence("sO") + P.fence("sL") + P.fence("sGD"))
        P.emit("sync", lambda e: e.wait_ge(P.sem["sV"][0], P.val("sV")))
        P.emit("sync", lambda e: e.wait_ge(P.sem["sS"][0], P.val("sS")))
        P.emit("sync", lambda e: e.wait_ge(P.sem["sT"][0], P.val("sT")))
        return _finish(nc, P, es)

    esAtt.close()
    esA.close()
    esB = ExitStack()
    sbB = lambda name, shape, dt: esB.enter_context(
        nc.sbuf_tensor(name, shape, dt))
    wgT = sbB("wgT", [128, 8 * 4096], BF16)
    wuT = sbB("wuT", [128, 8 * 4096], BF16)
    xqT = sbB("xqT", [128, 2 * 8 * 512], BF16)
    hbuf = sbB("hbuf", [128, 2 * 4096], F32)
    hqb1 = sbB("hqb1", [128, 1024], BF16)
    sgu = sbB("sgu", [128, 4], F32)

    wl_toks = []
    for c in range(8):
        wl_toks.append(P.dmat("sync", wgT[:, c * 4096:(c + 1) * 4096],
                              wgq_h.ap()[:, c * 128:(c + 1) * 128],
                              dep=(p2_done + P.fence("sGD"))
                              if c == 0 else []))
        wl_toks.append(P.dmat("sync", wuT[:, c * 4096:(c + 1) * 4096],
                              wuq_h.ap()[:, c * 128:(c + 1) * 128]))
    wl_done = [max((t for ln2, t in wl_toks if ln2 == ln), default=0)
               for ln in {ln for ln, _ in wl_toks}]
    wl_done = [(ln, max(t for l2, t in wl_toks if l2 == ln))
               for ln in {ln for ln, _ in wl_toks}]

    hq_st_hist = {}
    lpB_free = [("sS", 0)]
    silu_guard = {0: 0, 1: 0}
    h_written = {}
    xq_free = {}
    for g in range(4):
        xo = (g % 2) * 4096
        if g == 0:
            xl_dep0 = wl_done
        elif g >= 2:
            xl_dep0 = [xq_free[g - 2]]
        else:
            xl_dep0 = []
        xl_toks = []
        for c in range(8):
            xl_toks.append(P.dmat("sync",
                                  xqT[:, xo + c * 512:xo + (c + 1) * 512],
                                  xqq_h.ap()[g * 512:(g + 1) * 512,
                                             c * 128:(c + 1) * 128],
                                  dep=xl_dep0 if c == 0 else []))
        xl_done = [(ln, max(t for l2, t in xl_toks if l2 == ln))
                   for ln in {ln for ln, _ in xl_toks}]
        for tl in range(4):
            ti = g * 4 + tl
            hb = (ti % 2) * 4096
            P.emit("vector", lambda e, ti=ti: e.tensor_tensor(
                sgu[:, 0:1], sxinv[:, ti:ti + 1], wmean3[:, 0:1], ALU.mult),
                dep=xl_done + [("sS", P.val("sS"))])
            P.drain("vector")
            vsg = P.emit("vector", lambda e, ti=ti: e.tensor_tensor(
                sgu[:, 1:2], sxinv[:, ti:ti + 1], wmean3[:, 1:2], ALU.mult),
                inc="sV")
            for nci in range(8):
                pa = nci % 2
                dd0 = [("sV", vsg)] + lpB_free
                if nci >= 2:
                    dd0 = [("sV", h_written[nci - 2])]
                for c in range(8):
                    matmul(lp[:, pa * 512:(pa + 1) * 512],
                           xqT[:, xo + c * 512 + tl * 128:
                               xo + c * 512 + tl * 128 + 128],
                           wgT[:, c * 4096 + nci * 512:
                               c * 4096 + (nci + 1) * 512],
                           start=(c == 0), stop=(c == 7),
                           dep=dd0 if c == 0 else ())
                tgu = None
                for c in range(8):
                    tgu = matmul(
                        lp[:, 1024 + pa * 512:1024 + (pa + 1) * 512],
                        xqT[:, xo + c * 512 + tl * 128:
                            xo + c * 512 + tl * 128 + 128],
                        wuT[:, c * 4096 + nci * 512:
                            c * 4096 + (nci + 1) * 512],
                        start=(c == 0), stop=(c == 7),
                        inc="sT" if c == 7 else None)
                ssdep = [("sT", tgu)]
                if silu_guard[pa]:
                    ssdep.append(("sV", silu_guard[pa]))
                ss = P.emit("scalar", lambda e, pa=pa: e.activation(
                    sq[:, pa * 512:(pa + 1) * 512],
                    lp[:, pa * 512:(pa + 1) * 512], AF.Silu,
                    scale=sgu[:, 0:1]), dep=ssdep, inc="sS")
                vh = P.emit("vector", lambda e, nci=nci, pa=pa, hb=hb:
                            e.scalar_tensor_tensor(
                                hbuf[:, hb + nci * 512:hb + (nci + 1) * 512],
                                lp[:, 1024 + pa * 512:1024 + (pa + 1) * 512],
                                sgu[:, 1:2], sq[:, pa * 512:(pa + 1) * 512],
                                ALU.mult, ALU.mult),
                            dep=[("sS", ss)], inc="sV")
                h_written[nci] = vh
                silu_guard[pa] = vh
            P.emit("vector", lambda e, hb=hb: e.reduce_max(
                stats[:, 9:10], hbuf[:, hb:hb + 4096], axis=AX,
                apply_absolute_value=True),
                dep=[("sV", h_written[7])])
            P.drain("vector")
            P.emit("vector", lambda e: e.tensor_scalar(
                stats[:, 10:11], stats[:, 9:10], 1e-5, None, ALU.max))
            P.drain("vector")
            P.emit("vector", lambda e: e.reciprocal(
                stats[:, 11:12], stats[:, 10:11]))
            P.drain("vector")
            vsc = P.emit("vector", lambda e: e.tensor_scalar(
                stats[:, 12:13], stats[:, 11:12], 127.0, None, ALU.mult),
                inc="sV")
            lastq = vsc
            for qh in range(4):
                ob = qb_ if qh % 2 == 0 else hqb1
                st_prev = (hq_st_hist.get((ti, qh - 2))
                           if qh >= 2 else hq_st_hist.get((ti - 1, qh + 2)))
                vq = act_quant_chain(hbuf[:, hb + qh * 1024:
                                          hb + (qh + 1) * 1024],
                                     1024, stats[:, 12:13],
                                     dep_v=[("sV", lastq), st_prev],
                                     ob=ob)
                hq_st_hist[(ti, qh)] = P.dma(
                    "scalar",
                    hq_h.ap()[ti * 128:(ti + 1) * 128,
                              qh * 1024:(qh + 1) * 1024],
                    ob[:, 0:1024], dep=[("sV", vq)], sem="sSC")
                lastq = vq
            lpB_free = [("sV", h_written[6]), ("sV", h_written[7])]
        xq_free[g] = ("sT", tgu)
    b1_done = P.fence("sSC")
    b1_v = P.val("sV")
    b1_s = P.val("sS")
    if PHASE_LIMIT <= 7:
        P._w("sync", P.fence("sO") + P.fence("sL") + P.fence("sGD"))
        P.emit("sync", lambda e: e.wait_ge(P.sem["sV"][0], P.val("sV")))
        P.emit("sync", lambda e: e.wait_ge(P.sem["sS"][0], P.val("sS")))
        P.emit("sync", lambda e: e.wait_ge(P.sem["sT"][0], P.val("sT")))
        return _finish(nc, P, es)

    esB.close()
    esC = ExitStack()
    sbC = lambda name, shape, dt: esC.enter_context(
        nc.sbuf_tensor(name, shape, dt))
    wdT = sbC("wdT", [128, 32 * 1024], BF16)
    hqT = sbC("hqT", [128, 2 * 32 * 512], BF16)
    x1g = sbC("x1g", [128, 2 * 4 * 1024], F32)
    outb = sbC("outb", [128, 2 * 1024], F32)

    wdl_toks = []
    for ic in range(32):
        wdl_toks.append(P.dmat("sync", wdT[:, ic * 1024:(ic + 1) * 1024],
                               wdq_h.ap()[:, ic * 128:(ic + 1) * 128],
                               dep=b1_done + [("sV", b1_v), ("sS", b1_s)]
                               if ic == 0 else []))
    wdl_done = [(ln, max(t for l2, t in wdl_toks if l2 == ln))
                for ln in {ln for ln, _ in wdl_toks}]

    out_store = {}
    lpC_free = [("sV", 0)]
    hq_free = {}
    xg_free = {}
    for g in range(4):
        ho = (g % 2) * 16384
        xgo = (g % 2) * 4096
        if g == 0:
            hl_dep0 = wdl_done
        elif g >= 2:
            hl_dep0 = [hq_free[g - 2]]
        else:
            hl_dep0 = []
        hl_toks = []
        for ic in range(32):
            hl_toks.append(P.dmat("sync",
                                  hqT[:, ho + ic * 512:ho + (ic + 1) * 512],
                                  hq_h.ap()[g * 512:(g + 1) * 512,
                                            ic * 128:(ic + 1) * 128],
                                  dep=hl_dep0 if ic == 0 else []))
        hl_done = [(ln, max(t for l2, t in hl_toks if l2 == ln))
                   for ln in {ln for ln, _ in hl_toks}]
        if g >= 2:
            xg_dep0 = hl_done + [xg_free[g - 2]]
        else:
            xg_dep0 = hl_done
        xg_toks = []
        for a_ in range(4):
            xg_toks.append(P.dma(
                "sync", x1g[:, xgo + a_ * 1024:xgo + (a_ + 1) * 1024],
                x1_h.ap()[g * 512 + a_ * 128:g * 512 + (a_ + 1) * 128, :],
                dep=xg_dep0 if a_ == 0 else []))
        xg = [(ln, max(t for l2, t in xg_toks if l2 == ln))
              for ln in {ln for ln, _ in xg_toks}]
        for tl in range(4):
            ti = g * 4 + tl
            t_dn = None
            for ni in range(2):
                for ic in range(32):
                    dd = (xg + lpC_free) \
                        if (ic == 0 and ni == 0) else ()
                    t_dn = matmul(
                        lp[:, ni * 512:(ni + 1) * 512],
                        hqT[:, ho + ic * 512 + tl * 128:
                            ho + ic * 512 + tl * 128 + 128],
                        wdT[:, ic * 1024 + ni * 512:
                            ic * 1024 + (ni + 1) * 512],
                        start=(ic == 0), stop=(ic == 31), dep=dd,
                        inc="sT" if (ni == 1 and ic == 31) else None)
            vt4, cv4 = ln_chain(lp[:, 0:1024], lns[:],
                                dep_v=[("sT", t_dn)], dep_s=[("sT", t_dn)],
                                psum=True)
            po = tl % 2
            dep_out = [("sV", vt4), out_store.get(ti - 2)]
            vfin = P.emit("vector",
                          lambda e, tl=tl, po=po, xgo=xgo: e.tensor_add(
                              outb[:, po * 1024:(po + 1) * 1024],
                              x1g[:, xgo + tl * 1024:xgo + (tl + 1) * 1024],
                              lns[:]),
                          dep=dep_out, inc="sV")
            lpC_free = [("sV", cv4)]
            out_store[ti] = P.dma("scalar",
                                  out_h.ap()[ti * 128:(ti + 1) * 128, :],
                                  outb[:, po * 1024:(po + 1) * 1024],
                                  dep=[("sV", vfin)], sem="sSC")
        hq_free[g] = ("sT", t_dn)
        xg_free[g] = ("sV", vfin)
    esC.close()

    P._w("sync", P.fence("sSC") + P.fence("sGO") + P.fence("sO"))

    return _finish(nc, P, es)


def _finish(nc, P, es):
    with nc.Block() as block:
        @block.sync
        def _(sync):
            for fn in P.q["sync"]:
                fn(sync)

        @block.gpsimd
        def _(gpsimd):
            for fn in P.q["gpsimd"]:
                fn(gpsimd)

        @block.vector
        def _(vector):
            for fn in P.q["vector"]:
                fn(vector)

        @block.scalar
        def _(scalar):
            for fn in P.q["scalar"]:
                fn(scalar)

        @block.tensor
        def _(tensor):
            for fn in P.q["tensor"]:
                fn(tensor)

    return nc


def _host_inputs(core, x, wqkv, wo, wg, wu, wd):
    b, par = core // 2, core % 2
    js = JS[par]
    rows = np.concatenate([np.arange(512 * j, 512 * j + 512) for j in js])
    xb = np.ascontiguousarray(x[b])
    xq = np.ascontiguousarray(xb[rows])
    masks = np.zeros((4, 8, 128, 512), dtype=np.float32)
    for s_, j in enumerate(js):
        cap = CAPS[s_]
        q0 = 512 * j
        for kbrel in range(8):
            kb = cap - 8 + kbrel
            kabs = kb * 128 + np.arange(128)[:, None]
            qabs = q0 + np.arange(512)[None, :]
            masks[s_, kbrel] = (kabs <= qabs)
    masks = masks.transpose(2, 0, 1, 3).reshape(128, 4 * 8 * 512)
    masks = np.ascontiguousarray(masks).astype(ml_dtypes.bfloat16)
    pck = (np.arange(NKB)[None, :] * 128
           + np.arange(128)[:, None]).astype(np.float32)
    tq_tiles = np.concatenate([np.arange(4 * j, 4 * j + 4) for j in js])
    pcq = (tq_tiles[None, :] * 128
           + np.arange(128)[:, None]).astype(np.float32)
    fracb = np.ascontiguousarray(np.broadcast_to(
        (2.0 * np.arange(32) / 64.0).astype(np.float32), (128, 32)))
    ident = np.eye(128, dtype=np.float32).astype(ml_dtypes.bfloat16)
    return dict(xkv=xb, xq=xq, wqkv=wqkv, wo=wo, wg=wg, wu=wu, wd=wd,
                masks=masks, poscolk=pck, poscolq=pcq, fracb=fracb,
                ident=ident), rows


def kernel(L0_attn_x, L0_attn_wqkv, L0_attn_wo, L0_ffn_w_gate, L0_ffn_w_up,
           L0_ffn_w_down):
    x = np.ascontiguousarray(np.asarray(L0_attn_x, dtype=np.float32))
    wqkv = np.ascontiguousarray(np.asarray(L0_attn_wqkv, dtype=np.float32))
    wo = np.ascontiguousarray(np.asarray(L0_attn_wo, dtype=np.float32))
    wg = np.ascontiguousarray(np.asarray(L0_ffn_w_gate, dtype=np.float32))
    wu = np.ascontiguousarray(np.asarray(L0_ffn_w_up, dtype=np.float32))
    wd = np.ascontiguousarray(np.asarray(L0_ffn_w_down, dtype=np.float32))

    nc = build_nc()
    in_maps, rowmaps = [], []
    for core in range(8):
        im, rows = _host_inputs(core, x, wqkv, wo, wg, wu, wd)
        in_maps.append(im)
        rowmaps.append(rows)
    res = run_bass_kernel_spmd(nc, in_maps, core_ids=list(range(8)))
    out = np.empty((B, S, DM), dtype=np.float32)
    for core in range(8):
        b = core // 2
        out[b][rowmaps[core]] = np.asarray(res.results[core]["out"],
                                           dtype=np.float32)
    return out
